# version 24
# baseline (speedup 1.0000x reference)
import numpy as np
import ml_dtypes
import concourse.bass as bass
import concourse.mybir as mybir
from concourse.bass_utils import run_bass_kernel_spmd

import concourse.bass_utils as _bu
if not getattr(_bu, "_nosim_patched", False):
    _orig_run_command = _bu.run_command
    def _run_command_nosim(cmd, *a, **kw):
        cmd = ["--enable-birsim=false" if c == "--enable-birsim=true" else c for c in cmd]
        cmd = ["--enable-ldw-opt=true" if c == "--enable-ldw-opt=false" else c for c in cmd]
        return _orig_run_command(cmd, *a, **kw)
    _bu.run_command = _run_command_nosim
    _bu._nosim_patched = True

F32 = mybir.dt.float32
BF16 = mybir.dt.bfloat16
AF = mybir.ActivationFunctionType
ALU = mybir.AluOpType
AX = mybir.AxisListType

NCORE = 8
T = 2048
D = 1024
KC = 8
TILE = 512
NT = T // TILE
EPS_LN = 1e-5
EPS_ATTN = 1e-6
QUEUES = ("T", "V", "S", "P", "Y")
ALL_BLOCKS = ("l0a", "l0f", "l1a", "l1f")

SMALL_KEYS = frozenset(
    ["l1f_mcol", "l0a_carry", "l0a_aeff", "l0a_yeff", "l1a_zgg", "l1a_zg",
     "l0a_exch", "l1f_afft", "l1f_oht", "l1a_ae"]
    + [f"l1a_srun{h}" for h in range(16)])


class Prog:

    def __init__(self, nc):
        self.nc = nc
        self.ops = []
        self.writer = {}
        self.readers = {}
        self.barrier = ()

    def op(self, q, fn, r=(), w=(), dma=False, cc=False, tag="", out_dma=False):
        excl = [k for k in r if k.startswith("PB")]
        if excl:
            r = [k for k in r if not k.startswith("PB")]
            w = list(w) + excl
        deps = set(self.barrier)
        for k in r:
            i = self.writer.get(k)
            if i is not None:
                deps.add(i)
        for k in w:
            i = self.writer.get(k)
            if i is not None:
                deps.add(i)
            for j in self.readers.get(k, ()):
                deps.add(j)
        idx = len(self.ops)
        for k in w:
            self.writer[k] = idx
            self.readers[k] = []
        for k in r:
            self.readers.setdefault(k, []).append(idx)
        deps.discard(idx)
        small = any(k in SMALL_KEYS for k in w)
        self.ops.append(dict(q=q, fn=fn, deps=sorted(deps), dma=dma, cc=cc,
                             tag=tag, out_dma=out_dma, small=small))
        return idx

    def set_barrier(self):
        last = {}
        for i, o in enumerate(self.ops):
            last[o["q"]] = i
        self.barrier = tuple(sorted(last.values()))

    def finalize_and_emit(self):
        import contextlib
        nc = self.nc
        n = len(self.ops)
        for i, o in enumerate(self.ops):
            o["deps"] = [d for d in o["deps"]
                         if self.ops[d]["q"] != o["q"]
                         or self.ops[d]["dma"] or self.ops[d]["cc"]
                         or self.ops[d]["small"]]
        needed = [False] * n
        for o in self.ops:
            for d in o["deps"]:
                needed[d] = True
        n_dma_pool = {"P": 8, "Y": 6, "S": 2, "V": 1, "T": 1}

        stack = contextlib.ExitStack()
        qsem = {q: stack.enter_context(nc.semaphore(f"qs_{q}")) for q in QUEUES}
        pools = {q: [stack.enter_context(nc.semaphore(f"dp_{q}{i}"))
                     for i in range(n_dma_pool[q])] for q in QUEUES}
        ncc = sum(1 for o in self.ops if o["cc"])
        ccsems = [stack.enter_context(nc.semaphore(f"cc{i}")) for i in range(ncc)]

        qcount = {q: 0 for q in QUEUES}
        pool_idx = {q: 0 for q in QUEUES}
        pool_count = {}
        cci = 0
        for i, o in enumerate(self.ops):
            q = o["q"]
            if o["cc"]:
                o["sem"], o["val"], o["inc"] = ccsems[cci], 1, 1
                cci += 1
            elif o["dma"]:
                s = pools[q][pool_idx[q] % len(pools[q])]
                pool_idx[q] += 1
                pool_count[id(s)] = pool_count.get(id(s), 0) + 16
                o["sem"], o["val"], o["inc"] = s, pool_count[id(s)], 16
            elif needed[i]:
                qcount[q] += 1
                o["sem"], o["val"], o["inc"] = qsem[q], qcount[q], 1
            else:
                o["sem"] = None
        highwater = {q: {} for q in QUEUES}
        for i, o in enumerate(self.ops):
            q = o["q"]
            waits = []
            for d in o["deps"]:
                od = self.ops[d]
                if od["sem"] is None:
                    assert od["q"] == q and d < i, (d, i, od["tag"], o["tag"])
                    continue
                sem, val = od["sem"], od["val"]
                hw = highwater[q]
                if hw.get(id(sem), -1) < val:
                    waits.append((sem, val))
                    hw[id(sem)] = val
            o["waits"] = waits
        out_waits = {}
        for o in self.ops:
            if o["out_dma"]:
                prev = out_waits.get(id(o["sem"]), (None, 0))[1]
                out_waits[id(o["sem"])] = (o["sem"], max(prev, o["val"]))

        with stack:
            with nc.Block() as block:
                def section(qname):
                    def body(eng):
                        for o in self.ops:
                            if o["q"] != qname:
                                continue
                            for sem, val in o["waits"]:
                                eng.wait_ge(sem, val)
                            inst = o["fn"](eng)
                            if o["sem"] is not None:
                                inst.then_inc(o["sem"], o["inc"])
                        if qname == "Y":
                            for sem, val in out_waits.values():
                                eng.wait_ge(sem, val)
                    return body
                block.tensor(section("T"))
                block.vector(section("V"))
                block.scalar(section("S"))
                block.gpsimd(section("P"))
                block.sync(section("Y"))


class Ctx:

    def __init__(self, nc):
        self.nc = nc
        self.P = Prog(nc)
        self.dram_cache = {}
        self._bank = [nc.alloc_psum_tensor(f"pbank{i}", [128, 512], F32) for i in range(8)]
        self._rb = 0

    def dram_in(self, name, shape, dtype=None):
        if name not in self.dram_cache:
            self.dram_cache[name] = self.nc.dram_tensor(
                name, shape, dtype or F32, kind="ExternalInput").ap()
        return self.dram_cache[name]

    def bank(self, i, p0=0, p1=128, f0=0, f1=512):
        return self._bank[i][p0:p1, f0:f1]

    def bank_bf(self, i, p0=0, p1=128, f0=0, f1=1024):
        return self._bank[i][:].bitcast(BF16)[p0:p1, f0:f1]

    def rb(self, lo=0, hi=6):
        self._rb = (self._rb + 1) % (hi - lo)
        return lo + self._rb

    def sb(self, name, shape, dtype):
        h = self.nc.alloc_sbuf_tensor(name, shape, dtype)
        if not hasattr(self, "reg"):
            self.reg = {}
        self.reg[name] = h
        return h

    def arena_init(self):
        probe = self.nc.alloc_sbuf_tensor("arena_probe", [128, 1], F32)
        self.arena_base = self.nc.lookup_mloc(probe).addr + 32
        self.arena_limit = self.arena_base + self.nc.sbuf_bytes_remaining - 512
        self.arena_off = self.arena_base

    def arena_reset(self):
        self.arena_off = self.arena_base
        self.P.set_barrier()

    def ab(self, name, shape, dtype):
        nbytes = int(np.prod(shape[1:])) * (2 if dtype == BF16 else 4)
        off = (self.arena_off + 31) & ~31
        assert off + nbytes <= self.arena_limit, (
            f"arena overflow: {name} needs {nbytes}B at {off}, limit {self.arena_limit}")
        h = self.nc.alloc_sbuf_tensor_at(name, shape, dtype, offset=off)
        self.arena_off = off + nbytes
        self.reg[name] = h
        return h

    def mm(self, bi, lhsT, rhs, start, stop, r, bank_key=None, bank_ap=None):
        ap = bank_ap if bank_ap is not None else self.bank(bi)
        bk = bank_key or f"PB{bi}"
        return self.P.op("T", lambda e, ap=ap, lhsT=lhsT, rhs=rhs, start=start, stop=stop:
                         e.matmul(ap, lhsT, rhs, start=start, stop=stop),
                         r=list(r), w=[bk], tag="mm")

    def transpose(self, bank_ap, src, ident, r, w):
        return self.P.op("T", lambda e, a=bank_ap, s=src, i=ident: e.transpose(a, s, i),
                         r=list(r) + ["ident"], w=list(w), tag="tr")

    def act(self, dst, src, func, r, w, bias=0.0, scale=1.0, q="S"):
        return self.P.op(q, lambda e, d=dst, s=src, f=func, b=bias, sc=scale:
                         e.activation(d, s, f, bias=b, scale=sc),
                         r=list(r), w=list(w), tag=f"act.{func}")

    def tt(self, dst, a, b, op, r, w, q="V"):
        return self.P.op(q, lambda e, d=dst, a=a, b=b, op=op: e.tensor_tensor(d, a, b, op),
                         r=list(r), w=list(w), tag=f"tt.{op}")

    def ts(self, dst, a, s1, s2, op0, r, w, op1=ALU.bypass, q="V"):
        return self.P.op(q, lambda e, d=dst, a=a, s1=s1, s2=s2, op0=op0, op1=op1:
                         e.tensor_scalar(d, a, s1, s2, op0, op1),
                         r=list(r), w=list(w), tag=f"ts.{op0}")

    def stt(self, dst, in0, scal, in1, op0, op1, r, w, q="V"):
        return self.P.op(q, lambda e, d=dst, i0=in0, sc=scal, i1=in1, op0=op0, op1=op1:
                         e.scalar_tensor_tensor(d, i0, sc, i1, op0, op1),
                         r=list(r), w=list(w), tag="stt")

    def scan(self, dst, d0, d1, init, op0, op1, r, w, q="V"):
        return self.P.op(q, lambda e, d=dst, d0=d0, d1=d1, init=init, op0=op0, op1=op1:
                         e.tensor_tensor_scan(d, d0, d1, init, op0, op1),
                         r=list(r), w=list(w), tag="scan")

    def red(self, dst, src, op, r, w, q="V"):
        return self.P.op(q, lambda e, d=dst, s=src, op=op: e.tensor_reduce(d, s, AX.X, op),
                         r=list(r), w=list(w), tag=f"red.{op}")

    def rcp(self, dst, src, r, w):
        nc = self.nc
        def fn(e, d=dst, s=src):
            with nc.allow_low_precision(reason="bf16 norm inverse"):
                return e.reciprocal(d, s)
        return self.P.op("V", fn, r=list(r), w=list(w), tag="rcp")

    def copy(self, dst, src, r, w, q="V"):
        if q == "S":
            return self.P.op(q, lambda e, d=dst, s=src: e.copy(d, s),
                             r=list(r), w=list(w), tag="cp")
        return self.P.op(q, lambda e, d=dst, s=src: e.tensor_copy(d, s),
                         r=list(r), w=list(w), tag="cp")

    def memset(self, dst, val, w, q="V"):
        return self.P.op(q, lambda e, d=dst, v=val: e.memset(d, v), r=[], w=list(w), tag="ms")

    def dma(self, q, dst, src, r, w, out=False):
        return self.P.op(q, lambda e, d=dst, s=src: e.dma_start(out=d, in_=s),
                         r=list(r), w=list(w), dma=True, out_dma=out, tag="dma")

    def cc(self, kind, ins, outs, r, w):
        return self.P.op("P", lambda e, k=kind, i=ins, o=outs:
                         e.collective_compute(k, ALU.bypass, replica_groups=[list(range(NCORE))],
                                              ins=i, outs=o),
                         r=list(r), w=list(w), cc=True, tag=f"cc.{kind}")



def ln_stats(C, srcs, sqs, rms=False):
    bS, bQ = C.rb(), C.rb()
    for c in range(KC):
        sap, skey = srcs[c]
        qap, qkey = sqs[c]
        C.mm(bS, C.ones_bf[:], sap, start=(c == 0), stop=(c == KC - 1), r=[skey, "onesb"])
        C.mm(bQ, C.ones_bf[:], qap, start=(c == 0), stop=(c == KC - 1), r=[qkey, "onesb"])
    return ln_tilemath(C, bS, bQ, rms=rms)


def ln_tilemath(C, bS, bQ, pi=0, rms=False):
    st = C.lnst[pi]
    P = f"ln{pi}"
    mu, var, rstd, m = st["mu"], st["var"], st["rstd"], st["m"]
    C.act(mu[:], C.bank(bS), AF.Copy, scale=1.0 / D, r=[f"PB{bS}"], w=[P + ".mu"])
    C.act(var[:], C.bank(bQ), AF.Copy, scale=1.0 / D, r=[f"PB{bQ}"], w=[P + ".var"])
    C.tt(m[:], mu[:], mu[:], ALU.mult, r=[P + ".mu"], w=[P + ".m"])
    C.tt(var[:], var[:], m[:], ALU.subtract, r=[P + ".var", P + ".m"], w=[P + ".var"])
    C.act(m[:], var[:], AF.Ln, bias=C.c_eps_ln, r=[P + ".var", "consts"], w=[P + ".m"])
    C.act(rstd[:], m[:], AF.Exp, scale=-0.5, r=[P + ".m"], w=[P + ".rstd"])
    C.act(st["mub"][:], C.bank(bS), AF.Copy, scale=1.0 / D, r=[f"PB{bS}"], w=[P + ".mub"])
    if not rms:
        C.act(st["scb"][:], rstd[:], AF.Copy, r=[P + ".rstd"], w=[P + ".scb"])
        return {"mu": P + ".mub", "sc": P + ".scb", "scap": st["scb"],
                "muap": st["mub"], "pi": pi}
    n2, msc = st["n2"], st["msc"]
    C.ts(n2[:], var[:], float(D), None, ALU.mult, r=[P + ".var"], w=["ln.n2"])
    C.tt(n2[:], n2[:], rstd[:], ALU.mult, r=["ln.n2", P + ".rstd"], w=["ln.n2"])
    C.tt(n2[:], n2[:], rstd[:], ALU.mult, r=["ln.n2", P + ".rstd"], w=["ln.n2"])
    C.act(n2[:], n2[:], AF.Ln, r=["ln.n2"], w=["ln.n2"])
    C.act(n2[:], n2[:], AF.Exp, scale=0.5, r=["ln.n2"], w=["ln.n2"])
    C.ts(n2[:], n2[:], 1e-12, None, ALU.max, r=["ln.n2"], w=["ln.n2"])
    C.act(n2[:], n2[:], AF.Ln, r=["ln.n2"], w=["ln.n2"])
    C.act(n2[:], n2[:], AF.Exp, scale=-1.0, bias=C.c_ln32, r=["ln.n2", "consts"], w=["ln.n2"])
    C.tt(st["scb"][:], rstd[:], n2[:], ALU.mult, r=[P + ".rstd", "ln.n2"], w=[P + ".scb"])
    return {"mu": P + ".mub", "sc": P + ".scb", "scap": st["scb"],
            "muap": st["mub"], "pi": pi}


def ln_apply_dst(C, srcs, keys, dsts, gamma=None):
    pi = keys["pi"]
    st = C.lnst[pi]
    P = f"ln{pi}"
    mu, scap = keys["muap"], keys["scap"]
    for c in range(KC):
        sap, skey = srcs[c]
        dap, dkey = dsts[c]
        q = "V"
        d1 = st["d1"][c % 2]
        C.tt(d1[:], sap, mu[:], ALU.subtract, r=[skey, keys["mu"]], w=[f"ln.d1{c % 2}"], q=q)
        if gamma is None:
            C.tt(dap, d1[:], scap[:], ALU.mult, r=[f"ln.d1{c % 2}", keys["sc"]], w=[dkey], q=q)
        else:
            C.stt(dap, d1[:], gamma[:, c:c + 1], scap[:], ALU.mult, ALU.mult,
                  r=[f"ln.d1{c % 2}", "gamma", keys["sc"]], w=[dkey], q="V")


def ln_apply_resid(C, srcs, keys, t0, TT):
    pi = keys["pi"]
    st = C.lnst[pi]
    P = f"ln{pi}"
    mu, scap = keys["muap"], keys["scap"]
    for c in range(KC):
        sap, skey = srcs[c]
        q = "V"
        d1 = st["d1"][c % 2]
        d2 = C.lnd2f[c % 2]
        xap = C.xres[c][:, t0:t0 + TT]
        xkey = f"xres{c}t{t0 // TILE}"
        C.tt(d1[:], sap, mu[:], ALU.subtract, r=[skey, keys["mu"]], w=[f"ln.d1{c % 2}"], q=q)
        C.tt(d2[:], d1[:], scap[:], ALU.mult, r=[f"ln.d1{c % 2}", keys["sc"]], w=[f"ln.d2f{c % 2}"], q=q)
        C.tt(xap, xap, d2[:], ALU.add, r=[f"ln.d2f{c % 2}", xkey], w=[xkey], q=q)


def ln_full(C, src_handles, src_prefix, t0, TT, mode, dsts=None, gamma=None, rms=False, resid_t0=None):
    bS, bQ = C.rb(), C.rb()
    srcs = []
    pi = (t0 // TILE) % 2
    sfx = f"t{t0 // TILE}" if src_prefix == "xres" else ""
    for c in range(KC):
        sap = src_handles[c][:, t0:t0 + TT]
        skey = f"{src_prefix}{c}{sfx}"
        sq = C.lnsq[pi * 2 + c % 2]
        sqk = f"ln{pi}.sq{c % 2}"
        C.act(sq[:], sap, AF.Square, r=[skey], w=[sqk])
        C.mm(bS, C.ones_bf[:], sap, start=(c == 0), stop=(c == KC - 1), r=[skey, "onesb"])
        C.mm(bQ, C.ones_bf[:], sq[:], start=(c == 0), stop=(c == KC - 1), r=[sqk, "onesb"])
        srcs.append((sap, skey))
    keys = ln_tilemath(C, bS, bQ, pi=pi, rms=rms)
    if mode == "dst":
        dpairs = [(dsts[c][0], dsts[c][1]) for c in range(KC)]
        ln_apply_dst(C, srcs, keys, dpairs, gamma=gamma)
    else:
        ln_apply_resid(C, srcs, keys, t0 if resid_t0 is None else resid_t0, TT)



def make_fixtures(C, nc):
    C.xres = [C.sb(f"xres{c}", [128, T], BF16) for c in range(KC)]
    C.ones_f = C.sb("ones_f", [128, 128], F32)
    C.ones_bf = C.sb("ones_bf", [128, 128], BF16)
    C.ident = C.sb("ident", [128, 128], F32)
    C.identb = C.sb("identb", [128, 128], BF16)
    C.consts = C.sb("consts", [128, 8], F32)
    C.c_eps_ln = C.consts[:, 0:1]
    C.c_eps_attn = C.consts[:, 1:2]
    C.c_zero = C.consts[:, 2:3]
    C.c_one = C.consts[:, 3:4]
    C.c_ln32 = C.consts[:, 4:5]
    C.lnst = []
    shared_d1 = [C.sb(f"lnst_d1{i}", [128, TILE], BF16) for i in range(2)]
    shared_d2 = [C.sb(f"lnst_d2{i}", [128, TILE], BF16) for i in range(2)]
    C.lnd2f = [C.sb(f"lnst_d2f{i}", [128, TILE], F32) for i in range(2)]
    for pi in range(2):
        d = {k: C.sb(f"lnst{pi}_{k}", [128, TILE], F32)
             for k in ("mu", "var", "rstd", "m")}
        d["mub"] = C.sb(f"lnst{pi}_mub", [128, TILE], BF16)
        d["scb"] = C.sb(f"lnst{pi}_scb", [128, TILE], BF16)
        d["d1"], d["d2"] = shared_d1, shared_d2
        C.lnst.append(d)
    C.lnsq = [C.sb(f"lnsq{i}", [128, TILE], BF16) for i in range(4)]
    C.xio_sb = [C.sb(f"xio_sb{i}", [128, D], F32) for i in range(2)]

    IN_FIX = nc.dram_tensor("fix", [128, 8 + 128 + 128], F32, kind="ExternalInput").ap()
    C.dmaq = "Y"
    C.P.op("Y", lambda e: e.dma_start(out=C.consts[:], in_=IN_FIX[:, 0:8]),
           r=[], w=["consts"], dma=True, tag="dma.consts")
    C.P.op("Y", lambda e: e.dma_start(out=C.ones_f[:], in_=IN_FIX[:, 8:136]),
           r=[], w=["onesf"], dma=True, tag="dma.ones")
    C.P.op("P", lambda e: e.dma_start(out=C.ones_bf[:], in_=IN_FIX[:, 8:136]),
           r=[], w=["onesb"], dma=True, tag="dma.onesb")
    C.P.op("Y", lambda e: e.dma_start(out=C.ident[:], in_=IN_FIX[:, 136:264]),
           r=[], w=["ident"], dma=True, tag="dma.ident")
    C.P.op("P", lambda e: e.dma_start(out=C.identb[:], in_=IN_FIX[:, 136:264]),
           r=[], w=["identb"], dma=True, tag="dma.identb")


def emit_load_x_first(C, nc):
    XIN = nc.dram_tensor("xin", [T, D], F32, kind="ExternalInput").ap()
    xin_sb = C.xio_sb
    ntc = T // 128
    for tc in range(ntc):
        buf = xin_sb[tc % 2]
        bk = f"xiob{tc % 2}"
        C.dma("Y", buf[:], XIN[tc * 128:(tc + 1) * 128, :], r=[], w=[bk])
        for c in range(KC):
            bi = C.rb()
            C.transpose(self_ap(C, bi), buf[:, c * 128:(c + 1) * 128], C.ident[:],
                        r=[bk], w=[f"PB{bi}"])
            C.copy(C.xres[c][:, tc * 128:(tc + 1) * 128], C.bank(bi, 0, 128, 0, 128),
                   r=[f"PB{bi}"], w=[f"xres{c}t{tc // 4}"])


def self_ap(C, bi):
    return C.bank(bi, 0, 128, 0, 128)


def emit_load_x_fm(C, nc):
    XFM = nc.dram_tensor("xfm", [KC, 128, T], BF16, kind="ExternalInput").ap()
    for c in range(KC):
        C.dma("Y", C.xres[c][:], XFM[c], r=[], w=[f"xres{c}t{ti}" for ti in range(NT)])


def emit_store_x_fm(C, nc):
    XFO = nc.dram_tensor("xfo", [KC, 128, T], BF16, kind="ExternalOutput").ap()
    for c in range(KC):
        C.dma("Y", XFO[c], C.xres[c][:], r=[f"xres{c}t{ti}" for ti in range(NT)], w=[], out=True)


def emit_store_x_last(C, nc):
    OUT = nc.dram_tensor("out", [T, D], F32, kind="ExternalOutput").ap()
    out_sb = C.xio_sb
    ntc = T // 128
    for tc in range(ntc):
        buf = out_sb[tc % 2]
        bk = f"xiob{tc % 2}"
        for c in range(KC):
            bi = C.rb()
            bap = C.bank_bf(bi, 0, 128, 0, 128)
            C.transpose(bap, C.xres[c][:, tc * 128:(tc + 1) * 128], C.identb[:],
                        r=[f"xres{c}t{tc // 4}", "identb"], w=[f"PB{bi}"])
            C.copy(buf[:, c * 128:(c + 1) * 128], bap, r=[f"PB{bi}"], w=[bk])
        C.dma("Y", OUT[tc * 128:(tc + 1) * 128, :], buf[:], r=[bk], w=[], out=True)



def emit_l0_ffn(C, nc):
    U1 = nc.dram_tensor("u1", [D, 256], F32, kind="ExternalInput").ap()
    V1 = nc.dram_tensor("v1", [256, 4096], F32, kind="ExternalInput").ap()
    U2 = nc.dram_tensor("u2", [4096, 256], F32, kind="ExternalInput").ap()
    V2 = nc.dram_tensor("v2", [256, D], F32, kind="ExternalInput").ap()
    B1 = nc.dram_tensor("b1f", [128, 32], F32, kind="ExternalInput").ap()
    B2 = nc.dram_tensor("b2f", [128, 8], F32, kind="ExternalInput").ap()
    u1 = C.ab("u1sb", [128, 8 * 256], BF16)
    v1 = C.ab("v1sb", [128, 2 * 4096], BF16)
    u2 = C.ab("u2sb", [128, 32 * 256], BF16)
    v2 = C.ab("v2sb", [128, 2 * D], BF16)
    b1 = C.ab("b1sb", [128, 32], F32)
    b2 = C.ab("b2sb", [128, 8], F32)
    for k in range(8):
        C.dma("P", u1[:, k * 256:(k + 1) * 256], U1[k * 128:(k + 1) * 128, :], r=[], w=["u1"])
    for k in range(2):
        C.dma("P", v1[:, k * 4096:(k + 1) * 4096], V1[k * 128:(k + 1) * 128, :], r=[], w=["v1"])
    for k in range(32):
        C.dma("P", u2[:, k * 256:(k + 1) * 256], U2[k * 128:(k + 1) * 128, :], r=[], w=["u2"])
    for k in range(2):
        C.dma("P", v2[:, k * D:(k + 1) * D], V2[k * 128:(k + 1) * 128, :], r=[], w=["v2"])
    C.dma("Y", b1[:], B1, r=[], w=["b1"])
    C.dma("Y", b2[:], B2, r=[], w=["b2"])

    xn = [C.ab(f"ffn_xn{c}", [128, 2 * TILE], BF16) for c in range(KC)]
    p_sb = C.ab("ffn_p", [128, 2 * 2 * TILE], BF16)
    h_sb = C.ab("ffn_h", [128, 32 * 2 * TILE], BF16)
    q_sb = C.ab("ffn_q", [128, 2 * 2 * TILE], BF16)
    o_sb = [C.ab(f"ffn_o{c}", [128, 2 * TILE], BF16) for c in range(KC)]
    osq = [C.ab(f"ffn_osq{c}", [128, 2 * TILE], BF16) for c in range(KC)]

    phases = globals().get("_L0F_PHASES", 6)
    for tg in range(2):
        for tl in range(2):
            t = 2 * tg + tl
            dsts = [(xn[c][:, tl * TILE:(tl + 1) * TILE], f"ffn_xn{c}s{tl}") for c in range(KC)]
            ln_full(C, C.xres, "xres", t * TILE, TILE, "dst", dsts=dsts)
        if phases < 2:
            continue
        for rc in range(2):
            bts = [C.rb(), C.rb()]
            for kc in range(KC):
                lhs = u1[:, kc * 256 + rc * 128: kc * 256 + (rc + 1) * 128]
                for tl in range(2):
                    C.mm(bts[tl], lhs, xn[kc][:, tl * TILE:(tl + 1) * TILE],
                         start=(kc == 0), stop=(kc == KC - 1), r=["u1", f"ffn_xn{kc}s{tl}"])
            for tl in range(2):
                C.copy(p_sb[:, (2 * rc + tl) * TILE:(2 * rc + tl + 1) * TILE],
                       C.bank(bts[tl]), r=[f"PB{bts[tl]}"], w=[f"ffn_p{rc}s{tl}"], q="S")
        if phases < 3:
            continue
        for fc in range(32):
            bts = [C.rb(), C.rb()]
            for rc in range(2):
                lhs = v1[:, rc * 4096 + fc * 128: rc * 4096 + (fc + 1) * 128]
                for tl in range(2):
                    C.mm(bts[tl], lhs, p_sb[:, (2 * rc + tl) * TILE:(2 * rc + tl + 1) * TILE],
                         start=(rc == 0), stop=(rc == 1), r=["v1", f"ffn_p{rc}s{tl}"])
            for tl in range(2):
                C.act(h_sb[:, (2 * fc + tl) * TILE:(2 * fc + tl + 1) * TILE], C.bank(bts[tl]),
                      AF.Gelu_apprx_tanh, bias=b1[:, fc:fc + 1],
                      r=[f"PB{bts[tl]}", "b1"], w=[f"ffn_h{fc}s{tl}"])
        if phases < 4:
            continue
        for rc in range(2):
            bts = [C.rb(), C.rb()]
            for fc in range(32):
                lhs = u2[:, fc * 256 + rc * 128: fc * 256 + (rc + 1) * 128]
                for tl in range(2):
                    C.mm(bts[tl], lhs, h_sb[:, (2 * fc + tl) * TILE:(2 * fc + tl + 1) * TILE],
                         start=(fc == 0), stop=(fc == 31), r=["u2", f"ffn_h{fc}s{tl}"])
            for tl in range(2):
                C.copy(q_sb[:, (2 * rc + tl) * TILE:(2 * rc + tl + 1) * TILE],
                       C.bank(bts[tl]), r=[f"PB{bts[tl]}"], w=[f"ffn_q{rc}s{tl}"], q="S")
        if phases < 5:
            continue
        for dc in range(KC):
            bts = [C.rb(), C.rb()]
            for rc in range(2):
                lhs = v2[:, rc * D + dc * 128: rc * D + (dc + 1) * 128]
                for tl in range(2):
                    C.mm(bts[tl], lhs, q_sb[:, (2 * rc + tl) * TILE:(2 * rc + tl + 1) * TILE],
                         start=(rc == 0), stop=(rc == 1), r=["v2", f"ffn_q{rc}s{tl}"])
            for tl in range(2):
                C.ts(o_sb[dc][:, tl * TILE:(tl + 1) * TILE], C.bank(bts[tl]), b2[:, dc:dc + 1],
                     None, ALU.add, r=[f"PB{bts[tl]}", "b2"], w=[f"ffn_o{dc}s{tl}"])
                C.act(osq[dc][:, tl * TILE:(tl + 1) * TILE], C.bank(bts[tl]), AF.Square,
                      bias=b2[:, dc:dc + 1], r=[f"PB{bts[tl]}", "b2"], w=[f"ffn_osq{dc}s{tl}"])
        if phases < 6:
            continue
        for tl in range(2):
            t = 2 * tg + tl
            srcs = [(o_sb[c][:, tl * TILE:(tl + 1) * TILE], f"ffn_o{c}s{tl}") for c in range(KC)]
            sqs = [(osq[c][:, tl * TILE:(tl + 1) * TILE], f"ffn_osq{c}s{tl}") for c in range(KC)]
            keys = ln_stats(C, srcs, sqs)
            ln_apply_resid(C, srcs, keys, t * TILE, TILE)



def build_program(blocks, first, last):
    nc = bass.Bass("TRN2", target_bir_lowering=False, debug=False)
    C = Ctx(nc)
    make_fixtures(C, nc)
    C.arena_init()
    if first:
        emit_load_x_first(C, nc)
    else:
        emit_load_x_fm(C, nc)
    for b in blocks:
        C.arena_reset()
        if b == "l0f":
            emit_l0_ffn(C, nc)
        elif b == "l0a":
            emit_l0_attn(C, nc)
        elif b == "l1a":
            emit_l1_attn(C, nc)
        elif b == "l1f":
            emit_l1_ffn(C, nc)
    if last:
        emit_store_x_last(C, nc)
    else:
        emit_store_x_fm(C, nc)
    for name in globals().get("_TAPS", ()):
        h = C.reg[name]
        tp = nc.dram_tensor(f"tap_{name}", list(h[:].shape), h[:].dtype,
                            kind="ExternalOutput").ap()
        C.dma("Y", tp, h[:], r=[name], w=[], out=True)
    C.P.finalize_and_emit()
    return nc



def _fix_array():
    fix = np.zeros((128, 8 + 128 + 128), np.float32)
    fix[:, 0] = EPS_LN
    fix[:, 1] = EPS_ATTN
    fix[:, 2] = 0.0
    fix[:, 3] = 1.0
    fix[:, 4] = float(np.log(np.sqrt(1024.0)))
    fix[:, 8:136] = 1.0
    fix[:, 136:264] = np.eye(128, dtype=np.float32)
    return fix


def _common_weight_arrays(inp):
    a = {}
    a["u1"] = np.ascontiguousarray(inp["L0_ffn_U1"])
    a["v1"] = np.ascontiguousarray(inp["L0_ffn_V1"])
    a["u2"] = np.ascontiguousarray(inp["L0_ffn_U2"])
    a["v2"] = np.ascontiguousarray(inp["L0_ffn_V2"])
    a["b1f"] = np.ascontiguousarray(inp["L0_ffn_b1"].reshape(32, 128).T)
    a["b2f"] = np.ascontiguousarray(inp["L0_ffn_b2"].reshape(8, 128).T)
    a["wqa"] = np.ascontiguousarray(inp["L0_attn_w_qkva"].T)
    a["wq1"] = np.ascontiguousarray(np.asarray(inp["L1_attn_w_qkv"]).T)
    a["bq1"] = np.ascontiguousarray(np.asarray(inp["L1_attn_b_qkv"]).reshape(24, 128).T)
    a["wo1"] = np.ascontiguousarray(np.asarray(inp["L1_attn_w_out"]).T)
    a["bo1"] = np.ascontiguousarray(np.asarray(inp["L1_attn_b_out"]).reshape(8, 128).T)
    ii = np.arange(128)
    a["mut"] = (ii[:, None] <= ii[None, :]).astype(np.float32)
    a["hones"] = ((ii[:, None] < 64) == (ii[None, :] < 64)).astype(np.float32)
    a["cent"] = np.ascontiguousarray(np.asarray(inp["L1_ffn_centroids"]).T)
    a["lng"] = np.ascontiguousarray(inp["L1_ffn_ln_g"])
    a["lnb"] = np.ascontiguousarray(inp["L1_ffn_ln_b"])
    a["w1"] = np.ascontiguousarray(inp["L1_ffn_w1"])
    a["w2"] = np.ascontiguousarray(inp["L1_ffn_w2"])
    a["b1e"] = np.ascontiguousarray(
        np.asarray(inp["L1_ffn_b1"]).reshape(8, 32, 128).transpose(2, 0, 1).reshape(128, 256))
    a["b2e"] = np.ascontiguousarray(inp["L1_ffn_b2"])
    sel = np.zeros((8, 8 * 128), np.float32)
    for e in range(8):
        sel[e, e * 128:(e + 1) * 128] = 1.0
    a["sel"] = sel
    a["gam"] = np.ascontiguousarray(np.asarray(inp["L0_attn_gamma"]).reshape(8, 128).T)
    a["fix"] = _fix_array()
    return a


def _launch(blocks, first, last, per_core_maps):
    nc = build_program(blocks, first, last)
    res = run_bass_kernel_spmd(nc, per_core_maps, core_ids=list(range(NCORE)))
    return res.results


def kernel(**inputs):
    x = np.asarray(inputs["L0_attn_x"]).reshape(16384, D).astype(np.float32)
    shards = [np.ascontiguousarray(x[i * T:(i + 1) * T]) for i in range(NCORE)]
    wa = _common_weight_arrays(inputs)

    launches = globals().get("_TEST_LAUNCHES") or [(ALL_BLOCKS, True, True)]
    rmasks = []
    for r in range(NCORE):
        m = np.zeros((128, NCORE), np.float32)
        m[:, :r] = 1.0
        rmasks.append(m)
    cur = [{"xin": shards[i], "rmask": rmasks[i], "rmaskc": 1.0 - rmasks[i], **wa}
           for i in range(NCORE)]
    outs = None
    for blocks, first, last in launches:
        results = _launch(blocks, first, last, cur)
        if last:
            outs = [r["out"] for r in results]
        else:
            cur = [{**cur[i], "xfm": results[i]["xfo"]} for i in range(NCORE)]
    y = np.concatenate(outs, axis=0).reshape(1, 16384, D)
    return y.astype(np.float32)


def emit_l0_attn(C, nc):
    WQA = nc.dram_tensor("wqa", [D, 3 * D], F32, kind="ExternalInput").ap()
    GAM = nc.dram_tensor("gam", [128, 8], F32, kind="ExternalInput").ap()
    RM = C.dram_in("rmask", [128, 8])
    RMC = C.dram_in("rmaskc", [128, 8])
    gam = C.ab("gam_sb", [128, 8], F32)
    n2t = C.ab("lnst_n2", [128, TILE], F32)
    msct = C.ab("lnst_msc", [128, TILE], F32)
    C.lnst[0]["n2"] = C.lnst[1]["n2"] = n2t
    C.lnst[0]["msc"] = C.lnst[1]["msc"] = msct
    rm = C.ab("rm_sb", [128, 8], F32)
    rmc = C.ab("rmc_sb", [128, 8], F32)
    C.dma("Y", gam[:], GAM, r=[], w=["gamma"])
    C.dma("Y", rm[:], RM, r=[], w=["rmask"])
    C.dma("Y", rmc[:], RMC, r=[], w=["rmaskc"])

    h = [C.ab(f"l0a_h{c}", [128, T], BF16) for c in range(KC)]
    s1 = [C.ab(f"l0a_s1{c}", [128, T], BF16) for c in range(KC)]
    s2 = [C.ab(f"l0a_s2{i}", [128, T], BF16) for i in range(4)]
    qb = [C.ab(f"l0a_q{i}", [128, T], BF16) for i in range(2)]
    ab_ = [C.ab(f"l0a_a{i}", [128, T], F32) for i in range(2)]
    kvb = [C.ab(f"l0a_kv{i}", [128, T], BF16) for i in range(2)]
    yb = [C.ab(f"l0a_y{i}", [128, T], BF16) for i in range(2)]
    pb = [C.ab(f"l0a_p{i}", [128, T], BF16) for i in range(2)]
    wq = [C.ab(f"l0a_w{i}", [128, 8 * 128], BF16) for i in range(4)]
    exch = [C.ab(f"l0a_exch{i}", [128, 8], F32) for i in range(2)]
    gsb = [C.ab(f"l0a_gsb{i}", [128, 64], F32) for i in range(2)]
    carry = C.ab("l0a_carry", [128, 8], F32)
    aeff = C.ab("l0a_aeff", [128, 4], F32)
    yeff = C.ab("l0a_yeff", [128, 4], F32)

    cc_in = [nc.dram_tensor(f"l0a_ccin{hh}", [128, 8], F32) for hh in range(2)]
    cc_out = [nc.dram_tensor(f"l0a_ccout{hh}", [1024, 8], F32, addr_space="Shared")
              for hh in range(2)]

    for t in range(NT):
        dsts = [(h[c][:, t * TILE:(t + 1) * TILE], f"l0a_h{c}t{t}") for c in range(KC)]
        ln_full(C, C.xres, "xres", t * TILE, TILE, "dst", dsts=dsts, gamma=gam, rms=True)

    combined = [False, False]

    def fixup_half(hh, ci):
        cs = slice(hh * 4, hh * 4 + 4)
        if not combined[hh]:
            combined[hh] = True
            C.memset(carry[:, cs], 0.0, w=["l0a_carry"])
            for j in range(NCORE):
                C.ts(aeff[:], gsb[hh][:, j * 8:j * 8 + 4], rm[:, j:j + 1], rmc[:, j:j + 1],
                     ALU.mult, r=[f"l0a_gsb{hh}", "rmask", "rmaskc"], w=["l0a_aeff"], op1=ALU.add)
                C.ts(yeff[:], gsb[hh][:, j * 8 + 4:j * 8 + 8], rm[:, j:j + 1], None,
                     ALU.mult, r=[f"l0a_gsb{hh}", "rmask"], w=["l0a_yeff"])
                C.tt(carry[:, cs], carry[:, cs], aeff[:], ALU.mult, r=["l0a_carry", "l0a_aeff"], w=["l0a_carry"])
                C.tt(carry[:, cs], carry[:, cs], yeff[:], ALU.add, r=["l0a_carry", "l0a_yeff"], w=["l0a_carry"])
        c = hh * 4 + ci
        C.stt(s1[c][:], s2[ci][:], carry[:, c:c + 1], s1[c][:], ALU.mult, ALU.add,
              r=[f"l0a_s2h{hh}{ci}", "l0a_carry", f"l0a_s1{c}"], w=[f"l0a_s1{c}"])

    for half in range(2):
        for ci in range(4):
            c = half * 4 + ci
            si = c % 2
            kq, ka, kkv, ky, kp_ = (f"l0a_qb{si}", f"l0a_ab{si}", f"l0a_kvb{si}",
                                    f"l0a_yb{si}", f"l0a_pb{si}")
            for gi, (eg, dstbuf, dkey, evac) in enumerate([
                    (c, qb[si], kq, "copy"),
                    (16 + c, ab_[si], ka, "sigmoid"),
                    (8 + c, kvb[si], kkv, "copy")]):
                wbuf = wq[(3 * c + gi) % 4]
                wkey = f"l0a_w{(3 * c + gi) % 4}"
                C.dma("P", wbuf[:].rearrange("p (k e) -> p k e", k=KC),
                      WQA[:, eg * 128:(eg + 1) * 128].rearrange("(k p) e -> p k e", p=128),
                      r=[], w=[wkey])
                bts = [C.rb(), C.rb(), C.rb(), C.rb()]
                for k in range(KC):
                    lhs = wbuf[:, k * 128:(k + 1) * 128]
                    for t in range(NT):
                        C.mm(bts[t], lhs, h[k][:, t * TILE:(t + 1) * TILE],
                             start=(k == 0), stop=(k == KC - 1), r=[wkey, f"l0a_h{k}t{t}"])
                for t in range(NT):
                    dap = dstbuf[:, t * TILE:(t + 1) * TILE]
                    if evac == "sigmoid":
                        C.act(dap, C.bank(bts[t]), AF.Sigmoid, r=[f"PB{bts[t]}"], w=[ka])
                    else:
                        C.act(dap, C.bank(bts[t]), AF.Copy, r=[f"PB{bts[t]}"], w=[dkey])
            C.scan(yb[si][:], ab_[si][:], kvb[si][:], C.c_zero, ALU.mult, ALU.add,
                   r=[ka, kkv, "consts"], w=[ky])
            C.scan(pb[si][:], ab_[si][:], ab_[si][:], C.c_one, ALU.mult, ALU.min,
                   r=[ka, "consts"], w=[kp_])
            C.tt(s1[c][:], qb[si][:], yb[si][:], ALU.mult, r=[kq, ky], w=[f"l0a_s1{c}"])
            if half == 1:
                fixup_half(0, ci)
            C.tt(s2[ci][:], qb[si][:], pb[si][:], ALU.mult, r=[kq, kp_], w=[f"l0a_s2h{half}{ci}"])
            C.copy(exch[half][:, ci:ci + 1], pb[si][:, T - 1:T], r=[kp_], w=[f"l0a_exch{half}"])
            C.copy(exch[half][:, 4 + ci:5 + ci], yb[si][:, T - 1:T], r=[ky], w=[f"l0a_exch{half}"])
        C.dma("P", cc_in[half][:, :], exch[half][:], r=[f"l0a_exch{half}"], w=[f"l0a_ccin{half}"])
        C.cc("AllGather", [cc_in[half][:, :]], [cc_out[half][:, :]],
             r=[f"l0a_ccin{half}"], w=[f"l0a_ccout{half}"])
        C.dma("Y", gsb[half][:].rearrange("p (r f) -> p r f", r=8),
              cc_out[half][:, :].rearrange("(r p) f -> p r f", p=128),
              r=[f"l0a_ccout{half}"], w=[f"l0a_gsb{half}"])
    for ci in range(4):
        fixup_half(1, ci)
    for t in range(NT):
        ln_full(C, s1, "l0a_s1", t * TILE, TILE, "resid")


def emit_l0_ffn(C, nc):
    U1 = nc.dram_tensor("u1", [D, 256], F32, kind="ExternalInput").ap()
    V1 = nc.dram_tensor("v1", [256, 4096], F32, kind="ExternalInput").ap()
    U2 = nc.dram_tensor("u2", [4096, 256], F32, kind="ExternalInput").ap()
    V2 = nc.dram_tensor("v2", [256, D], F32, kind="ExternalInput").ap()
    B1 = nc.dram_tensor("b1f", [128, 32], F32, kind="ExternalInput").ap()
    B2 = nc.dram_tensor("b2f", [128, 8], F32, kind="ExternalInput").ap()
    u1 = C.ab("u1sb", [128, 8 * 256], BF16)
    v1 = C.ab("v1sb", [128, 2 * 4096], BF16)
    u2 = C.ab("u2sb", [128, 32 * 256], BF16)
    v2 = C.ab("v2sb", [128, 2 * D], BF16)
    b1 = C.ab("b1sb", [128, 32], F32)
    b2 = C.ab("b2sb", [128, 8], F32)
    for k in range(8):
        C.dma("P", u1[:, k * 256:(k + 1) * 256], U1[k * 128:(k + 1) * 128, :], r=[], w=["u1"])
    for k in range(2):
        C.dma("P", v1[:, k * 4096:(k + 1) * 4096], V1[k * 128:(k + 1) * 128, :], r=[], w=["v1"])
    for k in range(32):
        C.dma("P", u2[:, k * 256:(k + 1) * 256], U2[k * 128:(k + 1) * 128, :], r=[], w=["u2"])
    for k in range(2):
        C.dma("P", v2[:, k * D:(k + 1) * D], V2[k * 128:(k + 1) * 128, :], r=[], w=["v2"])
    C.dma("Y", b1[:], B1, r=[], w=["b1"])
    C.dma("Y", b2[:], B2, r=[], w=["b2"])

    xn = [C.ab(f"ffn_xn{c}", [128, TILE], BF16) for c in range(KC)]
    p_sb = C.ab("ffn_p", [128, 2 * TILE], BF16)
    h_sb = C.ab("ffn_h", [128, 32 * TILE], BF16)
    q_sb = C.ab("ffn_q", [128, 2 * TILE], BF16)
    o_sb = [C.ab(f"ffn_o{c}", [128, TILE], BF16) for c in range(KC)]
    osq = [C.ab(f"ffn_osq{c}", [128, TILE], BF16) for c in range(KC)]

    phases = globals().get("_L0F_PHASES", 6)
    ntt = globals().get("_L0F_NT", NT)
    for t in range(ntt):
        t0 = t * TILE
        dsts = [(xn[c][:], f"ffn_xn{c}") for c in range(KC)]
        ln_full(C, C.xres, "xres", t0, TILE, "dst", dsts=dsts)
        if phases < 2:
            continue
        for rc in range(2):
            bi = C.rb()
            for kc in range(KC):
                C.mm(bi, u1[:, kc * 256 + rc * 128: kc * 256 + (rc + 1) * 128], xn[kc][:],
                     start=(kc == 0), stop=(kc == KC - 1), r=["u1", f"ffn_xn{kc}"])
            C.copy(p_sb[:, rc * TILE:(rc + 1) * TILE], C.bank(bi), r=[f"PB{bi}"], w=[f"ffn_p{rc}"], q="S")
        if phases < 3:
            continue
        for fc in range(32):
            bi = C.rb()
            for rc in range(2):
                C.mm(bi, v1[:, rc * 4096 + fc * 128: rc * 4096 + (fc + 1) * 128],
                     p_sb[:, rc * TILE:(rc + 1) * TILE],
                     start=(rc == 0), stop=(rc == 1), r=["v1", f"ffn_p{rc}"])
            C.act(h_sb[:, fc * TILE:(fc + 1) * TILE], C.bank(bi), AF.Gelu_apprx_tanh,
                  bias=b1[:, fc:fc + 1], r=[f"PB{bi}", "b1"], w=[f"ffn_h{fc}"])
        if phases < 4:
            continue
        for rc in range(2):
            bi = C.rb()
            for fc in range(32):
                C.mm(bi, u2[:, fc * 256 + rc * 128: fc * 256 + (rc + 1) * 128],
                     h_sb[:, fc * TILE:(fc + 1) * TILE],
                     start=(fc == 0), stop=(fc == 31), r=["u2", f"ffn_h{fc}"])
            C.copy(q_sb[:, rc * TILE:(rc + 1) * TILE], C.bank(bi), r=[f"PB{bi}"], w=[f"ffn_q{rc}"], q="S")
        if phases < 5:
            continue
        for dc in range(KC):
            bi = C.rb()
            for rc in range(2):
                C.mm(bi, v2[:, rc * D + dc * 128: rc * D + (dc + 1) * 128],
                     q_sb[:, rc * TILE:(rc + 1) * TILE],
                     start=(rc == 0), stop=(rc == 1), r=["v2", f"ffn_q{rc}"])
            C.ts(o_sb[dc][:], C.bank(bi), b2[:, dc:dc + 1], None, ALU.add,
                 r=[f"PB{bi}", "b2"], w=[f"ffn_o{dc}"])
            C.act(osq[dc][:], C.bank(bi), AF.Square, bias=b2[:, dc:dc + 1],
                  r=[f"PB{bi}", "b2"], w=[f"ffn_osq{dc}"])
        if phases < 6:
            continue
        srcs = [(o_sb[c][:], f"ffn_o{c}") for c in range(KC)]
        sqs = [(osq[c][:], f"ffn_osq{c}") for c in range(KC)]
        keys = ln_stats(C, srcs, sqs)
        ln_apply_resid(C, srcs, keys, t0, TILE)



def build_program(blocks, first, last):
    nc = bass.Bass("TRN2", target_bir_lowering=False, debug=False)
    C = Ctx(nc)
    make_fixtures(C, nc)
    C.arena_init()
    if first:
        emit_load_x_first(C, nc)
    else:
        emit_load_x_fm(C, nc)
    for b in blocks:
        C.arena_reset()
        if b == "l0f":
            emit_l0_ffn(C, nc)
        elif b == "l0a":
            emit_l0_attn(C, nc)
        elif b == "l1a":
            emit_l1_attn(C, nc)
        elif b == "l1f":
            emit_l1_ffn(C, nc)
    if last:
        emit_store_x_last(C, nc)
    else:
        emit_store_x_fm(C, nc)
    for name in globals().get("_TAPS", ()):
        h = C.reg[name]
        tp = nc.dram_tensor(f"tap_{name}", list(h[:].shape), h[:].dtype,
                            kind="ExternalOutput").ap()
        C.dma("Y", tp, h[:], r=[name], w=[], out=True)
    C.P.finalize_and_emit()
    return nc



def _fix_array():
    fix = np.zeros((128, 8 + 128 + 128), np.float32)
    fix[:, 0] = EPS_LN
    fix[:, 1] = EPS_ATTN
    fix[:, 2] = 0.0
    fix[:, 3] = 1.0
    fix[:, 4] = float(np.log(np.sqrt(1024.0)))
    fix[:, 8:136] = 1.0
    fix[:, 136:264] = np.eye(128, dtype=np.float32)
    return fix


def _common_weight_arrays(inp):
    a = {}
    a["u1"] = np.ascontiguousarray(inp["L0_ffn_U1"])
    a["v1"] = np.ascontiguousarray(inp["L0_ffn_V1"])
    a["u2"] = np.ascontiguousarray(inp["L0_ffn_U2"])
    a["v2"] = np.ascontiguousarray(inp["L0_ffn_V2"])
    a["b1f"] = np.ascontiguousarray(inp["L0_ffn_b1"].reshape(32, 128).T)
    a["b2f"] = np.ascontiguousarray(inp["L0_ffn_b2"].reshape(8, 128).T)
    a["wqa"] = np.ascontiguousarray(inp["L0_attn_w_qkva"].T)
    a["wq1"] = np.ascontiguousarray(np.asarray(inp["L1_attn_w_qkv"]).T)
    a["bq1"] = np.ascontiguousarray(np.asarray(inp["L1_attn_b_qkv"]).reshape(24, 128).T)
    a["wo1"] = np.ascontiguousarray(np.asarray(inp["L1_attn_w_out"]).T)
    a["bo1"] = np.ascontiguousarray(np.asarray(inp["L1_attn_b_out"]).reshape(8, 128).T)
    ii = np.arange(128)
    a["mut"] = (ii[:, None] <= ii[None, :]).astype(np.float32)
    a["hones"] = ((ii[:, None] < 64) == (ii[None, :] < 64)).astype(np.float32)
    a["cent"] = np.ascontiguousarray(np.asarray(inp["L1_ffn_centroids"]).T)
    a["lng"] = np.ascontiguousarray(inp["L1_ffn_ln_g"])
    a["lnb"] = np.ascontiguousarray(inp["L1_ffn_ln_b"])
    a["w1"] = np.ascontiguousarray(inp["L1_ffn_w1"])
    a["w2"] = np.ascontiguousarray(inp["L1_ffn_w2"])
    a["b1e"] = np.ascontiguousarray(
        np.asarray(inp["L1_ffn_b1"]).reshape(8, 32, 128).transpose(2, 0, 1).reshape(128, 256))
    a["b2e"] = np.ascontiguousarray(inp["L1_ffn_b2"])
    sel = np.zeros((8, 8 * 128), np.float32)
    for e in range(8):
        sel[e, e * 128:(e + 1) * 128] = 1.0
    a["sel"] = sel
    a["gam"] = np.ascontiguousarray(np.asarray(inp["L0_attn_gamma"]).reshape(8, 128).T)
    a["fix"] = _fix_array()
    return a


def _launch(blocks, first, last, per_core_maps):
    nc = build_program(blocks, first, last)
    res = run_bass_kernel_spmd(nc, per_core_maps, core_ids=list(range(NCORE)))
    return res.results


def kernel(**inputs):
    x = np.asarray(inputs["L0_attn_x"]).reshape(16384, D).astype(np.float32)
    shards = [np.ascontiguousarray(x[i * T:(i + 1) * T]) for i in range(NCORE)]
    wa = _common_weight_arrays(inputs)

    launches = globals().get("_TEST_LAUNCHES") or [(ALL_BLOCKS, True, True)]
    rmasks = []
    for r in range(NCORE):
        m = np.zeros((128, NCORE), np.float32)
        m[:, :r] = 1.0
        rmasks.append(m)
    cur = [{"xin": shards[i], "rmask": rmasks[i], "rmaskc": 1.0 - rmasks[i], **wa}
           for i in range(NCORE)]
    outs = None
    for blocks, first, last in launches:
        results = _launch(blocks, first, last, cur)
        if last:
            outs = [r["out"] for r in results]
        else:
            cur = [{**cur[i], "xfm": results[i]["xfo"]} for i in range(NCORE)]
    y = np.concatenate(outs, axis=0).reshape(1, 16384, D)
    return y.astype(np.float32)


def emit_l0_attn(C, nc):
    WQA = nc.dram_tensor("wqa", [D, 3 * D], F32, kind="ExternalInput").ap()
    GAM = nc.dram_tensor("gam", [128, 8], F32, kind="ExternalInput").ap()
    RM = C.dram_in("rmask", [128, 8])
    RMC = C.dram_in("rmaskc", [128, 8])
    gam = C.ab("gam_sb", [128, 8], F32)
    n2t = C.ab("lnst_n2", [128, TILE], F32)
    msct = C.ab("lnst_msc", [128, TILE], F32)
    C.lnst[0]["n2"] = C.lnst[1]["n2"] = n2t
    C.lnst[0]["msc"] = C.lnst[1]["msc"] = msct
    rm = C.ab("rm_sb", [128, 8], F32)
    rmc = C.ab("rmc_sb", [128, 8], F32)
    C.dma("Y", gam[:], GAM, r=[], w=["gamma"])
    C.dma("Y", rm[:], RM, r=[], w=["rmask"])
    C.dma("Y", rmc[:], RMC, r=[], w=["rmaskc"])

    h = [C.ab(f"l0a_h{c}", [128, T], BF16) for c in range(KC)]
    s1 = [C.ab(f"l0a_s1{c}", [128, T], BF16) for c in range(KC)]
    s2 = [C.ab(f"l0a_s2{i}", [128, T], BF16) for i in range(4)]
    qb = C.ab("l0a_q", [128, T], F32)
    ab = C.ab("l0a_a", [128, T], F32)
    kvb = C.ab("l0a_kv", [128, T], F32)
    yb = C.ab("l0a_y", [128, T], F32)
    pb = C.ab("l0a_p", [128, T], F32)
    wq = [C.ab(f"l0a_w{i}", [128, 8 * 128], BF16) for i in range(4)]
    exch = C.ab("l0a_exch", [128, 8], F32)
    gsb = C.ab("l0a_gsb", [128, 64], F32)
    carry = C.ab("l0a_carry", [128, 4], F32)
    aeff = C.ab("l0a_aeff", [128, 4], F32)
    yeff = C.ab("l0a_yeff", [128, 4], F32)

    cc_in = [nc.dram_tensor(f"l0a_ccin{hh}", [128, 8], F32) for hh in range(2)]
    cc_out = [nc.dram_tensor(f"l0a_ccout{hh}", [1024, 8], F32, addr_space="Shared")
              for hh in range(2)]

    for t in range(NT):
        dsts = [(h[c][:, t * TILE:(t + 1) * TILE], f"l0a_h{c}t{t}") for c in range(KC)]
        ln_full(C, C.xres, "xres", t * TILE, TILE, "dst", dsts=dsts, gamma=gam, rms=True)

    combined = [False, False]

    def fixup_half(hh, ci):
        cs = slice(hh * 4, hh * 4 + 4)
        if not combined[hh]:
            combined[hh] = True
            C.memset(carry[:, cs], 0.0, w=["l0a_carry"])
            for j in range(NCORE):
                C.ts(aeff[:], gsb[hh][:, j * 8:j * 8 + 4], rm[:, j:j + 1], rmc[:, j:j + 1],
                     ALU.mult, r=[f"l0a_gsb{hh}", "rmask", "rmaskc"], w=["l0a_aeff"], op1=ALU.add)
                C.ts(yeff[:], gsb[hh][:, j * 8 + 4:j * 8 + 8], rm[:, j:j + 1], None,
                     ALU.mult, r=[f"l0a_gsb{hh}", "rmask"], w=["l0a_yeff"])
                C.tt(carry[:, cs], carry[:, cs], aeff[:], ALU.mult, r=["l0a_carry", "l0a_aeff"], w=["l0a_carry"])
                C.tt(carry[:, cs], carry[:, cs], yeff[:], ALU.add, r=["l0a_carry", "l0a_yeff"], w=["l0a_carry"])
        c = hh * 4 + ci
        C.stt(s1[c][:], s2[ci][:], carry[:, c:c + 1], s1[c][:], ALU.mult, ALU.add,
              r=[f"l0a_s2h{hh}{ci}", "l0a_carry", f"l0a_s1{c}"], w=[f"l0a_s1{c}"])

    for half in range(2):
        for ci in range(4):
            c = half * 4 + ci
            for gi, (eg, dstbuf, dkey, evac) in enumerate([
                    (c, qb, "l0a_qb", "copy"),
                    (16 + c, ab, "l0a_ab", "sigmoid"),
                    (8 + c, kvb, "l0a_kvb", "copy")]):
                wbuf = wq[(3 * c + gi) % 4]
                wkey = f"l0a_w{(3 * c + gi) % 4}"
                C.dma("P", wbuf[:].rearrange("p (k e) -> p k e", k=KC),
                      WQA[:, eg * 128:(eg + 1) * 128].rearrange("(k p) e -> p k e", p=128),
                      r=[], w=[wkey])
                for t in range(NT):
                    bi = C.rb()
                    for k in range(KC):
                        C.mm(bi, wbuf[:, k * 128:(k + 1) * 128],
                             h[k][:, t * TILE:(t + 1) * TILE],
                             start=(k == 0), stop=(k == KC - 1), r=[wkey, f"l0a_h{k}t{t}"])
                    dap = dstbuf[:, t * TILE:(t + 1) * TILE]
                    if evac == "sigmoid":
                        C.act(dap, C.bank(bi), AF.Sigmoid, r=[f"PB{bi}"], w=[dkey])
                    else:
                        C.act(dap, C.bank(bi), AF.Copy, r=[f"PB{bi}"], w=[dkey])
            C.scan(yb[:], ab[:], kvb[:], C.c_zero, ALU.mult, ALU.add,
                   r=["l0a_ab", "l0a_kvb", "consts"], w=["l0a_yb"])
            C.scan(pb[:], ab[:], ab[:], C.c_one, ALU.mult, ALU.min,
                   r=["l0a_ab", "consts"], w=["l0a_pb"])
            C.tt(s1[c][:], qb[:], yb[:], ALU.mult, r=["l0a_qb", "l0a_yb"], w=[f"l0a_s1{c}"])
            C.tt(s2[ci][:], qb[:], pb[:], ALU.mult, r=["l0a_qb", "l0a_pb"], w=[f"l0a_s2{ci}"])
            C.copy(exch[:, ci:ci + 1], pb[:, T - 1:T], r=["l0a_pb"], w=["l0a_exch"])
            C.copy(exch[:, 4 + ci:5 + ci], yb[:, T - 1:T], r=["l0a_yb"], w=["l0a_exch"])
        C.dma("P", cc_in[half][:, :], exch[:], r=["l0a_exch"], w=[f"l0a_ccin{half}"])
        C.cc("AllGather", [cc_in[half][:, :]], [cc_out[half][:, :]],
             r=[f"l0a_ccin{half}"], w=[f"l0a_ccout{half}"])
        C.dma("P", gsb[:].rearrange("p (r f) -> p r f", r=8),
              cc_out[half][:, :].rearrange("(r p) f -> p r f", p=128),
              r=[f"l0a_ccout{half}"], w=["l0a_gsb"])
        C.memset(carry[:], 0.0, w=["l0a_carry"])
        for j in range(NCORE):
            C.ts(aeff[:], gsb[:, j * 8:j * 8 + 4], rm[:, j:j + 1], rmc[:, j:j + 1],
                 ALU.mult, r=["l0a_gsb", "rmask", "rmaskc"], w=["l0a_aeff"], op1=ALU.add)
            C.ts(yeff[:], gsb[:, j * 8 + 4:j * 8 + 8], rm[:, j:j + 1], None,
                 ALU.mult, r=["l0a_gsb", "rmask"], w=["l0a_yeff"])
            C.tt(carry[:], carry[:], aeff[:], ALU.mult, r=["l0a_carry", "l0a_aeff"], w=["l0a_carry"])
            C.tt(carry[:], carry[:], yeff[:], ALU.add, r=["l0a_carry", "l0a_yeff"], w=["l0a_carry"])
        for ci in range(4):
            c = half * 4 + ci
            C.stt(s1[c][:], s2[ci][:], carry[:, ci:ci + 1], s1[c][:], ALU.mult, ALU.add,
                  r=[f"l0a_s2{ci}", "l0a_carry", f"l0a_s1{c}"], w=[f"l0a_s1{c}"])
    for t in range(NT):
        ln_full(C, s1, "l0a_s1", t * TILE, TILE, "resid")


def emit_l1_attn(C, nc):
    WQ = nc.dram_tensor("wq1", [D, 3 * D], F32, kind="ExternalInput").ap()
    BQ = nc.dram_tensor("bq1", [128, 24], F32, kind="ExternalInput").ap()
    WO = nc.dram_tensor("wo1", [D, D], F32, kind="ExternalInput").ap()
    BO = nc.dram_tensor("bo1", [128, 8], F32, kind="ExternalInput").ap()
    MUT = nc.dram_tensor("mut", [128, 128], F32, kind="ExternalInput").ap()
    HON = nc.dram_tensor("hones", [128, 128], F32, kind="ExternalInput").ap()
    RM = C.dram_in("rmask", [128, 8])
    bq = C.ab("l1a_bq", [128, 24], F32)
    bo = C.ab("l1a_bo", [128, 8], F32)
    mut = C.ab("l1a_mut", [128, 128], BF16)
    hon = C.ab("l1a_hon", [128, 128], BF16)
    rm = C.ab("l1a_rm", [128, 8], F32)
    C.dma("Y", bq[:], BQ, r=[], w=["l1a_bq"])
    C.dma("Y", bo[:], BO, r=[], w=["l1a_bo"])
    C.dma("P", mut[:], MUT, r=[], w=["l1a_mut"])
    C.dma("P", hon[:], HON, r=[], w=["l1a_hon"])
    C.dma("Y", rm[:], RM, r=[], w=["l1a_rm"])

    xn = [C.ab(f"l1a_xn{c}", [128, 2 * TILE], BF16) for c in range(KC)]
    qp = [C.ab(f"l1a_qp{c}", [128, T], BF16) for c in range(KC)]
    kp = [C.ab(f"l1a_kp{c}", [128, T], BF16) for c in range(KC)]
    vt = [C.ab(f"l1a_vt{tc}", [128, D], BF16) for tc in range(16)]
    ktst = C.ab("l1a_ktst", [128, D], BF16)[:]
    wq = [C.ab(f"l1a_w{i}", [128, 8 * 128], BF16) for i in range(2)]
    st1 = C.lnst[0]["d1"]
    st2 = C.lnst[0]["d2"]
    vtmp = C.ab("l1a_vtmp", [128, TILE], BF16)
    zg = C.ab("l1a_zg", [128, 8], F32)
    zgg = C.ab("l1a_zgg", [128, 8], F32)
    gz = C.ab("l1a_gz", [128, 64], F32)
    sg = C.ab("l1a_sg", [128, D // 2], F32)
    srun = C.ab("l1a_srun", [128, D // 2], F32)
    srb = C.ab("l1a_srb", [128, D // 2], BF16)
    kcum = [C.ab(f"l1a_kcum{i}", [128, T], BF16) for i in range(2)]
    sstage_ap = kcum[0][:].bitcast(F32)[0:128, 0:D // 2]
    sm = C.ab("l1a_sm", [128, 256], BF16)
    nst0 = C.lnst[0]["d1"][0]
    nst = [nst0, nst0]
    o_sb = xn
    aeff = C.ab("l1a_ae", [128, 8], F32)

    ccz_in = nc.dram_tensor("l1a_ccz_in", [128, 8], F32)
    ccz_out = nc.dram_tensor("l1a_ccz_out", [1024, 8], F32, addr_space="Shared")
    ccs_in = nc.dram_tensor("l1a_ccs_in", [128, D // 2], F32)
    ccs_out = nc.dram_tensor("l1a_ccs_out", [1024, D // 2], F32, addr_space="Shared")

    for tg in range(2):
      for tl in range(2):
        t = 2 * tg + tl
        dsts = [(xn[c][:, tl * TILE:(tl + 1) * TILE], f"l1a_xn{c}t{t}") for c in range(KC)]
        ln_full(C, C.xres, "xres", t * TILE, TILE, "dst", dsts=dsts)
      for eg in range(24):
        wbuf = wq[eg % 2]
        wkey = f"l1a_w{eg % 2}"
        C.dma("P", wbuf[:].rearrange("p (k e) -> p k e", k=KC),
              WQ[:, eg * 128:(eg + 1) * 128].rearrange("(k p) e -> p k e", p=128),
              r=[], w=[wkey])
        bts = [C.rb(), C.rb()]
        for k in range(KC):
            lhs = wbuf[:, k * 128:(k + 1) * 128]
            for tl in range(2):
                t = 2 * tg + tl
                C.mm(bts[tl], lhs, xn[k][:, tl * TILE:(tl + 1) * TILE],
                     start=(k == 0), stop=(k == KC - 1), r=[wkey, f"l1a_xn{k}t{t}"])
        bcol = bq[:, eg:eg + 1]
        for tl in range(2):
            t = 2 * tg + tl
            bi = bts[tl]
            if eg < 16:
                c = eg % 8
                dst = (qp if eg < 8 else kp)[c]
                dkey = f"l1a_{'qp' if eg < 8 else 'kp'}{c}"
                s1, s2 = C.lnd2f[t % 2], st2[t % 2]
                k1, k2 = f"ln.d2f{t % 2}", f"ln.d2{t % 2}"
                C.ts(s1[:], C.bank(bi), bcol, 0.0, ALU.add, op1=ALU.min,
                     r=[f"PB{bi}", "l1a_bq"], w=[k1])
                C.act(s1[:], s1[:], AF.Exp, r=[k1], w=[k1])
                C.ts(s2[:], C.bank(bi), bcol, 0.0, ALU.add, op1=ALU.max,
                     r=[f"PB{bi}", "l1a_bq"], w=[k2])
                C.tt(dst[:, t * TILE:(t + 1) * TILE], s1[:], s2[:], ALU.add,
                     r=[k1, k2], w=[dkey])
            else:
                ec = eg - 16
                C.ts(vtmp[:], C.bank(bi), bcol, None, ALU.add,
                     r=[f"PB{bi}", "l1a_bq"], w=["l1a_vtmp"])
                bj = C.rb()
                for stl in range(4):
                    C.transpose(C.bank_bf(bj, 0, 128, stl * 128, (stl + 1) * 128),
                                vtmp[:, stl * 128:(stl + 1) * 128], C.identb[:],
                                r=["l1a_vtmp", "identb"], w=[f"PB{bj}"])
                for stl in range(4):
                    tc = t * 4 + stl
                    C.copy(vt[tc][:, ec * 128:(ec + 1) * 128],
                           C.bank_bf(bj, 0, 128, stl * 128, (stl + 1) * 128),
                           r=[f"PB{bj}"], w=[f"l1a_vt{tc}"])

    for c in range(KC):
        C.red(zg[:, c:c + 1], kp[c][:], ALU.add, r=[f"l1a_kp{c}"], w=["l1a_zg"])
    C.dma("P", ccz_in[:, :], zg[:], r=["l1a_zg"], w=["ccz_in"])
    C.cc("AllGather", [ccz_in[:, :]], [ccz_out[:, :]], r=["ccz_in"], w=["ccz_out"])
    C.dma("Y", gz[:].rearrange("p (r f) -> p r f", r=8),
          ccz_out[:, :].rearrange("(r p) f -> p r f", p=128), r=["ccz_out"], w=["l1a_gz"])
    C.memset(zgg[:], 0.0, w=["l1a_zgg"])
    for j in range(NCORE):
        C.ts(aeff[:], gz[:, j * 8:(j + 1) * 8], rm[:, j:j + 1], None, ALU.mult,
             r=["l1a_gz", "l1a_rm"], w=["l1a_ae"])
        C.tt(zgg[:], zgg[:], aeff[:], ALU.add, r=["l1a_zgg", "l1a_ae"], w=["l1a_zgg"])
    for tc in range(16):
        for ec in range(KC):
            bj = C.rb(0, 4)
            bap = C.bank_bf(bj, 0, 128, 0, 128)
            C.transpose(bap, kp[ec][:, tc * 128:(tc + 1) * 128], C.identb[:],
                        r=[f"l1a_kp{ec}", "identb"], w=[f"PB{bj}"])
            C.copy(ktst[0:128, ec * 128:(ec + 1) * 128], bap, r=[f"PB{bj}"], w=["l1a_ktst"])
        for h in range(16):
            ch, hp = h // 2, (h % 2) * 64
            bi = 4 + (h % 2)
            C.mm(bi, ktst[0:128, h * 64:(h + 1) * 64], vt[tc][:, h * 64:(h + 1) * 64],
                 start=True, stop=True, r=["l1a_ktst", f"l1a_vt{tc}"],
                 bank_ap=C.bank(bi, hp, hp + 64, 0, 64))
            if tc == 0:
                C.copy(sg[hp:hp + 64, ch * 64:(ch + 1) * 64], C.bank(bi, hp, hp + 64, 0, 64),
                       r=[f"PB{bi}"], w=["l1a_sg"], q="S")
            else:
                C.tt(sg[hp:hp + 64, ch * 64:(ch + 1) * 64], sg[hp:hp + 64, ch * 64:(ch + 1) * 64],
                     C.bank(bi, hp, hp + 64, 0, 64), ALU.add, r=["l1a_sg", f"PB{bi}"],
                     w=["l1a_sg"])

    C.dma("P", ccs_in[:, :], sg[:], r=["l1a_sg"], w=["ccs_in"])
    C.cc("AllGather", [ccs_in[:, :]], [ccs_out[:, :]], r=["ccs_in"], w=["ccs_out"])
    C.memset(srun[:], 0.0, w=["l1a_srun"] + [f"l1a_srun{h}" for h in range(16)])
    for c in range(KC):
        kc_b = kcum[c % 2]
        kck = f"l1a_kcum{c % 2}"
        C.scan(kc_b[:], kp[c][:], kp[c][:], zgg[:, c:c + 1], ALU.add, ALU.max,
               r=[f"l1a_kp{c}", "l1a_zgg"], w=[kck])
        C.tt(kc_b[:], qp[c][:], kc_b[:], ALU.mult, r=[f"l1a_qp{c}", kck],
             w=[kck])
        for t in range(NT):
            bi = C.rb(0, 4)
            C.mm(bi, hon[:], kc_b[:, t * TILE:(t + 1) * TILE],
                 start=True, stop=True, r=[kck, "l1a_hon"])
            C.act(nst0[:], C.bank(bi), AF.Ln, bias=C.c_eps_attn,
                  r=[f"PB{bi}", "consts"], w=["ln.d10"])
            C.act(nst0[:], nst0[:], AF.Exp, scale=-1.0,
                  r=["ln.d10"], w=["ln.d10"])
            C.tt(qp[c][:, t * TILE:(t + 1) * TILE], qp[c][:, t * TILE:(t + 1) * TILE],
                 nst0[:], ALU.mult, r=[f"l1a_qp{c}", "ln.d10"], w=[f"l1a_qp{c}"])
    for j in range(NCORE):
        C.dma("Y", sstage_ap, ccs_out[j * 128:(j + 1) * 128, :],
              r=["ccs_out"], w=["l1a_sstage", "l1a_kcum0"])
        C.stt(srun[:], sstage_ap, rm[:, j:j + 1], srun[:],
              ALU.mult, ALU.add, r=["l1a_sstage", "l1a_rm", "l1a_srun"], w=["l1a_srun"] + [f"l1a_srun{h}" for h in range(16)])
    C.copy(srb[:], srun[:], r=["l1a_srun"], w=["l1a_srb"] + [f"l1a_srb{h}" for h in range(16)])

    for tc in range(16):
        for ec in range(KC):
            bj = C.rb(0, 4)
            bap = C.bank_bf(bj, 0, 128, 0, 128)
            C.transpose(bap, kp[ec][:, tc * 128:(tc + 1) * 128], C.identb[:],
                        r=[f"l1a_kp{ec}", "identb"], w=[f"PB{bj}"])
            C.copy(ktst[0:128, ec * 128:(ec + 1) * 128], bap, r=[f"PB{bj}"], w=["l1a_ktst"])
        for ch in range(8):
            bc = 4 + (ch % 2)
            bu = C.rb(0, 4)
            for par in range(2):
                h = 2 * ch + par
                hp = par * 64
                q_ap = qp[ch][hp:hp + 64, tc * 128:(tc + 1) * 128]
                k_ap = kp[ch][hp:hp + 64, tc * 128:(tc + 1) * 128]
                smv = sm[:, par * 128:(par + 1) * 128]
                smk = f"l1a_sm{par}"
                sreg = (slice(hp, hp + 64), slice(ch * 64, (ch + 1) * 64))
                bs = C.rb(0, 4)
                C.mm(bs, k_ap, q_ap, start=True, stop=True,
                     r=[f"l1a_kp{ch}", f"l1a_qp{ch}"],
                     bank_ap=C.bank(bs, 0, 128, 0, 128))
                C.tt(smv, C.bank(bs, 0, 128, 0, 128), mut[:], ALU.mult,
                     r=[f"PB{bs}", "l1a_mut"], w=[smk])
                C.mm(bc, vt[tc][:, h * 64:(h + 1) * 64], smv, start=True, stop=False,
                     r=[f"l1a_vt{tc}", smk], bank_ap=C.bank(bc, hp, hp + 64, 0, 128))
                C.mm(bc, srb[sreg], q_ap, start=False, stop=True,
                     r=[f"l1a_srb{h}", f"l1a_qp{ch}"], bank_ap=C.bank(bc, hp, hp + 64, 0, 128))
                C.mm(bu, ktst[0:128, h * 64:(h + 1) * 64], vt[tc][:, h * 64:(h + 1) * 64],
                     start=True, stop=True, r=["l1a_ktst", f"l1a_vt{tc}"],
                     bank_ap=C.bank(bu, hp, hp + 64, 0, 64))
            sboth = (slice(0, 128), slice(ch * 64, (ch + 1) * 64))
            hk = [f"l1a_srun{2 * ch}", f"l1a_srun{2 * ch + 1}"]
            bk = [f"l1a_srb{2 * ch}", f"l1a_srb{2 * ch + 1}"]
            C.tt(srun[sboth], srun[sboth], C.bank(bu, 0, 128, 0, 64), ALU.add,
                 r=hk + [f"PB{bu}"], w=hk)
            C.copy(srb[sboth], srun[sboth], r=hk, w=bk, q="S")
            C.copy(qp[ch][:, tc * 128:(tc + 1) * 128], C.bank(bc, 0, 128, 0, 128),
                   r=[f"PB{bc}"], w=[f"l1a_qp{ch}"])

    for tg in range(2):
        for oc in range(KC):
            wbuf = wq[oc % 2]
            wkey = f"l1a_w{oc % 2}"
            C.dma("P", wbuf[:].rearrange("p (k e) -> p k e", k=KC),
                  WO[:, oc * 128:(oc + 1) * 128].rearrange("(k p) e -> p k e", p=128),
                  r=[], w=[wkey])
            bts = [C.rb(0, 4), C.rb(0, 4)]
            for k in range(KC):
                lhs = wbuf[:, k * 128:(k + 1) * 128]
                for tl in range(2):
                    t = 2 * tg + tl
                    C.mm(bts[tl], lhs, qp[k][:, t * TILE:(t + 1) * TILE],
                         start=(k == 0), stop=(k == KC - 1), r=[wkey, f"l1a_qp{k}"])
            for tl in range(2):
                t = 2 * tg + tl
                C.ts(o_sb[oc][:, tl * TILE:(tl + 1) * TILE], C.bank(bts[tl]), bo[:, oc:oc + 1],
                     None, ALU.add, r=[f"PB{bts[tl]}", "l1a_bo"], w=[f"l1a_xn{oc}o{t}"])
        for tl in range(2):
            t = 2 * tg + tl
            srcs = [(o_sb[c][:, tl * TILE:(tl + 1) * TILE], f"l1a_xn{c}o{t}") for c in range(KC)]
            pi = t % 2
            bS, bQ = C.rb(0, 4), C.rb(0, 4)
            for c in range(KC):
                sap, skey = srcs[c]
                sq = C.lnsq[pi * 2 + c % 2]
                sqk = f"ln{pi}.sq{c % 2}"
                C.act(sq[:], sap, AF.Square, r=[skey], w=[sqk])
                C.mm(bS, C.ones_bf[:], sap, start=(c == 0), stop=(c == KC - 1), r=[skey, "onesb"])
                C.mm(bQ, C.ones_bf[:], sq[:], start=(c == 0), stop=(c == KC - 1), r=[sqk, "onesb"])
            keys = ln_tilemath(C, bS, bQ, pi=pi)
            ln_apply_resid(C, srcs, keys, t * TILE, TILE)


def emit_l1_ffn(C, nc):
    CEN = nc.dram_tensor("cent", [D, 8], F32, kind="ExternalInput").ap()
    LNG = nc.dram_tensor("lng", [8, D], F32, kind="ExternalInput").ap()
    LNB = nc.dram_tensor("lnb", [8, D], F32, kind="ExternalInput").ap()
    W1 = nc.dram_tensor("w1", [8, D, 4096], F32, kind="ExternalInput").ap()
    W2 = nc.dram_tensor("w2", [8, 4096, D], F32, kind="ExternalInput").ap()
    B1E = nc.dram_tensor("b1e", [128, 256], F32, kind="ExternalInput").ap()
    B2E = nc.dram_tensor("b2e", [8, D], F32, kind="ExternalInput").ap()
    SEL = nc.dram_tensor("sel", [8, 8 * 128], F32, kind="ExternalInput").ap()
    RMIT = 1024

    centb = C.ab("l1f_cent", [128, 64], BF16)
    lngb = C.ab("l1f_lng", [8, D], BF16)
    lnbb = C.ab("l1f_lnb", [8, D], BF16)
    selb = C.ab("l1f_sel", [8, 8 * 128], BF16)
    b2eb = C.ab("l1f_b2e", [8, D], BF16)
    b1e = C.ab("l1f_b1e", [128, 256], F32)
    xi = [C.ab(f"l1f_xi{c}", [128, RMIT], BF16) for c in range(KC)]
    xln = [C.ab(f"l1f_xln{c}", [128, RMIT], BF16) for c in range(KC)]
    Hm = [C.ab(f"l1f_h{f}", [128, RMIT], BF16) for f in range(8)]
    ffn = [C.ab(f"l1f_ffn{c}", [128, RMIT], BF16) for c in range(KC)]
    w1b = [C.ab(f"l1f_w1{i}", [128, 8 * 1024], BF16) for i in range(2)]
    w2b = C.ab("l1f_w2", [128, 8 * 1024], BF16)
    aff = C.ab("l1f_aff", [8, RMIT], F32)
    onehot = C.ab("l1f_oh", [8, RMIT], BF16)
    tmul = C.ab("l1f_tm", [8, RMIT], BF16)
    alphab = C.ab("l1f_al", [128, RMIT], BF16)
    maskb = C.ab("l1f_mb", [128, RMIT], BF16)
    afft = C.ab("l1f_afft", [128, 8], F32)
    mcol = C.ab("l1f_mcol", [128, 1], F32)
    oht = C.ab("l1f_oht", [128, 8], BF16)

    for k in range(KC):
        C.dma("P", centb[:, k * 8:(k + 1) * 8], CEN[k * 128:(k + 1) * 128, :], r=[], w=["l1f_cent"])
    C.dma("P", lngb[:], LNG, r=[], w=["l1f_lng"])
    C.dma("P", lnbb[:], LNB, r=[], w=["l1f_lnb"])
    C.dma("P", selb[:], SEL, r=[], w=["l1f_sel"])
    C.dma("P", b2eb[:], B2E, r=[], w=["l1f_b2e"])
    C.dma("Y", b1e[:], B1E, r=[], w=["l1f_b1e"])

    st_d = C.lnst[0]["d1"] + C.lnst[0]["d2"]
    kd = ["ln.d10", "ln.d11", "ln.d20", "ln.d21"]

    for p in range(2):
        P0 = p * RMIT
        for st in range(2):
            dsts = [(xi[c][:, st * TILE:(st + 1) * TILE], f"l1f_xi{c}s{st}") for c in range(KC)]
            ln_full(C, C.xres, "xres", P0 + st * TILE, TILE, "dst", dsts=dsts)
        for st in range(2):
            bi = C.rb(0, 4)
            bap = C.bank(bi, 0, 8, 0, TILE)
            for k in range(KC):
                C.mm(bi, centb[:, k * 8:(k + 1) * 8], xi[k][:, st * TILE:(st + 1) * TILE],
                     start=(k == 0), stop=(k == KC - 1), r=["l1f_cent", f"l1f_xi{k}s{st}"],
                     bank_ap=bap)
            C.copy(aff[0:8, st * TILE:(st + 1) * TILE], bap, r=[f"PB{bi}"], w=["l1f_aff"])
        for sl in range(8):
            bj = C.rb(0, 4)
            C.transpose(C.bank(bj, 0, 128, 0, 8), aff[0:8, sl * 128:(sl + 1) * 128],
                        C.ident[0:8, 0:8], r=["l1f_aff"], w=[f"PB{bj}"])
            C.copy(afft[:], C.bank(bj, 0, 128, 0, 8), r=[f"PB{bj}"], w=["l1f_afft"])
            C.red(mcol[:], afft[:], ALU.max, r=["l1f_afft"], w=["l1f_mcol"])
            C.ts(oht[:], afft[:], mcol[:], None, ALU.is_ge, r=["l1f_afft", "l1f_mcol"],
                 w=["l1f_oht"])
            bk = C.rb(0, 4)
            C.transpose(C.bank_bf(bk, 0, 8, 0, 128), oht[:], C.identb[:],
                        r=["l1f_oht", "identb"], w=[f"PB{bk}"])
            C.copy(onehot[0:8, sl * 128:(sl + 1) * 128], C.bank_bf(bk, 0, 8, 0, 128),
                   r=[f"PB{bk}"], w=["l1f_oh"])
        C.tt(tmul[0:8, :], aff[0:8, :], onehot[0:8, :], ALU.mult,
             r=["l1f_aff", "l1f_oh"], w=["l1f_tm"])
        for st in range(2):
            bi = C.rb(0, 4)
            C.mm(bi, C.ones_bf[0:8, :], tmul[0:8, st * TILE:(st + 1) * TILE],
                 start=True, stop=True, r=["l1f_tm", "onesb"])
            C.act(alphab[:, st * TILE:(st + 1) * TILE], C.bank(bi), AF.Sigmoid,
                  r=[f"PB{bi}"], w=["l1f_al"])
        for dc in range(KC):
            for st in range(2):
                bi = C.rb(0, 4)
                C.mm(bi, b2eb[0:8, dc * 128:(dc + 1) * 128],
                     onehot[0:8, st * TILE:(st + 1) * TILE], start=True, stop=True,
                     r=["l1f_b2e", "l1f_oh"])
                C.copy(ffn[dc][:, st * TILE:(st + 1) * TILE], C.bank(bi),
                       r=[f"PB{bi}"], w=[f"l1f_ffn{dc}"])
        for st in range(2):
            srcs = []
            bS, bQ = C.rb(0, 4), C.rb(0, 4)
            for c in range(KC):
                sap = xi[c][:, st * TILE:(st + 1) * TILE]
                sq = C.lnsq[st * 2 + c % 2]
                C.act(sq[:], sap, AF.Square, r=[f"l1f_xi{c}s{st}"], w=[f"ln{st}.sq{c % 2}"])
                C.mm(bS, C.ones_bf[:], sap, start=(c == 0), stop=(c == KC - 1),
                     r=[f"l1f_xi{c}s{st}", "onesb"])
                C.mm(bQ, C.ones_bf[:], sq[:], start=(c == 0), stop=(c == KC - 1),
                     r=[f"ln{st}.sq{c % 2}", "onesb"])
                srcs.append((sap, f"l1f_xi{c}s{st}"))
            keys = ln_tilemath(C, bS, bQ, pi=st)
            mu_t, rstd_t = C.lnst[st]["mub"], C.lnst[st]["scb"]
            for dc in range(KC):
                sap = srcs[dc][0]
                d1 = st_d[dc % 2]
                C.tt(d1[:], sap, mu_t[:], ALU.subtract, r=[f"l1f_xi{dc}s{st}", keys["mu"]],
                     w=[kd[dc % 2]])
                C.tt(d1[:], d1[:], rstd_t[:], ALU.mult, r=[kd[dc % 2], keys["sc"]],
                     w=[kd[dc % 2]])
                bg = C.rb(0, 4)
                C.mm(bg, lngb[0:8, dc * 128:(dc + 1) * 128],
                     onehot[0:8, st * TILE:(st + 1) * TILE], start=True, stop=True,
                     r=["l1f_lng", "l1f_oh"])
                d2 = st_d[2 + dc % 2]
                C.tt(d2[:], d1[:], C.bank(bg), ALU.mult, r=[kd[dc % 2], f"PB{bg}"],
                     w=[kd[2 + dc % 2]])
                bb = C.rb(0, 4)
                C.mm(bb, lnbb[0:8, dc * 128:(dc + 1) * 128],
                     onehot[0:8, st * TILE:(st + 1) * TILE], start=True, stop=True,
                     r=["l1f_lnb", "l1f_oh"])
                C.tt(xln[dc][:, st * TILE:(st + 1) * TILE], d2[:], C.bank(bb), ALU.add,
                     r=[kd[2 + dc % 2], f"PB{bb}"], w=[f"l1f_xln{dc}s{st}"])
        for e in range(8):
            for g in range(4):
                wb = w1b[g % 2]
                wk = f"l1f_w1{g % 2}"
                C.dma("P", wb[:].rearrange("p (k f) -> p k f", k=KC),
                      W1[e][:, g * 1024:(g + 1) * 1024].rearrange("(k p) f -> p k f", p=128),
                      r=[], w=[wk])
                C.dma("P", w2b[:].rearrange("p (k f) -> p k f", k=KC),
                      W2[e][g * 1024:(g + 1) * 1024, :].rearrange("(k p) f -> p k f", p=128),
                      r=[], w=["l1f_w2k"])
                if g == 0:
                    for st in range(2):
                        bm = C.rb(0, 4)
                        C.mm(bm, selb[0:8, e * 128:(e + 1) * 128],
                             onehot[0:8, st * TILE:(st + 1) * TILE], start=True, stop=True,
                             r=["l1f_sel", "l1f_oh"])
                        C.copy(maskb[:, st * TILE:(st + 1) * TILE], C.bank(bm),
                               r=[f"PB{bm}"], w=["l1f_mb"], q="S")
                for fcl in range(8):
                    b0, b1_ = C.rb(0, 4), C.rb(0, 4)
                    for k in range(KC):
                        lhs = wb[:, k * 1024 + fcl * 128: k * 1024 + (fcl + 1) * 128]
                        C.mm(b0, lhs, xln[k][:, 0:TILE],
                             start=(k == 0), stop=(k == KC - 1), r=[wk, f"l1f_xln{k}s0"])
                        C.mm(b1_, lhs, xln[k][:, TILE:2 * TILE],
                             start=(k == 0), stop=(k == KC - 1), r=[wk, f"l1f_xln{k}s1"])
                    for st, bi in ((0, b0), (1, b1_)):
                        d1 = st_d[(2 * fcl + st) % 4]
                        kdx = kd[(2 * fcl + st) % 4]
                        C.act(d1[:], C.bank(bi), AF.Relu,
                              bias=b1e[:, e * 32 + g * 8 + fcl: e * 32 + g * 8 + fcl + 1],
                              r=[f"PB{bi}", "l1f_b1e"], w=[kdx])
                        C.tt(Hm[fcl][:, st * TILE:(st + 1) * TILE], d1[:],
                             maskb[:, st * TILE:(st + 1) * TILE],
                             ALU.mult, r=[kdx, "l1f_mb"], w=[f"l1f_h{fcl}"])
                for dc in range(KC):
                    b0, b1_ = C.rb(0, 4), C.rb(0, 4)
                    for fcl in range(8):
                        lhs = w2b[:, fcl * 1024 + dc * 128: fcl * 1024 + (dc + 1) * 128]
                        C.mm(b0, lhs, Hm[fcl][:, 0:TILE],
                             start=(fcl == 0), stop=(fcl == 7), r=["l1f_w2k", f"l1f_h{fcl}"])
                        C.mm(b1_, lhs, Hm[fcl][:, TILE:2 * TILE],
                             start=(fcl == 0), stop=(fcl == 7), r=["l1f_w2k", f"l1f_h{fcl}"])
                    for st, bo in ((0, b0), (1, b1_)):
                        fap = ffn[dc][:, st * TILE:(st + 1) * TILE]
                        C.tt(fap, fap, C.bank(bo), ALU.add,
                             r=[f"l1f_ffn{dc}", f"PB{bo}"], w=[f"l1f_ffn{dc}"])
        for st in range(2):
            for dc in range(KC):
                d1 = st_d[dc % 2]
                C.tt(d1[:], ffn[dc][:, st * TILE:(st + 1) * TILE],
                     alphab[:, st * TILE:(st + 1) * TILE], ALU.mult,
                     r=[f"l1f_ffn{dc}", "l1f_al"], w=[kd[dc % 2]])
                C.tt(xln[dc][:, st * TILE:(st + 1) * TILE], d1[:],
                     xi[dc][:, st * TILE:(st + 1) * TILE], ALU.add,
                     r=[kd[dc % 2], f"l1f_xi{dc}s{st}"], w=[f"l1f_xln{dc}c{st}"])
            srcs2 = [(xln[c][:, st * TILE:(st + 1) * TILE], f"l1f_xln{c}c{st}") for c in range(KC)]
            pi2 = st % 2
            bS2, bQ2 = C.rb(0, 4), C.rb(0, 4)
            for c in range(KC):
                sap2, skey2 = srcs2[c]
                sq2 = C.lnsq[pi2 * 2 + c % 2]
                sqk2 = f"ln{pi2}.sq{c % 2}"
                C.act(sq2[:], sap2, AF.Square, r=[skey2], w=[sqk2])
                C.mm(bS2, C.ones_bf[:], sap2, start=(c == 0), stop=(c == KC - 1), r=[skey2, "onesb"])
                C.mm(bQ2, C.ones_bf[:], sq2[:], start=(c == 0), stop=(c == KC - 1), r=[sqk2, "onesb"])
            keys2 = ln_tilemath(C, bS2, bQ2, pi=pi2)
            ln_apply_resid(C, srcs2, keys2, P0 + st * TILE, TILE)


# version 25
# speedup vs baseline: 1.0707x; 1.0707x over previous
import numpy as np
import ml_dtypes
import concourse.bass as bass
import concourse.mybir as mybir
from concourse.bass_utils import run_bass_kernel_spmd

import concourse.bass_utils as _bu
if not getattr(_bu, "_nosim_patched", False):
    _orig_run_command = _bu.run_command
    def _run_command_nosim(cmd, *a, **kw):
        cmd = ["--enable-birsim=false" if c == "--enable-birsim=true" else c for c in cmd]
        cmd = ["--enable-ldw-opt=true" if c == "--enable-ldw-opt=false" else c for c in cmd]
        return _orig_run_command(cmd, *a, **kw)
    _bu.run_command = _run_command_nosim
    _bu._nosim_patched = True

F32 = mybir.dt.float32
BF16 = mybir.dt.bfloat16
AF = mybir.ActivationFunctionType
ALU = mybir.AluOpType
AX = mybir.AxisListType

NCORE = 8
T = 2048
D = 1024
KC = 8
TILE = 512
NT = T // TILE
EPS_LN = 1e-5
EPS_ATTN = 1e-6
QUEUES = ("T", "V", "S", "P", "Y")
ALL_BLOCKS = ("l0a", "l0f", "l1a", "l1f")

SMALL_KEYS = frozenset(
    ["l1f_mcol", "l0a_carry", "l0a_aeff", "l0a_yeff", "l1a_zgg", "l1a_zg",
     "l0a_exch", "l1f_afft", "l1f_oht", "l1a_ae"]
    + [f"l1a_srun{h}" for h in range(16)])


class Prog:

    def __init__(self, nc):
        self.nc = nc
        self.ops = []
        self.writer = {}
        self.readers = {}
        self.barrier = ()

    def op(self, q, fn, r=(), w=(), dma=False, cc=False, tag="", out_dma=False):
        excl = [k for k in r if k.startswith("PB")]
        if excl:
            r = [k for k in r if not k.startswith("PB")]
            w = list(w) + excl
        deps = set(self.barrier)
        for k in r:
            i = self.writer.get(k)
            if i is not None:
                deps.add(i)
        for k in w:
            i = self.writer.get(k)
            if i is not None:
                deps.add(i)
            for j in self.readers.get(k, ()):
                deps.add(j)
        idx = len(self.ops)
        for k in w:
            self.writer[k] = idx
            self.readers[k] = []
        for k in r:
            self.readers.setdefault(k, []).append(idx)
        deps.discard(idx)
        small = any(k in SMALL_KEYS for k in w)
        self.ops.append(dict(q=q, fn=fn, deps=sorted(deps), dma=dma, cc=cc,
                             tag=tag, out_dma=out_dma, small=small))
        return idx

    def set_barrier(self):
        last = {}
        for i, o in enumerate(self.ops):
            last[o["q"]] = i
        self.barrier = tuple(sorted(last.values()))

    def finalize_and_emit(self):
        import contextlib
        nc = self.nc
        n = len(self.ops)
        for i, o in enumerate(self.ops):
            o["deps"] = [d for d in o["deps"]
                         if self.ops[d]["q"] != o["q"]
                         or self.ops[d]["dma"] or self.ops[d]["cc"]
                         or self.ops[d]["small"]]
        needed = [False] * n
        for o in self.ops:
            for d in o["deps"]:
                needed[d] = True
        n_dma_pool = {"P": 8, "Y": 6, "S": 2, "V": 1, "T": 1}

        stack = contextlib.ExitStack()
        qsem = {q: stack.enter_context(nc.semaphore(f"qs_{q}")) for q in QUEUES}
        pools = {q: [stack.enter_context(nc.semaphore(f"dp_{q}{i}"))
                     for i in range(n_dma_pool[q])] for q in QUEUES}
        ncc = sum(1 for o in self.ops if o["cc"])
        ccsems = [stack.enter_context(nc.semaphore(f"cc{i}")) for i in range(ncc)]

        qcount = {q: 0 for q in QUEUES}
        pool_idx = {q: 0 for q in QUEUES}
        pool_count = {}
        cci = 0
        for i, o in enumerate(self.ops):
            q = o["q"]
            if o["cc"]:
                o["sem"], o["val"], o["inc"] = ccsems[cci], 1, 1
                cci += 1
            elif o["dma"]:
                s = pools[q][pool_idx[q] % len(pools[q])]
                pool_idx[q] += 1
                pool_count[id(s)] = pool_count.get(id(s), 0) + 16
                o["sem"], o["val"], o["inc"] = s, pool_count[id(s)], 16
            elif needed[i]:
                qcount[q] += 1
                o["sem"], o["val"], o["inc"] = qsem[q], qcount[q], 1
            else:
                o["sem"] = None
        highwater = {q: {} for q in QUEUES}
        for i, o in enumerate(self.ops):
            q = o["q"]
            waits = []
            for d in o["deps"]:
                od = self.ops[d]
                if od["sem"] is None:
                    assert od["q"] == q and d < i, (d, i, od["tag"], o["tag"])
                    continue
                sem, val = od["sem"], od["val"]
                hw = highwater[q]
                if hw.get(id(sem), -1) < val:
                    waits.append((sem, val))
                    hw[id(sem)] = val
            o["waits"] = waits
        out_waits = {}
        for o in self.ops:
            if o["out_dma"]:
                prev = out_waits.get(id(o["sem"]), (None, 0))[1]
                out_waits[id(o["sem"])] = (o["sem"], max(prev, o["val"]))

        with stack:
            with nc.Block() as block:
                def section(qname):
                    def body(eng):
                        for o in self.ops:
                            if o["q"] != qname:
                                continue
                            for sem, val in o["waits"]:
                                eng.wait_ge(sem, val)
                            inst = o["fn"](eng)
                            if o["sem"] is not None:
                                inst.then_inc(o["sem"], o["inc"])
                        if qname == "Y":
                            for sem, val in out_waits.values():
                                eng.wait_ge(sem, val)
                    return body
                block.tensor(section("T"))
                block.vector(section("V"))
                block.scalar(section("S"))
                block.gpsimd(section("P"))
                block.sync(section("Y"))


class Ctx:

    def __init__(self, nc):
        self.nc = nc
        self.P = Prog(nc)
        self.dram_cache = {}
        self._bank = [nc.alloc_psum_tensor(f"pbank{i}", [128, 512], F32) for i in range(8)]
        self._rb = 0

    def dram_in(self, name, shape, dtype=None):
        if name not in self.dram_cache:
            self.dram_cache[name] = self.nc.dram_tensor(
                name, shape, dtype or F32, kind="ExternalInput").ap()
        return self.dram_cache[name]

    def bank(self, i, p0=0, p1=128, f0=0, f1=512):
        return self._bank[i][p0:p1, f0:f1]

    def bank_bf(self, i, p0=0, p1=128, f0=0, f1=1024):
        return self._bank[i][:].bitcast(BF16)[p0:p1, f0:f1]

    def rb(self, lo=0, hi=6):
        self._rb = (self._rb + 1) % (hi - lo)
        return lo + self._rb

    def sb(self, name, shape, dtype):
        h = self.nc.alloc_sbuf_tensor(name, shape, dtype)
        if not hasattr(self, "reg"):
            self.reg = {}
        self.reg[name] = h
        return h

    def arena_init(self):
        probe = self.nc.alloc_sbuf_tensor("arena_probe", [128, 1], F32)
        self.arena_base = self.nc.lookup_mloc(probe).addr + 32
        self.arena_limit = self.arena_base + self.nc.sbuf_bytes_remaining - 512
        self.arena_off = self.arena_base

    def arena_reset(self):
        self.arena_off = self.arena_base
        self.P.set_barrier()

    def ab(self, name, shape, dtype):
        nbytes = int(np.prod(shape[1:])) * (2 if dtype == BF16 else 4)
        off = (self.arena_off + 31) & ~31
        assert off + nbytes <= self.arena_limit, (
            f"arena overflow: {name} needs {nbytes}B at {off}, limit {self.arena_limit}")
        h = self.nc.alloc_sbuf_tensor_at(name, shape, dtype, offset=off)
        self.arena_off = off + nbytes
        self.reg[name] = h
        return h

    def mm(self, bi, lhsT, rhs, start, stop, r, bank_key=None, bank_ap=None):
        ap = bank_ap if bank_ap is not None else self.bank(bi)
        bk = bank_key or f"PB{bi}"
        return self.P.op("T", lambda e, ap=ap, lhsT=lhsT, rhs=rhs, start=start, stop=stop:
                         e.matmul(ap, lhsT, rhs, start=start, stop=stop),
                         r=list(r), w=[bk], tag="mm")

    def transpose(self, bank_ap, src, ident, r, w):
        return self.P.op("T", lambda e, a=bank_ap, s=src, i=ident: e.transpose(a, s, i),
                         r=list(r) + ["ident"], w=list(w), tag="tr")

    def act(self, dst, src, func, r, w, bias=0.0, scale=1.0, q="S"):
        return self.P.op(q, lambda e, d=dst, s=src, f=func, b=bias, sc=scale:
                         e.activation(d, s, f, bias=b, scale=sc),
                         r=list(r), w=list(w), tag=f"act.{func}")

    def tt(self, dst, a, b, op, r, w, q="V"):
        return self.P.op(q, lambda e, d=dst, a=a, b=b, op=op: e.tensor_tensor(d, a, b, op),
                         r=list(r), w=list(w), tag=f"tt.{op}")

    def ts(self, dst, a, s1, s2, op0, r, w, op1=ALU.bypass, q="V"):
        return self.P.op(q, lambda e, d=dst, a=a, s1=s1, s2=s2, op0=op0, op1=op1:
                         e.tensor_scalar(d, a, s1, s2, op0, op1),
                         r=list(r), w=list(w), tag=f"ts.{op0}")

    def stt(self, dst, in0, scal, in1, op0, op1, r, w, q="V"):
        return self.P.op(q, lambda e, d=dst, i0=in0, sc=scal, i1=in1, op0=op0, op1=op1:
                         e.scalar_tensor_tensor(d, i0, sc, i1, op0, op1),
                         r=list(r), w=list(w), tag="stt")

    def scan(self, dst, d0, d1, init, op0, op1, r, w, q="V"):
        return self.P.op(q, lambda e, d=dst, d0=d0, d1=d1, init=init, op0=op0, op1=op1:
                         e.tensor_tensor_scan(d, d0, d1, init, op0, op1),
                         r=list(r), w=list(w), tag="scan")

    def red(self, dst, src, op, r, w, q="V"):
        return self.P.op(q, lambda e, d=dst, s=src, op=op: e.tensor_reduce(d, s, AX.X, op),
                         r=list(r), w=list(w), tag=f"red.{op}")

    def rcp(self, dst, src, r, w):
        nc = self.nc
        def fn(e, d=dst, s=src):
            with nc.allow_low_precision(reason="bf16 norm inverse"):
                return e.reciprocal(d, s)
        return self.P.op("V", fn, r=list(r), w=list(w), tag="rcp")

    def copy(self, dst, src, r, w, q="V"):
        if q == "S":
            return self.P.op(q, lambda e, d=dst, s=src: e.copy(d, s),
                             r=list(r), w=list(w), tag="cp")
        return self.P.op(q, lambda e, d=dst, s=src: e.tensor_copy(d, s),
                         r=list(r), w=list(w), tag="cp")

    def memset(self, dst, val, w, q="V"):
        return self.P.op(q, lambda e, d=dst, v=val: e.memset(d, v), r=[], w=list(w), tag="ms")

    def dma(self, q, dst, src, r, w, out=False):
        return self.P.op(q, lambda e, d=dst, s=src: e.dma_start(out=d, in_=s),
                         r=list(r), w=list(w), dma=True, out_dma=out, tag="dma")

    def cc(self, kind, ins, outs, r, w):
        return self.P.op("P", lambda e, k=kind, i=ins, o=outs:
                         e.collective_compute(k, ALU.bypass, replica_groups=[list(range(NCORE))],
                                              ins=i, outs=o),
                         r=list(r), w=list(w), cc=True, tag=f"cc.{kind}")



def ln_stats(C, srcs, sqs, rms=False):
    bS, bQ = C.rb(), C.rb()
    for c in range(KC):
        sap, skey = srcs[c]
        qap, qkey = sqs[c]
        C.mm(bS, C.ones_bf[:], sap, start=(c == 0), stop=(c == KC - 1), r=[skey, "onesb"])
        C.mm(bQ, C.ones_bf[:], qap, start=(c == 0), stop=(c == KC - 1), r=[qkey, "onesb"])
    return ln_tilemath(C, bS, bQ, rms=rms)


def ln_tilemath(C, bS, bQ, pi=0, rms=False):
    st = C.lnst[pi]
    P = f"ln{pi}"
    mu, var, rstd, m = st["mu"], st["var"], st["rstd"], st["m"]
    C.act(mu[:], C.bank(bS), AF.Copy, scale=1.0 / D, r=[f"PB{bS}"], w=[P + ".mu"])
    C.act(var[:], C.bank(bQ), AF.Copy, scale=1.0 / D, r=[f"PB{bQ}"], w=[P + ".var"])
    C.tt(m[:], mu[:], mu[:], ALU.mult, r=[P + ".mu"], w=[P + ".m"])
    C.tt(var[:], var[:], m[:], ALU.subtract, r=[P + ".var", P + ".m"], w=[P + ".var"])
    C.act(m[:], var[:], AF.Ln, bias=C.c_eps_ln, r=[P + ".var", "consts"], w=[P + ".m"])
    C.act(rstd[:], m[:], AF.Exp, scale=-0.5, r=[P + ".m"], w=[P + ".rstd"])
    C.act(st["mub"][:], C.bank(bS), AF.Copy, scale=1.0 / D, r=[f"PB{bS}"], w=[P + ".mub"])
    if not rms:
        C.act(st["scb"][:], rstd[:], AF.Copy, r=[P + ".rstd"], w=[P + ".scb"])
        return {"mu": P + ".mub", "sc": P + ".scb", "scap": st["scb"],
                "muap": st["mub"], "pi": pi}
    n2, msc = st["n2"], st["msc"]
    C.ts(n2[:], var[:], float(D), None, ALU.mult, r=[P + ".var"], w=["ln.n2"])
    C.tt(n2[:], n2[:], rstd[:], ALU.mult, r=["ln.n2", P + ".rstd"], w=["ln.n2"])
    C.tt(n2[:], n2[:], rstd[:], ALU.mult, r=["ln.n2", P + ".rstd"], w=["ln.n2"])
    C.act(n2[:], n2[:], AF.Ln, r=["ln.n2"], w=["ln.n2"])
    C.act(n2[:], n2[:], AF.Exp, scale=0.5, r=["ln.n2"], w=["ln.n2"])
    C.ts(n2[:], n2[:], 1e-12, None, ALU.max, r=["ln.n2"], w=["ln.n2"])
    C.act(n2[:], n2[:], AF.Ln, r=["ln.n2"], w=["ln.n2"])
    C.act(n2[:], n2[:], AF.Exp, scale=-1.0, bias=C.c_ln32, r=["ln.n2", "consts"], w=["ln.n2"])
    C.tt(st["scb"][:], rstd[:], n2[:], ALU.mult, r=[P + ".rstd", "ln.n2"], w=[P + ".scb"])
    return {"mu": P + ".mub", "sc": P + ".scb", "scap": st["scb"],
            "muap": st["mub"], "pi": pi}


def ln_apply_dst(C, srcs, keys, dsts, gamma=None):
    pi = keys["pi"]
    st = C.lnst[pi]
    P = f"ln{pi}"
    mu, scap = keys["muap"], keys["scap"]
    for c in range(KC):
        sap, skey = srcs[c]
        dap, dkey = dsts[c]
        q = "V"
        d1 = st["d1"][c % 2]
        C.tt(d1[:], sap, mu[:], ALU.subtract, r=[skey, keys["mu"]], w=[f"ln.d1{c % 2}"], q=q)
        if gamma is None:
            C.tt(dap, d1[:], scap[:], ALU.mult, r=[f"ln.d1{c % 2}", keys["sc"]], w=[dkey], q=q)
        else:
            C.stt(dap, d1[:], gamma[:, c:c + 1], scap[:], ALU.mult, ALU.mult,
                  r=[f"ln.d1{c % 2}", "gamma", keys["sc"]], w=[dkey], q="V")


def ln_apply_resid(C, srcs, keys, t0, TT):
    pi = keys["pi"]
    st = C.lnst[pi]
    P = f"ln{pi}"
    mu, scap = keys["muap"], keys["scap"]
    for c in range(KC):
        sap, skey = srcs[c]
        q = "V"
        d1 = st["d1"][c % 2]
        d2 = C.lnd2f[c % 2]
        xap = C.xres[c][:, t0:t0 + TT]
        xkey = f"xres{c}t{t0 // TILE}"
        C.tt(d1[:], sap, mu[:], ALU.subtract, r=[skey, keys["mu"]], w=[f"ln.d1{c % 2}"], q=q)
        C.tt(d2[:], d1[:], scap[:], ALU.mult, r=[f"ln.d1{c % 2}", keys["sc"]], w=[f"ln.d2f{c % 2}"], q=q)
        C.tt(xap, xap, d2[:], ALU.add, r=[f"ln.d2f{c % 2}", xkey], w=[xkey], q=q)


def ln_full(C, src_handles, src_prefix, t0, TT, mode, dsts=None, gamma=None, rms=False, resid_t0=None):
    bS, bQ = C.rb(), C.rb()
    srcs = []
    pi = (t0 // TILE) % 2
    sfx = f"t{t0 // TILE}" if src_prefix == "xres" else ""
    for c in range(KC):
        sap = src_handles[c][:, t0:t0 + TT]
        skey = f"{src_prefix}{c}{sfx}"
        sq = C.lnsq[pi * 2 + c % 2]
        sqk = f"ln{pi}.sq{c % 2}"
        C.act(sq[:], sap, AF.Square, r=[skey], w=[sqk])
        C.mm(bS, C.ones_bf[:], sap, start=(c == 0), stop=(c == KC - 1), r=[skey, "onesb"])
        C.mm(bQ, C.ones_bf[:], sq[:], start=(c == 0), stop=(c == KC - 1), r=[sqk, "onesb"])
        srcs.append((sap, skey))
    keys = ln_tilemath(C, bS, bQ, pi=pi, rms=rms)
    if mode == "dst":
        dpairs = [(dsts[c][0], dsts[c][1]) for c in range(KC)]
        ln_apply_dst(C, srcs, keys, dpairs, gamma=gamma)
    else:
        ln_apply_resid(C, srcs, keys, t0 if resid_t0 is None else resid_t0, TT)



def make_fixtures(C, nc):
    C.xres = [C.sb(f"xres{c}", [128, T], BF16) for c in range(KC)]
    C.ones_f = C.sb("ones_f", [128, 128], F32)
    C.ones_bf = C.sb("ones_bf", [128, 128], BF16)
    C.ident = C.sb("ident", [128, 128], F32)
    C.identb = C.sb("identb", [128, 128], BF16)
    C.consts = C.sb("consts", [128, 8], F32)
    C.c_eps_ln = C.consts[:, 0:1]
    C.c_eps_attn = C.consts[:, 1:2]
    C.c_zero = C.consts[:, 2:3]
    C.c_one = C.consts[:, 3:4]
    C.c_ln32 = C.consts[:, 4:5]
    C.lnst = []
    shared_d1 = [C.sb(f"lnst_d1{i}", [128, TILE], BF16) for i in range(2)]
    shared_d2 = [C.sb(f"lnst_d2{i}", [128, TILE], BF16) for i in range(2)]
    C.lnd2f = [C.sb(f"lnst_d2f{i}", [128, TILE], F32) for i in range(2)]
    for pi in range(2):
        d = {k: C.sb(f"lnst{pi}_{k}", [128, TILE], F32)
             for k in ("mu", "var", "rstd", "m")}
        d["mub"] = C.sb(f"lnst{pi}_mub", [128, TILE], BF16)
        d["scb"] = C.sb(f"lnst{pi}_scb", [128, TILE], BF16)
        d["d1"], d["d2"] = shared_d1, shared_d2
        C.lnst.append(d)
    C.lnsq = [C.sb(f"lnsq{i}", [128, TILE], BF16) for i in range(4)]
    C.xio_sb = [C.sb(f"xio_sb{i}", [128, D], F32) for i in range(2)]

    IN_FIX = nc.dram_tensor("fix", [128, 8 + 128 + 128], F32, kind="ExternalInput").ap()
    C.dmaq = "Y"
    C.P.op("Y", lambda e: e.dma_start(out=C.consts[:], in_=IN_FIX[:, 0:8]),
           r=[], w=["consts"], dma=True, tag="dma.consts")
    C.P.op("Y", lambda e: e.dma_start(out=C.ones_f[:], in_=IN_FIX[:, 8:136]),
           r=[], w=["onesf"], dma=True, tag="dma.ones")
    C.P.op("P", lambda e: e.dma_start(out=C.ones_bf[:], in_=IN_FIX[:, 8:136]),
           r=[], w=["onesb"], dma=True, tag="dma.onesb")
    C.P.op("Y", lambda e: e.dma_start(out=C.ident[:], in_=IN_FIX[:, 136:264]),
           r=[], w=["ident"], dma=True, tag="dma.ident")
    C.P.op("P", lambda e: e.dma_start(out=C.identb[:], in_=IN_FIX[:, 136:264]),
           r=[], w=["identb"], dma=True, tag="dma.identb")


def emit_load_x_first(C, nc):
    XIN = nc.dram_tensor("xin", [T, D], F32, kind="ExternalInput").ap()
    xin_sb = C.xio_sb
    ntc = T // 128
    for tc in range(ntc):
        buf = xin_sb[tc % 2]
        bk = f"xiob{tc % 2}"
        C.dma("Y", buf[:], XIN[tc * 128:(tc + 1) * 128, :], r=[], w=[bk])
        for c in range(KC):
            bi = C.rb()
            C.transpose(self_ap(C, bi), buf[:, c * 128:(c + 1) * 128], C.ident[:],
                        r=[bk], w=[f"PB{bi}"])
            C.copy(C.xres[c][:, tc * 128:(tc + 1) * 128], C.bank(bi, 0, 128, 0, 128),
                   r=[f"PB{bi}"], w=[f"xres{c}t{tc // 4}"])


def self_ap(C, bi):
    return C.bank(bi, 0, 128, 0, 128)


def emit_load_x_fm(C, nc):
    XFM = nc.dram_tensor("xfm", [KC, 128, T], BF16, kind="ExternalInput").ap()
    for c in range(KC):
        C.dma("Y", C.xres[c][:], XFM[c], r=[], w=[f"xres{c}t{ti}" for ti in range(NT)])


def emit_store_x_fm(C, nc):
    XFO = nc.dram_tensor("xfo", [KC, 128, T], BF16, kind="ExternalOutput").ap()
    for c in range(KC):
        C.dma("Y", XFO[c], C.xres[c][:], r=[f"xres{c}t{ti}" for ti in range(NT)], w=[], out=True)


def emit_store_x_last(C, nc):
    OUT = nc.dram_tensor("out", [T, D], F32, kind="ExternalOutput").ap()
    out_sb = C.xio_sb
    ntc = T // 128
    for tc in range(ntc):
        buf = out_sb[tc % 2]
        bk = f"xiob{tc % 2}"
        for c in range(KC):
            bi = C.rb()
            bap = C.bank_bf(bi, 0, 128, 0, 128)
            C.transpose(bap, C.xres[c][:, tc * 128:(tc + 1) * 128], C.identb[:],
                        r=[f"xres{c}t{tc // 4}", "identb"], w=[f"PB{bi}"])
            C.copy(buf[:, c * 128:(c + 1) * 128], bap, r=[f"PB{bi}"], w=[bk])
        C.dma("Y", OUT[tc * 128:(tc + 1) * 128, :], buf[:], r=[bk], w=[], out=True)



def emit_l0_ffn(C, nc):
    U1 = nc.dram_tensor("u1", [D, 256], F32, kind="ExternalInput").ap()
    V1 = nc.dram_tensor("v1", [256, 4096], F32, kind="ExternalInput").ap()
    U2 = nc.dram_tensor("u2", [4096, 256], F32, kind="ExternalInput").ap()
    V2 = nc.dram_tensor("v2", [256, D], F32, kind="ExternalInput").ap()
    B1 = nc.dram_tensor("b1f", [128, 32], F32, kind="ExternalInput").ap()
    B2 = nc.dram_tensor("b2f", [128, 8], F32, kind="ExternalInput").ap()
    u1 = C.ab("u1sb", [128, 8 * 256], BF16)
    v1 = C.ab("v1sb", [128, 2 * 4096], BF16)
    u2 = C.ab("u2sb", [128, 32 * 256], BF16)
    v2 = C.ab("v2sb", [128, 2 * D], BF16)
    b1 = C.ab("b1sb", [128, 32], F32)
    b2 = C.ab("b2sb", [128, 8], F32)
    for k in range(8):
        C.dma("P", u1[:, k * 256:(k + 1) * 256], U1[k * 128:(k + 1) * 128, :], r=[], w=["u1"])
    for k in range(2):
        C.dma("P", v1[:, k * 4096:(k + 1) * 4096], V1[k * 128:(k + 1) * 128, :], r=[], w=["v1"])
    for k in range(32):
        C.dma("P", u2[:, k * 256:(k + 1) * 256], U2[k * 128:(k + 1) * 128, :], r=[], w=["u2"])
    for k in range(2):
        C.dma("P", v2[:, k * D:(k + 1) * D], V2[k * 128:(k + 1) * 128, :], r=[], w=["v2"])
    C.dma("Y", b1[:], B1, r=[], w=["b1"])
    C.dma("Y", b2[:], B2, r=[], w=["b2"])

    xn = [C.ab(f"ffn_xn{c}", [128, 2 * TILE], BF16) for c in range(KC)]
    p_sb = C.ab("ffn_p", [128, 2 * 2 * TILE], BF16)
    h_sb = C.ab("ffn_h", [128, 32 * 2 * TILE], BF16)
    q_sb = C.ab("ffn_q", [128, 2 * 2 * TILE], BF16)
    o_sb = [C.ab(f"ffn_o{c}", [128, 2 * TILE], BF16) for c in range(KC)]
    osq = [C.ab(f"ffn_osq{c}", [128, 2 * TILE], BF16) for c in range(KC)]

    phases = globals().get("_L0F_PHASES", 6)
    for tg in range(2):
        for tl in range(2):
            t = 2 * tg + tl
            dsts = [(xn[c][:, tl * TILE:(tl + 1) * TILE], f"ffn_xn{c}s{tl}") for c in range(KC)]
            ln_full(C, C.xres, "xres", t * TILE, TILE, "dst", dsts=dsts)
        if phases < 2:
            continue
        for rc in range(2):
            bts = [C.rb(), C.rb()]
            for kc in range(KC):
                lhs = u1[:, kc * 256 + rc * 128: kc * 256 + (rc + 1) * 128]
                for tl in range(2):
                    C.mm(bts[tl], lhs, xn[kc][:, tl * TILE:(tl + 1) * TILE],
                         start=(kc == 0), stop=(kc == KC - 1), r=["u1", f"ffn_xn{kc}s{tl}"])
            for tl in range(2):
                C.copy(p_sb[:, (2 * rc + tl) * TILE:(2 * rc + tl + 1) * TILE],
                       C.bank(bts[tl]), r=[f"PB{bts[tl]}"], w=[f"ffn_p{rc}s{tl}"], q="S")
        if phases < 3:
            continue
        for fc in range(32):
            bts = [C.rb(), C.rb()]
            for rc in range(2):
                lhs = v1[:, rc * 4096 + fc * 128: rc * 4096 + (fc + 1) * 128]
                for tl in range(2):
                    C.mm(bts[tl], lhs, p_sb[:, (2 * rc + tl) * TILE:(2 * rc + tl + 1) * TILE],
                         start=(rc == 0), stop=(rc == 1), r=["v1", f"ffn_p{rc}s{tl}"])
            for tl in range(2):
                C.act(h_sb[:, (2 * fc + tl) * TILE:(2 * fc + tl + 1) * TILE], C.bank(bts[tl]),
                      AF.Gelu_apprx_tanh, bias=b1[:, fc:fc + 1],
                      r=[f"PB{bts[tl]}", "b1"], w=[f"ffn_h{fc}s{tl}"])
        if phases < 4:
            continue
        for rc in range(2):
            bts = [C.rb(), C.rb()]
            for fc in range(32):
                lhs = u2[:, fc * 256 + rc * 128: fc * 256 + (rc + 1) * 128]
                for tl in range(2):
                    C.mm(bts[tl], lhs, h_sb[:, (2 * fc + tl) * TILE:(2 * fc + tl + 1) * TILE],
                         start=(fc == 0), stop=(fc == 31), r=["u2", f"ffn_h{fc}s{tl}"])
            for tl in range(2):
                C.copy(q_sb[:, (2 * rc + tl) * TILE:(2 * rc + tl + 1) * TILE],
                       C.bank(bts[tl]), r=[f"PB{bts[tl]}"], w=[f"ffn_q{rc}s{tl}"], q="S")
        if phases < 5:
            continue
        for dc in range(KC):
            bts = [C.rb(), C.rb()]
            for rc in range(2):
                lhs = v2[:, rc * D + dc * 128: rc * D + (dc + 1) * 128]
                for tl in range(2):
                    C.mm(bts[tl], lhs, q_sb[:, (2 * rc + tl) * TILE:(2 * rc + tl + 1) * TILE],
                         start=(rc == 0), stop=(rc == 1), r=["v2", f"ffn_q{rc}s{tl}"])
            for tl in range(2):
                C.ts(o_sb[dc][:, tl * TILE:(tl + 1) * TILE], C.bank(bts[tl]), b2[:, dc:dc + 1],
                     None, ALU.add, r=[f"PB{bts[tl]}", "b2"], w=[f"ffn_o{dc}s{tl}"])
                C.act(osq[dc][:, tl * TILE:(tl + 1) * TILE], C.bank(bts[tl]), AF.Square,
                      bias=b2[:, dc:dc + 1], r=[f"PB{bts[tl]}", "b2"], w=[f"ffn_osq{dc}s{tl}"])
        if phases < 6:
            continue
        for tl in range(2):
            t = 2 * tg + tl
            srcs = [(o_sb[c][:, tl * TILE:(tl + 1) * TILE], f"ffn_o{c}s{tl}") for c in range(KC)]
            sqs = [(osq[c][:, tl * TILE:(tl + 1) * TILE], f"ffn_osq{c}s{tl}") for c in range(KC)]
            keys = ln_stats(C, srcs, sqs)
            ln_apply_resid(C, srcs, keys, t * TILE, TILE)



def build_program(blocks, first, last):
    nc = bass.Bass("TRN2", target_bir_lowering=False, debug=False)
    C = Ctx(nc)
    make_fixtures(C, nc)
    C.arena_init()
    if first:
        emit_load_x_first(C, nc)
    else:
        emit_load_x_fm(C, nc)
    for b in blocks:
        C.arena_reset()
        if b == "l0f":
            emit_l0_ffn(C, nc)
        elif b == "l0a":
            emit_l0_attn(C, nc)
        elif b == "l1a":
            emit_l1_attn(C, nc)
        elif b == "l1f":
            emit_l1_ffn(C, nc)
    if last:
        emit_store_x_last(C, nc)
    else:
        emit_store_x_fm(C, nc)
    for name in globals().get("_TAPS", ()):
        h = C.reg[name]
        tp = nc.dram_tensor(f"tap_{name}", list(h[:].shape), h[:].dtype,
                            kind="ExternalOutput").ap()
        C.dma("Y", tp, h[:], r=[name], w=[], out=True)
    C.P.finalize_and_emit()
    return nc



def _fix_array():
    fix = np.zeros((128, 8 + 128 + 128), np.float32)
    fix[:, 0] = EPS_LN
    fix[:, 1] = EPS_ATTN
    fix[:, 2] = 0.0
    fix[:, 3] = 1.0
    fix[:, 4] = float(np.log(np.sqrt(1024.0)))
    fix[:, 8:136] = 1.0
    fix[:, 136:264] = np.eye(128, dtype=np.float32)
    return fix


def _common_weight_arrays(inp):
    a = {}
    a["u1"] = np.ascontiguousarray(inp["L0_ffn_U1"])
    a["v1"] = np.ascontiguousarray(inp["L0_ffn_V1"])
    a["u2"] = np.ascontiguousarray(inp["L0_ffn_U2"])
    a["v2"] = np.ascontiguousarray(inp["L0_ffn_V2"])
    a["b1f"] = np.ascontiguousarray(inp["L0_ffn_b1"].reshape(32, 128).T)
    a["b2f"] = np.ascontiguousarray(inp["L0_ffn_b2"].reshape(8, 128).T)
    a["wqa"] = np.ascontiguousarray(inp["L0_attn_w_qkva"].T)
    a["wq1"] = np.ascontiguousarray(np.asarray(inp["L1_attn_w_qkv"]).T)
    a["bq1"] = np.ascontiguousarray(np.asarray(inp["L1_attn_b_qkv"]).reshape(24, 128).T)
    a["wo1"] = np.ascontiguousarray(np.asarray(inp["L1_attn_w_out"]).T)
    a["bo1"] = np.ascontiguousarray(np.asarray(inp["L1_attn_b_out"]).reshape(8, 128).T)
    ii = np.arange(128)
    a["mut"] = (ii[:, None] <= ii[None, :]).astype(np.float32)
    a["hones"] = ((ii[:, None] < 64) == (ii[None, :] < 64)).astype(np.float32)
    a["cent"] = np.ascontiguousarray(np.asarray(inp["L1_ffn_centroids"]).T)
    a["lng"] = np.ascontiguousarray(inp["L1_ffn_ln_g"])
    a["lnb"] = np.ascontiguousarray(inp["L1_ffn_ln_b"])
    a["w1"] = np.ascontiguousarray(inp["L1_ffn_w1"])
    a["w2"] = np.ascontiguousarray(inp["L1_ffn_w2"])
    a["b1e"] = np.ascontiguousarray(
        np.asarray(inp["L1_ffn_b1"]).reshape(8, 32, 128).transpose(2, 0, 1).reshape(128, 256))
    a["b2e"] = np.ascontiguousarray(inp["L1_ffn_b2"])
    sel = np.zeros((8, 8 * 128), np.float32)
    for e in range(8):
        sel[e, e * 128:(e + 1) * 128] = 1.0
    a["sel"] = sel
    a["gam"] = np.ascontiguousarray(np.asarray(inp["L0_attn_gamma"]).reshape(8, 128).T)
    a["fix"] = _fix_array()
    return a


def _launch(blocks, first, last, per_core_maps):
    nc = build_program(blocks, first, last)
    res = run_bass_kernel_spmd(nc, per_core_maps, core_ids=list(range(NCORE)))
    return res.results


def kernel(**inputs):
    x = np.asarray(inputs["L0_attn_x"]).reshape(16384, D).astype(np.float32)
    shards = [np.ascontiguousarray(x[i * T:(i + 1) * T]) for i in range(NCORE)]
    wa = _common_weight_arrays(inputs)

    launches = globals().get("_TEST_LAUNCHES") or [(ALL_BLOCKS, True, True)]
    rmasks = []
    for r in range(NCORE):
        m = np.zeros((128, NCORE), np.float32)
        m[:, :r] = 1.0
        rmasks.append(m)
    cur = [{"xin": shards[i], "rmask": rmasks[i], "rmaskc": 1.0 - rmasks[i], **wa}
           for i in range(NCORE)]
    outs = None
    for blocks, first, last in launches:
        results = _launch(blocks, first, last, cur)
        if last:
            outs = [r["out"] for r in results]
        else:
            cur = [{**cur[i], "xfm": results[i]["xfo"]} for i in range(NCORE)]
    y = np.concatenate(outs, axis=0).reshape(1, 16384, D)
    return y.astype(np.float32)


def emit_l0_attn(C, nc):
    WQA = nc.dram_tensor("wqa", [D, 3 * D], F32, kind="ExternalInput").ap()
    GAM = nc.dram_tensor("gam", [128, 8], F32, kind="ExternalInput").ap()
    RM = C.dram_in("rmask", [128, 8])
    RMC = C.dram_in("rmaskc", [128, 8])
    gam = C.ab("gam_sb", [128, 8], F32)
    n2t = C.ab("lnst_n2", [128, TILE], F32)
    msct = C.ab("lnst_msc", [128, TILE], F32)
    C.lnst[0]["n2"] = C.lnst[1]["n2"] = n2t
    C.lnst[0]["msc"] = C.lnst[1]["msc"] = msct
    rm = C.ab("rm_sb", [128, 8], F32)
    rmc = C.ab("rmc_sb", [128, 8], F32)
    C.dma("Y", gam[:], GAM, r=[], w=["gamma"])
    C.dma("Y", rm[:], RM, r=[], w=["rmask"])
    C.dma("Y", rmc[:], RMC, r=[], w=["rmaskc"])

    h = [C.ab(f"l0a_h{c}", [128, T], BF16) for c in range(KC)]
    s1 = [C.ab(f"l0a_s1{c}", [128, T], BF16) for c in range(KC)]
    s2 = [C.ab(f"l0a_s2{i}", [128, T], BF16) for i in range(4)]
    qb = [C.ab(f"l0a_q{i}", [128, T], BF16) for i in range(2)]
    ab_ = [C.ab(f"l0a_a{i}", [128, T], F32) for i in range(2)]
    kvb = [C.ab(f"l0a_kv{i}", [128, T], BF16) for i in range(2)]
    yb = [C.ab(f"l0a_y{i}", [128, T], BF16) for i in range(2)]
    pb = [C.ab(f"l0a_p{i}", [128, T], BF16) for i in range(2)]
    wq = [C.ab(f"l0a_w{i}", [128, 8 * 128], BF16) for i in range(4)]
    exch = [C.ab(f"l0a_exch{i}", [128, 8], F32) for i in range(2)]
    gsb = [C.ab(f"l0a_gsb{i}", [128, 64], F32) for i in range(2)]
    carry = C.ab("l0a_carry", [128, 8], F32)
    aeff = C.ab("l0a_aeff", [128, 4], F32)
    yeff = C.ab("l0a_yeff", [128, 4], F32)

    cc_in = [nc.dram_tensor(f"l0a_ccin{hh}", [128, 8], F32) for hh in range(2)]
    cc_out = [nc.dram_tensor(f"l0a_ccout{hh}", [1024, 8], F32, addr_space="Shared")
              for hh in range(2)]

    for t in range(NT):
        dsts = [(h[c][:, t * TILE:(t + 1) * TILE], f"l0a_h{c}t{t}") for c in range(KC)]
        ln_full(C, C.xres, "xres", t * TILE, TILE, "dst", dsts=dsts, gamma=gam, rms=True)

    combined = [False, False]

    def fixup_half(hh, ci):
        cs = slice(hh * 4, hh * 4 + 4)
        if not combined[hh]:
            combined[hh] = True
            C.memset(carry[:, cs], 0.0, w=["l0a_carry"])
            for j in range(NCORE):
                C.ts(aeff[:], gsb[hh][:, j * 8:j * 8 + 4], rm[:, j:j + 1], rmc[:, j:j + 1],
                     ALU.mult, r=[f"l0a_gsb{hh}", "rmask", "rmaskc"], w=["l0a_aeff"], op1=ALU.add)
                C.ts(yeff[:], gsb[hh][:, j * 8 + 4:j * 8 + 8], rm[:, j:j + 1], None,
                     ALU.mult, r=[f"l0a_gsb{hh}", "rmask"], w=["l0a_yeff"])
                C.tt(carry[:, cs], carry[:, cs], aeff[:], ALU.mult, r=["l0a_carry", "l0a_aeff"], w=["l0a_carry"])
                C.tt(carry[:, cs], carry[:, cs], yeff[:], ALU.add, r=["l0a_carry", "l0a_yeff"], w=["l0a_carry"])
        c = hh * 4 + ci
        C.stt(s1[c][:], s2[ci][:], carry[:, c:c + 1], s1[c][:], ALU.mult, ALU.add,
              r=[f"l0a_s2h{hh}{ci}", "l0a_carry", f"l0a_s1{c}"], w=[f"l0a_s1{c}"])

    for half in range(2):
        for ci in range(4):
            c = half * 4 + ci
            si = c % 2
            kq, ka, kkv, ky, kp_ = (f"l0a_qb{si}", f"l0a_ab{si}", f"l0a_kvb{si}",
                                    f"l0a_yb{si}", f"l0a_pb{si}")
            for gi, (eg, dstbuf, dkey, evac) in enumerate([
                    (c, qb[si], kq, "copy"),
                    (16 + c, ab_[si], ka, "sigmoid"),
                    (8 + c, kvb[si], kkv, "copy")]):
                wbuf = wq[(3 * c + gi) % 4]
                wkey = f"l0a_w{(3 * c + gi) % 4}"
                C.dma("P", wbuf[:].rearrange("p (k e) -> p k e", k=KC),
                      WQA[:, eg * 128:(eg + 1) * 128].rearrange("(k p) e -> p k e", p=128),
                      r=[], w=[wkey])
                bts = [C.rb(), C.rb(), C.rb(), C.rb()]
                for k in range(KC):
                    lhs = wbuf[:, k * 128:(k + 1) * 128]
                    for t in range(NT):
                        C.mm(bts[t], lhs, h[k][:, t * TILE:(t + 1) * TILE],
                             start=(k == 0), stop=(k == KC - 1), r=[wkey, f"l0a_h{k}t{t}"])
                for t in range(NT):
                    dap = dstbuf[:, t * TILE:(t + 1) * TILE]
                    if evac == "sigmoid":
                        C.act(dap, C.bank(bts[t]), AF.Sigmoid, r=[f"PB{bts[t]}"], w=[ka])
                    else:
                        C.act(dap, C.bank(bts[t]), AF.Copy, r=[f"PB{bts[t]}"], w=[dkey])
            C.scan(yb[si][:], ab_[si][:], kvb[si][:], C.c_zero, ALU.mult, ALU.add,
                   r=[ka, kkv, "consts"], w=[ky])
            C.scan(pb[si][:], ab_[si][:], ab_[si][:], C.c_one, ALU.mult, ALU.min,
                   r=[ka, "consts"], w=[kp_])
            C.tt(s1[c][:], qb[si][:], yb[si][:], ALU.mult, r=[kq, ky], w=[f"l0a_s1{c}"])
            if half == 1:
                fixup_half(0, ci)
            C.tt(s2[ci][:], qb[si][:], pb[si][:], ALU.mult, r=[kq, kp_], w=[f"l0a_s2h{half}{ci}"])
            C.copy(exch[half][:, ci:ci + 1], pb[si][:, T - 1:T], r=[kp_], w=[f"l0a_exch{half}"])
            C.copy(exch[half][:, 4 + ci:5 + ci], yb[si][:, T - 1:T], r=[ky], w=[f"l0a_exch{half}"])
        C.dma("P", cc_in[half][:, :], exch[half][:], r=[f"l0a_exch{half}"], w=[f"l0a_ccin{half}"])
        C.cc("AllGather", [cc_in[half][:, :]], [cc_out[half][:, :]],
             r=[f"l0a_ccin{half}"], w=[f"l0a_ccout{half}"])
        C.dma("Y", gsb[half][:].rearrange("p (r f) -> p r f", r=8),
              cc_out[half][:, :].rearrange("(r p) f -> p r f", p=128),
              r=[f"l0a_ccout{half}"], w=[f"l0a_gsb{half}"])
    for ci in range(4):
        fixup_half(1, ci)
    for t in range(NT):
        ln_full(C, s1, "l0a_s1", t * TILE, TILE, "resid")


def emit_l0_ffn(C, nc):
    U1 = nc.dram_tensor("u1", [D, 256], F32, kind="ExternalInput").ap()
    V1 = nc.dram_tensor("v1", [256, 4096], F32, kind="ExternalInput").ap()
    U2 = nc.dram_tensor("u2", [4096, 256], F32, kind="ExternalInput").ap()
    V2 = nc.dram_tensor("v2", [256, D], F32, kind="ExternalInput").ap()
    B1 = nc.dram_tensor("b1f", [128, 32], F32, kind="ExternalInput").ap()
    B2 = nc.dram_tensor("b2f", [128, 8], F32, kind="ExternalInput").ap()
    u1 = C.ab("u1sb", [128, 8 * 256], BF16)
    v1 = C.ab("v1sb", [128, 2 * 4096], BF16)
    u2 = C.ab("u2sb", [128, 32 * 256], BF16)
    v2 = C.ab("v2sb", [128, 2 * D], BF16)
    b1 = C.ab("b1sb", [128, 32], F32)
    b2 = C.ab("b2sb", [128, 8], F32)
    for k in range(8):
        C.dma("P", u1[:, k * 256:(k + 1) * 256], U1[k * 128:(k + 1) * 128, :], r=[], w=["u1"])
    for k in range(2):
        C.dma("P", v1[:, k * 4096:(k + 1) * 4096], V1[k * 128:(k + 1) * 128, :], r=[], w=["v1"])
    for k in range(32):
        C.dma("P", u2[:, k * 256:(k + 1) * 256], U2[k * 128:(k + 1) * 128, :], r=[], w=["u2"])
    for k in range(2):
        C.dma("P", v2[:, k * D:(k + 1) * D], V2[k * 128:(k + 1) * 128, :], r=[], w=["v2"])
    C.dma("Y", b1[:], B1, r=[], w=["b1"])
    C.dma("Y", b2[:], B2, r=[], w=["b2"])

    xn = [C.ab(f"ffn_xn{c}", [128, TILE], BF16) for c in range(KC)]
    p_sb = C.ab("ffn_p", [128, 2 * TILE], BF16)
    h_sb = C.ab("ffn_h", [128, 32 * TILE], BF16)
    q_sb = C.ab("ffn_q", [128, 2 * TILE], BF16)
    o_sb = [C.ab(f"ffn_o{c}", [128, TILE], BF16) for c in range(KC)]
    osq = [C.ab(f"ffn_osq{c}", [128, TILE], BF16) for c in range(KC)]

    phases = globals().get("_L0F_PHASES", 6)
    ntt = globals().get("_L0F_NT", NT)
    for t in range(ntt):
        t0 = t * TILE
        dsts = [(xn[c][:], f"ffn_xn{c}") for c in range(KC)]
        ln_full(C, C.xres, "xres", t0, TILE, "dst", dsts=dsts)
        if phases < 2:
            continue
        for rc in range(2):
            bi = C.rb()
            for kc in range(KC):
                C.mm(bi, u1[:, kc * 256 + rc * 128: kc * 256 + (rc + 1) * 128], xn[kc][:],
                     start=(kc == 0), stop=(kc == KC - 1), r=["u1", f"ffn_xn{kc}"])
            C.copy(p_sb[:, rc * TILE:(rc + 1) * TILE], C.bank(bi), r=[f"PB{bi}"], w=[f"ffn_p{rc}"], q="S")
        if phases < 3:
            continue
        for fc in range(32):
            bi = C.rb()
            for rc in range(2):
                C.mm(bi, v1[:, rc * 4096 + fc * 128: rc * 4096 + (fc + 1) * 128],
                     p_sb[:, rc * TILE:(rc + 1) * TILE],
                     start=(rc == 0), stop=(rc == 1), r=["v1", f"ffn_p{rc}"])
            C.act(h_sb[:, fc * TILE:(fc + 1) * TILE], C.bank(bi), AF.Gelu_apprx_tanh,
                  bias=b1[:, fc:fc + 1], r=[f"PB{bi}", "b1"], w=[f"ffn_h{fc}"])
        if phases < 4:
            continue
        for rc in range(2):
            bi = C.rb()
            for fc in range(32):
                C.mm(bi, u2[:, fc * 256 + rc * 128: fc * 256 + (rc + 1) * 128],
                     h_sb[:, fc * TILE:(fc + 1) * TILE],
                     start=(fc == 0), stop=(fc == 31), r=["u2", f"ffn_h{fc}"])
            C.copy(q_sb[:, rc * TILE:(rc + 1) * TILE], C.bank(bi), r=[f"PB{bi}"], w=[f"ffn_q{rc}"], q="S")
        if phases < 5:
            continue
        for dc in range(KC):
            bi = C.rb()
            for rc in range(2):
                C.mm(bi, v2[:, rc * D + dc * 128: rc * D + (dc + 1) * 128],
                     q_sb[:, rc * TILE:(rc + 1) * TILE],
                     start=(rc == 0), stop=(rc == 1), r=["v2", f"ffn_q{rc}"])
            C.ts(o_sb[dc][:], C.bank(bi), b2[:, dc:dc + 1], None, ALU.add,
                 r=[f"PB{bi}", "b2"], w=[f"ffn_o{dc}"])
            C.act(osq[dc][:], C.bank(bi), AF.Square, bias=b2[:, dc:dc + 1],
                  r=[f"PB{bi}", "b2"], w=[f"ffn_osq{dc}"])
        if phases < 6:
            continue
        srcs = [(o_sb[c][:], f"ffn_o{c}") for c in range(KC)]
        sqs = [(osq[c][:], f"ffn_osq{c}") for c in range(KC)]
        keys = ln_stats(C, srcs, sqs)
        ln_apply_resid(C, srcs, keys, t0, TILE)



def build_program(blocks, first, last):
    nc = bass.Bass("TRN2", target_bir_lowering=False, debug=False)
    C = Ctx(nc)
    make_fixtures(C, nc)
    C.arena_init()
    if first:
        emit_load_x_first(C, nc)
    else:
        emit_load_x_fm(C, nc)
    for b in blocks:
        C.arena_reset()
        if b == "l0f":
            emit_l0_ffn(C, nc)
        elif b == "l0a":
            emit_l0_attn(C, nc)
        elif b == "l1a":
            emit_l1_attn(C, nc)
        elif b == "l1f":
            emit_l1_ffn(C, nc)
    if last:
        emit_store_x_last(C, nc)
    else:
        emit_store_x_fm(C, nc)
    for name in globals().get("_TAPS", ()):
        h = C.reg[name]
        tp = nc.dram_tensor(f"tap_{name}", list(h[:].shape), h[:].dtype,
                            kind="ExternalOutput").ap()
        C.dma("Y", tp, h[:], r=[name], w=[], out=True)
    C.P.finalize_and_emit()
    return nc



def _fix_array():
    fix = np.zeros((128, 8 + 128 + 128), np.float32)
    fix[:, 0] = EPS_LN
    fix[:, 1] = EPS_ATTN
    fix[:, 2] = 0.0
    fix[:, 3] = 1.0
    fix[:, 4] = float(np.log(np.sqrt(1024.0)))
    fix[:, 8:136] = 1.0
    fix[:, 136:264] = np.eye(128, dtype=np.float32)
    return fix


def _common_weight_arrays(inp):
    a = {}
    a["u1"] = np.ascontiguousarray(inp["L0_ffn_U1"])
    a["v1"] = np.ascontiguousarray(inp["L0_ffn_V1"])
    a["u2"] = np.ascontiguousarray(inp["L0_ffn_U2"])
    a["v2"] = np.ascontiguousarray(inp["L0_ffn_V2"])
    a["b1f"] = np.ascontiguousarray(inp["L0_ffn_b1"].reshape(32, 128).T)
    a["b2f"] = np.ascontiguousarray(inp["L0_ffn_b2"].reshape(8, 128).T)
    a["wqa"] = np.ascontiguousarray(inp["L0_attn_w_qkva"].T)
    a["wq1"] = np.ascontiguousarray(np.asarray(inp["L1_attn_w_qkv"]).T)
    a["bq1"] = np.ascontiguousarray(np.asarray(inp["L1_attn_b_qkv"]).reshape(24, 128).T)
    a["wo1"] = np.ascontiguousarray(np.asarray(inp["L1_attn_w_out"]).T)
    a["bo1"] = np.ascontiguousarray(np.asarray(inp["L1_attn_b_out"]).reshape(8, 128).T)
    ii = np.arange(128)
    a["mut"] = (ii[:, None] <= ii[None, :]).astype(np.float32)
    a["hones"] = ((ii[:, None] < 64) == (ii[None, :] < 64)).astype(np.float32)
    a["cent"] = np.ascontiguousarray(np.asarray(inp["L1_ffn_centroids"]).T)
    a["lng"] = np.ascontiguousarray(inp["L1_ffn_ln_g"])
    a["lnb"] = np.ascontiguousarray(inp["L1_ffn_ln_b"])
    a["w1"] = np.ascontiguousarray(inp["L1_ffn_w1"])
    a["w2"] = np.ascontiguousarray(inp["L1_ffn_w2"])
    a["b1e"] = np.ascontiguousarray(
        np.asarray(inp["L1_ffn_b1"]).reshape(8, 32, 128).transpose(2, 0, 1).reshape(128, 256))
    a["b2e"] = np.ascontiguousarray(inp["L1_ffn_b2"])
    sel = np.zeros((8, 8 * 128), np.float32)
    for e in range(8):
        sel[e, e * 128:(e + 1) * 128] = 1.0
    a["sel"] = sel
    a["gam"] = np.ascontiguousarray(np.asarray(inp["L0_attn_gamma"]).reshape(8, 128).T)
    a["fix"] = _fix_array()
    return a


def _launch(blocks, first, last, per_core_maps):
    nc = build_program(blocks, first, last)
    res = run_bass_kernel_spmd(nc, per_core_maps, core_ids=list(range(NCORE)))
    return res.results


def kernel(**inputs):
    x = np.asarray(inputs["L0_attn_x"]).reshape(16384, D).astype(np.float32)
    shards = [np.ascontiguousarray(x[i * T:(i + 1) * T]) for i in range(NCORE)]
    wa = _common_weight_arrays(inputs)

    launches = globals().get("_TEST_LAUNCHES") or [(ALL_BLOCKS, True, True)]
    rmasks = []
    for r in range(NCORE):
        m = np.zeros((128, NCORE), np.float32)
        m[:, :r] = 1.0
        rmasks.append(m)
    cur = [{"xin": shards[i], "rmask": rmasks[i], "rmaskc": 1.0 - rmasks[i], **wa}
           for i in range(NCORE)]
    outs = None
    for blocks, first, last in launches:
        results = _launch(blocks, first, last, cur)
        if last:
            outs = [r["out"] for r in results]
        else:
            cur = [{**cur[i], "xfm": results[i]["xfo"]} for i in range(NCORE)]
    y = np.concatenate(outs, axis=0).reshape(1, 16384, D)
    return y.astype(np.float32)


def emit_l0_attn(C, nc):
    WQA = nc.dram_tensor("wqa", [D, 3 * D], F32, kind="ExternalInput").ap()
    GAM = nc.dram_tensor("gam", [128, 8], F32, kind="ExternalInput").ap()
    RM = C.dram_in("rmask", [128, 8])
    RMC = C.dram_in("rmaskc", [128, 8])
    gam = C.ab("gam_sb", [128, 8], F32)
    n2t = C.ab("lnst_n2", [128, TILE], F32)
    msct = C.ab("lnst_msc", [128, TILE], F32)
    C.lnst[0]["n2"] = C.lnst[1]["n2"] = n2t
    C.lnst[0]["msc"] = C.lnst[1]["msc"] = msct
    rm = C.ab("rm_sb", [128, 8], F32)
    rmc = C.ab("rmc_sb", [128, 8], F32)
    C.dma("Y", gam[:], GAM, r=[], w=["gamma"])
    C.dma("Y", rm[:], RM, r=[], w=["rmask"])
    C.dma("Y", rmc[:], RMC, r=[], w=["rmaskc"])

    h = [C.ab(f"l0a_h{c}", [128, T], BF16) for c in range(KC)]
    s1 = [C.ab(f"l0a_s1{c}", [128, T], BF16) for c in range(KC)]
    s2 = [C.ab(f"l0a_s2{i}", [128, T], BF16) for i in range(4)]
    qb = C.ab("l0a_q", [128, T], F32)
    ab = C.ab("l0a_a", [128, T], F32)
    kvb = C.ab("l0a_kv", [128, T], F32)
    yb = C.ab("l0a_y", [128, T], F32)
    pb = C.ab("l0a_p", [128, T], F32)
    wq = [C.ab(f"l0a_w{i}", [128, 8 * 128], BF16) for i in range(4)]
    exch = C.ab("l0a_exch", [128, 8], F32)
    gsb = C.ab("l0a_gsb", [128, 64], F32)
    carry = C.ab("l0a_carry", [128, 4], F32)
    aeff = C.ab("l0a_aeff", [128, 4], F32)
    yeff = C.ab("l0a_yeff", [128, 4], F32)

    cc_in = [nc.dram_tensor(f"l0a_ccin{hh}", [128, 8], F32) for hh in range(2)]
    cc_out = [nc.dram_tensor(f"l0a_ccout{hh}", [1024, 8], F32, addr_space="Shared")
              for hh in range(2)]

    for t in range(NT):
        dsts = [(h[c][:, t * TILE:(t + 1) * TILE], f"l0a_h{c}t{t}") for c in range(KC)]
        ln_full(C, C.xres, "xres", t * TILE, TILE, "dst", dsts=dsts, gamma=gam, rms=True)

    combined = [False, False]

    def fixup_half(hh, ci):
        cs = slice(hh * 4, hh * 4 + 4)
        if not combined[hh]:
            combined[hh] = True
            C.memset(carry[:, cs], 0.0, w=["l0a_carry"])
            for j in range(NCORE):
                C.ts(aeff[:], gsb[hh][:, j * 8:j * 8 + 4], rm[:, j:j + 1], rmc[:, j:j + 1],
                     ALU.mult, r=[f"l0a_gsb{hh}", "rmask", "rmaskc"], w=["l0a_aeff"], op1=ALU.add)
                C.ts(yeff[:], gsb[hh][:, j * 8 + 4:j * 8 + 8], rm[:, j:j + 1], None,
                     ALU.mult, r=[f"l0a_gsb{hh}", "rmask"], w=["l0a_yeff"])
                C.tt(carry[:, cs], carry[:, cs], aeff[:], ALU.mult, r=["l0a_carry", "l0a_aeff"], w=["l0a_carry"])
                C.tt(carry[:, cs], carry[:, cs], yeff[:], ALU.add, r=["l0a_carry", "l0a_yeff"], w=["l0a_carry"])
        c = hh * 4 + ci
        C.stt(s1[c][:], s2[ci][:], carry[:, c:c + 1], s1[c][:], ALU.mult, ALU.add,
              r=[f"l0a_s2h{hh}{ci}", "l0a_carry", f"l0a_s1{c}"], w=[f"l0a_s1{c}"])

    for half in range(2):
        for ci in range(4):
            c = half * 4 + ci
            for gi, (eg, dstbuf, dkey, evac) in enumerate([
                    (c, qb, "l0a_qb", "copy"),
                    (16 + c, ab, "l0a_ab", "sigmoid"),
                    (8 + c, kvb, "l0a_kvb", "copy")]):
                wbuf = wq[(3 * c + gi) % 4]
                wkey = f"l0a_w{(3 * c + gi) % 4}"
                C.dma("P", wbuf[:].rearrange("p (k e) -> p k e", k=KC),
                      WQA[:, eg * 128:(eg + 1) * 128].rearrange("(k p) e -> p k e", p=128),
                      r=[], w=[wkey])
                for t in range(NT):
                    bi = C.rb()
                    for k in range(KC):
                        C.mm(bi, wbuf[:, k * 128:(k + 1) * 128],
                             h[k][:, t * TILE:(t + 1) * TILE],
                             start=(k == 0), stop=(k == KC - 1), r=[wkey, f"l0a_h{k}t{t}"])
                    dap = dstbuf[:, t * TILE:(t + 1) * TILE]
                    if evac == "sigmoid":
                        C.act(dap, C.bank(bi), AF.Sigmoid, r=[f"PB{bi}"], w=[dkey])
                    else:
                        C.act(dap, C.bank(bi), AF.Copy, r=[f"PB{bi}"], w=[dkey])
            C.scan(yb[:], ab[:], kvb[:], C.c_zero, ALU.mult, ALU.add,
                   r=["l0a_ab", "l0a_kvb", "consts"], w=["l0a_yb"])
            C.scan(pb[:], ab[:], ab[:], C.c_one, ALU.mult, ALU.min,
                   r=["l0a_ab", "consts"], w=["l0a_pb"])
            C.tt(s1[c][:], qb[:], yb[:], ALU.mult, r=["l0a_qb", "l0a_yb"], w=[f"l0a_s1{c}"])
            C.tt(s2[ci][:], qb[:], pb[:], ALU.mult, r=["l0a_qb", "l0a_pb"], w=[f"l0a_s2{ci}"])
            C.copy(exch[:, ci:ci + 1], pb[:, T - 1:T], r=["l0a_pb"], w=["l0a_exch"])
            C.copy(exch[:, 4 + ci:5 + ci], yb[:, T - 1:T], r=["l0a_yb"], w=["l0a_exch"])
        C.dma("P", cc_in[half][:, :], exch[:], r=["l0a_exch"], w=[f"l0a_ccin{half}"])
        C.cc("AllGather", [cc_in[half][:, :]], [cc_out[half][:, :]],
             r=[f"l0a_ccin{half}"], w=[f"l0a_ccout{half}"])
        C.dma("P", gsb[:].rearrange("p (r f) -> p r f", r=8),
              cc_out[half][:, :].rearrange("(r p) f -> p r f", p=128),
              r=[f"l0a_ccout{half}"], w=["l0a_gsb"])
        C.memset(carry[:], 0.0, w=["l0a_carry"])
        for j in range(NCORE):
            C.ts(aeff[:], gsb[:, j * 8:j * 8 + 4], rm[:, j:j + 1], rmc[:, j:j + 1],
                 ALU.mult, r=["l0a_gsb", "rmask", "rmaskc"], w=["l0a_aeff"], op1=ALU.add)
            C.ts(yeff[:], gsb[:, j * 8 + 4:j * 8 + 8], rm[:, j:j + 1], None,
                 ALU.mult, r=["l0a_gsb", "rmask"], w=["l0a_yeff"])
            C.tt(carry[:], carry[:], aeff[:], ALU.mult, r=["l0a_carry", "l0a_aeff"], w=["l0a_carry"])
            C.tt(carry[:], carry[:], yeff[:], ALU.add, r=["l0a_carry", "l0a_yeff"], w=["l0a_carry"])
        for ci in range(4):
            c = half * 4 + ci
            C.stt(s1[c][:], s2[ci][:], carry[:, ci:ci + 1], s1[c][:], ALU.mult, ALU.add,
                  r=[f"l0a_s2{ci}", "l0a_carry", f"l0a_s1{c}"], w=[f"l0a_s1{c}"])
    for t in range(NT):
        ln_full(C, s1, "l0a_s1", t * TILE, TILE, "resid")


def emit_l1_attn(C, nc):
    WQ = nc.dram_tensor("wq1", [D, 3 * D], F32, kind="ExternalInput").ap()
    BQ = nc.dram_tensor("bq1", [128, 24], F32, kind="ExternalInput").ap()
    WO = nc.dram_tensor("wo1", [D, D], F32, kind="ExternalInput").ap()
    BO = nc.dram_tensor("bo1", [128, 8], F32, kind="ExternalInput").ap()
    MUT = nc.dram_tensor("mut", [128, 128], F32, kind="ExternalInput").ap()
    HON = nc.dram_tensor("hones", [128, 128], F32, kind="ExternalInput").ap()
    RM = C.dram_in("rmask", [128, 8])
    bq = C.ab("l1a_bq", [128, 24], F32)
    bo = C.ab("l1a_bo", [128, 8], F32)
    mut = C.ab("l1a_mut", [128, 128], BF16)
    hon = C.ab("l1a_hon", [128, 128], BF16)
    rm = C.ab("l1a_rm", [128, 8], F32)
    C.dma("Y", bq[:], BQ, r=[], w=["l1a_bq"])
    C.dma("Y", bo[:], BO, r=[], w=["l1a_bo"])
    C.dma("P", mut[:], MUT, r=[], w=["l1a_mut"])
    C.dma("P", hon[:], HON, r=[], w=["l1a_hon"])
    C.dma("Y", rm[:], RM, r=[], w=["l1a_rm"])

    xn = [C.ab(f"l1a_xn{c}", [128, 2 * TILE], BF16) for c in range(KC)]
    qp = [C.ab(f"l1a_qp{c}", [128, T], BF16) for c in range(KC)]
    kp = [C.ab(f"l1a_kp{c}", [128, T], BF16) for c in range(KC)]
    vt = [C.ab(f"l1a_vt{tc}", [128, D], BF16) for tc in range(16)]
    ktst = C.ab("l1a_ktst", [128, D], BF16)[:]
    wq = [C.ab(f"l1a_w{i}", [128, 8 * 128], BF16) for i in range(2)]
    st1 = C.lnst[0]["d1"]
    st2 = C.lnst[0]["d2"]
    vtmp = C.ab("l1a_vtmp", [128, TILE], BF16)
    zg = C.ab("l1a_zg", [128, 8], F32)
    zgg = C.ab("l1a_zgg", [128, 8], F32)
    gz = C.ab("l1a_gz", [128, 64], F32)
    sg = C.ab("l1a_sg", [128, D // 2], F32)
    srun = C.ab("l1a_srun", [128, D // 2], F32)
    srb = C.ab("l1a_srb", [128, D // 2], BF16)
    kcum = [C.ab(f"l1a_kcum{i}", [128, T], BF16) for i in range(2)]
    sstage_ap = kcum[0][:].bitcast(F32)[0:128, 0:D // 2]
    sm = C.ab("l1a_sm", [128, 256], BF16)
    nst0 = C.lnst[0]["d1"][0]
    nst = [nst0, nst0]
    o_sb = xn
    aeff = C.ab("l1a_ae", [128, 8], F32)

    ccz_in = nc.dram_tensor("l1a_ccz_in", [128, 8], F32)
    ccz_out = nc.dram_tensor("l1a_ccz_out", [1024, 8], F32, addr_space="Shared")
    ccs_in = nc.dram_tensor("l1a_ccs_in", [128, D // 2], F32)
    ccs_out = nc.dram_tensor("l1a_ccs_out", [1024, D // 2], F32, addr_space="Shared")

    for tg in range(2):
      for tl in range(2):
        t = 2 * tg + tl
        dsts = [(xn[c][:, tl * TILE:(tl + 1) * TILE], f"l1a_xn{c}t{t}") for c in range(KC)]
        ln_full(C, C.xres, "xres", t * TILE, TILE, "dst", dsts=dsts)
      for eg in range(24):
        wbuf = wq[eg % 2]
        wkey = f"l1a_w{eg % 2}"
        C.dma("P", wbuf[:].rearrange("p (k e) -> p k e", k=KC),
              WQ[:, eg * 128:(eg + 1) * 128].rearrange("(k p) e -> p k e", p=128),
              r=[], w=[wkey])
        bts = [C.rb(), C.rb()]
        for k in range(KC):
            lhs = wbuf[:, k * 128:(k + 1) * 128]
            for tl in range(2):
                t = 2 * tg + tl
                C.mm(bts[tl], lhs, xn[k][:, tl * TILE:(tl + 1) * TILE],
                     start=(k == 0), stop=(k == KC - 1), r=[wkey, f"l1a_xn{k}t{t}"])
        bcol = bq[:, eg:eg + 1]
        for tl in range(2):
            t = 2 * tg + tl
            bi = bts[tl]
            if eg < 16:
                c = eg % 8
                dst = (qp if eg < 8 else kp)[c]
                dkey = f"l1a_{'qp' if eg < 8 else 'kp'}{c}"
                s1, s2 = C.lnd2f[t % 2], st2[t % 2]
                k1, k2 = f"ln.d2f{t % 2}", f"ln.d2{t % 2}"
                C.ts(s1[:], C.bank(bi), bcol, 0.0, ALU.add, op1=ALU.min,
                     r=[f"PB{bi}", "l1a_bq"], w=[k1])
                C.act(s1[:], s1[:], AF.Exp, r=[k1], w=[k1])
                C.ts(s2[:], C.bank(bi), bcol, 0.0, ALU.add, op1=ALU.max,
                     r=[f"PB{bi}", "l1a_bq"], w=[k2])
                C.tt(dst[:, t * TILE:(t + 1) * TILE], s1[:], s2[:], ALU.add,
                     r=[k1, k2], w=[dkey])
            else:
                ec = eg - 16
                C.ts(vtmp[:], C.bank(bi), bcol, None, ALU.add,
                     r=[f"PB{bi}", "l1a_bq"], w=["l1a_vtmp"])
                bj = C.rb()
                for stl in range(4):
                    C.transpose(C.bank_bf(bj, 0, 128, stl * 128, (stl + 1) * 128),
                                vtmp[:, stl * 128:(stl + 1) * 128], C.identb[:],
                                r=["l1a_vtmp", "identb"], w=[f"PB{bj}"])
                for stl in range(4):
                    tc = t * 4 + stl
                    C.copy(vt[tc][:, ec * 128:(ec + 1) * 128],
                           C.bank_bf(bj, 0, 128, stl * 128, (stl + 1) * 128),
                           r=[f"PB{bj}"], w=[f"l1a_vt{tc}"])

    for c in range(KC):
        C.red(zg[:, c:c + 1], kp[c][:], ALU.add, r=[f"l1a_kp{c}"], w=["l1a_zg"])
    C.dma("P", ccz_in[:, :], zg[:], r=["l1a_zg"], w=["ccz_in"])
    C.cc("AllGather", [ccz_in[:, :]], [ccz_out[:, :]], r=["ccz_in"], w=["ccz_out"])
    C.dma("Y", gz[:].rearrange("p (r f) -> p r f", r=8),
          ccz_out[:, :].rearrange("(r p) f -> p r f", p=128), r=["ccz_out"], w=["l1a_gz"])
    C.memset(zgg[:], 0.0, w=["l1a_zgg"])
    for j in range(NCORE):
        C.ts(aeff[:], gz[:, j * 8:(j + 1) * 8], rm[:, j:j + 1], None, ALU.mult,
             r=["l1a_gz", "l1a_rm"], w=["l1a_ae"])
        C.tt(zgg[:], zgg[:], aeff[:], ALU.add, r=["l1a_zgg", "l1a_ae"], w=["l1a_zgg"])
    for tc in range(16):
        for ec in range(KC):
            bj = C.rb(0, 4)
            bap = C.bank_bf(bj, 0, 128, 0, 128)
            C.transpose(bap, kp[ec][:, tc * 128:(tc + 1) * 128], C.identb[:],
                        r=[f"l1a_kp{ec}", "identb"], w=[f"PB{bj}"])
            C.copy(ktst[0:128, ec * 128:(ec + 1) * 128], bap, r=[f"PB{bj}"], w=["l1a_ktst"])
        for h in range(16):
            ch, hp = h // 2, (h % 2) * 64
            bi = 4 + (h % 2)
            C.mm(bi, ktst[0:128, h * 64:(h + 1) * 64], vt[tc][:, h * 64:(h + 1) * 64],
                 start=True, stop=True, r=["l1a_ktst", f"l1a_vt{tc}"],
                 bank_ap=C.bank(bi, hp, hp + 64, 0, 64))
            if tc == 0:
                C.copy(sg[hp:hp + 64, ch * 64:(ch + 1) * 64], C.bank(bi, hp, hp + 64, 0, 64),
                       r=[f"PB{bi}"], w=["l1a_sg"], q="S")
            else:
                C.tt(sg[hp:hp + 64, ch * 64:(ch + 1) * 64], sg[hp:hp + 64, ch * 64:(ch + 1) * 64],
                     C.bank(bi, hp, hp + 64, 0, 64), ALU.add, r=["l1a_sg", f"PB{bi}"],
                     w=["l1a_sg"])

    C.dma("P", ccs_in[:, :], sg[:], r=["l1a_sg"], w=["ccs_in"])
    C.cc("AllGather", [ccs_in[:, :]], [ccs_out[:, :]], r=["ccs_in"], w=["ccs_out"])
    C.memset(srun[:], 0.0, w=["l1a_srun"] + [f"l1a_srun{h}" for h in range(16)])
    for c in range(KC):
        kc_b = kcum[c % 2]
        kck = f"l1a_kcum{c % 2}"
        C.scan(kc_b[:], kp[c][:], kp[c][:], zgg[:, c:c + 1], ALU.add, ALU.max,
               r=[f"l1a_kp{c}", "l1a_zgg"], w=[kck])
        C.tt(kc_b[:], qp[c][:], kc_b[:], ALU.mult, r=[f"l1a_qp{c}", kck],
             w=[kck])
        for t in range(NT):
            bi = C.rb(0, 4)
            C.mm(bi, hon[:], kc_b[:, t * TILE:(t + 1) * TILE],
                 start=True, stop=True, r=[kck, "l1a_hon"])
            C.act(nst0[:], C.bank(bi), AF.Ln, bias=C.c_eps_attn,
                  r=[f"PB{bi}", "consts"], w=["ln.d10"])
            C.act(nst0[:], nst0[:], AF.Exp, scale=-1.0,
                  r=["ln.d10"], w=["ln.d10"])
            C.tt(qp[c][:, t * TILE:(t + 1) * TILE], qp[c][:, t * TILE:(t + 1) * TILE],
                 nst0[:], ALU.mult, r=[f"l1a_qp{c}", "ln.d10"], w=[f"l1a_qp{c}"])
    for j in range(NCORE):
        C.dma("Y", sstage_ap, ccs_out[j * 128:(j + 1) * 128, :],
              r=["ccs_out"], w=["l1a_sstage", "l1a_kcum0"])
        C.stt(srun[:], sstage_ap, rm[:, j:j + 1], srun[:],
              ALU.mult, ALU.add, r=["l1a_sstage", "l1a_rm", "l1a_srun"], w=["l1a_srun"] + [f"l1a_srun{h}" for h in range(16)])
    C.copy(srb[:], srun[:], r=["l1a_srun"], w=["l1a_srb"] + [f"l1a_srb{h}" for h in range(16)])

    for tc in range(16):
        for ec in range(KC):
            bj = C.rb(0, 4)
            bap = C.bank_bf(bj, 0, 128, 0, 128)
            C.transpose(bap, kp[ec][:, tc * 128:(tc + 1) * 128], C.identb[:],
                        r=[f"l1a_kp{ec}", "identb"], w=[f"PB{bj}"])
            C.copy(ktst[0:128, ec * 128:(ec + 1) * 128], bap, r=[f"PB{bj}"], w=["l1a_ktst"])
        for ch in range(8):
            bc = 4 + (ch % 2)
            bu = C.rb(0, 4)
            for par in range(2):
                h = 2 * ch + par
                hp = par * 64
                q_ap = qp[ch][hp:hp + 64, tc * 128:(tc + 1) * 128]
                k_ap = kp[ch][hp:hp + 64, tc * 128:(tc + 1) * 128]
                smv = sm[:, par * 128:(par + 1) * 128]
                smk = f"l1a_sm{par}"
                sreg = (slice(hp, hp + 64), slice(ch * 64, (ch + 1) * 64))
                bs = C.rb(0, 4)
                C.mm(bs, k_ap, q_ap, start=True, stop=True,
                     r=[f"l1a_kp{ch}", f"l1a_qp{ch}"],
                     bank_ap=C.bank(bs, 0, 128, 0, 128))
                C.tt(smv, C.bank(bs, 0, 128, 0, 128), mut[:], ALU.mult,
                     r=[f"PB{bs}", "l1a_mut"], w=[smk])
                C.mm(bc, vt[tc][:, h * 64:(h + 1) * 64], smv, start=True, stop=False,
                     r=[f"l1a_vt{tc}", smk], bank_ap=C.bank(bc, hp, hp + 64, 0, 128))
                C.mm(bc, srb[sreg], q_ap, start=False, stop=True,
                     r=[f"l1a_srb{h}", f"l1a_qp{ch}"], bank_ap=C.bank(bc, hp, hp + 64, 0, 128))
                C.mm(bu, ktst[0:128, h * 64:(h + 1) * 64], vt[tc][:, h * 64:(h + 1) * 64],
                     start=True, stop=True, r=["l1a_ktst", f"l1a_vt{tc}"],
                     bank_ap=C.bank(bu, hp, hp + 64, 0, 64))
            sboth = (slice(0, 128), slice(ch * 64, (ch + 1) * 64))
            hk = [f"l1a_srun{2 * ch}", f"l1a_srun{2 * ch + 1}"]
            bk = [f"l1a_srb{2 * ch}", f"l1a_srb{2 * ch + 1}"]
            C.tt(srun[sboth], srun[sboth], C.bank(bu, 0, 128, 0, 64), ALU.add,
                 r=hk + [f"PB{bu}"], w=hk)
            C.copy(srb[sboth], srun[sboth], r=hk, w=bk, q="S")
            C.copy(qp[ch][:, tc * 128:(tc + 1) * 128], C.bank(bc, 0, 128, 0, 128),
                   r=[f"PB{bc}"], w=[f"l1a_qp{ch}"])

    for tg in range(2):
        for oc in range(KC):
            wbuf = wq[oc % 2]
            wkey = f"l1a_w{oc % 2}"
            C.dma("P", wbuf[:].rearrange("p (k e) -> p k e", k=KC),
                  WO[:, oc * 128:(oc + 1) * 128].rearrange("(k p) e -> p k e", p=128),
                  r=[], w=[wkey])
            bts = [C.rb(0, 4), C.rb(0, 4)]
            for k in range(KC):
                lhs = wbuf[:, k * 128:(k + 1) * 128]
                for tl in range(2):
                    t = 2 * tg + tl
                    C.mm(bts[tl], lhs, qp[k][:, t * TILE:(t + 1) * TILE],
                         start=(k == 0), stop=(k == KC - 1), r=[wkey, f"l1a_qp{k}"])
            for tl in range(2):
                t = 2 * tg + tl
                C.ts(o_sb[oc][:, tl * TILE:(tl + 1) * TILE], C.bank(bts[tl]), bo[:, oc:oc + 1],
                     None, ALU.add, r=[f"PB{bts[tl]}", "l1a_bo"], w=[f"l1a_xn{oc}o{t}"])
        for tl in range(2):
            t = 2 * tg + tl
            srcs = [(o_sb[c][:, tl * TILE:(tl + 1) * TILE], f"l1a_xn{c}o{t}") for c in range(KC)]
            pi = t % 2
            bS, bQ = C.rb(0, 4), C.rb(0, 4)
            for c in range(KC):
                sap, skey = srcs[c]
                sq = C.lnsq[pi * 2 + c % 2]
                sqk = f"ln{pi}.sq{c % 2}"
                C.act(sq[:], sap, AF.Square, r=[skey], w=[sqk])
                C.mm(bS, C.ones_bf[:], sap, start=(c == 0), stop=(c == KC - 1), r=[skey, "onesb"])
                C.mm(bQ, C.ones_bf[:], sq[:], start=(c == 0), stop=(c == KC - 1), r=[sqk, "onesb"])
            keys = ln_tilemath(C, bS, bQ, pi=pi)
            ln_apply_resid(C, srcs, keys, t * TILE, TILE)


def build_program(blocks, first, last):
    nc = bass.Bass("TRN2", target_bir_lowering=False, debug=False)
    C = Ctx(nc)
    make_fixtures(C, nc)
    C.arena_init()
    if first:
        emit_load_x_first(C, nc)
    else:
        emit_load_x_fm(C, nc)
    for b in blocks:
        C.arena_reset()
        if b == "l0f":
            emit_l0_ffn(C, nc)
        elif b == "l0a":
            emit_l0_attn(C, nc)
        elif b == "l1a":
            emit_l1_attn(C, nc)
        elif b == "l1f":
            emit_l1_ffn(C, nc)
    if last:
        emit_store_x_last(C, nc)
    else:
        emit_store_x_fm(C, nc)
    for name in globals().get("_TAPS", ()):
        h = C.reg[name]
        tp = nc.dram_tensor(f"tap_{name}", list(h[:].shape), h[:].dtype,
                            kind="ExternalOutput").ap()
        C.dma("Y", tp, h[:], r=[name], w=[], out=True)
    C.P.finalize_and_emit()
    return nc



def _fix_array():
    fix = np.zeros((128, 8 + 128 + 128), np.float32)
    fix[:, 0] = EPS_LN
    fix[:, 1] = EPS_ATTN
    fix[:, 2] = 0.0
    fix[:, 3] = 1.0
    fix[:, 4] = float(np.log(np.sqrt(1024.0)))
    fix[:, 8:136] = 1.0
    fix[:, 136:264] = np.eye(128, dtype=np.float32)
    return fix


def _common_weight_arrays(inp):
    a = {}
    a["u1"] = np.ascontiguousarray(inp["L0_ffn_U1"])
    a["v1"] = np.ascontiguousarray(inp["L0_ffn_V1"])
    a["u2"] = np.ascontiguousarray(inp["L0_ffn_U2"])
    a["v2"] = np.ascontiguousarray(inp["L0_ffn_V2"])
    a["b1f"] = np.ascontiguousarray(inp["L0_ffn_b1"].reshape(32, 128).T)
    a["b2f"] = np.ascontiguousarray(inp["L0_ffn_b2"].reshape(8, 128).T)
    a["wqa"] = np.ascontiguousarray(inp["L0_attn_w_qkva"].T)
    a["wq1"] = np.ascontiguousarray(np.asarray(inp["L1_attn_w_qkv"]).T)
    a["bq1"] = np.ascontiguousarray(np.asarray(inp["L1_attn_b_qkv"]).reshape(24, 128).T)
    a["wo1"] = np.ascontiguousarray(np.asarray(inp["L1_attn_w_out"]).T)
    a["bo1"] = np.ascontiguousarray(np.asarray(inp["L1_attn_b_out"]).reshape(8, 128).T)
    ii = np.arange(128)
    a["mut"] = (ii[:, None] <= ii[None, :]).astype(np.float32)
    a["hones"] = ((ii[:, None] < 64) == (ii[None, :] < 64)).astype(np.float32)
    a["cent"] = np.ascontiguousarray(np.asarray(inp["L1_ffn_centroids"]).T)
    a["lng"] = np.ascontiguousarray(inp["L1_ffn_ln_g"])
    a["lnb"] = np.ascontiguousarray(inp["L1_ffn_ln_b"])
    a["w1"] = np.ascontiguousarray(inp["L1_ffn_w1"])
    a["w2"] = np.ascontiguousarray(inp["L1_ffn_w2"])
    a["b1e"] = np.ascontiguousarray(
        np.asarray(inp["L1_ffn_b1"]).reshape(8, 32, 128).transpose(2, 0, 1).reshape(128, 256))
    a["b2e"] = np.ascontiguousarray(inp["L1_ffn_b2"])
    sel = np.zeros((8, 8 * 128), np.float32)
    for e in range(8):
        sel[e, e * 128:(e + 1) * 128] = 1.0
    a["sel"] = sel
    a["gam"] = np.ascontiguousarray(np.asarray(inp["L0_attn_gamma"]).reshape(8, 128).T)
    a["fix"] = _fix_array()
    return a


def _launch(blocks, first, last, per_core_maps):
    nc = build_program(blocks, first, last)
    res = run_bass_kernel_spmd(nc, per_core_maps, core_ids=list(range(NCORE)))
    return res.results


def kernel(**inputs):
    x = np.asarray(inputs["L0_attn_x"]).reshape(16384, D).astype(np.float32)
    shards = [np.ascontiguousarray(x[i * T:(i + 1) * T]) for i in range(NCORE)]
    wa = _common_weight_arrays(inputs)

    launches = globals().get("_TEST_LAUNCHES") or [(ALL_BLOCKS, True, True)]
    rmasks = []
    for r in range(NCORE):
        m = np.zeros((128, NCORE), np.float32)
        m[:, :r] = 1.0
        rmasks.append(m)
    cur = [{"xin": shards[i], "rmask": rmasks[i], "rmaskc": 1.0 - rmasks[i], **wa}
           for i in range(NCORE)]
    outs = None
    for blocks, first, last in launches:
        results = _launch(blocks, first, last, cur)
        if last:
            outs = [r["out"] for r in results]
        else:
            cur = [{**cur[i], "xfm": results[i]["xfo"]} for i in range(NCORE)]
    y = np.concatenate(outs, axis=0).reshape(1, 16384, D)
    return y.astype(np.float32)


def emit_l0_attn(C, nc):
    WQA = nc.dram_tensor("wqa", [D, 3 * D], F32, kind="ExternalInput").ap()
    GAM = nc.dram_tensor("gam", [128, 8], F32, kind="ExternalInput").ap()
    RM = C.dram_in("rmask", [128, 8])
    RMC = C.dram_in("rmaskc", [128, 8])
    gam = C.ab("gam_sb", [128, 8], F32)
    n2t = C.ab("lnst_n2", [128, TILE], F32)
    msct = C.ab("lnst_msc", [128, TILE], F32)
    C.lnst[0]["n2"] = C.lnst[1]["n2"] = n2t
    C.lnst[0]["msc"] = C.lnst[1]["msc"] = msct
    rm = C.ab("rm_sb", [128, 8], F32)
    rmc = C.ab("rmc_sb", [128, 8], F32)
    C.dma("Y", gam[:], GAM, r=[], w=["gamma"])
    C.dma("Y", rm[:], RM, r=[], w=["rmask"])
    C.dma("Y", rmc[:], RMC, r=[], w=["rmaskc"])

    h = [C.ab(f"l0a_h{c}", [128, T], BF16) for c in range(KC)]
    s1 = [C.ab(f"l0a_s1{c}", [128, T], BF16) for c in range(KC)]
    s2 = [C.ab(f"l0a_s2{i}", [128, T], BF16) for i in range(4)]
    qb = [C.ab(f"l0a_q{i}", [128, T], BF16) for i in range(2)]
    ab_ = [C.ab(f"l0a_a{i}", [128, T], F32) for i in range(2)]
    kvb = [C.ab(f"l0a_kv{i}", [128, T], BF16) for i in range(2)]
    yb = [C.ab(f"l0a_y{i}", [128, T], BF16) for i in range(2)]
    pb = [C.ab(f"l0a_p{i}", [128, T], BF16) for i in range(2)]
    wq = [C.ab(f"l0a_w{i}", [128, 8 * 128], BF16) for i in range(4)]
    exch = [C.ab(f"l0a_exch{i}", [128, 8], F32) for i in range(2)]
    gsb = [C.ab(f"l0a_gsb{i}", [128, 64], F32) for i in range(2)]
    carry = C.ab("l0a_carry", [128, 8], F32)
    aeff = C.ab("l0a_aeff", [128, 4], F32)
    yeff = C.ab("l0a_yeff", [128, 4], F32)

    cc_in = [nc.dram_tensor(f"l0a_ccin{hh}", [128, 8], F32) for hh in range(2)]
    cc_out = [nc.dram_tensor(f"l0a_ccout{hh}", [1024, 8], F32, addr_space="Shared")
              for hh in range(2)]

    for t in range(NT):
        dsts = [(h[c][:, t * TILE:(t + 1) * TILE], f"l0a_h{c}t{t}") for c in range(KC)]
        ln_full(C, C.xres, "xres", t * TILE, TILE, "dst", dsts=dsts, gamma=gam, rms=True)

    combined = [False, False]

    def fixup_half(hh, ci):
        cs = slice(hh * 4, hh * 4 + 4)
        if not combined[hh]:
            combined[hh] = True
            C.memset(carry[:, cs], 0.0, w=["l0a_carry"])
            for j in range(NCORE):
                C.ts(aeff[:], gsb[hh][:, j * 8:j * 8 + 4], rm[:, j:j + 1], rmc[:, j:j + 1],
                     ALU.mult, r=[f"l0a_gsb{hh}", "rmask", "rmaskc"], w=["l0a_aeff"], op1=ALU.add)
                C.ts(yeff[:], gsb[hh][:, j * 8 + 4:j * 8 + 8], rm[:, j:j + 1], None,
                     ALU.mult, r=[f"l0a_gsb{hh}", "rmask"], w=["l0a_yeff"])
                C.tt(carry[:, cs], carry[:, cs], aeff[:], ALU.mult, r=["l0a_carry", "l0a_aeff"], w=["l0a_carry"])
                C.tt(carry[:, cs], carry[:, cs], yeff[:], ALU.add, r=["l0a_carry", "l0a_yeff"], w=["l0a_carry"])
        c = hh * 4 + ci
        C.stt(s1[c][:], s2[ci][:], carry[:, c:c + 1], s1[c][:], ALU.mult, ALU.add,
              r=[f"l0a_s2h{hh}{ci}", "l0a_carry", f"l0a_s1{c}"], w=[f"l0a_s1{c}"])

    for half in range(2):
        for ci in range(4):
            c = half * 4 + ci
            si = c % 2
            kq, ka, kkv, ky, kp_ = (f"l0a_qb{si}", f"l0a_ab{si}", f"l0a_kvb{si}",
                                    f"l0a_yb{si}", f"l0a_pb{si}")
            for gi, (eg, dstbuf, dkey, evac) in enumerate([
                    (c, qb[si], kq, "copy"),
                    (16 + c, ab_[si], ka, "sigmoid"),
                    (8 + c, kvb[si], kkv, "copy")]):
                wbuf = wq[(3 * c + gi) % 4]
                wkey = f"l0a_w{(3 * c + gi) % 4}"
                C.dma("P", wbuf[:].rearrange("p (k e) -> p k e", k=KC),
                      WQA[:, eg * 128:(eg + 1) * 128].rearrange("(k p) e -> p k e", p=128),
                      r=[], w=[wkey])
                bts = [C.rb(), C.rb(), C.rb(), C.rb()]
                for k in range(KC):
                    lhs = wbuf[:, k * 128:(k + 1) * 128]
                    for t in range(NT):
                        C.mm(bts[t], lhs, h[k][:, t * TILE:(t + 1) * TILE],
                             start=(k == 0), stop=(k == KC - 1), r=[wkey, f"l0a_h{k}t{t}"])
                for t in range(NT):
                    dap = dstbuf[:, t * TILE:(t + 1) * TILE]
                    if evac == "sigmoid":
                        C.act(dap, C.bank(bts[t]), AF.Sigmoid, r=[f"PB{bts[t]}"], w=[ka])
                    else:
                        C.act(dap, C.bank(bts[t]), AF.Copy, r=[f"PB{bts[t]}"], w=[dkey])
            C.scan(yb[si][:], ab_[si][:], kvb[si][:], C.c_zero, ALU.mult, ALU.add,
                   r=[ka, kkv, "consts"], w=[ky])
            C.scan(pb[si][:], ab_[si][:], ab_[si][:], C.c_one, ALU.mult, ALU.min,
                   r=[ka, "consts"], w=[kp_])
            C.tt(s1[c][:], qb[si][:], yb[si][:], ALU.mult, r=[kq, ky], w=[f"l0a_s1{c}"])
            if half == 1:
                fixup_half(0, ci)
            C.tt(s2[ci][:], qb[si][:], pb[si][:], ALU.mult, r=[kq, kp_], w=[f"l0a_s2h{half}{ci}"])
            C.copy(exch[half][:, ci:ci + 1], pb[si][:, T - 1:T], r=[kp_], w=[f"l0a_exch{half}"])
            C.copy(exch[half][:, 4 + ci:5 + ci], yb[si][:, T - 1:T], r=[ky], w=[f"l0a_exch{half}"])
        C.dma("P", cc_in[half][:, :], exch[half][:], r=[f"l0a_exch{half}"], w=[f"l0a_ccin{half}"])
        C.cc("AllGather", [cc_in[half][:, :]], [cc_out[half][:, :]],
             r=[f"l0a_ccin{half}"], w=[f"l0a_ccout{half}"])
        C.dma("Y", gsb[half][:].rearrange("p (r f) -> p r f", r=8),
              cc_out[half][:, :].rearrange("(r p) f -> p r f", p=128),
              r=[f"l0a_ccout{half}"], w=[f"l0a_gsb{half}"])
    for ci in range(4):
        fixup_half(1, ci)
    for t in range(NT):
        ln_full(C, s1, "l0a_s1", t * TILE, TILE, "resid")


def emit_l0_ffn(C, nc):
    U1 = nc.dram_tensor("u1", [D, 256], F32, kind="ExternalInput").ap()
    V1 = nc.dram_tensor("v1", [256, 4096], F32, kind="ExternalInput").ap()
    U2 = nc.dram_tensor("u2", [4096, 256], F32, kind="ExternalInput").ap()
    V2 = nc.dram_tensor("v2", [256, D], F32, kind="ExternalInput").ap()
    B1 = nc.dram_tensor("b1f", [128, 32], F32, kind="ExternalInput").ap()
    B2 = nc.dram_tensor("b2f", [128, 8], F32, kind="ExternalInput").ap()
    u1 = C.ab("u1sb", [128, 8 * 256], BF16)
    v1 = C.ab("v1sb", [128, 2 * 4096], BF16)
    u2 = C.ab("u2sb", [128, 32 * 256], BF16)
    v2 = C.ab("v2sb", [128, 2 * D], BF16)
    b1 = C.ab("b1sb", [128, 32], F32)
    b2 = C.ab("b2sb", [128, 8], F32)
    for k in range(8):
        C.dma("P", u1[:, k * 256:(k + 1) * 256], U1[k * 128:(k + 1) * 128, :], r=[], w=["u1"])
    for k in range(2):
        C.dma("P", v1[:, k * 4096:(k + 1) * 4096], V1[k * 128:(k + 1) * 128, :], r=[], w=["v1"])
    for k in range(32):
        C.dma("P", u2[:, k * 256:(k + 1) * 256], U2[k * 128:(k + 1) * 128, :], r=[], w=["u2"])
    for k in range(2):
        C.dma("P", v2[:, k * D:(k + 1) * D], V2[k * 128:(k + 1) * 128, :], r=[], w=["v2"])
    C.dma("Y", b1[:], B1, r=[], w=["b1"])
    C.dma("Y", b2[:], B2, r=[], w=["b2"])

    xn = [C.ab(f"ffn_xn{c}", [128, TILE], BF16) for c in range(KC)]
    p_sb = C.ab("ffn_p", [128, 2 * TILE], BF16)
    h_sb = C.ab("ffn_h", [128, 32 * TILE], BF16)
    q_sb = C.ab("ffn_q", [128, 2 * TILE], BF16)
    o_sb = [C.ab(f"ffn_o{c}", [128, TILE], BF16) for c in range(KC)]
    osq = [C.ab(f"ffn_osq{c}", [128, TILE], BF16) for c in range(KC)]

    phases = globals().get("_L0F_PHASES", 6)
    ntt = globals().get("_L0F_NT", NT)
    for t in range(ntt):
        t0 = t * TILE
        dsts = [(xn[c][:], f"ffn_xn{c}") for c in range(KC)]
        ln_full(C, C.xres, "xres", t0, TILE, "dst", dsts=dsts)
        if phases < 2:
            continue
        for rc in range(2):
            bi = C.rb()
            for kc in range(KC):
                C.mm(bi, u1[:, kc * 256 + rc * 128: kc * 256 + (rc + 1) * 128], xn[kc][:],
                     start=(kc == 0), stop=(kc == KC - 1), r=["u1", f"ffn_xn{kc}"])
            C.copy(p_sb[:, rc * TILE:(rc + 1) * TILE], C.bank(bi), r=[f"PB{bi}"], w=[f"ffn_p{rc}"], q="S")
        if phases < 3:
            continue
        for fc in range(32):
            bi = C.rb()
            for rc in range(2):
                C.mm(bi, v1[:, rc * 4096 + fc * 128: rc * 4096 + (fc + 1) * 128],
                     p_sb[:, rc * TILE:(rc + 1) * TILE],
                     start=(rc == 0), stop=(rc == 1), r=["v1", f"ffn_p{rc}"])
            C.act(h_sb[:, fc * TILE:(fc + 1) * TILE], C.bank(bi), AF.Gelu_apprx_tanh,
                  bias=b1[:, fc:fc + 1], r=[f"PB{bi}", "b1"], w=[f"ffn_h{fc}"])
        if phases < 4:
            continue
        for rc in range(2):
            bi = C.rb()
            for fc in range(32):
                C.mm(bi, u2[:, fc * 256 + rc * 128: fc * 256 + (rc + 1) * 128],
                     h_sb[:, fc * TILE:(fc + 1) * TILE],
                     start=(fc == 0), stop=(fc == 31), r=["u2", f"ffn_h{fc}"])
            C.copy(q_sb[:, rc * TILE:(rc + 1) * TILE], C.bank(bi), r=[f"PB{bi}"], w=[f"ffn_q{rc}"], q="S")
        if phases < 5:
            continue
        for dc in range(KC):
            bi = C.rb()
            for rc in range(2):
                C.mm(bi, v2[:, rc * D + dc * 128: rc * D + (dc + 1) * 128],
                     q_sb[:, rc * TILE:(rc + 1) * TILE],
                     start=(rc == 0), stop=(rc == 1), r=["v2", f"ffn_q{rc}"])
            C.ts(o_sb[dc][:], C.bank(bi), b2[:, dc:dc + 1], None, ALU.add,
                 r=[f"PB{bi}", "b2"], w=[f"ffn_o{dc}"])
            C.act(osq[dc][:], C.bank(bi), AF.Square, bias=b2[:, dc:dc + 1],
                  r=[f"PB{bi}", "b2"], w=[f"ffn_osq{dc}"])
        if phases < 6:
            continue
        srcs = [(o_sb[c][:], f"ffn_o{c}") for c in range(KC)]
        sqs = [(osq[c][:], f"ffn_osq{c}") for c in range(KC)]
        keys = ln_stats(C, srcs, sqs)
        ln_apply_resid(C, srcs, keys, t0, TILE)



def build_program(blocks, first, last):
    nc = bass.Bass("TRN2", target_bir_lowering=False, debug=False)
    C = Ctx(nc)
    make_fixtures(C, nc)
    C.arena_init()
    if first:
        emit_load_x_first(C, nc)
    else:
        emit_load_x_fm(C, nc)
    for b in blocks:
        C.arena_reset()
        if b == "l0f":
            emit_l0_ffn(C, nc)
        elif b == "l0a":
            emit_l0_attn(C, nc)
        elif b == "l1a":
            emit_l1_attn(C, nc)
        elif b == "l1f":
            emit_l1_ffn(C, nc)
    if last:
        emit_store_x_last(C, nc)
    else:
        emit_store_x_fm(C, nc)
    for name in globals().get("_TAPS", ()):
        h = C.reg[name]
        tp = nc.dram_tensor(f"tap_{name}", list(h[:].shape), h[:].dtype,
                            kind="ExternalOutput").ap()
        C.dma("Y", tp, h[:], r=[name], w=[], out=True)
    C.P.finalize_and_emit()
    return nc



def _fix_array():
    fix = np.zeros((128, 8 + 128 + 128), np.float32)
    fix[:, 0] = EPS_LN
    fix[:, 1] = EPS_ATTN
    fix[:, 2] = 0.0
    fix[:, 3] = 1.0
    fix[:, 4] = float(np.log(np.sqrt(1024.0)))
    fix[:, 8:136] = 1.0
    fix[:, 136:264] = np.eye(128, dtype=np.float32)
    return fix


def _common_weight_arrays(inp):
    a = {}
    a["u1"] = np.ascontiguousarray(inp["L0_ffn_U1"])
    a["v1"] = np.ascontiguousarray(inp["L0_ffn_V1"])
    a["u2"] = np.ascontiguousarray(inp["L0_ffn_U2"])
    a["v2"] = np.ascontiguousarray(inp["L0_ffn_V2"])
    a["b1f"] = np.ascontiguousarray(inp["L0_ffn_b1"].reshape(32, 128).T)
    a["b2f"] = np.ascontiguousarray(inp["L0_ffn_b2"].reshape(8, 128).T)
    a["wqa"] = np.ascontiguousarray(inp["L0_attn_w_qkva"].T)
    a["wq1"] = np.ascontiguousarray(np.asarray(inp["L1_attn_w_qkv"]).T)
    a["bq1"] = np.ascontiguousarray(np.asarray(inp["L1_attn_b_qkv"]).reshape(24, 128).T)
    a["wo1"] = np.ascontiguousarray(np.asarray(inp["L1_attn_w_out"]).T)
    a["bo1"] = np.ascontiguousarray(np.asarray(inp["L1_attn_b_out"]).reshape(8, 128).T)
    ii = np.arange(128)
    a["mut"] = (ii[:, None] <= ii[None, :]).astype(np.float32)
    a["hones"] = ((ii[:, None] < 64) == (ii[None, :] < 64)).astype(np.float32)
    a["cent"] = np.ascontiguousarray(np.asarray(inp["L1_ffn_centroids"]).T)
    a["lng"] = np.ascontiguousarray(inp["L1_ffn_ln_g"])
    a["lnb"] = np.ascontiguousarray(inp["L1_ffn_ln_b"])
    a["w1"] = np.ascontiguousarray(inp["L1_ffn_w1"])
    a["w2"] = np.ascontiguousarray(inp["L1_ffn_w2"])
    a["b1e"] = np.ascontiguousarray(
        np.asarray(inp["L1_ffn_b1"]).reshape(8, 32, 128).transpose(2, 0, 1).reshape(128, 256))
    a["b2e"] = np.ascontiguousarray(inp["L1_ffn_b2"])
    sel = np.zeros((8, 8 * 128), np.float32)
    for e in range(8):
        sel[e, e * 128:(e + 1) * 128] = 1.0
    a["sel"] = sel
    a["gam"] = np.ascontiguousarray(np.asarray(inp["L0_attn_gamma"]).reshape(8, 128).T)
    a["fix"] = _fix_array()
    return a


def _launch(blocks, first, last, per_core_maps):
    nc = build_program(blocks, first, last)
    res = run_bass_kernel_spmd(nc, per_core_maps, core_ids=list(range(NCORE)))
    return res.results


def kernel(**inputs):
    x = np.asarray(inputs["L0_attn_x"]).reshape(16384, D).astype(np.float32)
    shards = [np.ascontiguousarray(x[i * T:(i + 1) * T]) for i in range(NCORE)]
    wa = _common_weight_arrays(inputs)

    launches = globals().get("_TEST_LAUNCHES") or [(ALL_BLOCKS, True, True)]
    rmasks = []
    for r in range(NCORE):
        m = np.zeros((128, NCORE), np.float32)
        m[:, :r] = 1.0
        rmasks.append(m)
    cur = [{"xin": shards[i], "rmask": rmasks[i], "rmaskc": 1.0 - rmasks[i], **wa}
           for i in range(NCORE)]
    outs = None
    for blocks, first, last in launches:
        results = _launch(blocks, first, last, cur)
        if last:
            outs = [r["out"] for r in results]
        else:
            cur = [{**cur[i], "xfm": results[i]["xfo"]} for i in range(NCORE)]
    y = np.concatenate(outs, axis=0).reshape(1, 16384, D)
    return y.astype(np.float32)


def emit_l0_attn(C, nc):
    WQA = nc.dram_tensor("wqa", [D, 3 * D], F32, kind="ExternalInput").ap()
    GAM = nc.dram_tensor("gam", [128, 8], F32, kind="ExternalInput").ap()
    RM = C.dram_in("rmask", [128, 8])
    RMC = C.dram_in("rmaskc", [128, 8])
    gam = C.ab("gam_sb", [128, 8], F32)
    n2t = C.ab("lnst_n2", [128, TILE], F32)
    msct = C.ab("lnst_msc", [128, TILE], F32)
    C.lnst[0]["n2"] = C.lnst[1]["n2"] = n2t
    C.lnst[0]["msc"] = C.lnst[1]["msc"] = msct
    rm = C.ab("rm_sb", [128, 8], F32)
    rmc = C.ab("rmc_sb", [128, 8], F32)
    C.dma("Y", gam[:], GAM, r=[], w=["gamma"])
    C.dma("Y", rm[:], RM, r=[], w=["rmask"])
    C.dma("Y", rmc[:], RMC, r=[], w=["rmaskc"])

    h = [C.ab(f"l0a_h{c}", [128, T], BF16) for c in range(KC)]
    s1 = [C.ab(f"l0a_s1{c}", [128, T], BF16) for c in range(KC)]
    s2 = [C.ab(f"l0a_s2{i}", [128, T], BF16) for i in range(4)]
    qb = C.ab("l0a_q", [128, T], F32)
    ab = C.ab("l0a_a", [128, T], F32)
    kvb = C.ab("l0a_kv", [128, T], F32)
    yb = C.ab("l0a_y", [128, T], F32)
    pb = C.ab("l0a_p", [128, T], F32)
    wq = [C.ab(f"l0a_w{i}", [128, 8 * 128], BF16) for i in range(4)]
    exch = C.ab("l0a_exch", [128, 8], F32)
    gsb = C.ab("l0a_gsb", [128, 64], F32)
    carry = C.ab("l0a_carry", [128, 4], F32)
    aeff = C.ab("l0a_aeff", [128, 4], F32)
    yeff = C.ab("l0a_yeff", [128, 4], F32)

    cc_in = [nc.dram_tensor(f"l0a_ccin{hh}", [128, 8], F32) for hh in range(2)]
    cc_out = [nc.dram_tensor(f"l0a_ccout{hh}", [1024, 8], F32, addr_space="Shared")
              for hh in range(2)]

    for t in range(NT):
        dsts = [(h[c][:, t * TILE:(t + 1) * TILE], f"l0a_h{c}t{t}") for c in range(KC)]
        ln_full(C, C.xres, "xres", t * TILE, TILE, "dst", dsts=dsts, gamma=gam, rms=True)

    combined = [False, False]

    def fixup_half(hh, ci):
        cs = slice(hh * 4, hh * 4 + 4)
        if not combined[hh]:
            combined[hh] = True
            C.memset(carry[:, cs], 0.0, w=["l0a_carry"])
            for j in range(NCORE):
                C.ts(aeff[:], gsb[hh][:, j * 8:j * 8 + 4], rm[:, j:j + 1], rmc[:, j:j + 1],
                     ALU.mult, r=[f"l0a_gsb{hh}", "rmask", "rmaskc"], w=["l0a_aeff"], op1=ALU.add)
                C.ts(yeff[:], gsb[hh][:, j * 8 + 4:j * 8 + 8], rm[:, j:j + 1], None,
                     ALU.mult, r=[f"l0a_gsb{hh}", "rmask"], w=["l0a_yeff"])
                C.tt(carry[:, cs], carry[:, cs], aeff[:], ALU.mult, r=["l0a_carry", "l0a_aeff"], w=["l0a_carry"])
                C.tt(carry[:, cs], carry[:, cs], yeff[:], ALU.add, r=["l0a_carry", "l0a_yeff"], w=["l0a_carry"])
        c = hh * 4 + ci
        C.stt(s1[c][:], s2[ci][:], carry[:, c:c + 1], s1[c][:], ALU.mult, ALU.add,
              r=[f"l0a_s2h{hh}{ci}", "l0a_carry", f"l0a_s1{c}"], w=[f"l0a_s1{c}"])

    for half in range(2):
        for ci in range(4):
            c = half * 4 + ci
            for gi, (eg, dstbuf, dkey, evac) in enumerate([
                    (c, qb, "l0a_qb", "copy"),
                    (16 + c, ab, "l0a_ab", "sigmoid"),
                    (8 + c, kvb, "l0a_kvb", "copy")]):
                wbuf = wq[(3 * c + gi) % 4]
                wkey = f"l0a_w{(3 * c + gi) % 4}"
                C.dma("P", wbuf[:].rearrange("p (k e) -> p k e", k=KC),
                      WQA[:, eg * 128:(eg + 1) * 128].rearrange("(k p) e -> p k e", p=128),
                      r=[], w=[wkey])
                for t in range(NT):
                    bi = C.rb()
                    for k in range(KC):
                        C.mm(bi, wbuf[:, k * 128:(k + 1) * 128],
                             h[k][:, t * TILE:(t + 1) * TILE],
                             start=(k == 0), stop=(k == KC - 1), r=[wkey, f"l0a_h{k}t{t}"])
                    dap = dstbuf[:, t * TILE:(t + 1) * TILE]
                    if evac == "sigmoid":
                        C.act(dap, C.bank(bi), AF.Sigmoid, r=[f"PB{bi}"], w=[dkey])
                    else:
                        C.act(dap, C.bank(bi), AF.Copy, r=[f"PB{bi}"], w=[dkey])
            C.scan(yb[:], ab[:], kvb[:], C.c_zero, ALU.mult, ALU.add,
                   r=["l0a_ab", "l0a_kvb", "consts"], w=["l0a_yb"])
            C.scan(pb[:], ab[:], ab[:], C.c_one, ALU.mult, ALU.min,
                   r=["l0a_ab", "consts"], w=["l0a_pb"])
            C.tt(s1[c][:], qb[:], yb[:], ALU.mult, r=["l0a_qb", "l0a_yb"], w=[f"l0a_s1{c}"])
            C.tt(s2[ci][:], qb[:], pb[:], ALU.mult, r=["l0a_qb", "l0a_pb"], w=[f"l0a_s2{ci}"])
            C.copy(exch[:, ci:ci + 1], pb[:, T - 1:T], r=["l0a_pb"], w=["l0a_exch"])
            C.copy(exch[:, 4 + ci:5 + ci], yb[:, T - 1:T], r=["l0a_yb"], w=["l0a_exch"])
        C.dma("P", cc_in[half][:, :], exch[:], r=["l0a_exch"], w=[f"l0a_ccin{half}"])
        C.cc("AllGather", [cc_in[half][:, :]], [cc_out[half][:, :]],
             r=[f"l0a_ccin{half}"], w=[f"l0a_ccout{half}"])
        C.dma("P", gsb[:].rearrange("p (r f) -> p r f", r=8),
              cc_out[half][:, :].rearrange("(r p) f -> p r f", p=128),
              r=[f"l0a_ccout{half}"], w=["l0a_gsb"])
        C.memset(carry[:], 0.0, w=["l0a_carry"])
        for j in range(NCORE):
            C.ts(aeff[:], gsb[:, j * 8:j * 8 + 4], rm[:, j:j + 1], rmc[:, j:j + 1],
                 ALU.mult, r=["l0a_gsb", "rmask", "rmaskc"], w=["l0a_aeff"], op1=ALU.add)
            C.ts(yeff[:], gsb[:, j * 8 + 4:j * 8 + 8], rm[:, j:j + 1], None,
                 ALU.mult, r=["l0a_gsb", "rmask"], w=["l0a_yeff"])
            C.tt(carry[:], carry[:], aeff[:], ALU.mult, r=["l0a_carry", "l0a_aeff"], w=["l0a_carry"])
            C.tt(carry[:], carry[:], yeff[:], ALU.add, r=["l0a_carry", "l0a_yeff"], w=["l0a_carry"])
        for ci in range(4):
            c = half * 4 + ci
            C.stt(s1[c][:], s2[ci][:], carry[:, ci:ci + 1], s1[c][:], ALU.mult, ALU.add,
                  r=[f"l0a_s2{ci}", "l0a_carry", f"l0a_s1{c}"], w=[f"l0a_s1{c}"])
    for t in range(NT):
        ln_full(C, s1, "l0a_s1", t * TILE, TILE, "resid")


def emit_l1_attn(C, nc):
    WQ = nc.dram_tensor("wq1", [D, 3 * D], F32, kind="ExternalInput").ap()
    BQ = nc.dram_tensor("bq1", [128, 24], F32, kind="ExternalInput").ap()
    WO = nc.dram_tensor("wo1", [D, D], F32, kind="ExternalInput").ap()
    BO = nc.dram_tensor("bo1", [128, 8], F32, kind="ExternalInput").ap()
    MUT = nc.dram_tensor("mut", [128, 128], F32, kind="ExternalInput").ap()
    HON = nc.dram_tensor("hones", [128, 128], F32, kind="ExternalInput").ap()
    RM = C.dram_in("rmask", [128, 8])
    bq = C.ab("l1a_bq", [128, 24], F32)
    bo = C.ab("l1a_bo", [128, 8], F32)
    mut = C.ab("l1a_mut", [128, 128], BF16)
    hon = C.ab("l1a_hon", [128, 128], BF16)
    rm = C.ab("l1a_rm", [128, 8], F32)
    C.dma("Y", bq[:], BQ, r=[], w=["l1a_bq"])
    C.dma("Y", bo[:], BO, r=[], w=["l1a_bo"])
    C.dma("P", mut[:], MUT, r=[], w=["l1a_mut"])
    C.dma("P", hon[:], HON, r=[], w=["l1a_hon"])
    C.dma("Y", rm[:], RM, r=[], w=["l1a_rm"])

    xn = [C.ab(f"l1a_xn{c}", [128, 2 * TILE], BF16) for c in range(KC)]
    qp = [C.ab(f"l1a_qp{c}", [128, T], BF16) for c in range(KC)]
    kp = [C.ab(f"l1a_kp{c}", [128, T], BF16) for c in range(KC)]
    vt = [C.ab(f"l1a_vt{tc}", [128, D], BF16) for tc in range(16)]
    ktst = C.ab("l1a_ktst", [128, D], BF16)[:]
    wq = [C.ab(f"l1a_w{i}", [128, 8 * 128], BF16) for i in range(2)]
    st1 = C.lnst[0]["d1"]
    st2 = C.lnst[0]["d2"]
    vtmp = C.ab("l1a_vtmp", [128, TILE], BF16)
    zg = C.ab("l1a_zg", [128, 8], F32)
    zgg = C.ab("l1a_zgg", [128, 8], F32)
    gz = C.ab("l1a_gz", [128, 64], F32)
    sg = C.ab("l1a_sg", [128, D // 2], F32)
    srun = C.ab("l1a_srun", [128, D // 2], F32)
    srb = C.ab("l1a_srb", [128, D // 2], BF16)
    kcum = [C.ab(f"l1a_kcum{i}", [128, T], BF16) for i in range(2)]
    sstage_ap = kcum[0][:].bitcast(F32)[0:128, 0:D // 2]
    sm = C.ab("l1a_sm", [128, 256], BF16)
    nst0 = C.lnst[0]["d1"][0]
    nst = [nst0, nst0]
    o_sb = xn
    aeff = C.ab("l1a_ae", [128, 8], F32)

    ccz_in = nc.dram_tensor("l1a_ccz_in", [128, 8], F32)
    ccz_out = nc.dram_tensor("l1a_ccz_out", [1024, 8], F32, addr_space="Shared")
    ccs_in = nc.dram_tensor("l1a_ccs_in", [128, D // 2], F32)
    ccs_out = nc.dram_tensor("l1a_ccs_out", [1024, D // 2], F32, addr_space="Shared")

    for tg in range(2):
      for tl in range(2):
        t = 2 * tg + tl
        dsts = [(xn[c][:, tl * TILE:(tl + 1) * TILE], f"l1a_xn{c}t{t}") for c in range(KC)]
        ln_full(C, C.xres, "xres", t * TILE, TILE, "dst", dsts=dsts)
      for eg in range(24):
        wbuf = wq[eg % 2]
        wkey = f"l1a_w{eg % 2}"
        C.dma("P", wbuf[:].rearrange("p (k e) -> p k e", k=KC),
              WQ[:, eg * 128:(eg + 1) * 128].rearrange("(k p) e -> p k e", p=128),
              r=[], w=[wkey])
        bts = [C.rb(), C.rb()]
        for k in range(KC):
            lhs = wbuf[:, k * 128:(k + 1) * 128]
            for tl in range(2):
                t = 2 * tg + tl
                C.mm(bts[tl], lhs, xn[k][:, tl * TILE:(tl + 1) * TILE],
                     start=(k == 0), stop=(k == KC - 1), r=[wkey, f"l1a_xn{k}t{t}"])
        bcol = bq[:, eg:eg + 1]
        for tl in range(2):
            t = 2 * tg + tl
            bi = bts[tl]
            if eg < 16:
                c = eg % 8
                dst = (qp if eg < 8 else kp)[c]
                dkey = f"l1a_{'qp' if eg < 8 else 'kp'}{c}"
                s1, s2 = C.lnd2f[t % 2], st2[t % 2]
                k1, k2 = f"ln.d2f{t % 2}", f"ln.d2{t % 2}"
                C.ts(s1[:], C.bank(bi), bcol, 0.0, ALU.add, op1=ALU.min,
                     r=[f"PB{bi}", "l1a_bq"], w=[k1])
                C.act(s1[:], s1[:], AF.Exp, r=[k1], w=[k1])
                C.ts(s2[:], C.bank(bi), bcol, 0.0, ALU.add, op1=ALU.max,
                     r=[f"PB{bi}", "l1a_bq"], w=[k2])
                C.tt(dst[:, t * TILE:(t + 1) * TILE], s1[:], s2[:], ALU.add,
                     r=[k1, k2], w=[dkey])
            else:
                ec = eg - 16
                C.ts(vtmp[:], C.bank(bi), bcol, None, ALU.add,
                     r=[f"PB{bi}", "l1a_bq"], w=["l1a_vtmp"])
                bj = C.rb()
                for stl in range(4):
                    C.transpose(C.bank_bf(bj, 0, 128, stl * 128, (stl + 1) * 128),
                                vtmp[:, stl * 128:(stl + 1) * 128], C.identb[:],
                                r=["l1a_vtmp", "identb"], w=[f"PB{bj}"])
                for stl in range(4):
                    tc = t * 4 + stl
                    C.copy(vt[tc][:, ec * 128:(ec + 1) * 128],
                           C.bank_bf(bj, 0, 128, stl * 128, (stl + 1) * 128),
                           r=[f"PB{bj}"], w=[f"l1a_vt{tc}"])

    for c in range(KC):
        C.red(zg[:, c:c + 1], kp[c][:], ALU.add, r=[f"l1a_kp{c}"], w=["l1a_zg"])
    C.dma("P", ccz_in[:, :], zg[:], r=["l1a_zg"], w=["ccz_in"])
    C.cc("AllGather", [ccz_in[:, :]], [ccz_out[:, :]], r=["ccz_in"], w=["ccz_out"])
    C.dma("Y", gz[:].rearrange("p (r f) -> p r f", r=8),
          ccz_out[:, :].rearrange("(r p) f -> p r f", p=128), r=["ccz_out"], w=["l1a_gz"])
    C.memset(zgg[:], 0.0, w=["l1a_zgg"])
    for j in range(NCORE):
        C.ts(aeff[:], gz[:, j * 8:(j + 1) * 8], rm[:, j:j + 1], None, ALU.mult,
             r=["l1a_gz", "l1a_rm"], w=["l1a_ae"])
        C.tt(zgg[:], zgg[:], aeff[:], ALU.add, r=["l1a_zgg", "l1a_ae"], w=["l1a_zgg"])
    for tc in range(16):
        for ec in range(KC):
            bj = C.rb(0, 4)
            bap = C.bank_bf(bj, 0, 128, 0, 128)
            C.transpose(bap, kp[ec][:, tc * 128:(tc + 1) * 128], C.identb[:],
                        r=[f"l1a_kp{ec}", "identb"], w=[f"PB{bj}"])
            C.copy(ktst[0:128, ec * 128:(ec + 1) * 128], bap, r=[f"PB{bj}"], w=["l1a_ktst"])
        for h in range(16):
            ch, hp = h // 2, (h % 2) * 64
            bi = 4 + (h % 2)
            C.mm(bi, ktst[0:128, h * 64:(h + 1) * 64], vt[tc][:, h * 64:(h + 1) * 64],
                 start=True, stop=True, r=["l1a_ktst", f"l1a_vt{tc}"],
                 bank_ap=C.bank(bi, hp, hp + 64, 0, 64))
            if tc == 0:
                C.copy(sg[hp:hp + 64, ch * 64:(ch + 1) * 64], C.bank(bi, hp, hp + 64, 0, 64),
                       r=[f"PB{bi}"], w=["l1a_sg"], q="S")
            else:
                C.tt(sg[hp:hp + 64, ch * 64:(ch + 1) * 64], sg[hp:hp + 64, ch * 64:(ch + 1) * 64],
                     C.bank(bi, hp, hp + 64, 0, 64), ALU.add, r=["l1a_sg", f"PB{bi}"],
                     w=["l1a_sg"])

    C.dma("P", ccs_in[:, :], sg[:], r=["l1a_sg"], w=["ccs_in"])
    C.cc("AllGather", [ccs_in[:, :]], [ccs_out[:, :]], r=["ccs_in"], w=["ccs_out"])
    C.memset(srun[:], 0.0, w=["l1a_srun"] + [f"l1a_srun{h}" for h in range(16)])
    for c in range(KC):
        kc_b = kcum[c % 2]
        kck = f"l1a_kcum{c % 2}"
        C.scan(kc_b[:], kp[c][:], kp[c][:], zgg[:, c:c + 1], ALU.add, ALU.max,
               r=[f"l1a_kp{c}", "l1a_zgg"], w=[kck])
        C.tt(kc_b[:], qp[c][:], kc_b[:], ALU.mult, r=[f"l1a_qp{c}", kck],
             w=[kck])
        for t in range(NT):
            bi = C.rb(0, 4)
            C.mm(bi, hon[:], kc_b[:, t * TILE:(t + 1) * TILE],
                 start=True, stop=True, r=[kck, "l1a_hon"])
            C.act(nst0[:], C.bank(bi), AF.Ln, bias=C.c_eps_attn,
                  r=[f"PB{bi}", "consts"], w=["ln.d10"])
            C.act(nst0[:], nst0[:], AF.Exp, scale=-1.0,
                  r=["ln.d10"], w=["ln.d10"])
            C.tt(qp[c][:, t * TILE:(t + 1) * TILE], qp[c][:, t * TILE:(t + 1) * TILE],
                 nst0[:], ALU.mult, r=[f"l1a_qp{c}", "ln.d10"], w=[f"l1a_qp{c}"])
    for j in range(NCORE):
        C.dma("Y", sstage_ap, ccs_out[j * 128:(j + 1) * 128, :],
              r=["ccs_out"], w=["l1a_sstage", "l1a_kcum0"])
        C.stt(srun[:], sstage_ap, rm[:, j:j + 1], srun[:],
              ALU.mult, ALU.add, r=["l1a_sstage", "l1a_rm", "l1a_srun"], w=["l1a_srun"] + [f"l1a_srun{h}" for h in range(16)])
    C.copy(srb[:], srun[:], r=["l1a_srun"], w=["l1a_srb"] + [f"l1a_srb{h}" for h in range(16)])

    for tc in range(16):
        for ec in range(KC):
            bj = C.rb(0, 4)
            bap = C.bank_bf(bj, 0, 128, 0, 128)
            C.transpose(bap, kp[ec][:, tc * 128:(tc + 1) * 128], C.identb[:],
                        r=[f"l1a_kp{ec}", "identb"], w=[f"PB{bj}"])
            C.copy(ktst[0:128, ec * 128:(ec + 1) * 128], bap, r=[f"PB{bj}"], w=["l1a_ktst"])
        for ch in range(8):
            bc = 4 + (ch % 2)
            bu = C.rb(0, 4)
            for par in range(2):
                h = 2 * ch + par
                hp = par * 64
                q_ap = qp[ch][hp:hp + 64, tc * 128:(tc + 1) * 128]
                k_ap = kp[ch][hp:hp + 64, tc * 128:(tc + 1) * 128]
                smv = sm[:, par * 128:(par + 1) * 128]
                smk = f"l1a_sm{par}"
                sreg = (slice(hp, hp + 64), slice(ch * 64, (ch + 1) * 64))
                bs = C.rb(0, 4)
                C.mm(bs, k_ap, q_ap, start=True, stop=True,
                     r=[f"l1a_kp{ch}", f"l1a_qp{ch}"],
                     bank_ap=C.bank(bs, 0, 128, 0, 128))
                C.tt(smv, C.bank(bs, 0, 128, 0, 128), mut[:], ALU.mult,
                     r=[f"PB{bs}", "l1a_mut"], w=[smk])
                C.mm(bc, vt[tc][:, h * 64:(h + 1) * 64], smv, start=True, stop=False,
                     r=[f"l1a_vt{tc}", smk], bank_ap=C.bank(bc, hp, hp + 64, 0, 128))
                C.mm(bc, srb[sreg], q_ap, start=False, stop=True,
                     r=[f"l1a_srb{h}", f"l1a_qp{ch}"], bank_ap=C.bank(bc, hp, hp + 64, 0, 128))
                C.mm(bu, ktst[0:128, h * 64:(h + 1) * 64], vt[tc][:, h * 64:(h + 1) * 64],
                     start=True, stop=True, r=["l1a_ktst", f"l1a_vt{tc}"],
                     bank_ap=C.bank(bu, hp, hp + 64, 0, 64))
            sboth = (slice(0, 128), slice(ch * 64, (ch + 1) * 64))
            hk = [f"l1a_srun{2 * ch}", f"l1a_srun{2 * ch + 1}"]
            bk = [f"l1a_srb{2 * ch}", f"l1a_srb{2 * ch + 1}"]
            C.tt(srun[sboth], srun[sboth], C.bank(bu, 0, 128, 0, 64), ALU.add,
                 r=hk + [f"PB{bu}"], w=hk)
            C.copy(srb[sboth], srun[sboth], r=hk, w=bk, q="S")
            C.copy(qp[ch][:, tc * 128:(tc + 1) * 128], C.bank(bc, 0, 128, 0, 128),
                   r=[f"PB{bc}"], w=[f"l1a_qp{ch}"])

    for tg in range(2):
        for oc in range(KC):
            wbuf = wq[oc % 2]
            wkey = f"l1a_w{oc % 2}"
            C.dma("P", wbuf[:].rearrange("p (k e) -> p k e", k=KC),
                  WO[:, oc * 128:(oc + 1) * 128].rearrange("(k p) e -> p k e", p=128),
                  r=[], w=[wkey])
            bts = [C.rb(0, 4), C.rb(0, 4)]
            for k in range(KC):
                lhs = wbuf[:, k * 128:(k + 1) * 128]
                for tl in range(2):
                    t = 2 * tg + tl
                    C.mm(bts[tl], lhs, qp[k][:, t * TILE:(t + 1) * TILE],
                         start=(k == 0), stop=(k == KC - 1), r=[wkey, f"l1a_qp{k}"])
            for tl in range(2):
                t = 2 * tg + tl
                C.ts(o_sb[oc][:, tl * TILE:(tl + 1) * TILE], C.bank(bts[tl]), bo[:, oc:oc + 1],
                     None, ALU.add, r=[f"PB{bts[tl]}", "l1a_bo"], w=[f"l1a_xn{oc}o{t}"])
        for tl in range(2):
            t = 2 * tg + tl
            srcs = [(o_sb[c][:, tl * TILE:(tl + 1) * TILE], f"l1a_xn{c}o{t}") for c in range(KC)]
            pi = t % 2
            bS, bQ = C.rb(0, 4), C.rb(0, 4)
            for c in range(KC):
                sap, skey = srcs[c]
                sq = C.lnsq[pi * 2 + c % 2]
                sqk = f"ln{pi}.sq{c % 2}"
                C.act(sq[:], sap, AF.Square, r=[skey], w=[sqk])
                C.mm(bS, C.ones_bf[:], sap, start=(c == 0), stop=(c == KC - 1), r=[skey, "onesb"])
                C.mm(bQ, C.ones_bf[:], sq[:], start=(c == 0), stop=(c == KC - 1), r=[sqk, "onesb"])
            keys = ln_tilemath(C, bS, bQ, pi=pi)
            ln_apply_resid(C, srcs, keys, t * TILE, TILE)


def emit_l1_ffn(C, nc):
    CEN = nc.dram_tensor("cent", [D, 8], F32, kind="ExternalInput").ap()
    LNG = nc.dram_tensor("lng", [8, D], F32, kind="ExternalInput").ap()
    LNB = nc.dram_tensor("lnb", [8, D], F32, kind="ExternalInput").ap()
    W1 = nc.dram_tensor("w1", [8, D, 4096], F32, kind="ExternalInput").ap()
    W2 = nc.dram_tensor("w2", [8, 4096, D], F32, kind="ExternalInput").ap()
    B1E = nc.dram_tensor("b1e", [128, 256], F32, kind="ExternalInput").ap()
    B2E = nc.dram_tensor("b2e", [8, D], F32, kind="ExternalInput").ap()
    SEL = nc.dram_tensor("sel", [8, 8 * 128], F32, kind="ExternalInput").ap()
    RMIT = 1024

    centb = C.ab("l1f_cent", [128, 64], BF16)
    lngb = C.ab("l1f_lng", [8, D], BF16)
    lnbb = C.ab("l1f_lnb", [8, D], BF16)
    selb = C.ab("l1f_sel", [8, 8 * 128], BF16)
    b2eb = C.ab("l1f_b2e", [8, D], BF16)
    b1e = C.ab("l1f_b1e", [128, 256], F32)
    xi = [C.ab(f"l1f_xi{c}", [128, RMIT], BF16) for c in range(KC)]
    xln = [C.ab(f"l1f_xln{c}", [128, RMIT], BF16) for c in range(KC)]
    Hm = [C.ab(f"l1f_h{f}", [128, RMIT], BF16) for f in range(8)]
    ffn = [C.ab(f"l1f_ffn{c}", [128, RMIT], BF16) for c in range(KC)]
    w1b = [C.ab(f"l1f_w1{i}", [128, 8 * 1024], BF16) for i in range(2)]
    w2b = C.ab("l1f_w2", [128, 8 * 1024], BF16)
    aff = C.ab("l1f_aff", [8, RMIT], F32)
    onehot = C.ab("l1f_oh", [8, RMIT], BF16)
    tmul = C.ab("l1f_tm", [8, RMIT], BF16)
    alphab = C.ab("l1f_al", [128, RMIT], BF16)
    maskb = C.ab("l1f_mb", [128, RMIT], BF16)
    afft = C.ab("l1f_afft", [128, 8], F32)
    mcol = C.ab("l1f_mcol", [128, 1], F32)
    oht = C.ab("l1f_oht", [128, 8], BF16)

    for k in range(KC):
        C.dma("P", centb[:, k * 8:(k + 1) * 8], CEN[k * 128:(k + 1) * 128, :], r=[], w=["l1f_cent"])
    C.dma("P", lngb[:], LNG, r=[], w=["l1f_lng"])
    C.dma("P", lnbb[:], LNB, r=[], w=["l1f_lnb"])
    C.dma("P", selb[:], SEL, r=[], w=["l1f_sel"])
    C.dma("P", b2eb[:], B2E, r=[], w=["l1f_b2e"])
    C.dma("Y", b1e[:], B1E, r=[], w=["l1f_b1e"])

    st_d = C.lnst[0]["d1"] + C.lnst[0]["d2"]
    kd = ["ln.d10", "ln.d11", "ln.d20", "ln.d21"]

    for p in range(2):
        P0 = p * RMIT
        for st in range(2):
            dsts = [(xi[c][:, st * TILE:(st + 1) * TILE], f"l1f_xi{c}s{st}") for c in range(KC)]
            ln_full(C, C.xres, "xres", P0 + st * TILE, TILE, "dst", dsts=dsts)
        for st in range(2):
            bi = C.rb(0, 4)
            bap = C.bank(bi, 0, 8, 0, TILE)
            for k in range(KC):
                C.mm(bi, centb[:, k * 8:(k + 1) * 8], xi[k][:, st * TILE:(st + 1) * TILE],
                     start=(k == 0), stop=(k == KC - 1), r=["l1f_cent", f"l1f_xi{k}s{st}"],
                     bank_ap=bap)
            C.copy(aff[0:8, st * TILE:(st + 1) * TILE], bap, r=[f"PB{bi}"], w=["l1f_aff"])
        for sl in range(8):
            bj = C.rb(0, 4)
            C.transpose(C.bank(bj, 0, 128, 0, 8), aff[0:8, sl * 128:(sl + 1) * 128],
                        C.ident[0:8, 0:8], r=["l1f_aff"], w=[f"PB{bj}"])
            C.copy(afft[:], C.bank(bj, 0, 128, 0, 8), r=[f"PB{bj}"], w=["l1f_afft"])
            C.red(mcol[:], afft[:], ALU.max, r=["l1f_afft"], w=["l1f_mcol"])
            C.ts(oht[:], afft[:], mcol[:], None, ALU.is_ge, r=["l1f_afft", "l1f_mcol"],
                 w=["l1f_oht"])
            bk = C.rb(0, 4)
            C.transpose(C.bank_bf(bk, 0, 8, 0, 128), oht[:], C.identb[:],
                        r=["l1f_oht", "identb"], w=[f"PB{bk}"])
            C.copy(onehot[0:8, sl * 128:(sl + 1) * 128], C.bank_bf(bk, 0, 8, 0, 128),
                   r=[f"PB{bk}"], w=["l1f_oh"])
        C.tt(tmul[0:8, :], aff[0:8, :], onehot[0:8, :], ALU.mult,
             r=["l1f_aff", "l1f_oh"], w=["l1f_tm"])
        for st in range(2):
            bi = C.rb(0, 4)
            C.mm(bi, C.ones_bf[0:8, :], tmul[0:8, st * TILE:(st + 1) * TILE],
                 start=True, stop=True, r=["l1f_tm", "onesb"])
            C.act(alphab[:, st * TILE:(st + 1) * TILE], C.bank(bi), AF.Sigmoid,
                  r=[f"PB{bi}"], w=["l1f_al"])
        for dc in range(KC):
            for st in range(2):
                bi = C.rb(0, 4)
                C.mm(bi, b2eb[0:8, dc * 128:(dc + 1) * 128],
                     onehot[0:8, st * TILE:(st + 1) * TILE], start=True, stop=True,
                     r=["l1f_b2e", "l1f_oh"])
                C.copy(ffn[dc][:, st * TILE:(st + 1) * TILE], C.bank(bi),
                       r=[f"PB{bi}"], w=[f"l1f_ffn{dc}"])
        for st in range(2):
            srcs = []
            bS, bQ = C.rb(0, 4), C.rb(0, 4)
            for c in range(KC):
                sap = xi[c][:, st * TILE:(st + 1) * TILE]
                sq = C.lnsq[st * 2 + c % 2]
                C.act(sq[:], sap, AF.Square, r=[f"l1f_xi{c}s{st}"], w=[f"ln{st}.sq{c % 2}"])
                C.mm(bS, C.ones_bf[:], sap, start=(c == 0), stop=(c == KC - 1),
                     r=[f"l1f_xi{c}s{st}", "onesb"])
                C.mm(bQ, C.ones_bf[:], sq[:], start=(c == 0), stop=(c == KC - 1),
                     r=[f"ln{st}.sq{c % 2}", "onesb"])
                srcs.append((sap, f"l1f_xi{c}s{st}"))
            keys = ln_tilemath(C, bS, bQ, pi=st)
            mu_t, rstd_t = C.lnst[st]["mub"], C.lnst[st]["scb"]
            for dc in range(KC):
                sap = srcs[dc][0]
                d1 = st_d[dc % 2]
                C.tt(d1[:], sap, mu_t[:], ALU.subtract, r=[f"l1f_xi{dc}s{st}", keys["mu"]],
                     w=[kd[dc % 2]])
                C.tt(d1[:], d1[:], rstd_t[:], ALU.mult, r=[kd[dc % 2], keys["sc"]],
                     w=[kd[dc % 2]])
                bg = C.rb(0, 4)
                C.mm(bg, lngb[0:8, dc * 128:(dc + 1) * 128],
                     onehot[0:8, st * TILE:(st + 1) * TILE], start=True, stop=True,
                     r=["l1f_lng", "l1f_oh"])
                d2 = st_d[2 + dc % 2]
                C.tt(d2[:], d1[:], C.bank(bg), ALU.mult, r=[kd[dc % 2], f"PB{bg}"],
                     w=[kd[2 + dc % 2]])
                bb = C.rb(0, 4)
                C.mm(bb, lnbb[0:8, dc * 128:(dc + 1) * 128],
                     onehot[0:8, st * TILE:(st + 1) * TILE], start=True, stop=True,
                     r=["l1f_lnb", "l1f_oh"])
                C.tt(xln[dc][:, st * TILE:(st + 1) * TILE], d2[:], C.bank(bb), ALU.add,
                     r=[kd[2 + dc % 2], f"PB{bb}"], w=[f"l1f_xln{dc}s{st}"])
        for e in range(8):
            for g in range(4):
                wb = w1b[g % 2]
                wk = f"l1f_w1{g % 2}"
                C.dma("P", wb[:].rearrange("p (k f) -> p k f", k=KC),
                      W1[e][:, g * 1024:(g + 1) * 1024].rearrange("(k p) f -> p k f", p=128),
                      r=[], w=[wk])
                C.dma("P", w2b[:].rearrange("p (k f) -> p k f", k=KC),
                      W2[e][g * 1024:(g + 1) * 1024, :].rearrange("(k p) f -> p k f", p=128),
                      r=[], w=["l1f_w2k"])
                if g == 0:
                    for st in range(2):
                        bm = C.rb(0, 4)
                        C.mm(bm, selb[0:8, e * 128:(e + 1) * 128],
                             onehot[0:8, st * TILE:(st + 1) * TILE], start=True, stop=True,
                             r=["l1f_sel", "l1f_oh"])
                        C.copy(maskb[:, st * TILE:(st + 1) * TILE], C.bank(bm),
                               r=[f"PB{bm}"], w=["l1f_mb"], q="S")
                for fcl in range(8):
                    b0, b1_ = C.rb(0, 4), C.rb(0, 4)
                    for k in range(KC):
                        lhs = wb[:, k * 1024 + fcl * 128: k * 1024 + (fcl + 1) * 128]
                        C.mm(b0, lhs, xln[k][:, 0:TILE],
                             start=(k == 0), stop=(k == KC - 1), r=[wk, f"l1f_xln{k}s0"])
                        C.mm(b1_, lhs, xln[k][:, TILE:2 * TILE],
                             start=(k == 0), stop=(k == KC - 1), r=[wk, f"l1f_xln{k}s1"])
                    for st, bi in ((0, b0), (1, b1_)):
                        d1 = st_d[(2 * fcl + st) % 4]
                        kdx = kd[(2 * fcl + st) % 4]
                        C.act(d1[:], C.bank(bi), AF.Relu,
                              bias=b1e[:, e * 32 + g * 8 + fcl: e * 32 + g * 8 + fcl + 1],
                              r=[f"PB{bi}", "l1f_b1e"], w=[kdx])
                        C.tt(Hm[fcl][:, st * TILE:(st + 1) * TILE], d1[:],
                             maskb[:, st * TILE:(st + 1) * TILE],
                             ALU.mult, r=[kdx, "l1f_mb"], w=[f"l1f_h{fcl}"])
                for dc in range(KC):
                    b0, b1_ = C.rb(0, 4), C.rb(0, 4)
                    for fcl in range(8):
                        lhs = w2b[:, fcl * 1024 + dc * 128: fcl * 1024 + (dc + 1) * 128]
                        C.mm(b0, lhs, Hm[fcl][:, 0:TILE],
                             start=(fcl == 0), stop=(fcl == 7), r=["l1f_w2k", f"l1f_h{fcl}"])
                        C.mm(b1_, lhs, Hm[fcl][:, TILE:2 * TILE],
                             start=(fcl == 0), stop=(fcl == 7), r=["l1f_w2k", f"l1f_h{fcl}"])
                    for st, bo in ((0, b0), (1, b1_)):
                        fap = ffn[dc][:, st * TILE:(st + 1) * TILE]
                        C.tt(fap, fap, C.bank(bo), ALU.add,
                             r=[f"l1f_ffn{dc}", f"PB{bo}"], w=[f"l1f_ffn{dc}"])
        for st in range(2):
            for dc in range(KC):
                d1 = st_d[dc % 2]
                C.tt(d1[:], ffn[dc][:, st * TILE:(st + 1) * TILE],
                     alphab[:, st * TILE:(st + 1) * TILE], ALU.mult,
                     r=[f"l1f_ffn{dc}", "l1f_al"], w=[kd[dc % 2]])
                C.tt(xln[dc][:, st * TILE:(st + 1) * TILE], d1[:],
                     xi[dc][:, st * TILE:(st + 1) * TILE], ALU.add,
                     r=[kd[dc % 2], f"l1f_xi{dc}s{st}"], w=[f"l1f_xln{dc}c{st}"])
            srcs2 = [(xln[c][:, st * TILE:(st + 1) * TILE], f"l1f_xln{c}c{st}") for c in range(KC)]
            pi2 = st % 2
            bS2, bQ2 = C.rb(0, 4), C.rb(0, 4)
            for c in range(KC):
                sap2, skey2 = srcs2[c]
                sq2 = C.lnsq[pi2 * 2 + c % 2]
                sqk2 = f"ln{pi2}.sq{c % 2}"
                C.act(sq2[:], sap2, AF.Square, r=[skey2], w=[sqk2])
                C.mm(bS2, C.ones_bf[:], sap2, start=(c == 0), stop=(c == KC - 1), r=[skey2, "onesb"])
                C.mm(bQ2, C.ones_bf[:], sq2[:], start=(c == 0), stop=(c == KC - 1), r=[sqk2, "onesb"])
            keys2 = ln_tilemath(C, bS2, bQ2, pi=pi2)
            ln_apply_resid(C, srcs2, keys2, P0 + st * TILE, TILE)


# version 26
# speedup vs baseline: 1.0734x; 1.0025x over previous
import numpy as np
import ml_dtypes
import concourse.bass as bass
import concourse.mybir as mybir
from concourse.bass_utils import run_bass_kernel_spmd

import concourse.bass_utils as _bu
if not getattr(_bu, "_nosim_patched", False):
    _orig_run_command = _bu.run_command
    def _run_command_nosim(cmd, *a, **kw):
        cmd = ["--enable-birsim=false" if c == "--enable-birsim=true" else c for c in cmd]
        cmd = ["--enable-ldw-opt=true" if c == "--enable-ldw-opt=false" else c for c in cmd]
        return _orig_run_command(cmd, *a, **kw)
    _bu.run_command = _run_command_nosim
    _bu._nosim_patched = True

F32 = mybir.dt.float32
BF16 = mybir.dt.bfloat16
AF = mybir.ActivationFunctionType
ALU = mybir.AluOpType
AX = mybir.AxisListType

NCORE = 8
T = 2048
D = 1024
KC = 8
TILE = 512
NT = T // TILE
EPS_LN = 1e-5
EPS_ATTN = 1e-6
QUEUES = ("T", "V", "S", "P", "Y")
ALL_BLOCKS = ("l0a", "l0f", "l1a", "l1f")

SMALL_KEYS = frozenset(
    ["l1f_mcol", "l0a_carry", "l0a_aeff", "l0a_yeff", "l1a_zgg", "l1a_zg",
     "l0a_exch", "l1f_afft", "l1f_oht", "l1a_ae"]
    + [f"l1a_srun{h}" for h in range(16)])


class Prog:

    def __init__(self, nc):
        self.nc = nc
        self.ops = []
        self.writer = {}
        self.readers = {}
        self.barrier = ()

    def op(self, q, fn, r=(), w=(), dma=False, cc=False, tag="", out_dma=False):
        excl = [k for k in r if k.startswith("PB")]
        if excl:
            r = [k for k in r if not k.startswith("PB")]
            w = list(w) + excl
        deps = set(self.barrier)
        for k in r:
            i = self.writer.get(k)
            if i is not None:
                deps.add(i)
        for k in w:
            i = self.writer.get(k)
            if i is not None:
                deps.add(i)
            for j in self.readers.get(k, ()):
                deps.add(j)
        idx = len(self.ops)
        for k in w:
            self.writer[k] = idx
            self.readers[k] = []
        for k in r:
            self.readers.setdefault(k, []).append(idx)
        deps.discard(idx)
        small = any(k in SMALL_KEYS for k in w)
        self.ops.append(dict(q=q, fn=fn, deps=sorted(deps), dma=dma, cc=cc,
                             tag=tag, out_dma=out_dma, small=small))
        return idx

    def set_barrier(self):
        last = {}
        for i, o in enumerate(self.ops):
            last[o["q"]] = i
        self.barrier = tuple(sorted(last.values()))

    def finalize_and_emit(self):
        import contextlib
        nc = self.nc
        n = len(self.ops)
        for i, o in enumerate(self.ops):
            o["deps"] = [d for d in o["deps"]
                         if self.ops[d]["q"] != o["q"]
                         or self.ops[d]["dma"] or self.ops[d]["cc"]
                         or self.ops[d]["small"]]
        needed = [False] * n
        for o in self.ops:
            for d in o["deps"]:
                needed[d] = True
        n_dma_pool = {"P": 8, "Y": 6, "S": 2, "V": 1, "T": 1}

        stack = contextlib.ExitStack()
        qsem = {q: stack.enter_context(nc.semaphore(f"qs_{q}")) for q in QUEUES}
        pools = {q: [stack.enter_context(nc.semaphore(f"dp_{q}{i}"))
                     for i in range(n_dma_pool[q])] for q in QUEUES}
        ncc = sum(1 for o in self.ops if o["cc"])
        ccsems = [stack.enter_context(nc.semaphore(f"cc{i}")) for i in range(ncc)]

        qcount = {q: 0 for q in QUEUES}
        pool_idx = {q: 0 for q in QUEUES}
        pool_count = {}
        cci = 0
        for i, o in enumerate(self.ops):
            q = o["q"]
            if o["cc"]:
                o["sem"], o["val"], o["inc"] = ccsems[cci], 1, 1
                cci += 1
            elif o["dma"]:
                s = pools[q][pool_idx[q] % len(pools[q])]
                pool_idx[q] += 1
                pool_count[id(s)] = pool_count.get(id(s), 0) + 16
                o["sem"], o["val"], o["inc"] = s, pool_count[id(s)], 16
            elif needed[i]:
                qcount[q] += 1
                o["sem"], o["val"], o["inc"] = qsem[q], qcount[q], 1
            else:
                o["sem"] = None
        highwater = {q: {} for q in QUEUES}
        for i, o in enumerate(self.ops):
            q = o["q"]
            waits = []
            for d in o["deps"]:
                od = self.ops[d]
                if od["sem"] is None:
                    assert od["q"] == q and d < i, (d, i, od["tag"], o["tag"])
                    continue
                sem, val = od["sem"], od["val"]
                hw = highwater[q]
                if hw.get(id(sem), -1) < val:
                    waits.append((sem, val))
                    hw[id(sem)] = val
            o["waits"] = waits
        out_waits = {}
        for o in self.ops:
            if o["out_dma"]:
                prev = out_waits.get(id(o["sem"]), (None, 0))[1]
                out_waits[id(o["sem"])] = (o["sem"], max(prev, o["val"]))

        with stack:
            with nc.Block() as block:
                def section(qname):
                    def body(eng):
                        for o in self.ops:
                            if o["q"] != qname:
                                continue
                            for sem, val in o["waits"]:
                                eng.wait_ge(sem, val)
                            inst = o["fn"](eng)
                            if o["sem"] is not None:
                                inst.then_inc(o["sem"], o["inc"])
                        if qname == "Y":
                            for sem, val in out_waits.values():
                                eng.wait_ge(sem, val)
                    return body
                block.tensor(section("T"))
                block.vector(section("V"))
                block.scalar(section("S"))
                block.gpsimd(section("P"))
                block.sync(section("Y"))


class Ctx:

    def __init__(self, nc):
        self.nc = nc
        self.P = Prog(nc)
        self.dram_cache = {}
        self._bank = [nc.alloc_psum_tensor(f"pbank{i}", [128, 512], F32) for i in range(8)]
        self._rb = 0

    def dram_in(self, name, shape, dtype=None):
        if name not in self.dram_cache:
            self.dram_cache[name] = self.nc.dram_tensor(
                name, shape, dtype or F32, kind="ExternalInput").ap()
        return self.dram_cache[name]

    def bank(self, i, p0=0, p1=128, f0=0, f1=512):
        return self._bank[i][p0:p1, f0:f1]

    def bank_bf(self, i, p0=0, p1=128, f0=0, f1=1024):
        return self._bank[i][:].bitcast(BF16)[p0:p1, f0:f1]

    def rb(self, lo=0, hi=6):
        self._rb = (self._rb + 1) % (hi - lo)
        return lo + self._rb

    def sb(self, name, shape, dtype):
        h = self.nc.alloc_sbuf_tensor(name, shape, dtype)
        if not hasattr(self, "reg"):
            self.reg = {}
        self.reg[name] = h
        return h

    def arena_init(self):
        probe = self.nc.alloc_sbuf_tensor("arena_probe", [128, 1], F32)
        self.arena_base = self.nc.lookup_mloc(probe).addr + 32
        self.arena_limit = self.arena_base + self.nc.sbuf_bytes_remaining - 512
        self.arena_off = self.arena_base

    def arena_reset(self):
        self.arena_off = self.arena_base
        self.P.set_barrier()

    def ab(self, name, shape, dtype):
        nbytes = int(np.prod(shape[1:])) * (2 if dtype == BF16 else 4)
        off = (self.arena_off + 31) & ~31
        assert off + nbytes <= self.arena_limit, (
            f"arena overflow: {name} needs {nbytes}B at {off}, limit {self.arena_limit}")
        h = self.nc.alloc_sbuf_tensor_at(name, shape, dtype, offset=off)
        self.arena_off = off + nbytes
        self.reg[name] = h
        return h

    def mm(self, bi, lhsT, rhs, start, stop, r, bank_key=None, bank_ap=None):
        ap = bank_ap if bank_ap is not None else self.bank(bi)
        bk = bank_key or f"PB{bi}"
        return self.P.op("T", lambda e, ap=ap, lhsT=lhsT, rhs=rhs, start=start, stop=stop:
                         e.matmul(ap, lhsT, rhs, start=start, stop=stop),
                         r=list(r), w=[bk], tag="mm")

    def transpose(self, bank_ap, src, ident, r, w):
        return self.P.op("T", lambda e, a=bank_ap, s=src, i=ident: e.transpose(a, s, i),
                         r=list(r) + ["ident"], w=list(w), tag="tr")

    def act(self, dst, src, func, r, w, bias=0.0, scale=1.0, q="S"):
        return self.P.op(q, lambda e, d=dst, s=src, f=func, b=bias, sc=scale:
                         e.activation(d, s, f, bias=b, scale=sc),
                         r=list(r), w=list(w), tag=f"act.{func}")

    def tt(self, dst, a, b, op, r, w, q="V"):
        return self.P.op(q, lambda e, d=dst, a=a, b=b, op=op: e.tensor_tensor(d, a, b, op),
                         r=list(r), w=list(w), tag=f"tt.{op}")

    def ts(self, dst, a, s1, s2, op0, r, w, op1=ALU.bypass, q="V"):
        return self.P.op(q, lambda e, d=dst, a=a, s1=s1, s2=s2, op0=op0, op1=op1:
                         e.tensor_scalar(d, a, s1, s2, op0, op1),
                         r=list(r), w=list(w), tag=f"ts.{op0}")

    def stt(self, dst, in0, scal, in1, op0, op1, r, w, q="V"):
        return self.P.op(q, lambda e, d=dst, i0=in0, sc=scal, i1=in1, op0=op0, op1=op1:
                         e.scalar_tensor_tensor(d, i0, sc, i1, op0, op1),
                         r=list(r), w=list(w), tag="stt")

    def scan(self, dst, d0, d1, init, op0, op1, r, w, q="V"):
        return self.P.op(q, lambda e, d=dst, d0=d0, d1=d1, init=init, op0=op0, op1=op1:
                         e.tensor_tensor_scan(d, d0, d1, init, op0, op1),
                         r=list(r), w=list(w), tag="scan")

    def red(self, dst, src, op, r, w, q="V"):
        return self.P.op(q, lambda e, d=dst, s=src, op=op: e.tensor_reduce(d, s, AX.X, op),
                         r=list(r), w=list(w), tag=f"red.{op}")

    def rcp(self, dst, src, r, w):
        nc = self.nc
        def fn(e, d=dst, s=src):
            with nc.allow_low_precision(reason="bf16 norm inverse"):
                return e.reciprocal(d, s)
        return self.P.op("V", fn, r=list(r), w=list(w), tag="rcp")

    def copy(self, dst, src, r, w, q="V"):
        if q == "S":
            return self.P.op(q, lambda e, d=dst, s=src: e.copy(d, s),
                             r=list(r), w=list(w), tag="cp")
        return self.P.op(q, lambda e, d=dst, s=src: e.tensor_copy(d, s),
                         r=list(r), w=list(w), tag="cp")

    def memset(self, dst, val, w, q="V"):
        return self.P.op(q, lambda e, d=dst, v=val: e.memset(d, v), r=[], w=list(w), tag="ms")

    def dma(self, q, dst, src, r, w, out=False):
        return self.P.op(q, lambda e, d=dst, s=src: e.dma_start(out=d, in_=s),
                         r=list(r), w=list(w), dma=True, out_dma=out, tag="dma")

    def cc(self, kind, ins, outs, r, w):
        return self.P.op("P", lambda e, k=kind, i=ins, o=outs:
                         e.collective_compute(k, ALU.bypass, replica_groups=[list(range(NCORE))],
                                              ins=i, outs=o),
                         r=list(r), w=list(w), cc=True, tag=f"cc.{kind}")



def ln_stats(C, srcs, sqs, rms=False):
    bS, bQ = C.rb(), C.rb()
    for c in range(KC):
        sap, skey = srcs[c]
        qap, qkey = sqs[c]
        C.mm(bS, C.ones_bf[:], sap, start=(c == 0), stop=(c == KC - 1), r=[skey, "onesb"])
        C.mm(bQ, C.ones_bf[:], qap, start=(c == 0), stop=(c == KC - 1), r=[qkey, "onesb"])
    return ln_tilemath(C, bS, bQ, rms=rms)


def ln_tilemath(C, bS, bQ, pi=0, rms=False):
    st = C.lnst[pi]
    P = f"ln{pi}"
    mu, var, rstd, m = st["mu"], st["var"], st["rstd"], st["m"]
    C.act(mu[:], C.bank(bS), AF.Copy, scale=1.0 / D, r=[f"PB{bS}"], w=[P + ".mu"])
    C.act(var[:], C.bank(bQ), AF.Copy, scale=1.0 / D, r=[f"PB{bQ}"], w=[P + ".var"])
    C.tt(m[:], mu[:], mu[:], ALU.mult, r=[P + ".mu"], w=[P + ".m"])
    C.tt(var[:], var[:], m[:], ALU.subtract, r=[P + ".var", P + ".m"], w=[P + ".var"])
    C.act(m[:], var[:], AF.Ln, bias=C.c_eps_ln, r=[P + ".var", "consts"], w=[P + ".m"])
    C.act(rstd[:], m[:], AF.Exp, scale=-0.5, r=[P + ".m"], w=[P + ".rstd"])
    C.act(st["mub"][:], C.bank(bS), AF.Copy, scale=1.0 / D, r=[f"PB{bS}"], w=[P + ".mub"])
    if not rms:
        C.act(st["scb"][:], rstd[:], AF.Copy, r=[P + ".rstd"], w=[P + ".scb"])
        return {"mu": P + ".mub", "sc": P + ".scb", "scap": st["scb"],
                "muap": st["mub"], "pi": pi}
    n2, msc = st["n2"], st["msc"]
    C.ts(n2[:], var[:], float(D), None, ALU.mult, r=[P + ".var"], w=["ln.n2"])
    C.tt(n2[:], n2[:], rstd[:], ALU.mult, r=["ln.n2", P + ".rstd"], w=["ln.n2"])
    C.tt(n2[:], n2[:], rstd[:], ALU.mult, r=["ln.n2", P + ".rstd"], w=["ln.n2"])
    C.act(n2[:], n2[:], AF.Ln, r=["ln.n2"], w=["ln.n2"])
    C.act(n2[:], n2[:], AF.Exp, scale=0.5, r=["ln.n2"], w=["ln.n2"])
    C.ts(n2[:], n2[:], 1e-12, None, ALU.max, r=["ln.n2"], w=["ln.n2"])
    C.act(n2[:], n2[:], AF.Ln, r=["ln.n2"], w=["ln.n2"])
    C.act(n2[:], n2[:], AF.Exp, scale=-1.0, bias=C.c_ln32, r=["ln.n2", "consts"], w=["ln.n2"])
    C.tt(st["scb"][:], rstd[:], n2[:], ALU.mult, r=[P + ".rstd", "ln.n2"], w=[P + ".scb"])
    return {"mu": P + ".mub", "sc": P + ".scb", "scap": st["scb"],
            "muap": st["mub"], "pi": pi}


def ln_apply_dst(C, srcs, keys, dsts, gamma=None):
    pi = keys["pi"]
    st = C.lnst[pi]
    P = f"ln{pi}"
    mu, scap = keys["muap"], keys["scap"]
    for c in range(KC):
        sap, skey = srcs[c]
        dap, dkey = dsts[c]
        q = "V"
        d1 = st["d1"][c % 2]
        C.tt(d1[:], sap, mu[:], ALU.subtract, r=[skey, keys["mu"]], w=[f"ln.d1{c % 2}"], q=q)
        if gamma is None:
            C.tt(dap, d1[:], scap[:], ALU.mult, r=[f"ln.d1{c % 2}", keys["sc"]], w=[dkey], q=q)
        else:
            C.stt(dap, d1[:], gamma[:, c:c + 1], scap[:], ALU.mult, ALU.mult,
                  r=[f"ln.d1{c % 2}", "gamma", keys["sc"]], w=[dkey], q="V")


def ln_apply_resid(C, srcs, keys, t0, TT):
    pi = keys["pi"]
    st = C.lnst[pi]
    P = f"ln{pi}"
    mu, scap = keys["muap"], keys["scap"]
    for c in range(KC):
        sap, skey = srcs[c]
        q = "V"
        d1 = st["d1"][c % 2]
        d2 = C.lnd2f[c % 2]
        xap = C.xres[c][:, t0:t0 + TT]
        xkey = f"xres{c}t{t0 // TILE}"
        C.tt(d1[:], sap, mu[:], ALU.subtract, r=[skey, keys["mu"]], w=[f"ln.d1{c % 2}"], q=q)
        C.tt(d2[:], d1[:], scap[:], ALU.mult, r=[f"ln.d1{c % 2}", keys["sc"]], w=[f"ln.d2f{c % 2}"], q=q)
        C.tt(xap, xap, d2[:], ALU.add, r=[f"ln.d2f{c % 2}", xkey], w=[xkey], q=q)


def ln_full(C, src_handles, src_prefix, t0, TT, mode, dsts=None, gamma=None, rms=False, resid_t0=None):
    bS, bQ = C.rb(), C.rb()
    srcs = []
    pi = (t0 // TILE) % 2
    sfx = f"t{t0 // TILE}" if src_prefix == "xres" else ""
    for c in range(KC):
        sap = src_handles[c][:, t0:t0 + TT]
        skey = f"{src_prefix}{c}{sfx}"
        sq = C.lnsq[pi * 2 + c % 2]
        sqk = f"ln{pi}.sq{c % 2}"
        C.act(sq[:], sap, AF.Square, r=[skey], w=[sqk])
        C.mm(bS, C.ones_bf[:], sap, start=(c == 0), stop=(c == KC - 1), r=[skey, "onesb"])
        C.mm(bQ, C.ones_bf[:], sq[:], start=(c == 0), stop=(c == KC - 1), r=[sqk, "onesb"])
        srcs.append((sap, skey))
    keys = ln_tilemath(C, bS, bQ, pi=pi, rms=rms)
    if mode == "dst":
        dpairs = [(dsts[c][0], dsts[c][1]) for c in range(KC)]
        ln_apply_dst(C, srcs, keys, dpairs, gamma=gamma)
    else:
        ln_apply_resid(C, srcs, keys, t0 if resid_t0 is None else resid_t0, TT)



def make_fixtures(C, nc):
    C.xres = [C.sb(f"xres{c}", [128, T], BF16) for c in range(KC)]
    C.ones_f = C.sb("ones_f", [128, 128], F32)
    C.ones_bf = C.sb("ones_bf", [128, 128], BF16)
    C.ident = C.sb("ident", [128, 128], F32)
    C.identb = C.sb("identb", [128, 128], BF16)
    C.consts = C.sb("consts", [128, 8], F32)
    C.c_eps_ln = C.consts[:, 0:1]
    C.c_eps_attn = C.consts[:, 1:2]
    C.c_zero = C.consts[:, 2:3]
    C.c_one = C.consts[:, 3:4]
    C.c_ln32 = C.consts[:, 4:5]
    C.lnst = []
    shared_d1 = [C.sb(f"lnst_d1{i}", [128, TILE], BF16) for i in range(2)]
    shared_d2 = [C.sb(f"lnst_d2{i}", [128, TILE], BF16) for i in range(2)]
    C.lnd2f = [C.sb(f"lnst_d2f{i}", [128, TILE], F32) for i in range(2)]
    for pi in range(2):
        d = {k: C.sb(f"lnst{pi}_{k}", [128, TILE], F32)
             for k in ("mu", "var", "rstd", "m")}
        d["mub"] = C.sb(f"lnst{pi}_mub", [128, TILE], BF16)
        d["scb"] = C.sb(f"lnst{pi}_scb", [128, TILE], BF16)
        d["d1"], d["d2"] = shared_d1, shared_d2
        C.lnst.append(d)
    C.lnsq = [C.sb(f"lnsq{i}", [128, TILE], BF16) for i in range(4)]
    C.xio_sb = [C.sb(f"xio_sb{i}", [128, D], F32) for i in range(2)]

    IN_FIX = nc.dram_tensor("fix", [128, 8 + 128 + 128], F32, kind="ExternalInput").ap()
    C.dmaq = "Y"
    C.P.op("Y", lambda e: e.dma_start(out=C.consts[:], in_=IN_FIX[:, 0:8]),
           r=[], w=["consts"], dma=True, tag="dma.consts")
    C.P.op("Y", lambda e: e.dma_start(out=C.ones_f[:], in_=IN_FIX[:, 8:136]),
           r=[], w=["onesf"], dma=True, tag="dma.ones")
    C.P.op("P", lambda e: e.dma_start(out=C.ones_bf[:], in_=IN_FIX[:, 8:136]),
           r=[], w=["onesb"], dma=True, tag="dma.onesb")
    C.P.op("Y", lambda e: e.dma_start(out=C.ident[:], in_=IN_FIX[:, 136:264]),
           r=[], w=["ident"], dma=True, tag="dma.ident")
    C.P.op("P", lambda e: e.dma_start(out=C.identb[:], in_=IN_FIX[:, 136:264]),
           r=[], w=["identb"], dma=True, tag="dma.identb")


def emit_load_x_first(C, nc):
    XIN = nc.dram_tensor("xin", [T, D], F32, kind="ExternalInput").ap()
    xin_sb = C.xio_sb
    ntc = T // 128
    for tc in range(ntc):
        buf = xin_sb[tc % 2]
        bk = f"xiob{tc % 2}"
        C.dma("Y", buf[:], XIN[tc * 128:(tc + 1) * 128, :], r=[], w=[bk])
        for c in range(KC):
            bi = C.rb()
            C.transpose(self_ap(C, bi), buf[:, c * 128:(c + 1) * 128], C.ident[:],
                        r=[bk], w=[f"PB{bi}"])
            C.copy(C.xres[c][:, tc * 128:(tc + 1) * 128], C.bank(bi, 0, 128, 0, 128),
                   r=[f"PB{bi}"], w=[f"xres{c}t{tc // 4}"])


def self_ap(C, bi):
    return C.bank(bi, 0, 128, 0, 128)


def emit_load_x_fm(C, nc):
    XFM = nc.dram_tensor("xfm", [KC, 128, T], BF16, kind="ExternalInput").ap()
    for c in range(KC):
        C.dma("Y", C.xres[c][:], XFM[c], r=[], w=[f"xres{c}t{ti}" for ti in range(NT)])


def emit_store_x_fm(C, nc):
    XFO = nc.dram_tensor("xfo", [KC, 128, T], BF16, kind="ExternalOutput").ap()
    for c in range(KC):
        C.dma("Y", XFO[c], C.xres[c][:], r=[f"xres{c}t{ti}" for ti in range(NT)], w=[], out=True)


def emit_store_x_last(C, nc):
    OUT = nc.dram_tensor("out", [T, D], F32, kind="ExternalOutput").ap()
    out_sb = C.xio_sb
    ntc = T // 128
    for tc in range(ntc):
        buf = out_sb[tc % 2]
        bk = f"xiob{tc % 2}"
        for c in range(KC):
            bi = C.rb()
            bap = C.bank_bf(bi, 0, 128, 0, 128)
            C.transpose(bap, C.xres[c][:, tc * 128:(tc + 1) * 128], C.identb[:],
                        r=[f"xres{c}t{tc // 4}", "identb"], w=[f"PB{bi}"])
            C.copy(buf[:, c * 128:(c + 1) * 128], bap, r=[f"PB{bi}"], w=[bk])
        C.dma("Y", OUT[tc * 128:(tc + 1) * 128, :], buf[:], r=[bk], w=[], out=True)



def emit_l0_ffn(C, nc):
    U1 = nc.dram_tensor("u1", [D, 256], F32, kind="ExternalInput").ap()
    V1 = nc.dram_tensor("v1", [256, 4096], F32, kind="ExternalInput").ap()
    U2 = nc.dram_tensor("u2", [4096, 256], F32, kind="ExternalInput").ap()
    V2 = nc.dram_tensor("v2", [256, D], F32, kind="ExternalInput").ap()
    B1 = nc.dram_tensor("b1f", [128, 32], F32, kind="ExternalInput").ap()
    B2 = nc.dram_tensor("b2f", [128, 8], F32, kind="ExternalInput").ap()
    u1 = C.ab("u1sb", [128, 8 * 256], BF16)
    v1 = C.ab("v1sb", [128, 2 * 4096], BF16)
    u2 = C.ab("u2sb", [128, 32 * 256], BF16)
    v2 = C.ab("v2sb", [128, 2 * D], BF16)
    b1 = C.ab("b1sb", [128, 32], F32)
    b2 = C.ab("b2sb", [128, 8], F32)
    for k in range(8):
        C.dma("P", u1[:, k * 256:(k + 1) * 256], U1[k * 128:(k + 1) * 128, :], r=[], w=["u1"])
    for k in range(2):
        C.dma("P", v1[:, k * 4096:(k + 1) * 4096], V1[k * 128:(k + 1) * 128, :], r=[], w=["v1"])
    for k in range(32):
        C.dma("P", u2[:, k * 256:(k + 1) * 256], U2[k * 128:(k + 1) * 128, :], r=[], w=["u2"])
    for k in range(2):
        C.dma("P", v2[:, k * D:(k + 1) * D], V2[k * 128:(k + 1) * 128, :], r=[], w=["v2"])
    C.dma("Y", b1[:], B1, r=[], w=["b1"])
    C.dma("Y", b2[:], B2, r=[], w=["b2"])

    xn = [C.ab(f"ffn_xn{c}", [128, 2 * TILE], BF16) for c in range(KC)]
    p_sb = C.ab("ffn_p", [128, 2 * 2 * TILE], BF16)
    h_sb = C.ab("ffn_h", [128, 32 * 2 * TILE], BF16)
    q_sb = C.ab("ffn_q", [128, 2 * 2 * TILE], BF16)
    o_sb = [C.ab(f"ffn_o{c}", [128, 2 * TILE], BF16) for c in range(KC)]
    osq = [C.ab(f"ffn_osq{c}", [128, 2 * TILE], BF16) for c in range(KC)]

    phases = globals().get("_L0F_PHASES", 6)
    for tg in range(2):
        for tl in range(2):
            t = 2 * tg + tl
            dsts = [(xn[c][:, tl * TILE:(tl + 1) * TILE], f"ffn_xn{c}s{tl}") for c in range(KC)]
            ln_full(C, C.xres, "xres", t * TILE, TILE, "dst", dsts=dsts)
        if phases < 2:
            continue
        for rc in range(2):
            bts = [C.rb(), C.rb()]
            for kc in range(KC):
                lhs = u1[:, kc * 256 + rc * 128: kc * 256 + (rc + 1) * 128]
                for tl in range(2):
                    C.mm(bts[tl], lhs, xn[kc][:, tl * TILE:(tl + 1) * TILE],
                         start=(kc == 0), stop=(kc == KC - 1), r=["u1", f"ffn_xn{kc}s{tl}"])
            for tl in range(2):
                C.copy(p_sb[:, (2 * rc + tl) * TILE:(2 * rc + tl + 1) * TILE],
                       C.bank(bts[tl]), r=[f"PB{bts[tl]}"], w=[f"ffn_p{rc}s{tl}"], q="S")
        if phases < 3:
            continue
        for fc in range(32):
            bts = [C.rb(), C.rb()]
            for rc in range(2):
                lhs = v1[:, rc * 4096 + fc * 128: rc * 4096 + (fc + 1) * 128]
                for tl in range(2):
                    C.mm(bts[tl], lhs, p_sb[:, (2 * rc + tl) * TILE:(2 * rc + tl + 1) * TILE],
                         start=(rc == 0), stop=(rc == 1), r=["v1", f"ffn_p{rc}s{tl}"])
            for tl in range(2):
                C.act(h_sb[:, (2 * fc + tl) * TILE:(2 * fc + tl + 1) * TILE], C.bank(bts[tl]),
                      AF.Gelu_apprx_tanh, bias=b1[:, fc:fc + 1],
                      r=[f"PB{bts[tl]}", "b1"], w=[f"ffn_h{fc}s{tl}"])
        if phases < 4:
            continue
        for rc in range(2):
            bts = [C.rb(), C.rb()]
            for fc in range(32):
                lhs = u2[:, fc * 256 + rc * 128: fc * 256 + (rc + 1) * 128]
                for tl in range(2):
                    C.mm(bts[tl], lhs, h_sb[:, (2 * fc + tl) * TILE:(2 * fc + tl + 1) * TILE],
                         start=(fc == 0), stop=(fc == 31), r=["u2", f"ffn_h{fc}s{tl}"])
            for tl in range(2):
                C.copy(q_sb[:, (2 * rc + tl) * TILE:(2 * rc + tl + 1) * TILE],
                       C.bank(bts[tl]), r=[f"PB{bts[tl]}"], w=[f"ffn_q{rc}s{tl}"], q="S")
        if phases < 5:
            continue
        for dc in range(KC):
            bts = [C.rb(), C.rb()]
            for rc in range(2):
                lhs = v2[:, rc * D + dc * 128: rc * D + (dc + 1) * 128]
                for tl in range(2):
                    C.mm(bts[tl], lhs, q_sb[:, (2 * rc + tl) * TILE:(2 * rc + tl + 1) * TILE],
                         start=(rc == 0), stop=(rc == 1), r=["v2", f"ffn_q{rc}s{tl}"])
            for tl in range(2):
                C.ts(o_sb[dc][:, tl * TILE:(tl + 1) * TILE], C.bank(bts[tl]), b2[:, dc:dc + 1],
                     None, ALU.add, r=[f"PB{bts[tl]}", "b2"], w=[f"ffn_o{dc}s{tl}"])
                C.act(osq[dc][:, tl * TILE:(tl + 1) * TILE], C.bank(bts[tl]), AF.Square,
                      bias=b2[:, dc:dc + 1], r=[f"PB{bts[tl]}", "b2"], w=[f"ffn_osq{dc}s{tl}"])
        if phases < 6:
            continue
        for tl in range(2):
            t = 2 * tg + tl
            srcs = [(o_sb[c][:, tl * TILE:(tl + 1) * TILE], f"ffn_o{c}s{tl}") for c in range(KC)]
            sqs = [(osq[c][:, tl * TILE:(tl + 1) * TILE], f"ffn_osq{c}s{tl}") for c in range(KC)]
            keys = ln_stats(C, srcs, sqs)
            ln_apply_resid(C, srcs, keys, t * TILE, TILE)



def build_program(blocks, first, last):
    nc = bass.Bass("TRN2", target_bir_lowering=False, debug=False)
    C = Ctx(nc)
    make_fixtures(C, nc)
    C.arena_init()
    if first:
        emit_load_x_first(C, nc)
    else:
        emit_load_x_fm(C, nc)
    for b in blocks:
        C.arena_reset()
        if b == "l0f":
            emit_l0_ffn(C, nc)
        elif b == "l0a":
            emit_l0_attn(C, nc)
        elif b == "l1a":
            emit_l1_attn(C, nc)
        elif b == "l1f":
            emit_l1_ffn(C, nc)
    if last:
        emit_store_x_last(C, nc)
    else:
        emit_store_x_fm(C, nc)
    for name in globals().get("_TAPS", ()):
        h = C.reg[name]
        tp = nc.dram_tensor(f"tap_{name}", list(h[:].shape), h[:].dtype,
                            kind="ExternalOutput").ap()
        C.dma("Y", tp, h[:], r=[name], w=[], out=True)
    C.P.finalize_and_emit()
    return nc



def _fix_array():
    fix = np.zeros((128, 8 + 128 + 128), np.float32)
    fix[:, 0] = EPS_LN
    fix[:, 1] = EPS_ATTN
    fix[:, 2] = 0.0
    fix[:, 3] = 1.0
    fix[:, 4] = float(np.log(np.sqrt(1024.0)))
    fix[:, 8:136] = 1.0
    fix[:, 136:264] = np.eye(128, dtype=np.float32)
    return fix


def _common_weight_arrays(inp):
    a = {}
    a["u1"] = np.ascontiguousarray(inp["L0_ffn_U1"])
    a["v1"] = np.ascontiguousarray(inp["L0_ffn_V1"])
    a["u2"] = np.ascontiguousarray(inp["L0_ffn_U2"])
    a["v2"] = np.ascontiguousarray(inp["L0_ffn_V2"])
    a["b1f"] = np.ascontiguousarray(inp["L0_ffn_b1"].reshape(32, 128).T)
    a["b2f"] = np.ascontiguousarray(inp["L0_ffn_b2"].reshape(8, 128).T)
    a["wqa"] = np.ascontiguousarray(inp["L0_attn_w_qkva"].T)
    a["wq1"] = np.ascontiguousarray(np.asarray(inp["L1_attn_w_qkv"]).T)
    a["bq1"] = np.ascontiguousarray(np.asarray(inp["L1_attn_b_qkv"]).reshape(24, 128).T)
    a["wo1"] = np.ascontiguousarray(np.asarray(inp["L1_attn_w_out"]).T)
    a["bo1"] = np.ascontiguousarray(np.asarray(inp["L1_attn_b_out"]).reshape(8, 128).T)
    ii = np.arange(128)
    a["mut"] = (ii[:, None] <= ii[None, :]).astype(np.float32)
    a["hones"] = ((ii[:, None] < 64) == (ii[None, :] < 64)).astype(np.float32)
    a["cent"] = np.ascontiguousarray(np.asarray(inp["L1_ffn_centroids"]).T)
    a["lng"] = np.ascontiguousarray(inp["L1_ffn_ln_g"])
    a["lnb"] = np.ascontiguousarray(inp["L1_ffn_ln_b"])
    a["w1"] = np.ascontiguousarray(inp["L1_ffn_w1"])
    a["w2"] = np.ascontiguousarray(inp["L1_ffn_w2"])
    a["b1e"] = np.ascontiguousarray(
        np.asarray(inp["L1_ffn_b1"]).reshape(8, 32, 128).transpose(2, 0, 1).reshape(128, 256))
    a["b2e"] = np.ascontiguousarray(inp["L1_ffn_b2"])
    sel = np.zeros((8, 8 * 128), np.float32)
    for e in range(8):
        sel[e, e * 128:(e + 1) * 128] = 1.0
    a["sel"] = sel
    a["gam"] = np.ascontiguousarray(np.asarray(inp["L0_attn_gamma"]).reshape(8, 128).T)
    a["fix"] = _fix_array()
    return a


def _launch(blocks, first, last, per_core_maps):
    nc = build_program(blocks, first, last)
    res = run_bass_kernel_spmd(nc, per_core_maps, core_ids=list(range(NCORE)))
    return res.results


def kernel(**inputs):
    x = np.asarray(inputs["L0_attn_x"]).reshape(16384, D).astype(np.float32)
    shards = [np.ascontiguousarray(x[i * T:(i + 1) * T]) for i in range(NCORE)]
    wa = _common_weight_arrays(inputs)

    launches = globals().get("_TEST_LAUNCHES") or [(ALL_BLOCKS, True, True)]
    rmasks = []
    for r in range(NCORE):
        m = np.zeros((128, NCORE), np.float32)
        m[:, :r] = 1.0
        rmasks.append(m)
    cur = [{"xin": shards[i], "rmask": rmasks[i], "rmaskc": 1.0 - rmasks[i], **wa}
           for i in range(NCORE)]
    outs = None
    for blocks, first, last in launches:
        results = _launch(blocks, first, last, cur)
        if last:
            outs = [r["out"] for r in results]
        else:
            cur = [{**cur[i], "xfm": results[i]["xfo"]} for i in range(NCORE)]
    y = np.concatenate(outs, axis=0).reshape(1, 16384, D)
    return y.astype(np.float32)


def emit_l0_attn(C, nc):
    WQA = nc.dram_tensor("wqa", [D, 3 * D], F32, kind="ExternalInput").ap()
    GAM = nc.dram_tensor("gam", [128, 8], F32, kind="ExternalInput").ap()
    RM = C.dram_in("rmask", [128, 8])
    RMC = C.dram_in("rmaskc", [128, 8])
    gam = C.ab("gam_sb", [128, 8], F32)
    n2t = C.ab("lnst_n2", [128, TILE], F32)
    msct = C.ab("lnst_msc", [128, TILE], F32)
    C.lnst[0]["n2"] = C.lnst[1]["n2"] = n2t
    C.lnst[0]["msc"] = C.lnst[1]["msc"] = msct
    rm = C.ab("rm_sb", [128, 8], F32)
    rmc = C.ab("rmc_sb", [128, 8], F32)
    C.dma("Y", gam[:], GAM, r=[], w=["gamma"])
    C.dma("Y", rm[:], RM, r=[], w=["rmask"])
    C.dma("Y", rmc[:], RMC, r=[], w=["rmaskc"])

    h = [C.ab(f"l0a_h{c}", [128, T], BF16) for c in range(KC)]
    s1 = [C.ab(f"l0a_s1{c}", [128, T], BF16) for c in range(KC)]
    s2 = [C.ab(f"l0a_s2{i}", [128, T], BF16) for i in range(4)]
    qb = [C.ab(f"l0a_q{i}", [128, T], BF16) for i in range(2)]
    ab_ = [C.ab(f"l0a_a{i}", [128, T], F32) for i in range(2)]
    kvb = [C.ab(f"l0a_kv{i}", [128, T], BF16) for i in range(2)]
    yb = [C.ab(f"l0a_y{i}", [128, T], BF16) for i in range(2)]
    pb = [C.ab(f"l0a_p{i}", [128, T], BF16) for i in range(2)]
    wq = [C.ab(f"l0a_w{i}", [128, 8 * 128], BF16) for i in range(4)]
    exch = [C.ab(f"l0a_exch{i}", [128, 8], F32) for i in range(2)]
    gsb = [C.ab(f"l0a_gsb{i}", [128, 64], F32) for i in range(2)]
    carry = C.ab("l0a_carry", [128, 8], F32)
    aeff = C.ab("l0a_aeff", [128, 4], F32)
    yeff = C.ab("l0a_yeff", [128, 4], F32)

    cc_in = [nc.dram_tensor(f"l0a_ccin{hh}", [128, 8], F32) for hh in range(2)]
    cc_out = [nc.dram_tensor(f"l0a_ccout{hh}", [1024, 8], F32, addr_space="Shared")
              for hh in range(2)]

    for t in range(NT):
        dsts = [(h[c][:, t * TILE:(t + 1) * TILE], f"l0a_h{c}t{t}") for c in range(KC)]
        ln_full(C, C.xres, "xres", t * TILE, TILE, "dst", dsts=dsts, gamma=gam, rms=True)

    combined = [False, False]

    def fixup_half(hh, ci):
        cs = slice(hh * 4, hh * 4 + 4)
        if not combined[hh]:
            combined[hh] = True
            C.memset(carry[:, cs], 0.0, w=["l0a_carry"])
            for j in range(NCORE):
                C.ts(aeff[:], gsb[hh][:, j * 8:j * 8 + 4], rm[:, j:j + 1], rmc[:, j:j + 1],
                     ALU.mult, r=[f"l0a_gsb{hh}", "rmask", "rmaskc"], w=["l0a_aeff"], op1=ALU.add)
                C.ts(yeff[:], gsb[hh][:, j * 8 + 4:j * 8 + 8], rm[:, j:j + 1], None,
                     ALU.mult, r=[f"l0a_gsb{hh}", "rmask"], w=["l0a_yeff"])
                C.tt(carry[:, cs], carry[:, cs], aeff[:], ALU.mult, r=["l0a_carry", "l0a_aeff"], w=["l0a_carry"])
                C.tt(carry[:, cs], carry[:, cs], yeff[:], ALU.add, r=["l0a_carry", "l0a_yeff"], w=["l0a_carry"])
        c = hh * 4 + ci
        C.stt(s1[c][:], s2[ci][:], carry[:, c:c + 1], s1[c][:], ALU.mult, ALU.add,
              r=[f"l0a_s2h{hh}{ci}", "l0a_carry", f"l0a_s1{c}"], w=[f"l0a_s1{c}"])

    for half in range(2):
        for ci in range(4):
            c = half * 4 + ci
            si = c % 2
            kq, ka, kkv, ky, kp_ = (f"l0a_qb{si}", f"l0a_ab{si}", f"l0a_kvb{si}",
                                    f"l0a_yb{si}", f"l0a_pb{si}")
            for gi, (eg, dstbuf, dkey, evac) in enumerate([
                    (c, qb[si], kq, "copy"),
                    (16 + c, ab_[si], ka, "sigmoid"),
                    (8 + c, kvb[si], kkv, "copy")]):
                wbuf = wq[(3 * c + gi) % 4]
                wkey = f"l0a_w{(3 * c + gi) % 4}"
                C.dma("P", wbuf[:].rearrange("p (k e) -> p k e", k=KC),
                      WQA[:, eg * 128:(eg + 1) * 128].rearrange("(k p) e -> p k e", p=128),
                      r=[], w=[wkey])
                bts = [C.rb(), C.rb(), C.rb(), C.rb()]
                for k in range(KC):
                    lhs = wbuf[:, k * 128:(k + 1) * 128]
                    for t in range(NT):
                        C.mm(bts[t], lhs, h[k][:, t * TILE:(t + 1) * TILE],
                             start=(k == 0), stop=(k == KC - 1), r=[wkey, f"l0a_h{k}t{t}"])
                for t in range(NT):
                    dap = dstbuf[:, t * TILE:(t + 1) * TILE]
                    if evac == "sigmoid":
                        C.act(dap, C.bank(bts[t]), AF.Sigmoid, r=[f"PB{bts[t]}"], w=[ka])
                    else:
                        C.act(dap, C.bank(bts[t]), AF.Copy, r=[f"PB{bts[t]}"], w=[dkey])
            C.scan(yb[si][:], ab_[si][:], kvb[si][:], C.c_zero, ALU.mult, ALU.add,
                   r=[ka, kkv, "consts"], w=[ky])
            C.scan(pb[si][:], ab_[si][:], ab_[si][:], C.c_one, ALU.mult, ALU.min,
                   r=[ka, "consts"], w=[kp_])
            C.tt(s1[c][:], qb[si][:], yb[si][:], ALU.mult, r=[kq, ky], w=[f"l0a_s1{c}"])
            if half == 1:
                fixup_half(0, ci)
            C.tt(s2[ci][:], qb[si][:], pb[si][:], ALU.mult, r=[kq, kp_], w=[f"l0a_s2h{half}{ci}"])
            C.copy(exch[half][:, ci:ci + 1], pb[si][:, T - 1:T], r=[kp_], w=[f"l0a_exch{half}"])
            C.copy(exch[half][:, 4 + ci:5 + ci], yb[si][:, T - 1:T], r=[ky], w=[f"l0a_exch{half}"])
        C.dma("P", cc_in[half][:, :], exch[half][:], r=[f"l0a_exch{half}"], w=[f"l0a_ccin{half}"])
        C.cc("AllGather", [cc_in[half][:, :]], [cc_out[half][:, :]],
             r=[f"l0a_ccin{half}"], w=[f"l0a_ccout{half}"])
        C.dma("Y", gsb[half][:].rearrange("p (r f) -> p r f", r=8),
              cc_out[half][:, :].rearrange("(r p) f -> p r f", p=128),
              r=[f"l0a_ccout{half}"], w=[f"l0a_gsb{half}"])
    for ci in range(4):
        fixup_half(1, ci)
    for t in range(NT):
        ln_full(C, s1, "l0a_s1", t * TILE, TILE, "resid")


def emit_l0_ffn(C, nc):
    U1 = nc.dram_tensor("u1", [D, 256], F32, kind="ExternalInput").ap()
    V1 = nc.dram_tensor("v1", [256, 4096], F32, kind="ExternalInput").ap()
    U2 = nc.dram_tensor("u2", [4096, 256], F32, kind="ExternalInput").ap()
    V2 = nc.dram_tensor("v2", [256, D], F32, kind="ExternalInput").ap()
    B1 = nc.dram_tensor("b1f", [128, 32], F32, kind="ExternalInput").ap()
    B2 = nc.dram_tensor("b2f", [128, 8], F32, kind="ExternalInput").ap()
    u1 = C.ab("u1sb", [128, 8 * 256], BF16)
    v1 = C.ab("v1sb", [128, 2 * 4096], BF16)
    u2 = C.ab("u2sb", [128, 32 * 256], BF16)
    v2 = C.ab("v2sb", [128, 2 * D], BF16)
    b1 = C.ab("b1sb", [128, 32], F32)
    b2 = C.ab("b2sb", [128, 8], F32)
    for k in range(8):
        C.dma("P", u1[:, k * 256:(k + 1) * 256], U1[k * 128:(k + 1) * 128, :], r=[], w=["u1"])
    for k in range(2):
        C.dma("P", v1[:, k * 4096:(k + 1) * 4096], V1[k * 128:(k + 1) * 128, :], r=[], w=["v1"])
    for k in range(32):
        C.dma("P", u2[:, k * 256:(k + 1) * 256], U2[k * 128:(k + 1) * 128, :], r=[], w=["u2"])
    for k in range(2):
        C.dma("P", v2[:, k * D:(k + 1) * D], V2[k * 128:(k + 1) * 128, :], r=[], w=["v2"])
    C.dma("Y", b1[:], B1, r=[], w=["b1"])
    C.dma("Y", b2[:], B2, r=[], w=["b2"])

    xn = [C.ab(f"ffn_xn{c}", [128, TILE], BF16) for c in range(KC)]
    p_sb = C.ab("ffn_p", [128, 2 * TILE], BF16)
    h_sb = C.ab("ffn_h", [128, 32 * TILE], BF16)
    q_sb = C.ab("ffn_q", [128, 2 * TILE], BF16)
    o_sb = [C.ab(f"ffn_o{c}", [128, TILE], BF16) for c in range(KC)]
    osq = [C.ab(f"ffn_osq{c}", [128, TILE], BF16) for c in range(KC)]

    phases = globals().get("_L0F_PHASES", 6)
    ntt = globals().get("_L0F_NT", NT)
    for t in range(ntt):
        t0 = t * TILE
        dsts = [(xn[c][:], f"ffn_xn{c}") for c in range(KC)]
        ln_full(C, C.xres, "xres", t0, TILE, "dst", dsts=dsts)
        if phases < 2:
            continue
        for rc in range(2):
            bi = C.rb()
            for kc in range(KC):
                C.mm(bi, u1[:, kc * 256 + rc * 128: kc * 256 + (rc + 1) * 128], xn[kc][:],
                     start=(kc == 0), stop=(kc == KC - 1), r=["u1", f"ffn_xn{kc}"])
            C.copy(p_sb[:, rc * TILE:(rc + 1) * TILE], C.bank(bi), r=[f"PB{bi}"], w=[f"ffn_p{rc}"], q="S")
        if phases < 3:
            continue
        for fc in range(32):
            bi = C.rb()
            for rc in range(2):
                C.mm(bi, v1[:, rc * 4096 + fc * 128: rc * 4096 + (fc + 1) * 128],
                     p_sb[:, rc * TILE:(rc + 1) * TILE],
                     start=(rc == 0), stop=(rc == 1), r=["v1", f"ffn_p{rc}"])
            C.act(h_sb[:, fc * TILE:(fc + 1) * TILE], C.bank(bi), AF.Gelu_apprx_tanh,
                  bias=b1[:, fc:fc + 1], r=[f"PB{bi}", "b1"], w=[f"ffn_h{fc}"])
        if phases < 4:
            continue
        for rc in range(2):
            bi = C.rb()
            for fc in range(32):
                C.mm(bi, u2[:, fc * 256 + rc * 128: fc * 256 + (rc + 1) * 128],
                     h_sb[:, fc * TILE:(fc + 1) * TILE],
                     start=(fc == 0), stop=(fc == 31), r=["u2", f"ffn_h{fc}"])
            C.copy(q_sb[:, rc * TILE:(rc + 1) * TILE], C.bank(bi), r=[f"PB{bi}"], w=[f"ffn_q{rc}"], q="S")
        if phases < 5:
            continue
        for dc in range(KC):
            bi = C.rb()
            for rc in range(2):
                C.mm(bi, v2[:, rc * D + dc * 128: rc * D + (dc + 1) * 128],
                     q_sb[:, rc * TILE:(rc + 1) * TILE],
                     start=(rc == 0), stop=(rc == 1), r=["v2", f"ffn_q{rc}"])
            C.ts(o_sb[dc][:], C.bank(bi), b2[:, dc:dc + 1], None, ALU.add,
                 r=[f"PB{bi}", "b2"], w=[f"ffn_o{dc}"])
            C.act(osq[dc][:], C.bank(bi), AF.Square, bias=b2[:, dc:dc + 1],
                  r=[f"PB{bi}", "b2"], w=[f"ffn_osq{dc}"])
        if phases < 6:
            continue
        srcs = [(o_sb[c][:], f"ffn_o{c}") for c in range(KC)]
        sqs = [(osq[c][:], f"ffn_osq{c}") for c in range(KC)]
        keys = ln_stats(C, srcs, sqs)
        ln_apply_resid(C, srcs, keys, t0, TILE)



def build_program(blocks, first, last):
    nc = bass.Bass("TRN2", target_bir_lowering=False, debug=False)
    C = Ctx(nc)
    make_fixtures(C, nc)
    C.arena_init()
    if first:
        emit_load_x_first(C, nc)
    else:
        emit_load_x_fm(C, nc)
    for b in blocks:
        C.arena_reset()
        if b == "l0f":
            emit_l0_ffn(C, nc)
        elif b == "l0a":
            emit_l0_attn(C, nc)
        elif b == "l1a":
            emit_l1_attn(C, nc)
        elif b == "l1f":
            emit_l1_ffn(C, nc)
    if last:
        emit_store_x_last(C, nc)
    else:
        emit_store_x_fm(C, nc)
    for name in globals().get("_TAPS", ()):
        h = C.reg[name]
        tp = nc.dram_tensor(f"tap_{name}", list(h[:].shape), h[:].dtype,
                            kind="ExternalOutput").ap()
        C.dma("Y", tp, h[:], r=[name], w=[], out=True)
    C.P.finalize_and_emit()
    return nc



def _fix_array():
    fix = np.zeros((128, 8 + 128 + 128), np.float32)
    fix[:, 0] = EPS_LN
    fix[:, 1] = EPS_ATTN
    fix[:, 2] = 0.0
    fix[:, 3] = 1.0
    fix[:, 4] = float(np.log(np.sqrt(1024.0)))
    fix[:, 8:136] = 1.0
    fix[:, 136:264] = np.eye(128, dtype=np.float32)
    return fix


def _common_weight_arrays(inp):
    a = {}
    a["u1"] = np.ascontiguousarray(inp["L0_ffn_U1"])
    a["v1"] = np.ascontiguousarray(inp["L0_ffn_V1"])
    a["u2"] = np.ascontiguousarray(inp["L0_ffn_U2"])
    a["v2"] = np.ascontiguousarray(inp["L0_ffn_V2"])
    a["b1f"] = np.ascontiguousarray(inp["L0_ffn_b1"].reshape(32, 128).T)
    a["b2f"] = np.ascontiguousarray(inp["L0_ffn_b2"].reshape(8, 128).T)
    a["wqa"] = np.ascontiguousarray(inp["L0_attn_w_qkva"].T)
    a["wq1"] = np.ascontiguousarray(np.asarray(inp["L1_attn_w_qkv"]).T)
    a["bq1"] = np.ascontiguousarray(np.asarray(inp["L1_attn_b_qkv"]).reshape(24, 128).T)
    a["wo1"] = np.ascontiguousarray(np.asarray(inp["L1_attn_w_out"]).T)
    a["bo1"] = np.ascontiguousarray(np.asarray(inp["L1_attn_b_out"]).reshape(8, 128).T)
    ii = np.arange(128)
    a["mut"] = (ii[:, None] <= ii[None, :]).astype(np.float32)
    a["hones"] = ((ii[:, None] < 64) == (ii[None, :] < 64)).astype(np.float32)
    a["cent"] = np.ascontiguousarray(np.asarray(inp["L1_ffn_centroids"]).T)
    a["lng"] = np.ascontiguousarray(inp["L1_ffn_ln_g"])
    a["lnb"] = np.ascontiguousarray(inp["L1_ffn_ln_b"])
    a["w1"] = np.ascontiguousarray(inp["L1_ffn_w1"])
    a["w2"] = np.ascontiguousarray(inp["L1_ffn_w2"])
    a["b1e"] = np.ascontiguousarray(
        np.asarray(inp["L1_ffn_b1"]).reshape(8, 32, 128).transpose(2, 0, 1).reshape(128, 256))
    a["b2e"] = np.ascontiguousarray(inp["L1_ffn_b2"])
    sel = np.zeros((8, 8 * 128), np.float32)
    for e in range(8):
        sel[e, e * 128:(e + 1) * 128] = 1.0
    a["sel"] = sel
    a["gam"] = np.ascontiguousarray(np.asarray(inp["L0_attn_gamma"]).reshape(8, 128).T)
    a["fix"] = _fix_array()
    return a


def _launch(blocks, first, last, per_core_maps):
    nc = build_program(blocks, first, last)
    res = run_bass_kernel_spmd(nc, per_core_maps, core_ids=list(range(NCORE)))
    return res.results


def kernel(**inputs):
    x = np.asarray(inputs["L0_attn_x"]).reshape(16384, D).astype(np.float32)
    shards = [np.ascontiguousarray(x[i * T:(i + 1) * T]) for i in range(NCORE)]
    wa = _common_weight_arrays(inputs)

    launches = globals().get("_TEST_LAUNCHES") or [(ALL_BLOCKS, True, True)]
    rmasks = []
    for r in range(NCORE):
        m = np.zeros((128, NCORE), np.float32)
        m[:, :r] = 1.0
        rmasks.append(m)
    cur = [{"xin": shards[i], "rmask": rmasks[i], "rmaskc": 1.0 - rmasks[i], **wa}
           for i in range(NCORE)]
    outs = None
    for blocks, first, last in launches:
        results = _launch(blocks, first, last, cur)
        if last:
            outs = [r["out"] for r in results]
        else:
            cur = [{**cur[i], "xfm": results[i]["xfo"]} for i in range(NCORE)]
    y = np.concatenate(outs, axis=0).reshape(1, 16384, D)
    return y.astype(np.float32)


def emit_l0_attn(C, nc):
    WQA = nc.dram_tensor("wqa", [D, 3 * D], F32, kind="ExternalInput").ap()
    GAM = nc.dram_tensor("gam", [128, 8], F32, kind="ExternalInput").ap()
    RM = C.dram_in("rmask", [128, 8])
    RMC = C.dram_in("rmaskc", [128, 8])
    gam = C.ab("gam_sb", [128, 8], F32)
    n2t = C.ab("lnst_n2", [128, TILE], F32)
    msct = C.ab("lnst_msc", [128, TILE], F32)
    C.lnst[0]["n2"] = C.lnst[1]["n2"] = n2t
    C.lnst[0]["msc"] = C.lnst[1]["msc"] = msct
    rm = C.ab("rm_sb", [128, 8], F32)
    rmc = C.ab("rmc_sb", [128, 8], F32)
    C.dma("Y", gam[:], GAM, r=[], w=["gamma"])
    C.dma("Y", rm[:], RM, r=[], w=["rmask"])
    C.dma("Y", rmc[:], RMC, r=[], w=["rmaskc"])

    h = [C.ab(f"l0a_h{c}", [128, T], BF16) for c in range(KC)]
    s1 = [C.ab(f"l0a_s1{c}", [128, T], BF16) for c in range(KC)]
    s2 = [C.ab(f"l0a_s2{i}", [128, T], BF16) for i in range(4)]
    qb = C.ab("l0a_q", [128, T], F32)
    ab = C.ab("l0a_a", [128, T], F32)
    kvb = C.ab("l0a_kv", [128, T], F32)
    yb = C.ab("l0a_y", [128, T], F32)
    pb = C.ab("l0a_p", [128, T], F32)
    wq = [C.ab(f"l0a_w{i}", [128, 8 * 128], BF16) for i in range(4)]
    exch = C.ab("l0a_exch", [128, 8], F32)
    gsb = C.ab("l0a_gsb", [128, 64], F32)
    carry = C.ab("l0a_carry", [128, 4], F32)
    aeff = C.ab("l0a_aeff", [128, 4], F32)
    yeff = C.ab("l0a_yeff", [128, 4], F32)

    cc_in = [nc.dram_tensor(f"l0a_ccin{hh}", [128, 8], F32) for hh in range(2)]
    cc_out = [nc.dram_tensor(f"l0a_ccout{hh}", [1024, 8], F32, addr_space="Shared")
              for hh in range(2)]

    for t in range(NT):
        dsts = [(h[c][:, t * TILE:(t + 1) * TILE], f"l0a_h{c}t{t}") for c in range(KC)]
        ln_full(C, C.xres, "xres", t * TILE, TILE, "dst", dsts=dsts, gamma=gam, rms=True)

    combined = [False, False]

    def fixup_half(hh, ci):
        cs = slice(hh * 4, hh * 4 + 4)
        if not combined[hh]:
            combined[hh] = True
            C.memset(carry[:, cs], 0.0, w=["l0a_carry"])
            for j in range(NCORE):
                C.ts(aeff[:], gsb[hh][:, j * 8:j * 8 + 4], rm[:, j:j + 1], rmc[:, j:j + 1],
                     ALU.mult, r=[f"l0a_gsb{hh}", "rmask", "rmaskc"], w=["l0a_aeff"], op1=ALU.add)
                C.ts(yeff[:], gsb[hh][:, j * 8 + 4:j * 8 + 8], rm[:, j:j + 1], None,
                     ALU.mult, r=[f"l0a_gsb{hh}", "rmask"], w=["l0a_yeff"])
                C.tt(carry[:, cs], carry[:, cs], aeff[:], ALU.mult, r=["l0a_carry", "l0a_aeff"], w=["l0a_carry"])
                C.tt(carry[:, cs], carry[:, cs], yeff[:], ALU.add, r=["l0a_carry", "l0a_yeff"], w=["l0a_carry"])
        c = hh * 4 + ci
        C.stt(s1[c][:], s2[ci][:], carry[:, c:c + 1], s1[c][:], ALU.mult, ALU.add,
              r=[f"l0a_s2h{hh}{ci}", "l0a_carry", f"l0a_s1{c}"], w=[f"l0a_s1{c}"])

    for half in range(2):
        for ci in range(4):
            c = half * 4 + ci
            for gi, (eg, dstbuf, dkey, evac) in enumerate([
                    (c, qb, "l0a_qb", "copy"),
                    (16 + c, ab, "l0a_ab", "sigmoid"),
                    (8 + c, kvb, "l0a_kvb", "copy")]):
                wbuf = wq[(3 * c + gi) % 4]
                wkey = f"l0a_w{(3 * c + gi) % 4}"
                C.dma("P", wbuf[:].rearrange("p (k e) -> p k e", k=KC),
                      WQA[:, eg * 128:(eg + 1) * 128].rearrange("(k p) e -> p k e", p=128),
                      r=[], w=[wkey])
                for t in range(NT):
                    bi = C.rb()
                    for k in range(KC):
                        C.mm(bi, wbuf[:, k * 128:(k + 1) * 128],
                             h[k][:, t * TILE:(t + 1) * TILE],
                             start=(k == 0), stop=(k == KC - 1), r=[wkey, f"l0a_h{k}t{t}"])
                    dap = dstbuf[:, t * TILE:(t + 1) * TILE]
                    if evac == "sigmoid":
                        C.act(dap, C.bank(bi), AF.Sigmoid, r=[f"PB{bi}"], w=[dkey])
                    else:
                        C.act(dap, C.bank(bi), AF.Copy, r=[f"PB{bi}"], w=[dkey])
            C.scan(yb[:], ab[:], kvb[:], C.c_zero, ALU.mult, ALU.add,
                   r=["l0a_ab", "l0a_kvb", "consts"], w=["l0a_yb"])
            C.scan(pb[:], ab[:], ab[:], C.c_one, ALU.mult, ALU.min,
                   r=["l0a_ab", "consts"], w=["l0a_pb"])
            C.tt(s1[c][:], qb[:], yb[:], ALU.mult, r=["l0a_qb", "l0a_yb"], w=[f"l0a_s1{c}"])
            C.tt(s2[ci][:], qb[:], pb[:], ALU.mult, r=["l0a_qb", "l0a_pb"], w=[f"l0a_s2{ci}"])
            C.copy(exch[:, ci:ci + 1], pb[:, T - 1:T], r=["l0a_pb"], w=["l0a_exch"])
            C.copy(exch[:, 4 + ci:5 + ci], yb[:, T - 1:T], r=["l0a_yb"], w=["l0a_exch"])
        C.dma("P", cc_in[half][:, :], exch[:], r=["l0a_exch"], w=[f"l0a_ccin{half}"])
        C.cc("AllGather", [cc_in[half][:, :]], [cc_out[half][:, :]],
             r=[f"l0a_ccin{half}"], w=[f"l0a_ccout{half}"])
        C.dma("P", gsb[:].rearrange("p (r f) -> p r f", r=8),
              cc_out[half][:, :].rearrange("(r p) f -> p r f", p=128),
              r=[f"l0a_ccout{half}"], w=["l0a_gsb"])
        C.memset(carry[:], 0.0, w=["l0a_carry"])
        for j in range(NCORE):
            C.ts(aeff[:], gsb[:, j * 8:j * 8 + 4], rm[:, j:j + 1], rmc[:, j:j + 1],
                 ALU.mult, r=["l0a_gsb", "rmask", "rmaskc"], w=["l0a_aeff"], op1=ALU.add)
            C.ts(yeff[:], gsb[:, j * 8 + 4:j * 8 + 8], rm[:, j:j + 1], None,
                 ALU.mult, r=["l0a_gsb", "rmask"], w=["l0a_yeff"])
            C.tt(carry[:], carry[:], aeff[:], ALU.mult, r=["l0a_carry", "l0a_aeff"], w=["l0a_carry"])
            C.tt(carry[:], carry[:], yeff[:], ALU.add, r=["l0a_carry", "l0a_yeff"], w=["l0a_carry"])
        for ci in range(4):
            c = half * 4 + ci
            C.stt(s1[c][:], s2[ci][:], carry[:, ci:ci + 1], s1[c][:], ALU.mult, ALU.add,
                  r=[f"l0a_s2{ci}", "l0a_carry", f"l0a_s1{c}"], w=[f"l0a_s1{c}"])
    for t in range(NT):
        ln_full(C, s1, "l0a_s1", t * TILE, TILE, "resid")


def emit_l1_attn(C, nc):
    WQ = nc.dram_tensor("wq1", [D, 3 * D], F32, kind="ExternalInput").ap()
    BQ = nc.dram_tensor("bq1", [128, 24], F32, kind="ExternalInput").ap()
    WO = nc.dram_tensor("wo1", [D, D], F32, kind="ExternalInput").ap()
    BO = nc.dram_tensor("bo1", [128, 8], F32, kind="ExternalInput").ap()
    MUT = nc.dram_tensor("mut", [128, 128], F32, kind="ExternalInput").ap()
    HON = nc.dram_tensor("hones", [128, 128], F32, kind="ExternalInput").ap()
    RM = C.dram_in("rmask", [128, 8])
    bq = C.ab("l1a_bq", [128, 24], F32)
    bo = C.ab("l1a_bo", [128, 8], F32)
    mut = C.ab("l1a_mut", [128, 128], BF16)
    hon = C.ab("l1a_hon", [128, 128], BF16)
    rm = C.ab("l1a_rm", [128, 8], F32)
    C.dma("Y", bq[:], BQ, r=[], w=["l1a_bq"])
    C.dma("Y", bo[:], BO, r=[], w=["l1a_bo"])
    C.dma("P", mut[:], MUT, r=[], w=["l1a_mut"])
    C.dma("P", hon[:], HON, r=[], w=["l1a_hon"])
    C.dma("Y", rm[:], RM, r=[], w=["l1a_rm"])

    xn = [C.ab(f"l1a_xn{c}", [128, 2 * TILE], BF16) for c in range(KC)]
    qp = [C.ab(f"l1a_qp{c}", [128, T], BF16) for c in range(KC)]
    kp = [C.ab(f"l1a_kp{c}", [128, T], BF16) for c in range(KC)]
    vt = [C.ab(f"l1a_vt{tc}", [128, D], BF16) for tc in range(16)]
    ktst = C.ab("l1a_ktst", [128, D], BF16)[:]
    wq = [C.ab(f"l1a_w{i}", [128, 8 * 128], BF16) for i in range(2)]
    st1 = C.lnst[0]["d1"]
    st2 = C.lnst[0]["d2"]
    vtmp = C.ab("l1a_vtmp", [128, TILE], BF16)
    zg = C.ab("l1a_zg", [128, 8], F32)
    zgg = C.ab("l1a_zgg", [128, 8], F32)
    gz = C.ab("l1a_gz", [128, 64], F32)
    sg = C.ab("l1a_sg", [128, D // 2], F32)
    srun = C.ab("l1a_srun", [128, D // 2], F32)
    srb = C.ab("l1a_srb", [128, D // 2], BF16)
    kcum = [C.ab(f"l1a_kcum{i}", [128, T], BF16) for i in range(2)]
    sstage_ap = kcum[0][:].bitcast(F32)[0:128, 0:D // 2]
    sm = C.ab("l1a_sm", [128, 256], BF16)
    nst0 = C.lnst[0]["d1"][0]
    nst = [nst0, nst0]
    o_sb = xn
    aeff = C.ab("l1a_ae", [128, 8], F32)

    ccz_in = nc.dram_tensor("l1a_ccz_in", [128, 8], F32)
    ccz_out = nc.dram_tensor("l1a_ccz_out", [1024, 8], F32, addr_space="Shared")
    ccs_in = nc.dram_tensor("l1a_ccs_in", [128, D // 2], F32)
    ccs_out = nc.dram_tensor("l1a_ccs_out", [1024, D // 2], F32, addr_space="Shared")

    for tg in range(2):
      for tl in range(2):
        t = 2 * tg + tl
        dsts = [(xn[c][:, tl * TILE:(tl + 1) * TILE], f"l1a_xn{c}t{t}") for c in range(KC)]
        ln_full(C, C.xres, "xres", t * TILE, TILE, "dst", dsts=dsts)
      for eg in range(24):
        wbuf = wq[eg % 2]
        wkey = f"l1a_w{eg % 2}"
        C.dma("P", wbuf[:].rearrange("p (k e) -> p k e", k=KC),
              WQ[:, eg * 128:(eg + 1) * 128].rearrange("(k p) e -> p k e", p=128),
              r=[], w=[wkey])
        bts = [C.rb(), C.rb()]
        for k in range(KC):
            lhs = wbuf[:, k * 128:(k + 1) * 128]
            for tl in range(2):
                t = 2 * tg + tl
                C.mm(bts[tl], lhs, xn[k][:, tl * TILE:(tl + 1) * TILE],
                     start=(k == 0), stop=(k == KC - 1), r=[wkey, f"l1a_xn{k}t{t}"])
        bcol = bq[:, eg:eg + 1]
        for tl in range(2):
            t = 2 * tg + tl
            bi = bts[tl]
            if eg < 16:
                c = eg % 8
                dst = (qp if eg < 8 else kp)[c]
                dkey = f"l1a_{'qp' if eg < 8 else 'kp'}{c}"
                s1, s2 = C.lnd2f[t % 2], st2[t % 2]
                k1, k2 = f"ln.d2f{t % 2}", f"ln.d2{t % 2}"
                C.ts(s1[:], C.bank(bi), bcol, 0.0, ALU.add, op1=ALU.min,
                     r=[f"PB{bi}", "l1a_bq"], w=[k1])
                C.act(s1[:], s1[:], AF.Exp, r=[k1], w=[k1])
                C.ts(s2[:], C.bank(bi), bcol, 0.0, ALU.add, op1=ALU.max,
                     r=[f"PB{bi}", "l1a_bq"], w=[k2])
                C.tt(dst[:, t * TILE:(t + 1) * TILE], s1[:], s2[:], ALU.add,
                     r=[k1, k2], w=[dkey])
            else:
                ec = eg - 16
                C.ts(vtmp[:], C.bank(bi), bcol, None, ALU.add,
                     r=[f"PB{bi}", "l1a_bq"], w=["l1a_vtmp"])
                bj = C.rb()
                for stl in range(4):
                    C.transpose(C.bank_bf(bj, 0, 128, stl * 128, (stl + 1) * 128),
                                vtmp[:, stl * 128:(stl + 1) * 128], C.identb[:],
                                r=["l1a_vtmp", "identb"], w=[f"PB{bj}"])
                for stl in range(4):
                    tc = t * 4 + stl
                    C.copy(vt[tc][:, ec * 128:(ec + 1) * 128],
                           C.bank_bf(bj, 0, 128, stl * 128, (stl + 1) * 128),
                           r=[f"PB{bj}"], w=[f"l1a_vt{tc}"])

    for c in range(KC):
        C.red(zg[:, c:c + 1], kp[c][:], ALU.add, r=[f"l1a_kp{c}"], w=["l1a_zg"])
    C.dma("P", ccz_in[:, :], zg[:], r=["l1a_zg"], w=["ccz_in"])
    C.cc("AllGather", [ccz_in[:, :]], [ccz_out[:, :]], r=["ccz_in"], w=["ccz_out"])
    C.dma("Y", gz[:].rearrange("p (r f) -> p r f", r=8),
          ccz_out[:, :].rearrange("(r p) f -> p r f", p=128), r=["ccz_out"], w=["l1a_gz"])
    C.memset(zgg[:], 0.0, w=["l1a_zgg"])
    for j in range(NCORE):
        C.ts(aeff[:], gz[:, j * 8:(j + 1) * 8], rm[:, j:j + 1], None, ALU.mult,
             r=["l1a_gz", "l1a_rm"], w=["l1a_ae"])
        C.tt(zgg[:], zgg[:], aeff[:], ALU.add, r=["l1a_zgg", "l1a_ae"], w=["l1a_zgg"])
    for tc in range(16):
        for ec in range(KC):
            bj = C.rb(0, 4)
            bap = C.bank_bf(bj, 0, 128, 0, 128)
            C.transpose(bap, kp[ec][:, tc * 128:(tc + 1) * 128], C.identb[:],
                        r=[f"l1a_kp{ec}", "identb"], w=[f"PB{bj}"])
            C.copy(ktst[0:128, ec * 128:(ec + 1) * 128], bap, r=[f"PB{bj}"], w=["l1a_ktst"])
        for h in range(16):
            ch, hp = h // 2, (h % 2) * 64
            bi = 4 + (h % 2)
            C.mm(bi, ktst[0:128, h * 64:(h + 1) * 64], vt[tc][:, h * 64:(h + 1) * 64],
                 start=True, stop=True, r=["l1a_ktst", f"l1a_vt{tc}"],
                 bank_ap=C.bank(bi, hp, hp + 64, 0, 64))
            if tc == 0:
                C.copy(sg[hp:hp + 64, ch * 64:(ch + 1) * 64], C.bank(bi, hp, hp + 64, 0, 64),
                       r=[f"PB{bi}"], w=["l1a_sg"], q="S")
            else:
                C.tt(sg[hp:hp + 64, ch * 64:(ch + 1) * 64], sg[hp:hp + 64, ch * 64:(ch + 1) * 64],
                     C.bank(bi, hp, hp + 64, 0, 64), ALU.add, r=["l1a_sg", f"PB{bi}"],
                     w=["l1a_sg"])

    C.dma("P", ccs_in[:, :], sg[:], r=["l1a_sg"], w=["ccs_in"])
    C.cc("AllGather", [ccs_in[:, :]], [ccs_out[:, :]], r=["ccs_in"], w=["ccs_out"])
    C.memset(srun[:], 0.0, w=["l1a_srun"] + [f"l1a_srun{h}" for h in range(16)])
    for c in range(KC):
        kc_b = kcum[c % 2]
        kck = f"l1a_kcum{c % 2}"
        C.scan(kc_b[:], kp[c][:], kp[c][:], zgg[:, c:c + 1], ALU.add, ALU.max,
               r=[f"l1a_kp{c}", "l1a_zgg"], w=[kck])
        C.tt(kc_b[:], qp[c][:], kc_b[:], ALU.mult, r=[f"l1a_qp{c}", kck],
             w=[kck])
        for t in range(NT):
            bi = C.rb(0, 4)
            C.mm(bi, hon[:], kc_b[:, t * TILE:(t + 1) * TILE],
                 start=True, stop=True, r=[kck, "l1a_hon"])
            C.act(nst0[:], C.bank(bi), AF.Ln, bias=C.c_eps_attn,
                  r=[f"PB{bi}", "consts"], w=["ln.d10"])
            C.act(nst0[:], nst0[:], AF.Exp, scale=-1.0,
                  r=["ln.d10"], w=["ln.d10"])
            C.tt(qp[c][:, t * TILE:(t + 1) * TILE], qp[c][:, t * TILE:(t + 1) * TILE],
                 nst0[:], ALU.mult, r=[f"l1a_qp{c}", "ln.d10"], w=[f"l1a_qp{c}"])
    for j in range(NCORE):
        C.dma("Y", sstage_ap, ccs_out[j * 128:(j + 1) * 128, :],
              r=["ccs_out"], w=["l1a_sstage", "l1a_kcum0"])
        C.stt(srun[:], sstage_ap, rm[:, j:j + 1], srun[:],
              ALU.mult, ALU.add, r=["l1a_sstage", "l1a_rm", "l1a_srun"], w=["l1a_srun"] + [f"l1a_srun{h}" for h in range(16)])
    C.copy(srb[:], srun[:], r=["l1a_srun"], w=["l1a_srb"] + [f"l1a_srb{h}" for h in range(16)])

    for tc in range(16):
        for ec in range(KC):
            bj = C.rb(0, 4)
            bap = C.bank_bf(bj, 0, 128, 0, 128)
            C.transpose(bap, kp[ec][:, tc * 128:(tc + 1) * 128], C.identb[:],
                        r=[f"l1a_kp{ec}", "identb"], w=[f"PB{bj}"])
            C.copy(ktst[0:128, ec * 128:(ec + 1) * 128], bap, r=[f"PB{bj}"], w=["l1a_ktst"])
        for ch in range(8):
            bc = 4 + (ch % 2)
            bu = C.rb(0, 4)
            for par in range(2):
                h = 2 * ch + par
                hp = par * 64
                q_ap = qp[ch][hp:hp + 64, tc * 128:(tc + 1) * 128]
                k_ap = kp[ch][hp:hp + 64, tc * 128:(tc + 1) * 128]
                smv = sm[:, par * 128:(par + 1) * 128]
                smk = f"l1a_sm{par}"
                sreg = (slice(hp, hp + 64), slice(ch * 64, (ch + 1) * 64))
                bs = C.rb(0, 4)
                C.mm(bs, k_ap, q_ap, start=True, stop=True,
                     r=[f"l1a_kp{ch}", f"l1a_qp{ch}"],
                     bank_ap=C.bank(bs, 0, 128, 0, 128))
                C.tt(smv, C.bank(bs, 0, 128, 0, 128), mut[:], ALU.mult,
                     r=[f"PB{bs}", "l1a_mut"], w=[smk])
                C.mm(bc, vt[tc][:, h * 64:(h + 1) * 64], smv, start=True, stop=False,
                     r=[f"l1a_vt{tc}", smk], bank_ap=C.bank(bc, hp, hp + 64, 0, 128))
                C.mm(bc, srb[sreg], q_ap, start=False, stop=True,
                     r=[f"l1a_srb{h}", f"l1a_qp{ch}"], bank_ap=C.bank(bc, hp, hp + 64, 0, 128))
                C.mm(bu, ktst[0:128, h * 64:(h + 1) * 64], vt[tc][:, h * 64:(h + 1) * 64],
                     start=True, stop=True, r=["l1a_ktst", f"l1a_vt{tc}"],
                     bank_ap=C.bank(bu, hp, hp + 64, 0, 64))
            sboth = (slice(0, 128), slice(ch * 64, (ch + 1) * 64))
            hk = [f"l1a_srun{2 * ch}", f"l1a_srun{2 * ch + 1}"]
            bk = [f"l1a_srb{2 * ch}", f"l1a_srb{2 * ch + 1}"]
            C.tt(srun[sboth], srun[sboth], C.bank(bu, 0, 128, 0, 64), ALU.add,
                 r=hk + [f"PB{bu}"], w=hk)
            C.copy(srb[sboth], srun[sboth], r=hk, w=bk, q="S")
            C.copy(qp[ch][:, tc * 128:(tc + 1) * 128], C.bank(bc, 0, 128, 0, 128),
                   r=[f"PB{bc}"], w=[f"l1a_qp{ch}"])

    for tg in range(2):
        for oc in range(KC):
            wbuf = wq[oc % 2]
            wkey = f"l1a_w{oc % 2}"
            C.dma("P", wbuf[:].rearrange("p (k e) -> p k e", k=KC),
                  WO[:, oc * 128:(oc + 1) * 128].rearrange("(k p) e -> p k e", p=128),
                  r=[], w=[wkey])
            bts = [C.rb(0, 4), C.rb(0, 4)]
            for k in range(KC):
                lhs = wbuf[:, k * 128:(k + 1) * 128]
                for tl in range(2):
                    t = 2 * tg + tl
                    C.mm(bts[tl], lhs, qp[k][:, t * TILE:(t + 1) * TILE],
                         start=(k == 0), stop=(k == KC - 1), r=[wkey, f"l1a_qp{k}"])
            for tl in range(2):
                t = 2 * tg + tl
                C.ts(o_sb[oc][:, tl * TILE:(tl + 1) * TILE], C.bank(bts[tl]), bo[:, oc:oc + 1],
                     None, ALU.add, r=[f"PB{bts[tl]}", "l1a_bo"], w=[f"l1a_xn{oc}o{t}"])
        for tl in range(2):
            t = 2 * tg + tl
            srcs = [(o_sb[c][:, tl * TILE:(tl + 1) * TILE], f"l1a_xn{c}o{t}") for c in range(KC)]
            pi = t % 2
            bS, bQ = C.rb(0, 4), C.rb(0, 4)
            for c in range(KC):
                sap, skey = srcs[c]
                sq = C.lnsq[pi * 2 + c % 2]
                sqk = f"ln{pi}.sq{c % 2}"
                C.act(sq[:], sap, AF.Square, r=[skey], w=[sqk])
                C.mm(bS, C.ones_bf[:], sap, start=(c == 0), stop=(c == KC - 1), r=[skey, "onesb"])
                C.mm(bQ, C.ones_bf[:], sq[:], start=(c == 0), stop=(c == KC - 1), r=[sqk, "onesb"])
            keys = ln_tilemath(C, bS, bQ, pi=pi)
            ln_apply_resid(C, srcs, keys, t * TILE, TILE)


def build_program(blocks, first, last):
    nc = bass.Bass("TRN2", target_bir_lowering=False, debug=False)
    C = Ctx(nc)
    make_fixtures(C, nc)
    C.arena_init()
    if first:
        emit_load_x_first(C, nc)
    else:
        emit_load_x_fm(C, nc)
    for b in blocks:
        C.arena_reset()
        if b == "l0f":
            emit_l0_ffn(C, nc)
        elif b == "l0a":
            emit_l0_attn(C, nc)
        elif b == "l1a":
            emit_l1_attn(C, nc)
        elif b == "l1f":
            emit_l1_ffn(C, nc)
    if last:
        emit_store_x_last(C, nc)
    else:
        emit_store_x_fm(C, nc)
    for name in globals().get("_TAPS", ()):
        h = C.reg[name]
        tp = nc.dram_tensor(f"tap_{name}", list(h[:].shape), h[:].dtype,
                            kind="ExternalOutput").ap()
        C.dma("Y", tp, h[:], r=[name], w=[], out=True)
    C.P.finalize_and_emit()
    return nc



def _fix_array():
    fix = np.zeros((128, 8 + 128 + 128), np.float32)
    fix[:, 0] = EPS_LN
    fix[:, 1] = EPS_ATTN
    fix[:, 2] = 0.0
    fix[:, 3] = 1.0
    fix[:, 4] = float(np.log(np.sqrt(1024.0)))
    fix[:, 8:136] = 1.0
    fix[:, 136:264] = np.eye(128, dtype=np.float32)
    return fix


def _common_weight_arrays(inp):
    a = {}
    a["u1"] = np.ascontiguousarray(inp["L0_ffn_U1"])
    a["v1"] = np.ascontiguousarray(inp["L0_ffn_V1"])
    a["u2"] = np.ascontiguousarray(inp["L0_ffn_U2"])
    a["v2"] = np.ascontiguousarray(inp["L0_ffn_V2"])
    a["b1f"] = np.ascontiguousarray(inp["L0_ffn_b1"].reshape(32, 128).T)
    a["b2f"] = np.ascontiguousarray(inp["L0_ffn_b2"].reshape(8, 128).T)
    a["wqa"] = np.ascontiguousarray(inp["L0_attn_w_qkva"].T)
    a["wq1"] = np.ascontiguousarray(np.asarray(inp["L1_attn_w_qkv"]).T)
    a["bq1"] = np.ascontiguousarray(np.asarray(inp["L1_attn_b_qkv"]).reshape(24, 128).T)
    a["wo1"] = np.ascontiguousarray(np.asarray(inp["L1_attn_w_out"]).T)
    a["bo1"] = np.ascontiguousarray(np.asarray(inp["L1_attn_b_out"]).reshape(8, 128).T)
    ii = np.arange(128)
    a["mut"] = (ii[:, None] <= ii[None, :]).astype(np.float32)
    a["hones"] = ((ii[:, None] < 64) == (ii[None, :] < 64)).astype(np.float32)
    a["cent"] = np.ascontiguousarray(np.asarray(inp["L1_ffn_centroids"]).T)
    a["lng"] = np.ascontiguousarray(inp["L1_ffn_ln_g"])
    a["lnb"] = np.ascontiguousarray(inp["L1_ffn_ln_b"])
    a["w1"] = np.ascontiguousarray(inp["L1_ffn_w1"])
    a["w2"] = np.ascontiguousarray(inp["L1_ffn_w2"])
    a["b1e"] = np.ascontiguousarray(
        np.asarray(inp["L1_ffn_b1"]).reshape(8, 32, 128).transpose(2, 0, 1).reshape(128, 256))
    a["b2e"] = np.ascontiguousarray(inp["L1_ffn_b2"])
    sel = np.zeros((8, 8 * 128), np.float32)
    for e in range(8):
        sel[e, e * 128:(e + 1) * 128] = 1.0
    a["sel"] = sel
    a["gam"] = np.ascontiguousarray(np.asarray(inp["L0_attn_gamma"]).reshape(8, 128).T)
    a["fix"] = _fix_array()
    return a


def _launch(blocks, first, last, per_core_maps):
    nc = build_program(blocks, first, last)
    res = run_bass_kernel_spmd(nc, per_core_maps, core_ids=list(range(NCORE)))
    return res.results


def kernel(**inputs):
    x = np.asarray(inputs["L0_attn_x"]).reshape(16384, D).astype(np.float32)
    shards = [np.ascontiguousarray(x[i * T:(i + 1) * T]) for i in range(NCORE)]
    wa = _common_weight_arrays(inputs)

    launches = globals().get("_TEST_LAUNCHES") or [(ALL_BLOCKS, True, True)]
    rmasks = []
    for r in range(NCORE):
        m = np.zeros((128, NCORE), np.float32)
        m[:, :r] = 1.0
        rmasks.append(m)
    cur = [{"xin": shards[i], "rmask": rmasks[i], "rmaskc": 1.0 - rmasks[i], **wa}
           for i in range(NCORE)]
    outs = None
    for blocks, first, last in launches:
        results = _launch(blocks, first, last, cur)
        if last:
            outs = [r["out"] for r in results]
        else:
            cur = [{**cur[i], "xfm": results[i]["xfo"]} for i in range(NCORE)]
    y = np.concatenate(outs, axis=0).reshape(1, 16384, D)
    return y.astype(np.float32)


def emit_l0_attn(C, nc):
    WQA = nc.dram_tensor("wqa", [D, 3 * D], F32, kind="ExternalInput").ap()
    GAM = nc.dram_tensor("gam", [128, 8], F32, kind="ExternalInput").ap()
    RM = C.dram_in("rmask", [128, 8])
    RMC = C.dram_in("rmaskc", [128, 8])
    gam = C.ab("gam_sb", [128, 8], F32)
    n2t = C.ab("lnst_n2", [128, TILE], F32)
    msct = C.ab("lnst_msc", [128, TILE], F32)
    C.lnst[0]["n2"] = C.lnst[1]["n2"] = n2t
    C.lnst[0]["msc"] = C.lnst[1]["msc"] = msct
    rm = C.ab("rm_sb", [128, 8], F32)
    rmc = C.ab("rmc_sb", [128, 8], F32)
    C.dma("Y", gam[:], GAM, r=[], w=["gamma"])
    C.dma("Y", rm[:], RM, r=[], w=["rmask"])
    C.dma("Y", rmc[:], RMC, r=[], w=["rmaskc"])

    h = [C.ab(f"l0a_h{c}", [128, T], BF16) for c in range(KC)]
    s1 = [C.ab(f"l0a_s1{c}", [128, T], BF16) for c in range(KC)]
    s2 = [C.ab(f"l0a_s2{i}", [128, T], BF16) for i in range(4)]
    qb = [C.ab(f"l0a_q{i}", [128, T], BF16) for i in range(2)]
    ab_ = [C.ab(f"l0a_a{i}", [128, T], F32) for i in range(2)]
    kvb = [C.ab(f"l0a_kv{i}", [128, T], BF16) for i in range(2)]
    yb = [C.ab(f"l0a_y{i}", [128, T], BF16) for i in range(2)]
    pb = [C.ab(f"l0a_p{i}", [128, T], BF16) for i in range(2)]
    wq = [C.ab(f"l0a_w{i}", [128, 8 * 128], BF16) for i in range(4)]
    exch = [C.ab(f"l0a_exch{i}", [128, 8], F32) for i in range(2)]
    gsb = [C.ab(f"l0a_gsb{i}", [128, 64], F32) for i in range(2)]
    carry = C.ab("l0a_carry", [128, 8], F32)
    aeff = C.ab("l0a_aeff", [128, 4], F32)
    yeff = C.ab("l0a_yeff", [128, 4], F32)

    cc_in = [nc.dram_tensor(f"l0a_ccin{hh}", [128, 8], F32) for hh in range(2)]
    cc_out = [nc.dram_tensor(f"l0a_ccout{hh}", [1024, 8], F32, addr_space="Shared")
              for hh in range(2)]

    for t in range(NT):
        dsts = [(h[c][:, t * TILE:(t + 1) * TILE], f"l0a_h{c}t{t}") for c in range(KC)]
        ln_full(C, C.xres, "xres", t * TILE, TILE, "dst", dsts=dsts, gamma=gam, rms=True)

    combined = [False, False]

    def fixup_half(hh, ci):
        cs = slice(hh * 4, hh * 4 + 4)
        if not combined[hh]:
            combined[hh] = True
            C.memset(carry[:, cs], 0.0, w=["l0a_carry"])
            for j in range(NCORE):
                C.ts(aeff[:], gsb[hh][:, j * 8:j * 8 + 4], rm[:, j:j + 1], rmc[:, j:j + 1],
                     ALU.mult, r=[f"l0a_gsb{hh}", "rmask", "rmaskc"], w=["l0a_aeff"], op1=ALU.add)
                C.ts(yeff[:], gsb[hh][:, j * 8 + 4:j * 8 + 8], rm[:, j:j + 1], None,
                     ALU.mult, r=[f"l0a_gsb{hh}", "rmask"], w=["l0a_yeff"])
                C.tt(carry[:, cs], carry[:, cs], aeff[:], ALU.mult, r=["l0a_carry", "l0a_aeff"], w=["l0a_carry"])
                C.tt(carry[:, cs], carry[:, cs], yeff[:], ALU.add, r=["l0a_carry", "l0a_yeff"], w=["l0a_carry"])
        c = hh * 4 + ci
        C.stt(s1[c][:], s2[ci][:], carry[:, c:c + 1], s1[c][:], ALU.mult, ALU.add,
              r=[f"l0a_s2h{hh}{ci}", "l0a_carry", f"l0a_s1{c}"], w=[f"l0a_s1{c}"])

    for half in range(2):
        for ci in range(4):
            c = half * 4 + ci
            si = c % 2
            kq, ka, kkv, ky, kp_ = (f"l0a_qb{si}", f"l0a_ab{si}", f"l0a_kvb{si}",
                                    f"l0a_yb{si}", f"l0a_pb{si}")
            for gi, (eg, dstbuf, dkey, evac) in enumerate([
                    (c, qb[si], kq, "copy"),
                    (16 + c, ab_[si], ka, "sigmoid"),
                    (8 + c, kvb[si], kkv, "copy")]):
                wbuf = wq[(3 * c + gi) % 4]
                wkey = f"l0a_w{(3 * c + gi) % 4}"
                C.dma("P", wbuf[:].rearrange("p (k e) -> p k e", k=KC),
                      WQA[:, eg * 128:(eg + 1) * 128].rearrange("(k p) e -> p k e", p=128),
                      r=[], w=[wkey])
                bts = [C.rb(), C.rb(), C.rb(), C.rb()]
                for k in range(KC):
                    lhs = wbuf[:, k * 128:(k + 1) * 128]
                    for t in range(NT):
                        C.mm(bts[t], lhs, h[k][:, t * TILE:(t + 1) * TILE],
                             start=(k == 0), stop=(k == KC - 1), r=[wkey, f"l0a_h{k}t{t}"])
                for t in range(NT):
                    dap = dstbuf[:, t * TILE:(t + 1) * TILE]
                    if evac == "sigmoid":
                        C.act(dap, C.bank(bts[t]), AF.Sigmoid, r=[f"PB{bts[t]}"], w=[ka])
                    else:
                        C.act(dap, C.bank(bts[t]), AF.Copy, r=[f"PB{bts[t]}"], w=[dkey])
            C.scan(yb[si][:], ab_[si][:], kvb[si][:], C.c_zero, ALU.mult, ALU.add,
                   r=[ka, kkv, "consts"], w=[ky])
            C.scan(pb[si][:], ab_[si][:], ab_[si][:], C.c_one, ALU.mult, ALU.min,
                   r=[ka, "consts"], w=[kp_])
            C.tt(s1[c][:], qb[si][:], yb[si][:], ALU.mult, r=[kq, ky], w=[f"l0a_s1{c}"])
            if half == 1:
                fixup_half(0, ci)
            C.tt(s2[ci][:], qb[si][:], pb[si][:], ALU.mult, r=[kq, kp_], w=[f"l0a_s2h{half}{ci}"])
            C.copy(exch[half][:, ci:ci + 1], pb[si][:, T - 1:T], r=[kp_], w=[f"l0a_exch{half}"])
            C.copy(exch[half][:, 4 + ci:5 + ci], yb[si][:, T - 1:T], r=[ky], w=[f"l0a_exch{half}"])
        C.dma("P", cc_in[half][:, :], exch[half][:], r=[f"l0a_exch{half}"], w=[f"l0a_ccin{half}"])
        C.cc("AllGather", [cc_in[half][:, :]], [cc_out[half][:, :]],
             r=[f"l0a_ccin{half}"], w=[f"l0a_ccout{half}"])
        C.dma("Y", gsb[half][:].rearrange("p (r f) -> p r f", r=8),
              cc_out[half][:, :].rearrange("(r p) f -> p r f", p=128),
              r=[f"l0a_ccout{half}"], w=[f"l0a_gsb{half}"])
    for ci in range(4):
        fixup_half(1, ci)
    for t in range(NT):
        ln_full(C, s1, "l0a_s1", t * TILE, TILE, "resid")


def emit_l0_ffn(C, nc):
    U1 = nc.dram_tensor("u1", [D, 256], F32, kind="ExternalInput").ap()
    V1 = nc.dram_tensor("v1", [256, 4096], F32, kind="ExternalInput").ap()
    U2 = nc.dram_tensor("u2", [4096, 256], F32, kind="ExternalInput").ap()
    V2 = nc.dram_tensor("v2", [256, D], F32, kind="ExternalInput").ap()
    B1 = nc.dram_tensor("b1f", [128, 32], F32, kind="ExternalInput").ap()
    B2 = nc.dram_tensor("b2f", [128, 8], F32, kind="ExternalInput").ap()
    u1 = C.ab("u1sb", [128, 8 * 256], BF16)
    v1 = C.ab("v1sb", [128, 2 * 4096], BF16)
    u2 = C.ab("u2sb", [128, 32 * 256], BF16)
    v2 = C.ab("v2sb", [128, 2 * D], BF16)
    b1 = C.ab("b1sb", [128, 32], F32)
    b2 = C.ab("b2sb", [128, 8], F32)
    for k in range(8):
        C.dma("P", u1[:, k * 256:(k + 1) * 256], U1[k * 128:(k + 1) * 128, :], r=[], w=["u1"])
    for k in range(2):
        C.dma("P", v1[:, k * 4096:(k + 1) * 4096], V1[k * 128:(k + 1) * 128, :], r=[], w=["v1"])
    for k in range(32):
        C.dma("P", u2[:, k * 256:(k + 1) * 256], U2[k * 128:(k + 1) * 128, :], r=[], w=["u2"])
    for k in range(2):
        C.dma("P", v2[:, k * D:(k + 1) * D], V2[k * 128:(k + 1) * 128, :], r=[], w=["v2"])
    C.dma("Y", b1[:], B1, r=[], w=["b1"])
    C.dma("Y", b2[:], B2, r=[], w=["b2"])

    xn = [C.ab(f"ffn_xn{c}", [128, TILE], BF16) for c in range(KC)]
    p_sb = C.ab("ffn_p", [128, 2 * TILE], BF16)
    h_sb = C.ab("ffn_h", [128, 32 * TILE], BF16)
    q_sb = C.ab("ffn_q", [128, 2 * TILE], BF16)
    o_sb = [C.ab(f"ffn_o{c}", [128, TILE], BF16) for c in range(KC)]
    osq = [C.ab(f"ffn_osq{c}", [128, TILE], BF16) for c in range(KC)]

    phases = globals().get("_L0F_PHASES", 6)
    ntt = globals().get("_L0F_NT", NT)
    for t in range(ntt):
        t0 = t * TILE
        dsts = [(xn[c][:], f"ffn_xn{c}") for c in range(KC)]
        ln_full(C, C.xres, "xres", t0, TILE, "dst", dsts=dsts)
        if phases < 2:
            continue
        for rc in range(2):
            bi = C.rb()
            for kc in range(KC):
                C.mm(bi, u1[:, kc * 256 + rc * 128: kc * 256 + (rc + 1) * 128], xn[kc][:],
                     start=(kc == 0), stop=(kc == KC - 1), r=["u1", f"ffn_xn{kc}"])
            C.copy(p_sb[:, rc * TILE:(rc + 1) * TILE], C.bank(bi), r=[f"PB{bi}"], w=[f"ffn_p{rc}"], q="S")
        if phases < 3:
            continue
        for fc in range(32):
            bi = C.rb()
            for rc in range(2):
                C.mm(bi, v1[:, rc * 4096 + fc * 128: rc * 4096 + (fc + 1) * 128],
                     p_sb[:, rc * TILE:(rc + 1) * TILE],
                     start=(rc == 0), stop=(rc == 1), r=["v1", f"ffn_p{rc}"])
            C.act(h_sb[:, fc * TILE:(fc + 1) * TILE], C.bank(bi), AF.Gelu_apprx_tanh,
                  bias=b1[:, fc:fc + 1], r=[f"PB{bi}", "b1"], w=[f"ffn_h{fc}"])
        if phases < 4:
            continue
        for rc in range(2):
            bi = C.rb()
            for fc in range(32):
                C.mm(bi, u2[:, fc * 256 + rc * 128: fc * 256 + (rc + 1) * 128],
                     h_sb[:, fc * TILE:(fc + 1) * TILE],
                     start=(fc == 0), stop=(fc == 31), r=["u2", f"ffn_h{fc}"])
            C.copy(q_sb[:, rc * TILE:(rc + 1) * TILE], C.bank(bi), r=[f"PB{bi}"], w=[f"ffn_q{rc}"], q="S")
        if phases < 5:
            continue
        for dc in range(KC):
            bi = C.rb()
            for rc in range(2):
                C.mm(bi, v2[:, rc * D + dc * 128: rc * D + (dc + 1) * 128],
                     q_sb[:, rc * TILE:(rc + 1) * TILE],
                     start=(rc == 0), stop=(rc == 1), r=["v2", f"ffn_q{rc}"])
            C.ts(o_sb[dc][:], C.bank(bi), b2[:, dc:dc + 1], None, ALU.add,
                 r=[f"PB{bi}", "b2"], w=[f"ffn_o{dc}"])
            C.act(osq[dc][:], C.bank(bi), AF.Square, bias=b2[:, dc:dc + 1],
                  r=[f"PB{bi}", "b2"], w=[f"ffn_osq{dc}"])
        if phases < 6:
            continue
        srcs = [(o_sb[c][:], f"ffn_o{c}") for c in range(KC)]
        sqs = [(osq[c][:], f"ffn_osq{c}") for c in range(KC)]
        keys = ln_stats(C, srcs, sqs)
        ln_apply_resid(C, srcs, keys, t0, TILE)



def build_program(blocks, first, last):
    nc = bass.Bass("TRN2", target_bir_lowering=False, debug=False)
    C = Ctx(nc)
    make_fixtures(C, nc)
    C.arena_init()
    if first:
        emit_load_x_first(C, nc)
    else:
        emit_load_x_fm(C, nc)
    for b in blocks:
        C.arena_reset()
        if b == "l0f":
            emit_l0_ffn(C, nc)
        elif b == "l0a":
            emit_l0_attn(C, nc)
        elif b == "l1a":
            emit_l1_attn(C, nc)
        elif b == "l1f":
            emit_l1_ffn(C, nc)
    if last:
        emit_store_x_last(C, nc)
    else:
        emit_store_x_fm(C, nc)
    for name in globals().get("_TAPS", ()):
        h = C.reg[name]
        tp = nc.dram_tensor(f"tap_{name}", list(h[:].shape), h[:].dtype,
                            kind="ExternalOutput").ap()
        C.dma("Y", tp, h[:], r=[name], w=[], out=True)
    C.P.finalize_and_emit()
    return nc



def _fix_array():
    fix = np.zeros((128, 8 + 128 + 128), np.float32)
    fix[:, 0] = EPS_LN
    fix[:, 1] = EPS_ATTN
    fix[:, 2] = 0.0
    fix[:, 3] = 1.0
    fix[:, 4] = float(np.log(np.sqrt(1024.0)))
    fix[:, 8:136] = 1.0
    fix[:, 136:264] = np.eye(128, dtype=np.float32)
    return fix


def _common_weight_arrays(inp):
    a = {}
    a["u1"] = np.ascontiguousarray(inp["L0_ffn_U1"])
    a["v1"] = np.ascontiguousarray(inp["L0_ffn_V1"])
    a["u2"] = np.ascontiguousarray(inp["L0_ffn_U2"])
    a["v2"] = np.ascontiguousarray(inp["L0_ffn_V2"])
    a["b1f"] = np.ascontiguousarray(inp["L0_ffn_b1"].reshape(32, 128).T)
    a["b2f"] = np.ascontiguousarray(inp["L0_ffn_b2"].reshape(8, 128).T)
    a["wqa"] = np.ascontiguousarray(inp["L0_attn_w_qkva"].T)
    a["wq1"] = np.ascontiguousarray(np.asarray(inp["L1_attn_w_qkv"]).T)
    a["bq1"] = np.ascontiguousarray(np.asarray(inp["L1_attn_b_qkv"]).reshape(24, 128).T)
    a["wo1"] = np.ascontiguousarray(np.asarray(inp["L1_attn_w_out"]).T)
    a["bo1"] = np.ascontiguousarray(np.asarray(inp["L1_attn_b_out"]).reshape(8, 128).T)
    ii = np.arange(128)
    a["mut"] = (ii[:, None] <= ii[None, :]).astype(np.float32)
    a["hones"] = ((ii[:, None] < 64) == (ii[None, :] < 64)).astype(np.float32)
    a["cent"] = np.ascontiguousarray(np.asarray(inp["L1_ffn_centroids"]).T)
    a["lng"] = np.ascontiguousarray(inp["L1_ffn_ln_g"])
    a["lnb"] = np.ascontiguousarray(inp["L1_ffn_ln_b"])
    a["w1"] = np.ascontiguousarray(inp["L1_ffn_w1"])
    a["w2"] = np.ascontiguousarray(inp["L1_ffn_w2"])
    a["b1e"] = np.ascontiguousarray(
        np.asarray(inp["L1_ffn_b1"]).reshape(8, 32, 128).transpose(2, 0, 1).reshape(128, 256))
    a["b2e"] = np.ascontiguousarray(inp["L1_ffn_b2"])
    sel = np.zeros((8, 8 * 128), np.float32)
    for e in range(8):
        sel[e, e * 128:(e + 1) * 128] = 1.0
    a["sel"] = sel
    a["gam"] = np.ascontiguousarray(np.asarray(inp["L0_attn_gamma"]).reshape(8, 128).T)
    a["fix"] = _fix_array()
    return a


def _launch(blocks, first, last, per_core_maps):
    nc = build_program(blocks, first, last)
    res = run_bass_kernel_spmd(nc, per_core_maps, core_ids=list(range(NCORE)))
    return res.results


def kernel(**inputs):
    x = np.asarray(inputs["L0_attn_x"]).reshape(16384, D).astype(np.float32)
    shards = [np.ascontiguousarray(x[i * T:(i + 1) * T]) for i in range(NCORE)]
    wa = _common_weight_arrays(inputs)

    launches = globals().get("_TEST_LAUNCHES") or [(ALL_BLOCKS, True, True)]
    rmasks = []
    for r in range(NCORE):
        m = np.zeros((128, NCORE), np.float32)
        m[:, :r] = 1.0
        rmasks.append(m)
    cur = [{"xin": shards[i], "rmask": rmasks[i], "rmaskc": 1.0 - rmasks[i], **wa}
           for i in range(NCORE)]
    outs = None
    for blocks, first, last in launches:
        results = _launch(blocks, first, last, cur)
        if last:
            outs = [r["out"] for r in results]
        else:
            cur = [{**cur[i], "xfm": results[i]["xfo"]} for i in range(NCORE)]
    y = np.concatenate(outs, axis=0).reshape(1, 16384, D)
    return y.astype(np.float32)


def emit_l0_attn(C, nc):
    WQA = nc.dram_tensor("wqa", [D, 3 * D], F32, kind="ExternalInput").ap()
    GAM = nc.dram_tensor("gam", [128, 8], F32, kind="ExternalInput").ap()
    RM = C.dram_in("rmask", [128, 8])
    RMC = C.dram_in("rmaskc", [128, 8])
    gam = C.ab("gam_sb", [128, 8], F32)
    n2t = C.ab("lnst_n2", [128, TILE], F32)
    msct = C.ab("lnst_msc", [128, TILE], F32)
    C.lnst[0]["n2"] = C.lnst[1]["n2"] = n2t
    C.lnst[0]["msc"] = C.lnst[1]["msc"] = msct
    rm = C.ab("rm_sb", [128, 8], F32)
    rmc = C.ab("rmc_sb", [128, 8], F32)
    C.dma("Y", gam[:], GAM, r=[], w=["gamma"])
    C.dma("Y", rm[:], RM, r=[], w=["rmask"])
    C.dma("Y", rmc[:], RMC, r=[], w=["rmaskc"])

    h = [C.ab(f"l0a_h{c}", [128, T], BF16) for c in range(KC)]
    s1 = [C.ab(f"l0a_s1{c}", [128, T], BF16) for c in range(KC)]
    s2 = [C.ab(f"l0a_s2{i}", [128, T], BF16) for i in range(4)]
    qb = C.ab("l0a_q", [128, T], F32)
    ab = C.ab("l0a_a", [128, T], F32)
    kvb = C.ab("l0a_kv", [128, T], F32)
    yb = C.ab("l0a_y", [128, T], F32)
    pb = C.ab("l0a_p", [128, T], F32)
    wq = [C.ab(f"l0a_w{i}", [128, 8 * 128], BF16) for i in range(4)]
    exch = C.ab("l0a_exch", [128, 8], F32)
    gsb = C.ab("l0a_gsb", [128, 64], F32)
    carry = C.ab("l0a_carry", [128, 4], F32)
    aeff = C.ab("l0a_aeff", [128, 4], F32)
    yeff = C.ab("l0a_yeff", [128, 4], F32)

    cc_in = [nc.dram_tensor(f"l0a_ccin{hh}", [128, 8], F32) for hh in range(2)]
    cc_out = [nc.dram_tensor(f"l0a_ccout{hh}", [1024, 8], F32, addr_space="Shared")
              for hh in range(2)]

    for t in range(NT):
        dsts = [(h[c][:, t * TILE:(t + 1) * TILE], f"l0a_h{c}t{t}") for c in range(KC)]
        ln_full(C, C.xres, "xres", t * TILE, TILE, "dst", dsts=dsts, gamma=gam, rms=True)

    combined = [False, False]

    def fixup_half(hh, ci):
        cs = slice(hh * 4, hh * 4 + 4)
        if not combined[hh]:
            combined[hh] = True
            C.memset(carry[:, cs], 0.0, w=["l0a_carry"])
            for j in range(NCORE):
                C.ts(aeff[:], gsb[hh][:, j * 8:j * 8 + 4], rm[:, j:j + 1], rmc[:, j:j + 1],
                     ALU.mult, r=[f"l0a_gsb{hh}", "rmask", "rmaskc"], w=["l0a_aeff"], op1=ALU.add)
                C.ts(yeff[:], gsb[hh][:, j * 8 + 4:j * 8 + 8], rm[:, j:j + 1], None,
                     ALU.mult, r=[f"l0a_gsb{hh}", "rmask"], w=["l0a_yeff"])
                C.tt(carry[:, cs], carry[:, cs], aeff[:], ALU.mult, r=["l0a_carry", "l0a_aeff"], w=["l0a_carry"])
                C.tt(carry[:, cs], carry[:, cs], yeff[:], ALU.add, r=["l0a_carry", "l0a_yeff"], w=["l0a_carry"])
        c = hh * 4 + ci
        C.stt(s1[c][:], s2[ci][:], carry[:, c:c + 1], s1[c][:], ALU.mult, ALU.add,
              r=[f"l0a_s2h{hh}{ci}", "l0a_carry", f"l0a_s1{c}"], w=[f"l0a_s1{c}"])

    for half in range(2):
        for ci in range(4):
            c = half * 4 + ci
            for gi, (eg, dstbuf, dkey, evac) in enumerate([
                    (c, qb, "l0a_qb", "copy"),
                    (16 + c, ab, "l0a_ab", "sigmoid"),
                    (8 + c, kvb, "l0a_kvb", "copy")]):
                wbuf = wq[(3 * c + gi) % 4]
                wkey = f"l0a_w{(3 * c + gi) % 4}"
                C.dma("P", wbuf[:].rearrange("p (k e) -> p k e", k=KC),
                      WQA[:, eg * 128:(eg + 1) * 128].rearrange("(k p) e -> p k e", p=128),
                      r=[], w=[wkey])
                for t in range(NT):
                    bi = C.rb()
                    for k in range(KC):
                        C.mm(bi, wbuf[:, k * 128:(k + 1) * 128],
                             h[k][:, t * TILE:(t + 1) * TILE],
                             start=(k == 0), stop=(k == KC - 1), r=[wkey, f"l0a_h{k}t{t}"])
                    dap = dstbuf[:, t * TILE:(t + 1) * TILE]
                    if evac == "sigmoid":
                        C.act(dap, C.bank(bi), AF.Sigmoid, r=[f"PB{bi}"], w=[dkey])
                    else:
                        C.act(dap, C.bank(bi), AF.Copy, r=[f"PB{bi}"], w=[dkey])
            C.scan(yb[:], ab[:], kvb[:], C.c_zero, ALU.mult, ALU.add,
                   r=["l0a_ab", "l0a_kvb", "consts"], w=["l0a_yb"])
            C.scan(pb[:], ab[:], ab[:], C.c_one, ALU.mult, ALU.min,
                   r=["l0a_ab", "consts"], w=["l0a_pb"])
            C.tt(s1[c][:], qb[:], yb[:], ALU.mult, r=["l0a_qb", "l0a_yb"], w=[f"l0a_s1{c}"])
            C.tt(s2[ci][:], qb[:], pb[:], ALU.mult, r=["l0a_qb", "l0a_pb"], w=[f"l0a_s2{ci}"])
            C.copy(exch[:, ci:ci + 1], pb[:, T - 1:T], r=["l0a_pb"], w=["l0a_exch"])
            C.copy(exch[:, 4 + ci:5 + ci], yb[:, T - 1:T], r=["l0a_yb"], w=["l0a_exch"])
        C.dma("P", cc_in[half][:, :], exch[:], r=["l0a_exch"], w=[f"l0a_ccin{half}"])
        C.cc("AllGather", [cc_in[half][:, :]], [cc_out[half][:, :]],
             r=[f"l0a_ccin{half}"], w=[f"l0a_ccout{half}"])
        C.dma("P", gsb[:].rearrange("p (r f) -> p r f", r=8),
              cc_out[half][:, :].rearrange("(r p) f -> p r f", p=128),
              r=[f"l0a_ccout{half}"], w=["l0a_gsb"])
        C.memset(carry[:], 0.0, w=["l0a_carry"])
        for j in range(NCORE):
            C.ts(aeff[:], gsb[:, j * 8:j * 8 + 4], rm[:, j:j + 1], rmc[:, j:j + 1],
                 ALU.mult, r=["l0a_gsb", "rmask", "rmaskc"], w=["l0a_aeff"], op1=ALU.add)
            C.ts(yeff[:], gsb[:, j * 8 + 4:j * 8 + 8], rm[:, j:j + 1], None,
                 ALU.mult, r=["l0a_gsb", "rmask"], w=["l0a_yeff"])
            C.tt(carry[:], carry[:], aeff[:], ALU.mult, r=["l0a_carry", "l0a_aeff"], w=["l0a_carry"])
            C.tt(carry[:], carry[:], yeff[:], ALU.add, r=["l0a_carry", "l0a_yeff"], w=["l0a_carry"])
        for ci in range(4):
            c = half * 4 + ci
            C.stt(s1[c][:], s2[ci][:], carry[:, ci:ci + 1], s1[c][:], ALU.mult, ALU.add,
                  r=[f"l0a_s2{ci}", "l0a_carry", f"l0a_s1{c}"], w=[f"l0a_s1{c}"])
    for t in range(NT):
        ln_full(C, s1, "l0a_s1", t * TILE, TILE, "resid")


def emit_l1_attn(C, nc):
    WQ = nc.dram_tensor("wq1", [D, 3 * D], F32, kind="ExternalInput").ap()
    BQ = nc.dram_tensor("bq1", [128, 24], F32, kind="ExternalInput").ap()
    WO = nc.dram_tensor("wo1", [D, D], F32, kind="ExternalInput").ap()
    BO = nc.dram_tensor("bo1", [128, 8], F32, kind="ExternalInput").ap()
    MUT = nc.dram_tensor("mut", [128, 128], F32, kind="ExternalInput").ap()
    HON = nc.dram_tensor("hones", [128, 128], F32, kind="ExternalInput").ap()
    RM = C.dram_in("rmask", [128, 8])
    bq = C.ab("l1a_bq", [128, 24], F32)
    bo = C.ab("l1a_bo", [128, 8], F32)
    mut = C.ab("l1a_mut", [128, 128], BF16)
    hon = C.ab("l1a_hon", [128, 128], BF16)
    rm = C.ab("l1a_rm", [128, 8], F32)
    C.dma("Y", bq[:], BQ, r=[], w=["l1a_bq"])
    C.dma("Y", bo[:], BO, r=[], w=["l1a_bo"])
    C.dma("P", mut[:], MUT, r=[], w=["l1a_mut"])
    C.dma("P", hon[:], HON, r=[], w=["l1a_hon"])
    C.dma("Y", rm[:], RM, r=[], w=["l1a_rm"])

    xn = [C.ab(f"l1a_xn{c}", [128, 2 * TILE], BF16) for c in range(KC)]
    qp = [C.ab(f"l1a_qp{c}", [128, T], BF16) for c in range(KC)]
    kp = [C.ab(f"l1a_kp{c}", [128, T], BF16) for c in range(KC)]
    vt = [C.ab(f"l1a_vt{tc}", [128, D], BF16) for tc in range(16)]
    ktst = C.ab("l1a_ktst", [128, D], BF16)[:]
    wq = [C.ab(f"l1a_w{i}", [128, 8 * 128], BF16) for i in range(2)]
    st1 = C.lnst[0]["d1"]
    st2 = C.lnst[0]["d2"]
    vtmp = C.ab("l1a_vtmp", [128, TILE], BF16)
    zg = C.ab("l1a_zg", [128, 8], F32)
    zgg = C.ab("l1a_zgg", [128, 8], F32)
    gz = C.ab("l1a_gz", [128, 64], F32)
    sg = C.ab("l1a_sg", [128, D // 2], F32)
    srun = C.ab("l1a_srun", [128, D // 2], F32)
    srb = C.ab("l1a_srb", [128, D // 2], BF16)
    kcum = [C.ab(f"l1a_kcum{i}", [128, T], BF16) for i in range(2)]
    sstage_ap = kcum[0][:].bitcast(F32)[0:128, 0:D // 2]
    sm = C.ab("l1a_sm", [128, 256], BF16)
    nst0 = C.lnst[0]["d1"][0]
    nst = [nst0, nst0]
    o_sb = xn
    aeff = C.ab("l1a_ae", [128, 8], F32)

    ccz_in = nc.dram_tensor("l1a_ccz_in", [128, 8], F32)
    ccz_out = nc.dram_tensor("l1a_ccz_out", [1024, 8], F32, addr_space="Shared")
    ccs_in = nc.dram_tensor("l1a_ccs_in", [128, D // 2], F32)
    ccs_out = nc.dram_tensor("l1a_ccs_out", [1024, D // 2], F32, addr_space="Shared")

    for tg in range(2):
      for tl in range(2):
        t = 2 * tg + tl
        dsts = [(xn[c][:, tl * TILE:(tl + 1) * TILE], f"l1a_xn{c}t{t}") for c in range(KC)]
        ln_full(C, C.xres, "xres", t * TILE, TILE, "dst", dsts=dsts)
      for eg in range(24):
        wbuf = wq[eg % 2]
        wkey = f"l1a_w{eg % 2}"
        C.dma("P", wbuf[:].rearrange("p (k e) -> p k e", k=KC),
              WQ[:, eg * 128:(eg + 1) * 128].rearrange("(k p) e -> p k e", p=128),
              r=[], w=[wkey])
        bts = [C.rb(), C.rb()]
        for k in range(KC):
            lhs = wbuf[:, k * 128:(k + 1) * 128]
            for tl in range(2):
                t = 2 * tg + tl
                C.mm(bts[tl], lhs, xn[k][:, tl * TILE:(tl + 1) * TILE],
                     start=(k == 0), stop=(k == KC - 1), r=[wkey, f"l1a_xn{k}t{t}"])
        bcol = bq[:, eg:eg + 1]
        for tl in range(2):
            t = 2 * tg + tl
            bi = bts[tl]
            if eg < 16:
                c = eg % 8
                dst = (qp if eg < 8 else kp)[c]
                dkey = f"l1a_{'qp' if eg < 8 else 'kp'}{c}"
                s1, s2 = C.lnd2f[t % 2], st2[t % 2]
                k1, k2 = f"ln.d2f{t % 2}", f"ln.d2{t % 2}"
                C.ts(s1[:], C.bank(bi), bcol, 0.0, ALU.add, op1=ALU.min,
                     r=[f"PB{bi}", "l1a_bq"], w=[k1])
                C.act(s1[:], s1[:], AF.Exp, r=[k1], w=[k1])
                C.ts(s2[:], C.bank(bi), bcol, 0.0, ALU.add, op1=ALU.max,
                     r=[f"PB{bi}", "l1a_bq"], w=[k2])
                C.tt(dst[:, t * TILE:(t + 1) * TILE], s1[:], s2[:], ALU.add,
                     r=[k1, k2], w=[dkey])
            else:
                ec = eg - 16
                C.ts(vtmp[:], C.bank(bi), bcol, None, ALU.add,
                     r=[f"PB{bi}", "l1a_bq"], w=["l1a_vtmp"])
                bj = C.rb()
                for stl in range(4):
                    C.transpose(C.bank_bf(bj, 0, 128, stl * 128, (stl + 1) * 128),
                                vtmp[:, stl * 128:(stl + 1) * 128], C.identb[:],
                                r=["l1a_vtmp", "identb"], w=[f"PB{bj}"])
                for stl in range(4):
                    tc = t * 4 + stl
                    C.copy(vt[tc][:, ec * 128:(ec + 1) * 128],
                           C.bank_bf(bj, 0, 128, stl * 128, (stl + 1) * 128),
                           r=[f"PB{bj}"], w=[f"l1a_vt{tc}"])

    for c in range(KC):
        C.red(zg[:, c:c + 1], kp[c][:], ALU.add, r=[f"l1a_kp{c}"], w=["l1a_zg"])
    C.dma("P", ccz_in[:, :], zg[:], r=["l1a_zg"], w=["ccz_in"])
    C.cc("AllGather", [ccz_in[:, :]], [ccz_out[:, :]], r=["ccz_in"], w=["ccz_out"])
    C.dma("Y", gz[:].rearrange("p (r f) -> p r f", r=8),
          ccz_out[:, :].rearrange("(r p) f -> p r f", p=128), r=["ccz_out"], w=["l1a_gz"])
    C.memset(zgg[:], 0.0, w=["l1a_zgg"])
    for j in range(NCORE):
        C.ts(aeff[:], gz[:, j * 8:(j + 1) * 8], rm[:, j:j + 1], None, ALU.mult,
             r=["l1a_gz", "l1a_rm"], w=["l1a_ae"])
        C.tt(zgg[:], zgg[:], aeff[:], ALU.add, r=["l1a_zgg", "l1a_ae"], w=["l1a_zgg"])
    for tc in range(16):
        for ec in range(KC):
            bj = C.rb(0, 4)
            bap = C.bank_bf(bj, 0, 128, 0, 128)
            C.transpose(bap, kp[ec][:, tc * 128:(tc + 1) * 128], C.identb[:],
                        r=[f"l1a_kp{ec}", "identb"], w=[f"PB{bj}"])
            C.copy(ktst[0:128, ec * 128:(ec + 1) * 128], bap, r=[f"PB{bj}"], w=["l1a_ktst"])
        for h in range(16):
            ch, hp = h // 2, (h % 2) * 64
            bi = 4 + (h % 2)
            C.mm(bi, ktst[0:128, h * 64:(h + 1) * 64], vt[tc][:, h * 64:(h + 1) * 64],
                 start=True, stop=True, r=["l1a_ktst", f"l1a_vt{tc}"],
                 bank_ap=C.bank(bi, hp, hp + 64, 0, 64))
            if tc == 0:
                C.copy(sg[hp:hp + 64, ch * 64:(ch + 1) * 64], C.bank(bi, hp, hp + 64, 0, 64),
                       r=[f"PB{bi}"], w=["l1a_sg"], q="S")
            else:
                C.tt(sg[hp:hp + 64, ch * 64:(ch + 1) * 64], sg[hp:hp + 64, ch * 64:(ch + 1) * 64],
                     C.bank(bi, hp, hp + 64, 0, 64), ALU.add, r=["l1a_sg", f"PB{bi}"],
                     w=["l1a_sg"])

    C.dma("P", ccs_in[:, :], sg[:], r=["l1a_sg"], w=["ccs_in"])
    C.cc("AllGather", [ccs_in[:, :]], [ccs_out[:, :]], r=["ccs_in"], w=["ccs_out"])
    C.memset(srun[:], 0.0, w=["l1a_srun"] + [f"l1a_srun{h}" for h in range(16)])
    for c in range(KC):
        kc_b = kcum[c % 2]
        kck = f"l1a_kcum{c % 2}"
        C.scan(kc_b[:], kp[c][:], kp[c][:], zgg[:, c:c + 1], ALU.add, ALU.max,
               r=[f"l1a_kp{c}", "l1a_zgg"], w=[kck])
        C.tt(kc_b[:], qp[c][:], kc_b[:], ALU.mult, r=[f"l1a_qp{c}", kck],
             w=[kck])
        for t in range(NT):
            bi = C.rb(0, 4)
            C.mm(bi, hon[:], kc_b[:, t * TILE:(t + 1) * TILE],
                 start=True, stop=True, r=[kck, "l1a_hon"])
            C.act(nst0[:], C.bank(bi), AF.Ln, bias=C.c_eps_attn,
                  r=[f"PB{bi}", "consts"], w=["ln.d10"])
            C.act(nst0[:], nst0[:], AF.Exp, scale=-1.0,
                  r=["ln.d10"], w=["ln.d10"])
            C.tt(qp[c][:, t * TILE:(t + 1) * TILE], qp[c][:, t * TILE:(t + 1) * TILE],
                 nst0[:], ALU.mult, r=[f"l1a_qp{c}", "ln.d10"], w=[f"l1a_qp{c}"])
    for j in range(NCORE):
        C.dma("Y", sstage_ap, ccs_out[j * 128:(j + 1) * 128, :],
              r=["ccs_out"], w=["l1a_sstage", "l1a_kcum0"])
        C.stt(srun[:], sstage_ap, rm[:, j:j + 1], srun[:],
              ALU.mult, ALU.add, r=["l1a_sstage", "l1a_rm", "l1a_srun"], w=["l1a_srun"] + [f"l1a_srun{h}" for h in range(16)])
    C.copy(srb[:], srun[:], r=["l1a_srun"], w=["l1a_srb"] + [f"l1a_srb{h}" for h in range(16)])

    for tc in range(16):
        for ec in range(KC):
            bj = C.rb(0, 4)
            bap = C.bank_bf(bj, 0, 128, 0, 128)
            C.transpose(bap, kp[ec][:, tc * 128:(tc + 1) * 128], C.identb[:],
                        r=[f"l1a_kp{ec}", "identb"], w=[f"PB{bj}"])
            C.copy(ktst[0:128, ec * 128:(ec + 1) * 128], bap, r=[f"PB{bj}"], w=["l1a_ktst"])
        for ch in range(8):
            bc = 4 + (ch % 2)
            bu = C.rb(0, 4)
            for par in range(2):
                h = 2 * ch + par
                hp = par * 64
                q_ap = qp[ch][hp:hp + 64, tc * 128:(tc + 1) * 128]
                k_ap = kp[ch][hp:hp + 64, tc * 128:(tc + 1) * 128]
                smv = sm[:, par * 128:(par + 1) * 128]
                smk = f"l1a_sm{par}"
                sreg = (slice(hp, hp + 64), slice(ch * 64, (ch + 1) * 64))
                bs = C.rb(0, 4)
                C.mm(bs, k_ap, q_ap, start=True, stop=True,
                     r=[f"l1a_kp{ch}", f"l1a_qp{ch}"],
                     bank_ap=C.bank(bs, 0, 128, 0, 128))
                C.tt(smv, C.bank(bs, 0, 128, 0, 128), mut[:], ALU.mult,
                     r=[f"PB{bs}", "l1a_mut"], w=[smk])
                C.mm(bc, vt[tc][:, h * 64:(h + 1) * 64], smv, start=True, stop=False,
                     r=[f"l1a_vt{tc}", smk], bank_ap=C.bank(bc, hp, hp + 64, 0, 128))
                C.mm(bc, srb[sreg], q_ap, start=False, stop=True,
                     r=[f"l1a_srb{h}", f"l1a_qp{ch}"], bank_ap=C.bank(bc, hp, hp + 64, 0, 128))
                C.mm(bu, ktst[0:128, h * 64:(h + 1) * 64], vt[tc][:, h * 64:(h + 1) * 64],
                     start=True, stop=True, r=["l1a_ktst", f"l1a_vt{tc}"],
                     bank_ap=C.bank(bu, hp, hp + 64, 0, 64))
            sboth = (slice(0, 128), slice(ch * 64, (ch + 1) * 64))
            hk = [f"l1a_srun{2 * ch}", f"l1a_srun{2 * ch + 1}"]
            bk = [f"l1a_srb{2 * ch}", f"l1a_srb{2 * ch + 1}"]
            C.tt(srun[sboth], srun[sboth], C.bank(bu, 0, 128, 0, 64), ALU.add,
                 r=hk + [f"PB{bu}"], w=hk)
            C.copy(srb[sboth], srun[sboth], r=hk, w=bk, q="S")
            C.copy(qp[ch][:, tc * 128:(tc + 1) * 128], C.bank(bc, 0, 128, 0, 128),
                   r=[f"PB{bc}"], w=[f"l1a_qp{ch}"])

    for tg in range(2):
        for oc in range(KC):
            wbuf = wq[oc % 2]
            wkey = f"l1a_w{oc % 2}"
            C.dma("P", wbuf[:].rearrange("p (k e) -> p k e", k=KC),
                  WO[:, oc * 128:(oc + 1) * 128].rearrange("(k p) e -> p k e", p=128),
                  r=[], w=[wkey])
            bts = [C.rb(0, 4), C.rb(0, 4)]
            for k in range(KC):
                lhs = wbuf[:, k * 128:(k + 1) * 128]
                for tl in range(2):
                    t = 2 * tg + tl
                    C.mm(bts[tl], lhs, qp[k][:, t * TILE:(t + 1) * TILE],
                         start=(k == 0), stop=(k == KC - 1), r=[wkey, f"l1a_qp{k}"])
            for tl in range(2):
                t = 2 * tg + tl
                C.ts(o_sb[oc][:, tl * TILE:(tl + 1) * TILE], C.bank(bts[tl]), bo[:, oc:oc + 1],
                     None, ALU.add, r=[f"PB{bts[tl]}", "l1a_bo"], w=[f"l1a_xn{oc}o{t}"])
        for tl in range(2):
            t = 2 * tg + tl
            srcs = [(o_sb[c][:, tl * TILE:(tl + 1) * TILE], f"l1a_xn{c}o{t}") for c in range(KC)]
            pi = t % 2
            bS, bQ = C.rb(0, 4), C.rb(0, 4)
            for c in range(KC):
                sap, skey = srcs[c]
                sq = C.lnsq[pi * 2 + c % 2]
                sqk = f"ln{pi}.sq{c % 2}"
                C.act(sq[:], sap, AF.Square, r=[skey], w=[sqk])
                C.mm(bS, C.ones_bf[:], sap, start=(c == 0), stop=(c == KC - 1), r=[skey, "onesb"])
                C.mm(bQ, C.ones_bf[:], sq[:], start=(c == 0), stop=(c == KC - 1), r=[sqk, "onesb"])
            keys = ln_tilemath(C, bS, bQ, pi=pi)
            ln_apply_resid(C, srcs, keys, t * TILE, TILE)


def emit_l1_ffn(C, nc):
    CEN = nc.dram_tensor("cent", [D, 8], F32, kind="ExternalInput").ap()
    LNG = nc.dram_tensor("lng", [8, D], F32, kind="ExternalInput").ap()
    LNB = nc.dram_tensor("lnb", [8, D], F32, kind="ExternalInput").ap()
    W1 = nc.dram_tensor("w1", [8, D, 4096], F32, kind="ExternalInput").ap()
    W2 = nc.dram_tensor("w2", [8, 4096, D], F32, kind="ExternalInput").ap()
    B1E = nc.dram_tensor("b1e", [128, 256], F32, kind="ExternalInput").ap()
    B2E = nc.dram_tensor("b2e", [8, D], F32, kind="ExternalInput").ap()
    SEL = nc.dram_tensor("sel", [8, 8 * 128], F32, kind="ExternalInput").ap()
    RMIT = 1024

    centb = C.ab("l1f_cent", [128, 64], BF16)
    lngb = C.ab("l1f_lng", [8, D], BF16)
    lnbb = C.ab("l1f_lnb", [8, D], BF16)
    selb = C.ab("l1f_sel", [8, 8 * 128], BF16)
    b2eb = C.ab("l1f_b2e", [8, D], BF16)
    b1e = C.ab("l1f_b1e", [128, 256], F32)
    xi = [C.ab(f"l1f_xi{c}", [128, RMIT], BF16) for c in range(KC)]
    xln = [C.ab(f"l1f_xln{c}", [128, RMIT], BF16) for c in range(KC)]
    Hm = [C.ab(f"l1f_h{f}", [128, RMIT], BF16) for f in range(8)]
    ffn = [C.ab(f"l1f_ffn{c}", [128, RMIT], BF16) for c in range(KC)]
    w1b = [C.ab(f"l1f_w1{i}", [128, 8 * 1024], BF16) for i in range(2)]
    w2b = C.ab("l1f_w2", [128, 8 * 1024], BF16)
    aff = C.ab("l1f_aff", [8, RMIT], F32)
    onehot = C.ab("l1f_oh", [8, RMIT], BF16)
    tmul = C.ab("l1f_tm", [8, RMIT], BF16)
    alphab = C.ab("l1f_al", [128, RMIT], BF16)
    maskb = C.ab("l1f_mb", [128, RMIT], BF16)
    afft = C.ab("l1f_afft", [128, 8], F32)
    mcol = C.ab("l1f_mcol", [128, 1], F32)
    oht = C.ab("l1f_oht", [128, 8], BF16)

    for k in range(KC):
        C.dma("P", centb[:, k * 8:(k + 1) * 8], CEN[k * 128:(k + 1) * 128, :], r=[], w=["l1f_cent"])
    C.dma("P", lngb[:], LNG, r=[], w=["l1f_lng"])
    C.dma("P", lnbb[:], LNB, r=[], w=["l1f_lnb"])
    C.dma("P", selb[:], SEL, r=[], w=["l1f_sel"])
    C.dma("P", b2eb[:], B2E, r=[], w=["l1f_b2e"])
    C.dma("Y", b1e[:], B1E, r=[], w=["l1f_b1e"])

    st_d = C.lnst[0]["d1"] + C.lnst[0]["d2"]
    kd = ["ln.d10", "ln.d11", "ln.d20", "ln.d21"]

    for p in range(2):
        P0 = p * RMIT
        for st in range(2):
            dsts = [(xi[c][:, st * TILE:(st + 1) * TILE], f"l1f_xi{c}s{st}") for c in range(KC)]
            ln_full(C, C.xres, "xres", P0 + st * TILE, TILE, "dst", dsts=dsts)
        for st in range(2):
            bi = C.rb(0, 6)
            bap = C.bank(bi, 0, 8, 0, TILE)
            for k in range(KC):
                C.mm(bi, centb[:, k * 8:(k + 1) * 8], xi[k][:, st * TILE:(st + 1) * TILE],
                     start=(k == 0), stop=(k == KC - 1), r=["l1f_cent", f"l1f_xi{k}s{st}"],
                     bank_ap=bap)
            C.copy(aff[0:8, st * TILE:(st + 1) * TILE], bap, r=[f"PB{bi}"], w=["l1f_aff"])
        for sl in range(8):
            bj = C.rb(0, 6)
            C.transpose(C.bank(bj, 0, 128, 0, 8), aff[0:8, sl * 128:(sl + 1) * 128],
                        C.ident[0:8, 0:8], r=["l1f_aff"], w=[f"PB{bj}"])
            C.copy(afft[:], C.bank(bj, 0, 128, 0, 8), r=[f"PB{bj}"], w=["l1f_afft"])
            C.red(mcol[:], afft[:], ALU.max, r=["l1f_afft"], w=["l1f_mcol"])
            C.ts(oht[:], afft[:], mcol[:], None, ALU.is_ge, r=["l1f_afft", "l1f_mcol"],
                 w=["l1f_oht"])
            bk = C.rb(0, 6)
            C.transpose(C.bank_bf(bk, 0, 8, 0, 128), oht[:], C.identb[:],
                        r=["l1f_oht", "identb"], w=[f"PB{bk}"])
            C.copy(onehot[0:8, sl * 128:(sl + 1) * 128], C.bank_bf(bk, 0, 8, 0, 128),
                   r=[f"PB{bk}"], w=["l1f_oh"])
        C.tt(tmul[0:8, :], aff[0:8, :], onehot[0:8, :], ALU.mult,
             r=["l1f_aff", "l1f_oh"], w=["l1f_tm"])
        for st in range(2):
            bi = C.rb(0, 6)
            C.mm(bi, C.ones_bf[0:8, :], tmul[0:8, st * TILE:(st + 1) * TILE],
                 start=True, stop=True, r=["l1f_tm", "onesb"])
            C.act(alphab[:, st * TILE:(st + 1) * TILE], C.bank(bi), AF.Sigmoid,
                  r=[f"PB{bi}"], w=["l1f_al"])
        for dc in range(KC):
            for st in range(2):
                bi = C.rb(0, 6)
                C.mm(bi, b2eb[0:8, dc * 128:(dc + 1) * 128],
                     onehot[0:8, st * TILE:(st + 1) * TILE], start=True, stop=True,
                     r=["l1f_b2e", "l1f_oh"])
                C.copy(ffn[dc][:, st * TILE:(st + 1) * TILE], C.bank(bi),
                       r=[f"PB{bi}"], w=[f"l1f_ffn{dc}"])
        for st in range(2):
            srcs = []
            bS, bQ = C.rb(0, 6), C.rb(0, 6)
            for c in range(KC):
                sap = xi[c][:, st * TILE:(st + 1) * TILE]
                sq = C.lnsq[st * 2 + c % 2]
                C.act(sq[:], sap, AF.Square, r=[f"l1f_xi{c}s{st}"], w=[f"ln{st}.sq{c % 2}"])
                C.mm(bS, C.ones_bf[:], sap, start=(c == 0), stop=(c == KC - 1),
                     r=[f"l1f_xi{c}s{st}", "onesb"])
                C.mm(bQ, C.ones_bf[:], sq[:], start=(c == 0), stop=(c == KC - 1),
                     r=[f"ln{st}.sq{c % 2}", "onesb"])
                srcs.append((sap, f"l1f_xi{c}s{st}"))
            keys = ln_tilemath(C, bS, bQ, pi=st)
            mu_t, rstd_t = C.lnst[st]["mub"], C.lnst[st]["scb"]
            for dc in range(KC):
                sap = srcs[dc][0]
                d1 = st_d[dc % 2]
                C.tt(d1[:], sap, mu_t[:], ALU.subtract, r=[f"l1f_xi{dc}s{st}", keys["mu"]],
                     w=[kd[dc % 2]])
                C.tt(d1[:], d1[:], rstd_t[:], ALU.mult, r=[kd[dc % 2], keys["sc"]],
                     w=[kd[dc % 2]])
                bg = C.rb(0, 6)
                C.mm(bg, lngb[0:8, dc * 128:(dc + 1) * 128],
                     onehot[0:8, st * TILE:(st + 1) * TILE], start=True, stop=True,
                     r=["l1f_lng", "l1f_oh"])
                d2 = st_d[2 + dc % 2]
                C.tt(d2[:], d1[:], C.bank(bg), ALU.mult, r=[kd[dc % 2], f"PB{bg}"],
                     w=[kd[2 + dc % 2]])
                bb = C.rb(0, 6)
                C.mm(bb, lnbb[0:8, dc * 128:(dc + 1) * 128],
                     onehot[0:8, st * TILE:(st + 1) * TILE], start=True, stop=True,
                     r=["l1f_lnb", "l1f_oh"])
                C.tt(xln[dc][:, st * TILE:(st + 1) * TILE], d2[:], C.bank(bb), ALU.add,
                     r=[kd[2 + dc % 2], f"PB{bb}"], w=[f"l1f_xln{dc}s{st}"])
        for e in range(8):
            for g in range(4):
                wb = w1b[g % 2]
                wk = f"l1f_w1{g % 2}"
                C.dma("P", wb[:].rearrange("p (k f) -> p k f", k=KC),
                      W1[e][:, g * 1024:(g + 1) * 1024].rearrange("(k p) f -> p k f", p=128),
                      r=[], w=[wk])
                C.dma("P", w2b[:].rearrange("p (k f) -> p k f", k=KC),
                      W2[e][g * 1024:(g + 1) * 1024, :].rearrange("(k p) f -> p k f", p=128),
                      r=[], w=["l1f_w2k"])
                if g == 0:
                    for st in range(2):
                        bm = C.rb(0, 6)
                        C.mm(bm, selb[0:8, e * 128:(e + 1) * 128],
                             onehot[0:8, st * TILE:(st + 1) * TILE], start=True, stop=True,
                             r=["l1f_sel", "l1f_oh"])
                        C.copy(maskb[:, st * TILE:(st + 1) * TILE], C.bank(bm),
                               r=[f"PB{bm}"], w=["l1f_mb"], q="S")
                for fcl in range(8):
                    b0, b1_ = C.rb(0, 6), C.rb(0, 6)
                    for k in range(KC):
                        lhs = wb[:, k * 1024 + fcl * 128: k * 1024 + (fcl + 1) * 128]
                        C.mm(b0, lhs, xln[k][:, 0:TILE],
                             start=(k == 0), stop=(k == KC - 1), r=[wk, f"l1f_xln{k}s0"])
                        C.mm(b1_, lhs, xln[k][:, TILE:2 * TILE],
                             start=(k == 0), stop=(k == KC - 1), r=[wk, f"l1f_xln{k}s1"])
                    for st, bi in ((0, b0), (1, b1_)):
                        d1 = st_d[(2 * fcl + st) % 4]
                        kdx = kd[(2 * fcl + st) % 4]
                        C.act(d1[:], C.bank(bi), AF.Relu,
                              bias=b1e[:, e * 32 + g * 8 + fcl: e * 32 + g * 8 + fcl + 1],
                              r=[f"PB{bi}", "l1f_b1e"], w=[kdx])
                        C.tt(Hm[fcl][:, st * TILE:(st + 1) * TILE], d1[:],
                             maskb[:, st * TILE:(st + 1) * TILE],
                             ALU.mult, r=[kdx, "l1f_mb"], w=[f"l1f_h{fcl}"])
                for dc in range(KC):
                    b0, b1_ = C.rb(0, 6), C.rb(0, 6)
                    for fcl in range(8):
                        lhs = w2b[:, fcl * 1024 + dc * 128: fcl * 1024 + (dc + 1) * 128]
                        C.mm(b0, lhs, Hm[fcl][:, 0:TILE],
                             start=(fcl == 0), stop=(fcl == 7), r=["l1f_w2k", f"l1f_h{fcl}"])
                        C.mm(b1_, lhs, Hm[fcl][:, TILE:2 * TILE],
                             start=(fcl == 0), stop=(fcl == 7), r=["l1f_w2k", f"l1f_h{fcl}"])
                    for st, bo in ((0, b0), (1, b1_)):
                        fap = ffn[dc][:, st * TILE:(st + 1) * TILE]
                        C.tt(fap, fap, C.bank(bo), ALU.add,
                             r=[f"l1f_ffn{dc}", f"PB{bo}"], w=[f"l1f_ffn{dc}"])
        for st in range(2):
            for dc in range(KC):
                d1 = st_d[dc % 2]
                C.tt(d1[:], ffn[dc][:, st * TILE:(st + 1) * TILE],
                     alphab[:, st * TILE:(st + 1) * TILE], ALU.mult,
                     r=[f"l1f_ffn{dc}", "l1f_al"], w=[kd[dc % 2]])
                C.tt(xln[dc][:, st * TILE:(st + 1) * TILE], d1[:],
                     xi[dc][:, st * TILE:(st + 1) * TILE], ALU.add,
                     r=[kd[dc % 2], f"l1f_xi{dc}s{st}"], w=[f"l1f_xln{dc}c{st}"])
            srcs2 = [(xln[c][:, st * TILE:(st + 1) * TILE], f"l1f_xln{c}c{st}") for c in range(KC)]
            pi2 = st % 2
            bS2, bQ2 = C.rb(0, 6), C.rb(0, 6)
            for c in range(KC):
                sap2, skey2 = srcs2[c]
                sq2 = C.lnsq[pi2 * 2 + c % 2]
                sqk2 = f"ln{pi2}.sq{c % 2}"
                C.act(sq2[:], sap2, AF.Square, r=[skey2], w=[sqk2])
                C.mm(bS2, C.ones_bf[:], sap2, start=(c == 0), stop=(c == KC - 1), r=[skey2, "onesb"])
                C.mm(bQ2, C.ones_bf[:], sq2[:], start=(c == 0), stop=(c == KC - 1), r=[sqk2, "onesb"])
            keys2 = ln_tilemath(C, bS2, bQ2, pi=pi2)
            ln_apply_resid(C, srcs2, keys2, P0 + st * TILE, TILE)


# version 27
# speedup vs baseline: 1.0767x; 1.0032x over previous
import numpy as np
import ml_dtypes
import concourse.bass as bass
import concourse.mybir as mybir
from concourse.bass_utils import run_bass_kernel_spmd

import concourse.bass_utils as _bu
if not getattr(_bu, "_nosim_patched", False):
    _orig_run_command = _bu.run_command
    def _run_command_nosim(cmd, *a, **kw):
        cmd = ["--enable-birsim=false" if c == "--enable-birsim=true" else c for c in cmd]
        cmd = ["--enable-ldw-opt=true" if c == "--enable-ldw-opt=false" else c for c in cmd]
        return _orig_run_command(cmd, *a, **kw)
    _bu.run_command = _run_command_nosim
    _bu._nosim_patched = True

F32 = mybir.dt.float32
BF16 = mybir.dt.bfloat16
AF = mybir.ActivationFunctionType
ALU = mybir.AluOpType
AX = mybir.AxisListType

NCORE = 8
T = 2048
D = 1024
KC = 8
TILE = 512
NT = T // TILE
EPS_LN = 1e-5
EPS_ATTN = 1e-6
QUEUES = ("T", "V", "S", "P", "Y")
ALL_BLOCKS = ("l0a", "l0f", "l1a", "l1f")

SMALL_KEYS = frozenset(
    ["l1f_mcol", "l0a_carry", "l0a_aeff", "l0a_yeff", "l1a_zgg", "l1a_zg",
     "l0a_exch", "l1f_afft", "l1f_oht", "l1a_ae"]
    + [f"l1a_srun{h}" for h in range(16)])


class Prog:

    def __init__(self, nc):
        self.nc = nc
        self.ops = []
        self.writer = {}
        self.readers = {}
        self.barrier = ()

    def op(self, q, fn, r=(), w=(), dma=False, cc=False, tag="", out_dma=False):
        excl = [k for k in r if k.startswith("PB")]
        if excl:
            r = [k for k in r if not k.startswith("PB")]
            w = list(w) + excl
        deps = set(self.barrier)
        for k in r:
            i = self.writer.get(k)
            if i is not None:
                deps.add(i)
        for k in w:
            i = self.writer.get(k)
            if i is not None:
                deps.add(i)
            for j in self.readers.get(k, ()):
                deps.add(j)
        idx = len(self.ops)
        for k in w:
            self.writer[k] = idx
            self.readers[k] = []
        for k in r:
            self.readers.setdefault(k, []).append(idx)
        deps.discard(idx)
        small = any(k in SMALL_KEYS for k in w)
        self.ops.append(dict(q=q, fn=fn, deps=sorted(deps), dma=dma, cc=cc,
                             tag=tag, out_dma=out_dma, small=small))
        return idx

    def set_barrier(self):
        last = {}
        for i, o in enumerate(self.ops):
            last[o["q"]] = i
        self.barrier = tuple(sorted(last.values()))

    def finalize_and_emit(self):
        import contextlib
        nc = self.nc
        n = len(self.ops)
        for i, o in enumerate(self.ops):
            o["deps"] = [d for d in o["deps"]
                         if self.ops[d]["q"] != o["q"]
                         or self.ops[d]["dma"] or self.ops[d]["cc"]
                         or self.ops[d]["small"]]
        needed = [False] * n
        for o in self.ops:
            for d in o["deps"]:
                needed[d] = True
        n_dma_pool = {"P": 8, "Y": 6, "S": 2, "V": 1, "T": 1}

        stack = contextlib.ExitStack()
        qsem = {q: stack.enter_context(nc.semaphore(f"qs_{q}")) for q in QUEUES}
        pools = {q: [stack.enter_context(nc.semaphore(f"dp_{q}{i}"))
                     for i in range(n_dma_pool[q])] for q in QUEUES}
        ncc = sum(1 for o in self.ops if o["cc"])
        ccsems = [stack.enter_context(nc.semaphore(f"cc{i}")) for i in range(ncc)]

        qcount = {q: 0 for q in QUEUES}
        pool_idx = {q: 0 for q in QUEUES}
        pool_count = {}
        cci = 0
        for i, o in enumerate(self.ops):
            q = o["q"]
            if o["cc"]:
                o["sem"], o["val"], o["inc"] = ccsems[cci], 1, 1
                cci += 1
            elif o["dma"]:
                s = pools[q][pool_idx[q] % len(pools[q])]
                pool_idx[q] += 1
                pool_count[id(s)] = pool_count.get(id(s), 0) + 16
                o["sem"], o["val"], o["inc"] = s, pool_count[id(s)], 16
            elif needed[i]:
                qcount[q] += 1
                o["sem"], o["val"], o["inc"] = qsem[q], qcount[q], 1
            else:
                o["sem"] = None
        highwater = {q: {} for q in QUEUES}
        for i, o in enumerate(self.ops):
            q = o["q"]
            waits = []
            for d in o["deps"]:
                od = self.ops[d]
                if od["sem"] is None:
                    assert od["q"] == q and d < i, (d, i, od["tag"], o["tag"])
                    continue
                sem, val = od["sem"], od["val"]
                hw = highwater[q]
                if hw.get(id(sem), -1) < val:
                    waits.append((sem, val))
                    hw[id(sem)] = val
            o["waits"] = waits
        out_waits = {}
        for o in self.ops:
            if o["out_dma"]:
                prev = out_waits.get(id(o["sem"]), (None, 0))[1]
                out_waits[id(o["sem"])] = (o["sem"], max(prev, o["val"]))

        with stack:
            with nc.Block() as block:
                def section(qname):
                    def body(eng):
                        for o in self.ops:
                            if o["q"] != qname:
                                continue
                            for sem, val in o["waits"]:
                                eng.wait_ge(sem, val)
                            inst = o["fn"](eng)
                            if o["sem"] is not None:
                                inst.then_inc(o["sem"], o["inc"])
                        if qname == "Y":
                            for sem, val in out_waits.values():
                                eng.wait_ge(sem, val)
                    return body
                block.tensor(section("T"))
                block.vector(section("V"))
                block.scalar(section("S"))
                block.gpsimd(section("P"))
                block.sync(section("Y"))


class Ctx:

    def __init__(self, nc):
        self.nc = nc
        self.P = Prog(nc)
        self.dram_cache = {}
        self._bank = [nc.alloc_psum_tensor(f"pbank{i}", [128, 512], F32) for i in range(8)]
        self._rb = 0

    def dram_in(self, name, shape, dtype=None):
        if name not in self.dram_cache:
            self.dram_cache[name] = self.nc.dram_tensor(
                name, shape, dtype or F32, kind="ExternalInput").ap()
        return self.dram_cache[name]

    def bank(self, i, p0=0, p1=128, f0=0, f1=512):
        return self._bank[i][p0:p1, f0:f1]

    def bank_bf(self, i, p0=0, p1=128, f0=0, f1=1024):
        return self._bank[i][:].bitcast(BF16)[p0:p1, f0:f1]

    def rb(self, lo=0, hi=6):
        self._rb = (self._rb + 1) % (hi - lo)
        return lo + self._rb

    def sb(self, name, shape, dtype):
        h = self.nc.alloc_sbuf_tensor(name, shape, dtype)
        if not hasattr(self, "reg"):
            self.reg = {}
        self.reg[name] = h
        return h

    def arena_init(self):
        probe = self.nc.alloc_sbuf_tensor("arena_probe", [128, 1], F32)
        self.arena_base = self.nc.lookup_mloc(probe).addr + 32
        self.arena_limit = self.arena_base + self.nc.sbuf_bytes_remaining - 512
        self.arena_off = self.arena_base

    def arena_reset(self):
        self.arena_off = self.arena_base
        self.P.set_barrier()

    def ab(self, name, shape, dtype):
        nbytes = int(np.prod(shape[1:])) * (2 if dtype == BF16 else 4)
        off = (self.arena_off + 31) & ~31
        assert off + nbytes <= self.arena_limit, (
            f"arena overflow: {name} needs {nbytes}B at {off}, limit {self.arena_limit}")
        h = self.nc.alloc_sbuf_tensor_at(name, shape, dtype, offset=off)
        self.arena_off = off + nbytes
        self.reg[name] = h
        return h

    def mm(self, bi, lhsT, rhs, start, stop, r, bank_key=None, bank_ap=None):
        ap = bank_ap if bank_ap is not None else self.bank(bi)
        bk = bank_key or f"PB{bi}"
        return self.P.op("T", lambda e, ap=ap, lhsT=lhsT, rhs=rhs, start=start, stop=stop:
                         e.matmul(ap, lhsT, rhs, start=start, stop=stop),
                         r=list(r), w=[bk], tag="mm")

    def transpose(self, bank_ap, src, ident, r, w):
        return self.P.op("T", lambda e, a=bank_ap, s=src, i=ident: e.transpose(a, s, i),
                         r=list(r) + ["ident"], w=list(w), tag="tr")

    def act(self, dst, src, func, r, w, bias=0.0, scale=1.0, q="S"):
        return self.P.op(q, lambda e, d=dst, s=src, f=func, b=bias, sc=scale:
                         e.activation(d, s, f, bias=b, scale=sc),
                         r=list(r), w=list(w), tag=f"act.{func}")

    def tt(self, dst, a, b, op, r, w, q="V"):
        return self.P.op(q, lambda e, d=dst, a=a, b=b, op=op: e.tensor_tensor(d, a, b, op),
                         r=list(r), w=list(w), tag=f"tt.{op}")

    def ts(self, dst, a, s1, s2, op0, r, w, op1=ALU.bypass, q="V"):
        return self.P.op(q, lambda e, d=dst, a=a, s1=s1, s2=s2, op0=op0, op1=op1:
                         e.tensor_scalar(d, a, s1, s2, op0, op1),
                         r=list(r), w=list(w), tag=f"ts.{op0}")

    def stt(self, dst, in0, scal, in1, op0, op1, r, w, q="V"):
        return self.P.op(q, lambda e, d=dst, i0=in0, sc=scal, i1=in1, op0=op0, op1=op1:
                         e.scalar_tensor_tensor(d, i0, sc, i1, op0, op1),
                         r=list(r), w=list(w), tag="stt")

    def scan(self, dst, d0, d1, init, op0, op1, r, w, q="V"):
        return self.P.op(q, lambda e, d=dst, d0=d0, d1=d1, init=init, op0=op0, op1=op1:
                         e.tensor_tensor_scan(d, d0, d1, init, op0, op1),
                         r=list(r), w=list(w), tag="scan")

    def red(self, dst, src, op, r, w, q="V"):
        return self.P.op(q, lambda e, d=dst, s=src, op=op: e.tensor_reduce(d, s, AX.X, op),
                         r=list(r), w=list(w), tag=f"red.{op}")

    def rcp(self, dst, src, r, w):
        nc = self.nc
        def fn(e, d=dst, s=src):
            with nc.allow_low_precision(reason="bf16 norm inverse"):
                return e.reciprocal(d, s)
        return self.P.op("V", fn, r=list(r), w=list(w), tag="rcp")

    def copy(self, dst, src, r, w, q="V"):
        if q == "S":
            return self.P.op(q, lambda e, d=dst, s=src: e.copy(d, s),
                             r=list(r), w=list(w), tag="cp")
        return self.P.op(q, lambda e, d=dst, s=src: e.tensor_copy(d, s),
                         r=list(r), w=list(w), tag="cp")

    def memset(self, dst, val, w, q="V"):
        return self.P.op(q, lambda e, d=dst, v=val: e.memset(d, v), r=[], w=list(w), tag="ms")

    def dma(self, q, dst, src, r, w, out=False):
        return self.P.op(q, lambda e, d=dst, s=src: e.dma_start(out=d, in_=s),
                         r=list(r), w=list(w), dma=True, out_dma=out, tag="dma")

    def cc(self, kind, ins, outs, r, w):
        return self.P.op("P", lambda e, k=kind, i=ins, o=outs:
                         e.collective_compute(k, ALU.bypass, replica_groups=[list(range(NCORE))],
                                              ins=i, outs=o),
                         r=list(r), w=list(w), cc=True, tag=f"cc.{kind}")



def ln_stats(C, srcs, sqs, rms=False):
    bS, bQ = C.rb(), C.rb()
    for c in range(KC):
        sap, skey = srcs[c]
        qap, qkey = sqs[c]
        C.mm(bS, C.ones_bf[:], sap, start=(c == 0), stop=(c == KC - 1), r=[skey, "onesb"])
        C.mm(bQ, C.ones_bf[:], qap, start=(c == 0), stop=(c == KC - 1), r=[qkey, "onesb"])
    return ln_tilemath(C, bS, bQ, rms=rms)


def ln_tilemath(C, bS, bQ, pi=0, rms=False):
    st = C.lnst[pi]
    P = f"ln{pi}"
    mu, var, rstd, m = st["mu"], st["var"], st["rstd"], st["m"]
    C.act(mu[:], C.bank(bS), AF.Copy, scale=1.0 / D, r=[f"PB{bS}"], w=[P + ".mu"])
    C.act(var[:], C.bank(bQ), AF.Copy, scale=1.0 / D, r=[f"PB{bQ}"], w=[P + ".var"])
    C.tt(m[:], mu[:], mu[:], ALU.mult, r=[P + ".mu"], w=[P + ".m"])
    C.tt(var[:], var[:], m[:], ALU.subtract, r=[P + ".var", P + ".m"], w=[P + ".var"])
    C.act(m[:], var[:], AF.Ln, bias=C.c_eps_ln, r=[P + ".var", "consts"], w=[P + ".m"])
    C.act(rstd[:], m[:], AF.Exp, scale=-0.5, r=[P + ".m"], w=[P + ".rstd"])
    C.act(st["mub"][:], C.bank(bS), AF.Copy, scale=1.0 / D, r=[f"PB{bS}"], w=[P + ".mub"])
    if not rms:
        C.act(st["scb"][:], rstd[:], AF.Copy, r=[P + ".rstd"], w=[P + ".scb"])
        return {"mu": P + ".mub", "sc": P + ".scb", "scap": st["scb"],
                "muap": st["mub"], "pi": pi}
    n2, msc = st["n2"], st["msc"]
    C.ts(n2[:], var[:], float(D), None, ALU.mult, r=[P + ".var"], w=["ln.n2"])
    C.tt(n2[:], n2[:], rstd[:], ALU.mult, r=["ln.n2", P + ".rstd"], w=["ln.n2"])
    C.tt(n2[:], n2[:], rstd[:], ALU.mult, r=["ln.n2", P + ".rstd"], w=["ln.n2"])
    C.act(n2[:], n2[:], AF.Ln, r=["ln.n2"], w=["ln.n2"])
    C.act(n2[:], n2[:], AF.Exp, scale=0.5, r=["ln.n2"], w=["ln.n2"])
    C.ts(n2[:], n2[:], 1e-12, None, ALU.max, r=["ln.n2"], w=["ln.n2"])
    C.act(n2[:], n2[:], AF.Ln, r=["ln.n2"], w=["ln.n2"])
    C.act(n2[:], n2[:], AF.Exp, scale=-1.0, bias=C.c_ln32, r=["ln.n2", "consts"], w=["ln.n2"])
    C.tt(st["scb"][:], rstd[:], n2[:], ALU.mult, r=[P + ".rstd", "ln.n2"], w=[P + ".scb"])
    return {"mu": P + ".mub", "sc": P + ".scb", "scap": st["scb"],
            "muap": st["mub"], "pi": pi}


def ln_apply_dst(C, srcs, keys, dsts, gamma=None):
    pi = keys["pi"]
    st = C.lnst[pi]
    P = f"ln{pi}"
    mu, scap = keys["muap"], keys["scap"]
    for c in range(KC):
        sap, skey = srcs[c]
        dap, dkey = dsts[c]
        q = "V"
        d1 = st["d1"][c % 2]
        C.tt(d1[:], sap, mu[:], ALU.subtract, r=[skey, keys["mu"]], w=[f"ln.d1{c % 2}"], q=q)
        if gamma is None:
            C.tt(dap, d1[:], scap[:], ALU.mult, r=[f"ln.d1{c % 2}", keys["sc"]], w=[dkey], q=q)
        else:
            C.stt(dap, d1[:], gamma[:, c:c + 1], scap[:], ALU.mult, ALU.mult,
                  r=[f"ln.d1{c % 2}", "gamma", keys["sc"]], w=[dkey], q="V")


def ln_apply_resid(C, srcs, keys, t0, TT):
    pi = keys["pi"]
    st = C.lnst[pi]
    P = f"ln{pi}"
    mu, scap = keys["muap"], keys["scap"]
    for c in range(KC):
        sap, skey = srcs[c]
        q = "V"
        d1 = st["d1"][c % 2]
        d2 = C.lnd2f[c % 2]
        xap = C.xres[c][:, t0:t0 + TT]
        xkey = f"xres{c}t{t0 // TILE}"
        C.tt(d1[:], sap, mu[:], ALU.subtract, r=[skey, keys["mu"]], w=[f"ln.d1{c % 2}"], q=q)
        C.tt(d2[:], d1[:], scap[:], ALU.mult, r=[f"ln.d1{c % 2}", keys["sc"]], w=[f"ln.d2f{c % 2}"], q=q)
        C.tt(xap, xap, d2[:], ALU.add, r=[f"ln.d2f{c % 2}", xkey], w=[xkey], q=q)


def ln_full(C, src_handles, src_prefix, t0, TT, mode, dsts=None, gamma=None, rms=False, resid_t0=None):
    bS, bQ = C.rb(), C.rb()
    srcs = []
    pi = (t0 // TILE) % 2
    sfx = f"t{t0 // TILE}" if src_prefix == "xres" else ""
    for c in range(KC):
        sap = src_handles[c][:, t0:t0 + TT]
        skey = f"{src_prefix}{c}{sfx}"
        sq = C.lnsq[pi * 2 + c % 2]
        sqk = f"ln{pi}.sq{c % 2}"
        C.act(sq[:], sap, AF.Square, r=[skey], w=[sqk])
        C.mm(bS, C.ones_bf[:], sap, start=(c == 0), stop=(c == KC - 1), r=[skey, "onesb"])
        C.mm(bQ, C.ones_bf[:], sq[:], start=(c == 0), stop=(c == KC - 1), r=[sqk, "onesb"])
        srcs.append((sap, skey))
    keys = ln_tilemath(C, bS, bQ, pi=pi, rms=rms)
    if mode == "dst":
        dpairs = [(dsts[c][0], dsts[c][1]) for c in range(KC)]
        ln_apply_dst(C, srcs, keys, dpairs, gamma=gamma)
    else:
        ln_apply_resid(C, srcs, keys, t0 if resid_t0 is None else resid_t0, TT)



def make_fixtures(C, nc):
    C.xres = [C.sb(f"xres{c}", [128, T], BF16) for c in range(KC)]
    C.ones_f = C.sb("ones_f", [128, 128], F32)
    C.ones_bf = C.sb("ones_bf", [128, 128], BF16)
    C.ident = C.sb("ident", [128, 128], F32)
    C.identb = C.sb("identb", [128, 128], BF16)
    C.consts = C.sb("consts", [128, 8], F32)
    C.c_eps_ln = C.consts[:, 0:1]
    C.c_eps_attn = C.consts[:, 1:2]
    C.c_zero = C.consts[:, 2:3]
    C.c_one = C.consts[:, 3:4]
    C.c_ln32 = C.consts[:, 4:5]
    C.lnst = []
    shared_d1 = [C.sb(f"lnst_d1{i}", [128, TILE], BF16) for i in range(2)]
    shared_d2 = [C.sb(f"lnst_d2{i}", [128, TILE], BF16) for i in range(2)]
    C.lnd2f = [C.sb(f"lnst_d2f{i}", [128, TILE], F32) for i in range(2)]
    for pi in range(2):
        d = {k: C.sb(f"lnst{pi}_{k}", [128, TILE], F32)
             for k in ("mu", "var", "rstd", "m")}
        d["mub"] = C.sb(f"lnst{pi}_mub", [128, TILE], BF16)
        d["scb"] = C.sb(f"lnst{pi}_scb", [128, TILE], BF16)
        d["d1"], d["d2"] = shared_d1, shared_d2
        C.lnst.append(d)
    C.lnsq = [C.sb(f"lnsq{i}", [128, TILE], BF16) for i in range(4)]
    C.xio_sb = [C.sb(f"xio_sb{i}", [128, D], F32) for i in range(2)]

    IN_FIX = nc.dram_tensor("fix", [128, 8 + 128 + 128], F32, kind="ExternalInput").ap()
    C.dmaq = "Y"
    C.P.op("Y", lambda e: e.dma_start(out=C.consts[:], in_=IN_FIX[:, 0:8]),
           r=[], w=["consts"], dma=True, tag="dma.consts")
    C.P.op("Y", lambda e: e.dma_start(out=C.ones_f[:], in_=IN_FIX[:, 8:136]),
           r=[], w=["onesf"], dma=True, tag="dma.ones")
    C.P.op("P", lambda e: e.dma_start(out=C.ones_bf[:], in_=IN_FIX[:, 8:136]),
           r=[], w=["onesb"], dma=True, tag="dma.onesb")
    C.P.op("Y", lambda e: e.dma_start(out=C.ident[:], in_=IN_FIX[:, 136:264]),
           r=[], w=["ident"], dma=True, tag="dma.ident")
    C.P.op("P", lambda e: e.dma_start(out=C.identb[:], in_=IN_FIX[:, 136:264]),
           r=[], w=["identb"], dma=True, tag="dma.identb")
    for wi in range(40):
        C.mm(7, C.ones_bf[:], C.ones_bf[:, 0:128], start=True, stop=True,
             r=["onesb"], bank_key="PBwarm", bank_ap=C.bank(7, 0, 128, 0, 128))


def emit_load_x_first(C, nc):
    XIN = nc.dram_tensor("xin", [T, D], F32, kind="ExternalInput").ap()
    xin_sb = C.xio_sb
    ntc = T // 128
    for tc in range(ntc):
        buf = xin_sb[tc % 2]
        bk = f"xiob{tc % 2}"
        C.dma("Y", buf[:], XIN[tc * 128:(tc + 1) * 128, :], r=[], w=[bk])
        for c in range(KC):
            bi = C.rb()
            C.transpose(self_ap(C, bi), buf[:, c * 128:(c + 1) * 128], C.ident[:],
                        r=[bk], w=[f"PB{bi}"])
            C.copy(C.xres[c][:, tc * 128:(tc + 1) * 128], C.bank(bi, 0, 128, 0, 128),
                   r=[f"PB{bi}"], w=[f"xres{c}t{tc // 4}"])


def self_ap(C, bi):
    return C.bank(bi, 0, 128, 0, 128)


def emit_load_x_fm(C, nc):
    XFM = nc.dram_tensor("xfm", [KC, 128, T], BF16, kind="ExternalInput").ap()
    for c in range(KC):
        C.dma("Y", C.xres[c][:], XFM[c], r=[], w=[f"xres{c}t{ti}" for ti in range(NT)])


def emit_store_x_fm(C, nc):
    XFO = nc.dram_tensor("xfo", [KC, 128, T], BF16, kind="ExternalOutput").ap()
    for c in range(KC):
        C.dma("Y", XFO[c], C.xres[c][:], r=[f"xres{c}t{ti}" for ti in range(NT)], w=[], out=True)


def emit_store_x_last(C, nc):
    OUT = nc.dram_tensor("out", [T, D], F32, kind="ExternalOutput").ap()
    out_sb = C.xio_sb
    ntc = T // 128
    for tc in range(ntc):
        buf = out_sb[tc % 2]
        bk = f"xiob{tc % 2}"
        for c in range(KC):
            bi = C.rb()
            bap = C.bank_bf(bi, 0, 128, 0, 128)
            C.transpose(bap, C.xres[c][:, tc * 128:(tc + 1) * 128], C.identb[:],
                        r=[f"xres{c}t{tc // 4}", "identb"], w=[f"PB{bi}"])
            C.copy(buf[:, c * 128:(c + 1) * 128], bap, r=[f"PB{bi}"], w=[bk])
        C.dma("Y", OUT[tc * 128:(tc + 1) * 128, :], buf[:], r=[bk], w=[], out=True)



def emit_l0_ffn(C, nc):
    U1 = nc.dram_tensor("u1", [D, 256], F32, kind="ExternalInput").ap()
    V1 = nc.dram_tensor("v1", [256, 4096], F32, kind="ExternalInput").ap()
    U2 = nc.dram_tensor("u2", [4096, 256], F32, kind="ExternalInput").ap()
    V2 = nc.dram_tensor("v2", [256, D], F32, kind="ExternalInput").ap()
    B1 = nc.dram_tensor("b1f", [128, 32], F32, kind="ExternalInput").ap()
    B2 = nc.dram_tensor("b2f", [128, 8], F32, kind="ExternalInput").ap()
    u1 = C.ab("u1sb", [128, 8 * 256], BF16)
    v1 = C.ab("v1sb", [128, 2 * 4096], BF16)
    u2 = C.ab("u2sb", [128, 32 * 256], BF16)
    v2 = C.ab("v2sb", [128, 2 * D], BF16)
    b1 = C.ab("b1sb", [128, 32], F32)
    b2 = C.ab("b2sb", [128, 8], F32)
    for k in range(8):
        C.dma("P", u1[:, k * 256:(k + 1) * 256], U1[k * 128:(k + 1) * 128, :], r=[], w=["u1"])
    for k in range(2):
        C.dma("P", v1[:, k * 4096:(k + 1) * 4096], V1[k * 128:(k + 1) * 128, :], r=[], w=["v1"])
    for k in range(32):
        C.dma("P", u2[:, k * 256:(k + 1) * 256], U2[k * 128:(k + 1) * 128, :], r=[], w=["u2"])
    for k in range(2):
        C.dma("P", v2[:, k * D:(k + 1) * D], V2[k * 128:(k + 1) * 128, :], r=[], w=["v2"])
    C.dma("Y", b1[:], B1, r=[], w=["b1"])
    C.dma("Y", b2[:], B2, r=[], w=["b2"])

    xn = [C.ab(f"ffn_xn{c}", [128, 2 * TILE], BF16) for c in range(KC)]
    p_sb = C.ab("ffn_p", [128, 2 * 2 * TILE], BF16)
    h_sb = C.ab("ffn_h", [128, 32 * 2 * TILE], BF16)
    q_sb = C.ab("ffn_q", [128, 2 * 2 * TILE], BF16)
    o_sb = [C.ab(f"ffn_o{c}", [128, 2 * TILE], BF16) for c in range(KC)]
    osq = [C.ab(f"ffn_osq{c}", [128, 2 * TILE], BF16) for c in range(KC)]

    phases = globals().get("_L0F_PHASES", 6)
    for tg in range(2):
        for tl in range(2):
            t = 2 * tg + tl
            dsts = [(xn[c][:, tl * TILE:(tl + 1) * TILE], f"ffn_xn{c}s{tl}") for c in range(KC)]
            ln_full(C, C.xres, "xres", t * TILE, TILE, "dst", dsts=dsts)
        if phases < 2:
            continue
        for rc in range(2):
            bts = [C.rb(), C.rb()]
            for kc in range(KC):
                lhs = u1[:, kc * 256 + rc * 128: kc * 256 + (rc + 1) * 128]
                for tl in range(2):
                    C.mm(bts[tl], lhs, xn[kc][:, tl * TILE:(tl + 1) * TILE],
                         start=(kc == 0), stop=(kc == KC - 1), r=["u1", f"ffn_xn{kc}s{tl}"])
            for tl in range(2):
                C.copy(p_sb[:, (2 * rc + tl) * TILE:(2 * rc + tl + 1) * TILE],
                       C.bank(bts[tl]), r=[f"PB{bts[tl]}"], w=[f"ffn_p{rc}s{tl}"], q="S")
        if phases < 3:
            continue
        for fc in range(32):
            bts = [C.rb(), C.rb()]
            for rc in range(2):
                lhs = v1[:, rc * 4096 + fc * 128: rc * 4096 + (fc + 1) * 128]
                for tl in range(2):
                    C.mm(bts[tl], lhs, p_sb[:, (2 * rc + tl) * TILE:(2 * rc + tl + 1) * TILE],
                         start=(rc == 0), stop=(rc == 1), r=["v1", f"ffn_p{rc}s{tl}"])
            for tl in range(2):
                C.act(h_sb[:, (2 * fc + tl) * TILE:(2 * fc + tl + 1) * TILE], C.bank(bts[tl]),
                      AF.Gelu_apprx_tanh, bias=b1[:, fc:fc + 1],
                      r=[f"PB{bts[tl]}", "b1"], w=[f"ffn_h{fc}s{tl}"])
        if phases < 4:
            continue
        for rc in range(2):
            bts = [C.rb(), C.rb()]
            for fc in range(32):
                lhs = u2[:, fc * 256 + rc * 128: fc * 256 + (rc + 1) * 128]
                for tl in range(2):
                    C.mm(bts[tl], lhs, h_sb[:, (2 * fc + tl) * TILE:(2 * fc + tl + 1) * TILE],
                         start=(fc == 0), stop=(fc == 31), r=["u2", f"ffn_h{fc}s{tl}"])
            for tl in range(2):
                C.copy(q_sb[:, (2 * rc + tl) * TILE:(2 * rc + tl + 1) * TILE],
                       C.bank(bts[tl]), r=[f"PB{bts[tl]}"], w=[f"ffn_q{rc}s{tl}"], q="S")
        if phases < 5:
            continue
        for dc in range(KC):
            bts = [C.rb(), C.rb()]
            for rc in range(2):
                lhs = v2[:, rc * D + dc * 128: rc * D + (dc + 1) * 128]
                for tl in range(2):
                    C.mm(bts[tl], lhs, q_sb[:, (2 * rc + tl) * TILE:(2 * rc + tl + 1) * TILE],
                         start=(rc == 0), stop=(rc == 1), r=["v2", f"ffn_q{rc}s{tl}"])
            for tl in range(2):
                C.ts(o_sb[dc][:, tl * TILE:(tl + 1) * TILE], C.bank(bts[tl]), b2[:, dc:dc + 1],
                     None, ALU.add, r=[f"PB{bts[tl]}", "b2"], w=[f"ffn_o{dc}s{tl}"])
                C.act(osq[dc][:, tl * TILE:(tl + 1) * TILE], C.bank(bts[tl]), AF.Square,
                      bias=b2[:, dc:dc + 1], r=[f"PB{bts[tl]}", "b2"], w=[f"ffn_osq{dc}s{tl}"])
        if phases < 6:
            continue
        for tl in range(2):
            t = 2 * tg + tl
            srcs = [(o_sb[c][:, tl * TILE:(tl + 1) * TILE], f"ffn_o{c}s{tl}") for c in range(KC)]
            sqs = [(osq[c][:, tl * TILE:(tl + 1) * TILE], f"ffn_osq{c}s{tl}") for c in range(KC)]
            keys = ln_stats(C, srcs, sqs)
            ln_apply_resid(C, srcs, keys, t * TILE, TILE)



def build_program(blocks, first, last):
    nc = bass.Bass("TRN2", target_bir_lowering=False, debug=False)
    C = Ctx(nc)
    make_fixtures(C, nc)
    C.arena_init()
    if first:
        emit_load_x_first(C, nc)
    else:
        emit_load_x_fm(C, nc)
    for b in blocks:
        C.arena_reset()
        if b == "l0f":
            emit_l0_ffn(C, nc)
        elif b == "l0a":
            emit_l0_attn(C, nc)
        elif b == "l1a":
            emit_l1_attn(C, nc)
        elif b == "l1f":
            emit_l1_ffn(C, nc)
    if last:
        emit_store_x_last(C, nc)
    else:
        emit_store_x_fm(C, nc)
    for name in globals().get("_TAPS", ()):
        h = C.reg[name]
        tp = nc.dram_tensor(f"tap_{name}", list(h[:].shape), h[:].dtype,
                            kind="ExternalOutput").ap()
        C.dma("Y", tp, h[:], r=[name], w=[], out=True)
    C.P.finalize_and_emit()
    return nc



def _fix_array():
    fix = np.zeros((128, 8 + 128 + 128), np.float32)
    fix[:, 0] = EPS_LN
    fix[:, 1] = EPS_ATTN
    fix[:, 2] = 0.0
    fix[:, 3] = 1.0
    fix[:, 4] = float(np.log(np.sqrt(1024.0)))
    fix[:, 8:136] = 1.0
    fix[:, 136:264] = np.eye(128, dtype=np.float32)
    return fix


def _common_weight_arrays(inp):
    a = {}
    a["u1"] = np.ascontiguousarray(inp["L0_ffn_U1"])
    a["v1"] = np.ascontiguousarray(inp["L0_ffn_V1"])
    a["u2"] = np.ascontiguousarray(inp["L0_ffn_U2"])
    a["v2"] = np.ascontiguousarray(inp["L0_ffn_V2"])
    a["b1f"] = np.ascontiguousarray(inp["L0_ffn_b1"].reshape(32, 128).T)
    a["b2f"] = np.ascontiguousarray(inp["L0_ffn_b2"].reshape(8, 128).T)
    a["wqa"] = np.ascontiguousarray(inp["L0_attn_w_qkva"].T)
    a["wq1"] = np.ascontiguousarray(np.asarray(inp["L1_attn_w_qkv"]).T)
    a["bq1"] = np.ascontiguousarray(np.asarray(inp["L1_attn_b_qkv"]).reshape(24, 128).T)
    a["wo1"] = np.ascontiguousarray(np.asarray(inp["L1_attn_w_out"]).T)
    a["bo1"] = np.ascontiguousarray(np.asarray(inp["L1_attn_b_out"]).reshape(8, 128).T)
    ii = np.arange(128)
    a["mut"] = (ii[:, None] <= ii[None, :]).astype(np.float32)
    a["hones"] = ((ii[:, None] < 64) == (ii[None, :] < 64)).astype(np.float32)
    a["cent"] = np.ascontiguousarray(np.asarray(inp["L1_ffn_centroids"]).T)
    a["lng"] = np.ascontiguousarray(inp["L1_ffn_ln_g"])
    a["lnb"] = np.ascontiguousarray(inp["L1_ffn_ln_b"])
    a["w1"] = np.ascontiguousarray(inp["L1_ffn_w1"])
    a["w2"] = np.ascontiguousarray(inp["L1_ffn_w2"])
    a["b1e"] = np.ascontiguousarray(
        np.asarray(inp["L1_ffn_b1"]).reshape(8, 32, 128).transpose(2, 0, 1).reshape(128, 256))
    a["b2e"] = np.ascontiguousarray(inp["L1_ffn_b2"])
    sel = np.zeros((8, 8 * 128), np.float32)
    for e in range(8):
        sel[e, e * 128:(e + 1) * 128] = 1.0
    a["sel"] = sel
    a["gam"] = np.ascontiguousarray(np.asarray(inp["L0_attn_gamma"]).reshape(8, 128).T)
    a["fix"] = _fix_array()
    return a


def _launch(blocks, first, last, per_core_maps):
    nc = build_program(blocks, first, last)
    res = run_bass_kernel_spmd(nc, per_core_maps, core_ids=list(range(NCORE)))
    return res.results


def kernel(**inputs):
    x = np.asarray(inputs["L0_attn_x"]).reshape(16384, D).astype(np.float32)
    shards = [np.ascontiguousarray(x[i * T:(i + 1) * T]) for i in range(NCORE)]
    wa = _common_weight_arrays(inputs)

    launches = globals().get("_TEST_LAUNCHES") or [(ALL_BLOCKS, True, True)]
    rmasks = []
    for r in range(NCORE):
        m = np.zeros((128, NCORE), np.float32)
        m[:, :r] = 1.0
        rmasks.append(m)
    cur = [{"xin": shards[i], "rmask": rmasks[i], "rmaskc": 1.0 - rmasks[i], **wa}
           for i in range(NCORE)]
    outs = None
    for blocks, first, last in launches:
        results = _launch(blocks, first, last, cur)
        if last:
            outs = [r["out"] for r in results]
        else:
            cur = [{**cur[i], "xfm": results[i]["xfo"]} for i in range(NCORE)]
    y = np.concatenate(outs, axis=0).reshape(1, 16384, D)
    return y.astype(np.float32)


def emit_l0_attn(C, nc):
    WQA = nc.dram_tensor("wqa", [D, 3 * D], F32, kind="ExternalInput").ap()
    GAM = nc.dram_tensor("gam", [128, 8], F32, kind="ExternalInput").ap()
    RM = C.dram_in("rmask", [128, 8])
    RMC = C.dram_in("rmaskc", [128, 8])
    gam = C.ab("gam_sb", [128, 8], F32)
    n2t = C.ab("lnst_n2", [128, TILE], F32)
    msct = C.ab("lnst_msc", [128, TILE], F32)
    C.lnst[0]["n2"] = C.lnst[1]["n2"] = n2t
    C.lnst[0]["msc"] = C.lnst[1]["msc"] = msct
    rm = C.ab("rm_sb", [128, 8], F32)
    rmc = C.ab("rmc_sb", [128, 8], F32)
    C.dma("Y", gam[:], GAM, r=[], w=["gamma"])
    C.dma("Y", rm[:], RM, r=[], w=["rmask"])
    C.dma("Y", rmc[:], RMC, r=[], w=["rmaskc"])

    h = [C.ab(f"l0a_h{c}", [128, T], BF16) for c in range(KC)]
    s1 = [C.ab(f"l0a_s1{c}", [128, T], BF16) for c in range(KC)]
    s2 = [C.ab(f"l0a_s2{i}", [128, T], BF16) for i in range(4)]
    qb = [C.ab(f"l0a_q{i}", [128, T], BF16) for i in range(2)]
    ab_ = [C.ab(f"l0a_a{i}", [128, T], F32) for i in range(2)]
    kvb = [C.ab(f"l0a_kv{i}", [128, T], BF16) for i in range(2)]
    yb = [C.ab(f"l0a_y{i}", [128, T], BF16) for i in range(2)]
    pb = [C.ab(f"l0a_p{i}", [128, T], BF16) for i in range(2)]
    wq = [C.ab(f"l0a_w{i}", [128, 8 * 128], BF16) for i in range(4)]
    exch = [C.ab(f"l0a_exch{i}", [128, 8], F32) for i in range(2)]
    gsb = [C.ab(f"l0a_gsb{i}", [128, 64], F32) for i in range(2)]
    carry = C.ab("l0a_carry", [128, 8], F32)
    aeff = C.ab("l0a_aeff", [128, 4], F32)
    yeff = C.ab("l0a_yeff", [128, 4], F32)

    cc_in = [nc.dram_tensor(f"l0a_ccin{hh}", [128, 8], F32) for hh in range(2)]
    cc_out = [nc.dram_tensor(f"l0a_ccout{hh}", [1024, 8], F32, addr_space="Shared")
              for hh in range(2)]

    for t in range(NT):
        dsts = [(h[c][:, t * TILE:(t + 1) * TILE], f"l0a_h{c}t{t}") for c in range(KC)]
        ln_full(C, C.xres, "xres", t * TILE, TILE, "dst", dsts=dsts, gamma=gam, rms=True)

    combined = [False, False]

    def fixup_half(hh, ci):
        cs = slice(hh * 4, hh * 4 + 4)
        if not combined[hh]:
            combined[hh] = True
            C.memset(carry[:, cs], 0.0, w=["l0a_carry"])
            for j in range(NCORE):
                C.ts(aeff[:], gsb[hh][:, j * 8:j * 8 + 4], rm[:, j:j + 1], rmc[:, j:j + 1],
                     ALU.mult, r=[f"l0a_gsb{hh}", "rmask", "rmaskc"], w=["l0a_aeff"], op1=ALU.add)
                C.ts(yeff[:], gsb[hh][:, j * 8 + 4:j * 8 + 8], rm[:, j:j + 1], None,
                     ALU.mult, r=[f"l0a_gsb{hh}", "rmask"], w=["l0a_yeff"])
                C.tt(carry[:, cs], carry[:, cs], aeff[:], ALU.mult, r=["l0a_carry", "l0a_aeff"], w=["l0a_carry"])
                C.tt(carry[:, cs], carry[:, cs], yeff[:], ALU.add, r=["l0a_carry", "l0a_yeff"], w=["l0a_carry"])
        c = hh * 4 + ci
        C.stt(s1[c][:], s2[ci][:], carry[:, c:c + 1], s1[c][:], ALU.mult, ALU.add,
              r=[f"l0a_s2h{hh}{ci}", "l0a_carry", f"l0a_s1{c}"], w=[f"l0a_s1{c}"])

    for half in range(2):
        for ci in range(4):
            c = half * 4 + ci
            si = c % 2
            kq, ka, kkv, ky, kp_ = (f"l0a_qb{si}", f"l0a_ab{si}", f"l0a_kvb{si}",
                                    f"l0a_yb{si}", f"l0a_pb{si}")
            for gi, (eg, dstbuf, dkey, evac) in enumerate([
                    (c, qb[si], kq, "copy"),
                    (16 + c, ab_[si], ka, "sigmoid"),
                    (8 + c, kvb[si], kkv, "copy")]):
                wbuf = wq[(3 * c + gi) % 4]
                wkey = f"l0a_w{(3 * c + gi) % 4}"
                C.dma("P", wbuf[:].rearrange("p (k e) -> p k e", k=KC),
                      WQA[:, eg * 128:(eg + 1) * 128].rearrange("(k p) e -> p k e", p=128),
                      r=[], w=[wkey])
                bts = [C.rb(), C.rb(), C.rb(), C.rb()]
                for k in range(KC):
                    lhs = wbuf[:, k * 128:(k + 1) * 128]
                    for t in range(NT):
                        C.mm(bts[t], lhs, h[k][:, t * TILE:(t + 1) * TILE],
                             start=(k == 0), stop=(k == KC - 1), r=[wkey, f"l0a_h{k}t{t}"])
                for t in range(NT):
                    dap = dstbuf[:, t * TILE:(t + 1) * TILE]
                    if evac == "sigmoid":
                        C.act(dap, C.bank(bts[t]), AF.Sigmoid, r=[f"PB{bts[t]}"], w=[ka])
                    else:
                        C.act(dap, C.bank(bts[t]), AF.Copy, r=[f"PB{bts[t]}"], w=[dkey])
            C.scan(yb[si][:], ab_[si][:], kvb[si][:], C.c_zero, ALU.mult, ALU.add,
                   r=[ka, kkv, "consts"], w=[ky])
            C.scan(pb[si][:], ab_[si][:], ab_[si][:], C.c_one, ALU.mult, ALU.min,
                   r=[ka, "consts"], w=[kp_])
            C.tt(s1[c][:], qb[si][:], yb[si][:], ALU.mult, r=[kq, ky], w=[f"l0a_s1{c}"])
            if half == 1:
                fixup_half(0, ci)
            C.tt(s2[ci][:], qb[si][:], pb[si][:], ALU.mult, r=[kq, kp_], w=[f"l0a_s2h{half}{ci}"])
            C.copy(exch[half][:, ci:ci + 1], pb[si][:, T - 1:T], r=[kp_], w=[f"l0a_exch{half}"])
            C.copy(exch[half][:, 4 + ci:5 + ci], yb[si][:, T - 1:T], r=[ky], w=[f"l0a_exch{half}"])
        C.dma("P", cc_in[half][:, :], exch[half][:], r=[f"l0a_exch{half}"], w=[f"l0a_ccin{half}"])
        C.cc("AllGather", [cc_in[half][:, :]], [cc_out[half][:, :]],
             r=[f"l0a_ccin{half}"], w=[f"l0a_ccout{half}"])
        C.dma("Y", gsb[half][:].rearrange("p (r f) -> p r f", r=8),
              cc_out[half][:, :].rearrange("(r p) f -> p r f", p=128),
              r=[f"l0a_ccout{half}"], w=[f"l0a_gsb{half}"])
    for ci in range(4):
        fixup_half(1, ci)
    for t in range(NT):
        ln_full(C, s1, "l0a_s1", t * TILE, TILE, "resid")


def emit_l0_ffn(C, nc):
    U1 = nc.dram_tensor("u1", [D, 256], F32, kind="ExternalInput").ap()
    V1 = nc.dram_tensor("v1", [256, 4096], F32, kind="ExternalInput").ap()
    U2 = nc.dram_tensor("u2", [4096, 256], F32, kind="ExternalInput").ap()
    V2 = nc.dram_tensor("v2", [256, D], F32, kind="ExternalInput").ap()
    B1 = nc.dram_tensor("b1f", [128, 32], F32, kind="ExternalInput").ap()
    B2 = nc.dram_tensor("b2f", [128, 8], F32, kind="ExternalInput").ap()
    u1 = C.ab("u1sb", [128, 8 * 256], BF16)
    v1 = C.ab("v1sb", [128, 2 * 4096], BF16)
    u2 = C.ab("u2sb", [128, 32 * 256], BF16)
    v2 = C.ab("v2sb", [128, 2 * D], BF16)
    b1 = C.ab("b1sb", [128, 32], F32)
    b2 = C.ab("b2sb", [128, 8], F32)
    for k in range(8):
        C.dma("P", u1[:, k * 256:(k + 1) * 256], U1[k * 128:(k + 1) * 128, :], r=[], w=["u1"])
    for k in range(2):
        C.dma("P", v1[:, k * 4096:(k + 1) * 4096], V1[k * 128:(k + 1) * 128, :], r=[], w=["v1"])
    for k in range(32):
        C.dma("P", u2[:, k * 256:(k + 1) * 256], U2[k * 128:(k + 1) * 128, :], r=[], w=["u2"])
    for k in range(2):
        C.dma("P", v2[:, k * D:(k + 1) * D], V2[k * 128:(k + 1) * 128, :], r=[], w=["v2"])
    C.dma("Y", b1[:], B1, r=[], w=["b1"])
    C.dma("Y", b2[:], B2, r=[], w=["b2"])

    xn = [C.ab(f"ffn_xn{c}", [128, TILE], BF16) for c in range(KC)]
    p_sb = C.ab("ffn_p", [128, 2 * TILE], BF16)
    h_sb = C.ab("ffn_h", [128, 32 * TILE], BF16)
    q_sb = C.ab("ffn_q", [128, 2 * TILE], BF16)
    o_sb = [C.ab(f"ffn_o{c}", [128, TILE], BF16) for c in range(KC)]
    osq = [C.ab(f"ffn_osq{c}", [128, TILE], BF16) for c in range(KC)]

    phases = globals().get("_L0F_PHASES", 6)
    ntt = globals().get("_L0F_NT", NT)
    for t in range(ntt):
        t0 = t * TILE
        dsts = [(xn[c][:], f"ffn_xn{c}") for c in range(KC)]
        ln_full(C, C.xres, "xres", t0, TILE, "dst", dsts=dsts)
        if phases < 2:
            continue
        for rc in range(2):
            bi = C.rb()
            for kc in range(KC):
                C.mm(bi, u1[:, kc * 256 + rc * 128: kc * 256 + (rc + 1) * 128], xn[kc][:],
                     start=(kc == 0), stop=(kc == KC - 1), r=["u1", f"ffn_xn{kc}"])
            C.copy(p_sb[:, rc * TILE:(rc + 1) * TILE], C.bank(bi), r=[f"PB{bi}"], w=[f"ffn_p{rc}"], q="S")
        if phases < 3:
            continue
        for fc in range(32):
            bi = C.rb()
            for rc in range(2):
                C.mm(bi, v1[:, rc * 4096 + fc * 128: rc * 4096 + (fc + 1) * 128],
                     p_sb[:, rc * TILE:(rc + 1) * TILE],
                     start=(rc == 0), stop=(rc == 1), r=["v1", f"ffn_p{rc}"])
            C.act(h_sb[:, fc * TILE:(fc + 1) * TILE], C.bank(bi), AF.Gelu_apprx_tanh,
                  bias=b1[:, fc:fc + 1], r=[f"PB{bi}", "b1"], w=[f"ffn_h{fc}"])
        if phases < 4:
            continue
        for rc in range(2):
            bi = C.rb()
            for fc in range(32):
                C.mm(bi, u2[:, fc * 256 + rc * 128: fc * 256 + (rc + 1) * 128],
                     h_sb[:, fc * TILE:(fc + 1) * TILE],
                     start=(fc == 0), stop=(fc == 31), r=["u2", f"ffn_h{fc}"])
            C.copy(q_sb[:, rc * TILE:(rc + 1) * TILE], C.bank(bi), r=[f"PB{bi}"], w=[f"ffn_q{rc}"], q="S")
        if phases < 5:
            continue
        for dc in range(KC):
            bi = C.rb()
            for rc in range(2):
                C.mm(bi, v2[:, rc * D + dc * 128: rc * D + (dc + 1) * 128],
                     q_sb[:, rc * TILE:(rc + 1) * TILE],
                     start=(rc == 0), stop=(rc == 1), r=["v2", f"ffn_q{rc}"])
            C.ts(o_sb[dc][:], C.bank(bi), b2[:, dc:dc + 1], None, ALU.add,
                 r=[f"PB{bi}", "b2"], w=[f"ffn_o{dc}"])
            C.act(osq[dc][:], C.bank(bi), AF.Square, bias=b2[:, dc:dc + 1],
                  r=[f"PB{bi}", "b2"], w=[f"ffn_osq{dc}"])
        if phases < 6:
            continue
        srcs = [(o_sb[c][:], f"ffn_o{c}") for c in range(KC)]
        sqs = [(osq[c][:], f"ffn_osq{c}") for c in range(KC)]
        keys = ln_stats(C, srcs, sqs)
        ln_apply_resid(C, srcs, keys, t0, TILE)



def build_program(blocks, first, last):
    nc = bass.Bass("TRN2", target_bir_lowering=False, debug=False)
    C = Ctx(nc)
    make_fixtures(C, nc)
    C.arena_init()
    if first:
        emit_load_x_first(C, nc)
    else:
        emit_load_x_fm(C, nc)
    for b in blocks:
        C.arena_reset()
        if b == "l0f":
            emit_l0_ffn(C, nc)
        elif b == "l0a":
            emit_l0_attn(C, nc)
        elif b == "l1a":
            emit_l1_attn(C, nc)
        elif b == "l1f":
            emit_l1_ffn(C, nc)
    if last:
        emit_store_x_last(C, nc)
    else:
        emit_store_x_fm(C, nc)
    for name in globals().get("_TAPS", ()):
        h = C.reg[name]
        tp = nc.dram_tensor(f"tap_{name}", list(h[:].shape), h[:].dtype,
                            kind="ExternalOutput").ap()
        C.dma("Y", tp, h[:], r=[name], w=[], out=True)
    C.P.finalize_and_emit()
    return nc



def _fix_array():
    fix = np.zeros((128, 8 + 128 + 128), np.float32)
    fix[:, 0] = EPS_LN
    fix[:, 1] = EPS_ATTN
    fix[:, 2] = 0.0
    fix[:, 3] = 1.0
    fix[:, 4] = float(np.log(np.sqrt(1024.0)))
    fix[:, 8:136] = 1.0
    fix[:, 136:264] = np.eye(128, dtype=np.float32)
    return fix


def _common_weight_arrays(inp):
    a = {}
    a["u1"] = np.ascontiguousarray(inp["L0_ffn_U1"])
    a["v1"] = np.ascontiguousarray(inp["L0_ffn_V1"])
    a["u2"] = np.ascontiguousarray(inp["L0_ffn_U2"])
    a["v2"] = np.ascontiguousarray(inp["L0_ffn_V2"])
    a["b1f"] = np.ascontiguousarray(inp["L0_ffn_b1"].reshape(32, 128).T)
    a["b2f"] = np.ascontiguousarray(inp["L0_ffn_b2"].reshape(8, 128).T)
    a["wqa"] = np.ascontiguousarray(inp["L0_attn_w_qkva"].T)
    a["wq1"] = np.ascontiguousarray(np.asarray(inp["L1_attn_w_qkv"]).T)
    a["bq1"] = np.ascontiguousarray(np.asarray(inp["L1_attn_b_qkv"]).reshape(24, 128).T)
    a["wo1"] = np.ascontiguousarray(np.asarray(inp["L1_attn_w_out"]).T)
    a["bo1"] = np.ascontiguousarray(np.asarray(inp["L1_attn_b_out"]).reshape(8, 128).T)
    ii = np.arange(128)
    a["mut"] = (ii[:, None] <= ii[None, :]).astype(np.float32)
    a["hones"] = ((ii[:, None] < 64) == (ii[None, :] < 64)).astype(np.float32)
    a["cent"] = np.ascontiguousarray(np.asarray(inp["L1_ffn_centroids"]).T)
    a["lng"] = np.ascontiguousarray(inp["L1_ffn_ln_g"])
    a["lnb"] = np.ascontiguousarray(inp["L1_ffn_ln_b"])
    a["w1"] = np.ascontiguousarray(inp["L1_ffn_w1"])
    a["w2"] = np.ascontiguousarray(inp["L1_ffn_w2"])
    a["b1e"] = np.ascontiguousarray(
        np.asarray(inp["L1_ffn_b1"]).reshape(8, 32, 128).transpose(2, 0, 1).reshape(128, 256))
    a["b2e"] = np.ascontiguousarray(inp["L1_ffn_b2"])
    sel = np.zeros((8, 8 * 128), np.float32)
    for e in range(8):
        sel[e, e * 128:(e + 1) * 128] = 1.0
    a["sel"] = sel
    a["gam"] = np.ascontiguousarray(np.asarray(inp["L0_attn_gamma"]).reshape(8, 128).T)
    a["fix"] = _fix_array()
    return a


def _launch(blocks, first, last, per_core_maps):
    nc = build_program(blocks, first, last)
    res = run_bass_kernel_spmd(nc, per_core_maps, core_ids=list(range(NCORE)))
    return res.results


def kernel(**inputs):
    x = np.asarray(inputs["L0_attn_x"]).reshape(16384, D).astype(np.float32)
    shards = [np.ascontiguousarray(x[i * T:(i + 1) * T]) for i in range(NCORE)]
    wa = _common_weight_arrays(inputs)

    launches = globals().get("_TEST_LAUNCHES") or [(ALL_BLOCKS, True, True)]
    rmasks = []
    for r in range(NCORE):
        m = np.zeros((128, NCORE), np.float32)
        m[:, :r] = 1.0
        rmasks.append(m)
    cur = [{"xin": shards[i], "rmask": rmasks[i], "rmaskc": 1.0 - rmasks[i], **wa}
           for i in range(NCORE)]
    outs = None
    for blocks, first, last in launches:
        results = _launch(blocks, first, last, cur)
        if last:
            outs = [r["out"] for r in results]
        else:
            cur = [{**cur[i], "xfm": results[i]["xfo"]} for i in range(NCORE)]
    y = np.concatenate(outs, axis=0).reshape(1, 16384, D)
    return y.astype(np.float32)


def emit_l0_attn(C, nc):
    WQA = nc.dram_tensor("wqa", [D, 3 * D], F32, kind="ExternalInput").ap()
    GAM = nc.dram_tensor("gam", [128, 8], F32, kind="ExternalInput").ap()
    RM = C.dram_in("rmask", [128, 8])
    RMC = C.dram_in("rmaskc", [128, 8])
    gam = C.ab("gam_sb", [128, 8], F32)
    n2t = C.ab("lnst_n2", [128, TILE], F32)
    msct = C.ab("lnst_msc", [128, TILE], F32)
    C.lnst[0]["n2"] = C.lnst[1]["n2"] = n2t
    C.lnst[0]["msc"] = C.lnst[1]["msc"] = msct
    rm = C.ab("rm_sb", [128, 8], F32)
    rmc = C.ab("rmc_sb", [128, 8], F32)
    C.dma("Y", gam[:], GAM, r=[], w=["gamma"])
    C.dma("Y", rm[:], RM, r=[], w=["rmask"])
    C.dma("Y", rmc[:], RMC, r=[], w=["rmaskc"])

    h = [C.ab(f"l0a_h{c}", [128, T], BF16) for c in range(KC)]
    s1 = [C.ab(f"l0a_s1{c}", [128, T], BF16) for c in range(KC)]
    s2 = [C.ab(f"l0a_s2{i}", [128, T], BF16) for i in range(4)]
    qb = C.ab("l0a_q", [128, T], F32)
    ab = C.ab("l0a_a", [128, T], F32)
    kvb = C.ab("l0a_kv", [128, T], F32)
    yb = C.ab("l0a_y", [128, T], F32)
    pb = C.ab("l0a_p", [128, T], F32)
    wq = [C.ab(f"l0a_w{i}", [128, 8 * 128], BF16) for i in range(4)]
    exch = C.ab("l0a_exch", [128, 8], F32)
    gsb = C.ab("l0a_gsb", [128, 64], F32)
    carry = C.ab("l0a_carry", [128, 4], F32)
    aeff = C.ab("l0a_aeff", [128, 4], F32)
    yeff = C.ab("l0a_yeff", [128, 4], F32)

    cc_in = [nc.dram_tensor(f"l0a_ccin{hh}", [128, 8], F32) for hh in range(2)]
    cc_out = [nc.dram_tensor(f"l0a_ccout{hh}", [1024, 8], F32, addr_space="Shared")
              for hh in range(2)]

    for t in range(NT):
        dsts = [(h[c][:, t * TILE:(t + 1) * TILE], f"l0a_h{c}t{t}") for c in range(KC)]
        ln_full(C, C.xres, "xres", t * TILE, TILE, "dst", dsts=dsts, gamma=gam, rms=True)

    combined = [False, False]

    def fixup_half(hh, ci):
        cs = slice(hh * 4, hh * 4 + 4)
        if not combined[hh]:
            combined[hh] = True
            C.memset(carry[:, cs], 0.0, w=["l0a_carry"])
            for j in range(NCORE):
                C.ts(aeff[:], gsb[hh][:, j * 8:j * 8 + 4], rm[:, j:j + 1], rmc[:, j:j + 1],
                     ALU.mult, r=[f"l0a_gsb{hh}", "rmask", "rmaskc"], w=["l0a_aeff"], op1=ALU.add)
                C.ts(yeff[:], gsb[hh][:, j * 8 + 4:j * 8 + 8], rm[:, j:j + 1], None,
                     ALU.mult, r=[f"l0a_gsb{hh}", "rmask"], w=["l0a_yeff"])
                C.tt(carry[:, cs], carry[:, cs], aeff[:], ALU.mult, r=["l0a_carry", "l0a_aeff"], w=["l0a_carry"])
                C.tt(carry[:, cs], carry[:, cs], yeff[:], ALU.add, r=["l0a_carry", "l0a_yeff"], w=["l0a_carry"])
        c = hh * 4 + ci
        C.stt(s1[c][:], s2[ci][:], carry[:, c:c + 1], s1[c][:], ALU.mult, ALU.add,
              r=[f"l0a_s2h{hh}{ci}", "l0a_carry", f"l0a_s1{c}"], w=[f"l0a_s1{c}"])

    for half in range(2):
        for ci in range(4):
            c = half * 4 + ci
            for gi, (eg, dstbuf, dkey, evac) in enumerate([
                    (c, qb, "l0a_qb", "copy"),
                    (16 + c, ab, "l0a_ab", "sigmoid"),
                    (8 + c, kvb, "l0a_kvb", "copy")]):
                wbuf = wq[(3 * c + gi) % 4]
                wkey = f"l0a_w{(3 * c + gi) % 4}"
                C.dma("P", wbuf[:].rearrange("p (k e) -> p k e", k=KC),
                      WQA[:, eg * 128:(eg + 1) * 128].rearrange("(k p) e -> p k e", p=128),
                      r=[], w=[wkey])
                for t in range(NT):
                    bi = C.rb()
                    for k in range(KC):
                        C.mm(bi, wbuf[:, k * 128:(k + 1) * 128],
                             h[k][:, t * TILE:(t + 1) * TILE],
                             start=(k == 0), stop=(k == KC - 1), r=[wkey, f"l0a_h{k}t{t}"])
                    dap = dstbuf[:, t * TILE:(t + 1) * TILE]
                    if evac == "sigmoid":
                        C.act(dap, C.bank(bi), AF.Sigmoid, r=[f"PB{bi}"], w=[dkey])
                    else:
                        C.act(dap, C.bank(bi), AF.Copy, r=[f"PB{bi}"], w=[dkey])
            C.scan(yb[:], ab[:], kvb[:], C.c_zero, ALU.mult, ALU.add,
                   r=["l0a_ab", "l0a_kvb", "consts"], w=["l0a_yb"])
            C.scan(pb[:], ab[:], ab[:], C.c_one, ALU.mult, ALU.min,
                   r=["l0a_ab", "consts"], w=["l0a_pb"])
            C.tt(s1[c][:], qb[:], yb[:], ALU.mult, r=["l0a_qb", "l0a_yb"], w=[f"l0a_s1{c}"])
            C.tt(s2[ci][:], qb[:], pb[:], ALU.mult, r=["l0a_qb", "l0a_pb"], w=[f"l0a_s2{ci}"])
            C.copy(exch[:, ci:ci + 1], pb[:, T - 1:T], r=["l0a_pb"], w=["l0a_exch"])
            C.copy(exch[:, 4 + ci:5 + ci], yb[:, T - 1:T], r=["l0a_yb"], w=["l0a_exch"])
        C.dma("P", cc_in[half][:, :], exch[:], r=["l0a_exch"], w=[f"l0a_ccin{half}"])
        C.cc("AllGather", [cc_in[half][:, :]], [cc_out[half][:, :]],
             r=[f"l0a_ccin{half}"], w=[f"l0a_ccout{half}"])
        C.dma("P", gsb[:].rearrange("p (r f) -> p r f", r=8),
              cc_out[half][:, :].rearrange("(r p) f -> p r f", p=128),
              r=[f"l0a_ccout{half}"], w=["l0a_gsb"])
        C.memset(carry[:], 0.0, w=["l0a_carry"])
        for j in range(NCORE):
            C.ts(aeff[:], gsb[:, j * 8:j * 8 + 4], rm[:, j:j + 1], rmc[:, j:j + 1],
                 ALU.mult, r=["l0a_gsb", "rmask", "rmaskc"], w=["l0a_aeff"], op1=ALU.add)
            C.ts(yeff[:], gsb[:, j * 8 + 4:j * 8 + 8], rm[:, j:j + 1], None,
                 ALU.mult, r=["l0a_gsb", "rmask"], w=["l0a_yeff"])
            C.tt(carry[:], carry[:], aeff[:], ALU.mult, r=["l0a_carry", "l0a_aeff"], w=["l0a_carry"])
            C.tt(carry[:], carry[:], yeff[:], ALU.add, r=["l0a_carry", "l0a_yeff"], w=["l0a_carry"])
        for ci in range(4):
            c = half * 4 + ci
            C.stt(s1[c][:], s2[ci][:], carry[:, ci:ci + 1], s1[c][:], ALU.mult, ALU.add,
                  r=[f"l0a_s2{ci}", "l0a_carry", f"l0a_s1{c}"], w=[f"l0a_s1{c}"])
    for t in range(NT):
        ln_full(C, s1, "l0a_s1", t * TILE, TILE, "resid")


def emit_l1_attn(C, nc):
    WQ = nc.dram_tensor("wq1", [D, 3 * D], F32, kind="ExternalInput").ap()
    BQ = nc.dram_tensor("bq1", [128, 24], F32, kind="ExternalInput").ap()
    WO = nc.dram_tensor("wo1", [D, D], F32, kind="ExternalInput").ap()
    BO = nc.dram_tensor("bo1", [128, 8], F32, kind="ExternalInput").ap()
    MUT = nc.dram_tensor("mut", [128, 128], F32, kind="ExternalInput").ap()
    HON = nc.dram_tensor("hones", [128, 128], F32, kind="ExternalInput").ap()
    RM = C.dram_in("rmask", [128, 8])
    bq = C.ab("l1a_bq", [128, 24], F32)
    bo = C.ab("l1a_bo", [128, 8], F32)
    mut = C.ab("l1a_mut", [128, 128], BF16)
    hon = C.ab("l1a_hon", [128, 128], BF16)
    rm = C.ab("l1a_rm", [128, 8], F32)
    C.dma("Y", bq[:], BQ, r=[], w=["l1a_bq"])
    C.dma("Y", bo[:], BO, r=[], w=["l1a_bo"])
    C.dma("P", mut[:], MUT, r=[], w=["l1a_mut"])
    C.dma("P", hon[:], HON, r=[], w=["l1a_hon"])
    C.dma("Y", rm[:], RM, r=[], w=["l1a_rm"])

    xn = [C.ab(f"l1a_xn{c}", [128, 2 * TILE], BF16) for c in range(KC)]
    qp = [C.ab(f"l1a_qp{c}", [128, T], BF16) for c in range(KC)]
    kp = [C.ab(f"l1a_kp{c}", [128, T], BF16) for c in range(KC)]
    vt = [C.ab(f"l1a_vt{tc}", [128, D], BF16) for tc in range(16)]
    ktst = C.ab("l1a_ktst", [128, D], BF16)[:]
    wq = [C.ab(f"l1a_w{i}", [128, 8 * 128], BF16) for i in range(2)]
    st1 = C.lnst[0]["d1"]
    st2 = C.lnst[0]["d2"]
    vtmp = C.ab("l1a_vtmp", [128, TILE], BF16)
    zg = C.ab("l1a_zg", [128, 8], F32)
    zgg = C.ab("l1a_zgg", [128, 8], F32)
    gz = C.ab("l1a_gz", [128, 64], F32)
    sg = C.ab("l1a_sg", [128, D // 2], F32)
    srun = C.ab("l1a_srun", [128, D // 2], F32)
    srb = C.ab("l1a_srb", [128, D // 2], BF16)
    kcum = [C.ab(f"l1a_kcum{i}", [128, T], BF16) for i in range(2)]
    sstage_ap = kcum[0][:].bitcast(F32)[0:128, 0:D // 2]
    sm = C.ab("l1a_sm", [128, 256], BF16)
    nst0 = C.lnst[0]["d1"][0]
    nst = [nst0, nst0]
    o_sb = xn
    aeff = C.ab("l1a_ae", [128, 8], F32)

    ccz_in = nc.dram_tensor("l1a_ccz_in", [128, 8], F32)
    ccz_out = nc.dram_tensor("l1a_ccz_out", [1024, 8], F32, addr_space="Shared")
    ccs_in = nc.dram_tensor("l1a_ccs_in", [128, D // 2], F32)
    ccs_out = nc.dram_tensor("l1a_ccs_out", [1024, D // 2], F32, addr_space="Shared")

    for tg in range(2):
      for tl in range(2):
        t = 2 * tg + tl
        dsts = [(xn[c][:, tl * TILE:(tl + 1) * TILE], f"l1a_xn{c}t{t}") for c in range(KC)]
        ln_full(C, C.xres, "xres", t * TILE, TILE, "dst", dsts=dsts)
      for eg in range(24):
        wbuf = wq[eg % 2]
        wkey = f"l1a_w{eg % 2}"
        C.dma("P", wbuf[:].rearrange("p (k e) -> p k e", k=KC),
              WQ[:, eg * 128:(eg + 1) * 128].rearrange("(k p) e -> p k e", p=128),
              r=[], w=[wkey])
        bts = [C.rb(), C.rb()]
        for k in range(KC):
            lhs = wbuf[:, k * 128:(k + 1) * 128]
            for tl in range(2):
                t = 2 * tg + tl
                C.mm(bts[tl], lhs, xn[k][:, tl * TILE:(tl + 1) * TILE],
                     start=(k == 0), stop=(k == KC - 1), r=[wkey, f"l1a_xn{k}t{t}"])
        bcol = bq[:, eg:eg + 1]
        for tl in range(2):
            t = 2 * tg + tl
            bi = bts[tl]
            if eg < 16:
                c = eg % 8
                dst = (qp if eg < 8 else kp)[c]
                dkey = f"l1a_{'qp' if eg < 8 else 'kp'}{c}"
                s1, s2 = C.lnd2f[t % 2], st2[t % 2]
                k1, k2 = f"ln.d2f{t % 2}", f"ln.d2{t % 2}"
                C.ts(s1[:], C.bank(bi), bcol, 0.0, ALU.add, op1=ALU.min,
                     r=[f"PB{bi}", "l1a_bq"], w=[k1])
                C.act(s1[:], s1[:], AF.Exp, r=[k1], w=[k1])
                C.ts(s2[:], C.bank(bi), bcol, 0.0, ALU.add, op1=ALU.max,
                     r=[f"PB{bi}", "l1a_bq"], w=[k2])
                C.tt(dst[:, t * TILE:(t + 1) * TILE], s1[:], s2[:], ALU.add,
                     r=[k1, k2], w=[dkey])
            else:
                ec = eg - 16
                C.ts(vtmp[:], C.bank(bi), bcol, None, ALU.add,
                     r=[f"PB{bi}", "l1a_bq"], w=["l1a_vtmp"])
                bj = C.rb()
                for stl in range(4):
                    C.transpose(C.bank_bf(bj, 0, 128, stl * 128, (stl + 1) * 128),
                                vtmp[:, stl * 128:(stl + 1) * 128], C.identb[:],
                                r=["l1a_vtmp", "identb"], w=[f"PB{bj}"])
                for stl in range(4):
                    tc = t * 4 + stl
                    C.copy(vt[tc][:, ec * 128:(ec + 1) * 128],
                           C.bank_bf(bj, 0, 128, stl * 128, (stl + 1) * 128),
                           r=[f"PB{bj}"], w=[f"l1a_vt{tc}"])

    for c in range(KC):
        C.red(zg[:, c:c + 1], kp[c][:], ALU.add, r=[f"l1a_kp{c}"], w=["l1a_zg"])
    C.dma("P", ccz_in[:, :], zg[:], r=["l1a_zg"], w=["ccz_in"])
    C.cc("AllGather", [ccz_in[:, :]], [ccz_out[:, :]], r=["ccz_in"], w=["ccz_out"])
    C.dma("Y", gz[:].rearrange("p (r f) -> p r f", r=8),
          ccz_out[:, :].rearrange("(r p) f -> p r f", p=128), r=["ccz_out"], w=["l1a_gz"])
    C.memset(zgg[:], 0.0, w=["l1a_zgg"])
    for j in range(NCORE):
        C.ts(aeff[:], gz[:, j * 8:(j + 1) * 8], rm[:, j:j + 1], None, ALU.mult,
             r=["l1a_gz", "l1a_rm"], w=["l1a_ae"])
        C.tt(zgg[:], zgg[:], aeff[:], ALU.add, r=["l1a_zgg", "l1a_ae"], w=["l1a_zgg"])
    for tc in range(16):
        for ec in range(KC):
            bj = C.rb(0, 4)
            bap = C.bank_bf(bj, 0, 128, 0, 128)
            C.transpose(bap, kp[ec][:, tc * 128:(tc + 1) * 128], C.identb[:],
                        r=[f"l1a_kp{ec}", "identb"], w=[f"PB{bj}"])
            C.copy(ktst[0:128, ec * 128:(ec + 1) * 128], bap, r=[f"PB{bj}"], w=["l1a_ktst"])
        for h in range(16):
            ch, hp = h // 2, (h % 2) * 64
            bi = 4 + (h % 2)
            C.mm(bi, ktst[0:128, h * 64:(h + 1) * 64], vt[tc][:, h * 64:(h + 1) * 64],
                 start=True, stop=True, r=["l1a_ktst", f"l1a_vt{tc}"],
                 bank_ap=C.bank(bi, hp, hp + 64, 0, 64))
            if tc == 0:
                C.copy(sg[hp:hp + 64, ch * 64:(ch + 1) * 64], C.bank(bi, hp, hp + 64, 0, 64),
                       r=[f"PB{bi}"], w=["l1a_sg"], q="S")
            else:
                C.tt(sg[hp:hp + 64, ch * 64:(ch + 1) * 64], sg[hp:hp + 64, ch * 64:(ch + 1) * 64],
                     C.bank(bi, hp, hp + 64, 0, 64), ALU.add, r=["l1a_sg", f"PB{bi}"],
                     w=["l1a_sg"])

    C.dma("P", ccs_in[:, :], sg[:], r=["l1a_sg"], w=["ccs_in"])
    C.cc("AllGather", [ccs_in[:, :]], [ccs_out[:, :]], r=["ccs_in"], w=["ccs_out"])
    C.memset(srun[:], 0.0, w=["l1a_srun"] + [f"l1a_srun{h}" for h in range(16)])
    for c in range(KC):
        kc_b = kcum[c % 2]
        kck = f"l1a_kcum{c % 2}"
        C.scan(kc_b[:], kp[c][:], kp[c][:], zgg[:, c:c + 1], ALU.add, ALU.max,
               r=[f"l1a_kp{c}", "l1a_zgg"], w=[kck])
        C.tt(kc_b[:], qp[c][:], kc_b[:], ALU.mult, r=[f"l1a_qp{c}", kck],
             w=[kck])
        for t in range(NT):
            bi = C.rb(0, 4)
            C.mm(bi, hon[:], kc_b[:, t * TILE:(t + 1) * TILE],
                 start=True, stop=True, r=[kck, "l1a_hon"])
            C.act(nst0[:], C.bank(bi), AF.Ln, bias=C.c_eps_attn,
                  r=[f"PB{bi}", "consts"], w=["ln.d10"])
            C.act(nst0[:], nst0[:], AF.Exp, scale=-1.0,
                  r=["ln.d10"], w=["ln.d10"])
            C.tt(qp[c][:, t * TILE:(t + 1) * TILE], qp[c][:, t * TILE:(t + 1) * TILE],
                 nst0[:], ALU.mult, r=[f"l1a_qp{c}", "ln.d10"], w=[f"l1a_qp{c}"])
    for j in range(NCORE):
        C.dma("Y", sstage_ap, ccs_out[j * 128:(j + 1) * 128, :],
              r=["ccs_out"], w=["l1a_sstage", "l1a_kcum0"])
        C.stt(srun[:], sstage_ap, rm[:, j:j + 1], srun[:],
              ALU.mult, ALU.add, r=["l1a_sstage", "l1a_rm", "l1a_srun"], w=["l1a_srun"] + [f"l1a_srun{h}" for h in range(16)])
    C.copy(srb[:], srun[:], r=["l1a_srun"], w=["l1a_srb"] + [f"l1a_srb{h}" for h in range(16)])

    for tc in range(16):
        for ec in range(KC):
            bj = C.rb(0, 4)
            bap = C.bank_bf(bj, 0, 128, 0, 128)
            C.transpose(bap, kp[ec][:, tc * 128:(tc + 1) * 128], C.identb[:],
                        r=[f"l1a_kp{ec}", "identb"], w=[f"PB{bj}"])
            C.copy(ktst[0:128, ec * 128:(ec + 1) * 128], bap, r=[f"PB{bj}"], w=["l1a_ktst"])
        for ch in range(8):
            bc = 4 + (ch % 2)
            bu = C.rb(0, 4)
            for par in range(2):
                h = 2 * ch + par
                hp = par * 64
                q_ap = qp[ch][hp:hp + 64, tc * 128:(tc + 1) * 128]
                k_ap = kp[ch][hp:hp + 64, tc * 128:(tc + 1) * 128]
                smv = sm[:, par * 128:(par + 1) * 128]
                smk = f"l1a_sm{par}"
                sreg = (slice(hp, hp + 64), slice(ch * 64, (ch + 1) * 64))
                bs = C.rb(0, 4)
                C.mm(bs, k_ap, q_ap, start=True, stop=True,
                     r=[f"l1a_kp{ch}", f"l1a_qp{ch}"],
                     bank_ap=C.bank(bs, 0, 128, 0, 128))
                C.tt(smv, C.bank(bs, 0, 128, 0, 128), mut[:], ALU.mult,
                     r=[f"PB{bs}", "l1a_mut"], w=[smk])
                C.mm(bc, vt[tc][:, h * 64:(h + 1) * 64], smv, start=True, stop=False,
                     r=[f"l1a_vt{tc}", smk], bank_ap=C.bank(bc, hp, hp + 64, 0, 128))
                C.mm(bc, srb[sreg], q_ap, start=False, stop=True,
                     r=[f"l1a_srb{h}", f"l1a_qp{ch}"], bank_ap=C.bank(bc, hp, hp + 64, 0, 128))
                C.mm(bu, ktst[0:128, h * 64:(h + 1) * 64], vt[tc][:, h * 64:(h + 1) * 64],
                     start=True, stop=True, r=["l1a_ktst", f"l1a_vt{tc}"],
                     bank_ap=C.bank(bu, hp, hp + 64, 0, 64))
            sboth = (slice(0, 128), slice(ch * 64, (ch + 1) * 64))
            hk = [f"l1a_srun{2 * ch}", f"l1a_srun{2 * ch + 1}"]
            bk = [f"l1a_srb{2 * ch}", f"l1a_srb{2 * ch + 1}"]
            C.tt(srun[sboth], srun[sboth], C.bank(bu, 0, 128, 0, 64), ALU.add,
                 r=hk + [f"PB{bu}"], w=hk)
            C.copy(srb[sboth], srun[sboth], r=hk, w=bk, q="S")
            C.copy(qp[ch][:, tc * 128:(tc + 1) * 128], C.bank(bc, 0, 128, 0, 128),
                   r=[f"PB{bc}"], w=[f"l1a_qp{ch}"])

    for tg in range(2):
        for oc in range(KC):
            wbuf = wq[oc % 2]
            wkey = f"l1a_w{oc % 2}"
            C.dma("P", wbuf[:].rearrange("p (k e) -> p k e", k=KC),
                  WO[:, oc * 128:(oc + 1) * 128].rearrange("(k p) e -> p k e", p=128),
                  r=[], w=[wkey])
            bts = [C.rb(0, 4), C.rb(0, 4)]
            for k in range(KC):
                lhs = wbuf[:, k * 128:(k + 1) * 128]
                for tl in range(2):
                    t = 2 * tg + tl
                    C.mm(bts[tl], lhs, qp[k][:, t * TILE:(t + 1) * TILE],
                         start=(k == 0), stop=(k == KC - 1), r=[wkey, f"l1a_qp{k}"])
            for tl in range(2):
                t = 2 * tg + tl
                C.ts(o_sb[oc][:, tl * TILE:(tl + 1) * TILE], C.bank(bts[tl]), bo[:, oc:oc + 1],
                     None, ALU.add, r=[f"PB{bts[tl]}", "l1a_bo"], w=[f"l1a_xn{oc}o{t}"])
        for tl in range(2):
            t = 2 * tg + tl
            srcs = [(o_sb[c][:, tl * TILE:(tl + 1) * TILE], f"l1a_xn{c}o{t}") for c in range(KC)]
            pi = t % 2
            bS, bQ = C.rb(0, 4), C.rb(0, 4)
            for c in range(KC):
                sap, skey = srcs[c]
                sq = C.lnsq[pi * 2 + c % 2]
                sqk = f"ln{pi}.sq{c % 2}"
                C.act(sq[:], sap, AF.Square, r=[skey], w=[sqk])
                C.mm(bS, C.ones_bf[:], sap, start=(c == 0), stop=(c == KC - 1), r=[skey, "onesb"])
                C.mm(bQ, C.ones_bf[:], sq[:], start=(c == 0), stop=(c == KC - 1), r=[sqk, "onesb"])
            keys = ln_tilemath(C, bS, bQ, pi=pi)
            ln_apply_resid(C, srcs, keys, t * TILE, TILE)


def build_program(blocks, first, last):
    nc = bass.Bass("TRN2", target_bir_lowering=False, debug=False)
    C = Ctx(nc)
    make_fixtures(C, nc)
    C.arena_init()
    if first:
        emit_load_x_first(C, nc)
    else:
        emit_load_x_fm(C, nc)
    for b in blocks:
        C.arena_reset()
        if b == "l0f":
            emit_l0_ffn(C, nc)
        elif b == "l0a":
            emit_l0_attn(C, nc)
        elif b == "l1a":
            emit_l1_attn(C, nc)
        elif b == "l1f":
            emit_l1_ffn(C, nc)
    if last:
        emit_store_x_last(C, nc)
    else:
        emit_store_x_fm(C, nc)
    for name in globals().get("_TAPS", ()):
        h = C.reg[name]
        tp = nc.dram_tensor(f"tap_{name}", list(h[:].shape), h[:].dtype,
                            kind="ExternalOutput").ap()
        C.dma("Y", tp, h[:], r=[name], w=[], out=True)
    C.P.finalize_and_emit()
    return nc



def _fix_array():
    fix = np.zeros((128, 8 + 128 + 128), np.float32)
    fix[:, 0] = EPS_LN
    fix[:, 1] = EPS_ATTN
    fix[:, 2] = 0.0
    fix[:, 3] = 1.0
    fix[:, 4] = float(np.log(np.sqrt(1024.0)))
    fix[:, 8:136] = 1.0
    fix[:, 136:264] = np.eye(128, dtype=np.float32)
    return fix


def _common_weight_arrays(inp):
    a = {}
    a["u1"] = np.ascontiguousarray(inp["L0_ffn_U1"])
    a["v1"] = np.ascontiguousarray(inp["L0_ffn_V1"])
    a["u2"] = np.ascontiguousarray(inp["L0_ffn_U2"])
    a["v2"] = np.ascontiguousarray(inp["L0_ffn_V2"])
    a["b1f"] = np.ascontiguousarray(inp["L0_ffn_b1"].reshape(32, 128).T)
    a["b2f"] = np.ascontiguousarray(inp["L0_ffn_b2"].reshape(8, 128).T)
    a["wqa"] = np.ascontiguousarray(inp["L0_attn_w_qkva"].T)
    a["wq1"] = np.ascontiguousarray(np.asarray(inp["L1_attn_w_qkv"]).T)
    a["bq1"] = np.ascontiguousarray(np.asarray(inp["L1_attn_b_qkv"]).reshape(24, 128).T)
    a["wo1"] = np.ascontiguousarray(np.asarray(inp["L1_attn_w_out"]).T)
    a["bo1"] = np.ascontiguousarray(np.asarray(inp["L1_attn_b_out"]).reshape(8, 128).T)
    ii = np.arange(128)
    a["mut"] = (ii[:, None] <= ii[None, :]).astype(np.float32)
    a["hones"] = ((ii[:, None] < 64) == (ii[None, :] < 64)).astype(np.float32)
    a["cent"] = np.ascontiguousarray(np.asarray(inp["L1_ffn_centroids"]).T)
    a["lng"] = np.ascontiguousarray(inp["L1_ffn_ln_g"])
    a["lnb"] = np.ascontiguousarray(inp["L1_ffn_ln_b"])
    a["w1"] = np.ascontiguousarray(inp["L1_ffn_w1"])
    a["w2"] = np.ascontiguousarray(inp["L1_ffn_w2"])
    a["b1e"] = np.ascontiguousarray(
        np.asarray(inp["L1_ffn_b1"]).reshape(8, 32, 128).transpose(2, 0, 1).reshape(128, 256))
    a["b2e"] = np.ascontiguousarray(inp["L1_ffn_b2"])
    sel = np.zeros((8, 8 * 128), np.float32)
    for e in range(8):
        sel[e, e * 128:(e + 1) * 128] = 1.0
    a["sel"] = sel
    a["gam"] = np.ascontiguousarray(np.asarray(inp["L0_attn_gamma"]).reshape(8, 128).T)
    a["fix"] = _fix_array()
    return a


def _launch(blocks, first, last, per_core_maps):
    nc = build_program(blocks, first, last)
    res = run_bass_kernel_spmd(nc, per_core_maps, core_ids=list(range(NCORE)))
    return res.results


def kernel(**inputs):
    x = np.asarray(inputs["L0_attn_x"]).reshape(16384, D).astype(np.float32)
    shards = [np.ascontiguousarray(x[i * T:(i + 1) * T]) for i in range(NCORE)]
    wa = _common_weight_arrays(inputs)

    launches = globals().get("_TEST_LAUNCHES") or [(ALL_BLOCKS, True, True)]
    rmasks = []
    for r in range(NCORE):
        m = np.zeros((128, NCORE), np.float32)
        m[:, :r] = 1.0
        rmasks.append(m)
    cur = [{"xin": shards[i], "rmask": rmasks[i], "rmaskc": 1.0 - rmasks[i], **wa}
           for i in range(NCORE)]
    outs = None
    for blocks, first, last in launches:
        results = _launch(blocks, first, last, cur)
        if last:
            outs = [r["out"] for r in results]
        else:
            cur = [{**cur[i], "xfm": results[i]["xfo"]} for i in range(NCORE)]
    y = np.concatenate(outs, axis=0).reshape(1, 16384, D)
    return y.astype(np.float32)


def emit_l0_attn(C, nc):
    WQA = nc.dram_tensor("wqa", [D, 3 * D], F32, kind="ExternalInput").ap()
    GAM = nc.dram_tensor("gam", [128, 8], F32, kind="ExternalInput").ap()
    RM = C.dram_in("rmask", [128, 8])
    RMC = C.dram_in("rmaskc", [128, 8])
    gam = C.ab("gam_sb", [128, 8], F32)
    n2t = C.ab("lnst_n2", [128, TILE], F32)
    msct = C.ab("lnst_msc", [128, TILE], F32)
    C.lnst[0]["n2"] = C.lnst[1]["n2"] = n2t
    C.lnst[0]["msc"] = C.lnst[1]["msc"] = msct
    rm = C.ab("rm_sb", [128, 8], F32)
    rmc = C.ab("rmc_sb", [128, 8], F32)
    C.dma("Y", gam[:], GAM, r=[], w=["gamma"])
    C.dma("Y", rm[:], RM, r=[], w=["rmask"])
    C.dma("Y", rmc[:], RMC, r=[], w=["rmaskc"])

    h = [C.ab(f"l0a_h{c}", [128, T], BF16) for c in range(KC)]
    s1 = [C.ab(f"l0a_s1{c}", [128, T], BF16) for c in range(KC)]
    s2 = [C.ab(f"l0a_s2{i}", [128, T], BF16) for i in range(4)]
    qb = [C.ab(f"l0a_q{i}", [128, T], BF16) for i in range(2)]
    ab_ = [C.ab(f"l0a_a{i}", [128, T], F32) for i in range(2)]
    kvb = [C.ab(f"l0a_kv{i}", [128, T], BF16) for i in range(2)]
    yb = [C.ab(f"l0a_y{i}", [128, T], BF16) for i in range(2)]
    pb = [C.ab(f"l0a_p{i}", [128, T], BF16) for i in range(2)]
    wq = [C.ab(f"l0a_w{i}", [128, 8 * 128], BF16) for i in range(4)]
    exch = [C.ab(f"l0a_exch{i}", [128, 8], F32) for i in range(2)]
    gsb = [C.ab(f"l0a_gsb{i}", [128, 64], F32) for i in range(2)]
    carry = C.ab("l0a_carry", [128, 8], F32)
    aeff = C.ab("l0a_aeff", [128, 4], F32)
    yeff = C.ab("l0a_yeff", [128, 4], F32)

    cc_in = [nc.dram_tensor(f"l0a_ccin{hh}", [128, 8], F32) for hh in range(2)]
    cc_out = [nc.dram_tensor(f"l0a_ccout{hh}", [1024, 8], F32, addr_space="Shared")
              for hh in range(2)]

    for t in range(NT):
        dsts = [(h[c][:, t * TILE:(t + 1) * TILE], f"l0a_h{c}t{t}") for c in range(KC)]
        ln_full(C, C.xres, "xres", t * TILE, TILE, "dst", dsts=dsts, gamma=gam, rms=True)

    combined = [False, False]

    def fixup_half(hh, ci):
        cs = slice(hh * 4, hh * 4 + 4)
        if not combined[hh]:
            combined[hh] = True
            C.memset(carry[:, cs], 0.0, w=["l0a_carry"])
            for j in range(NCORE):
                C.ts(aeff[:], gsb[hh][:, j * 8:j * 8 + 4], rm[:, j:j + 1], rmc[:, j:j + 1],
                     ALU.mult, r=[f"l0a_gsb{hh}", "rmask", "rmaskc"], w=["l0a_aeff"], op1=ALU.add)
                C.ts(yeff[:], gsb[hh][:, j * 8 + 4:j * 8 + 8], rm[:, j:j + 1], None,
                     ALU.mult, r=[f"l0a_gsb{hh}", "rmask"], w=["l0a_yeff"])
                C.tt(carry[:, cs], carry[:, cs], aeff[:], ALU.mult, r=["l0a_carry", "l0a_aeff"], w=["l0a_carry"])
                C.tt(carry[:, cs], carry[:, cs], yeff[:], ALU.add, r=["l0a_carry", "l0a_yeff"], w=["l0a_carry"])
        c = hh * 4 + ci
        C.stt(s1[c][:], s2[ci][:], carry[:, c:c + 1], s1[c][:], ALU.mult, ALU.add,
              r=[f"l0a_s2h{hh}{ci}", "l0a_carry", f"l0a_s1{c}"], w=[f"l0a_s1{c}"])

    for half in range(2):
        for ci in range(4):
            c = half * 4 + ci
            si = c % 2
            kq, ka, kkv, ky, kp_ = (f"l0a_qb{si}", f"l0a_ab{si}", f"l0a_kvb{si}",
                                    f"l0a_yb{si}", f"l0a_pb{si}")
            for gi, (eg, dstbuf, dkey, evac) in enumerate([
                    (c, qb[si], kq, "copy"),
                    (16 + c, ab_[si], ka, "sigmoid"),
                    (8 + c, kvb[si], kkv, "copy")]):
                wbuf = wq[(3 * c + gi) % 4]
                wkey = f"l0a_w{(3 * c + gi) % 4}"
                C.dma("P", wbuf[:].rearrange("p (k e) -> p k e", k=KC),
                      WQA[:, eg * 128:(eg + 1) * 128].rearrange("(k p) e -> p k e", p=128),
                      r=[], w=[wkey])
                bts = [C.rb(), C.rb(), C.rb(), C.rb()]
                for k in range(KC):
                    lhs = wbuf[:, k * 128:(k + 1) * 128]
                    for t in range(NT):
                        C.mm(bts[t], lhs, h[k][:, t * TILE:(t + 1) * TILE],
                             start=(k == 0), stop=(k == KC - 1), r=[wkey, f"l0a_h{k}t{t}"])
                for t in range(NT):
                    dap = dstbuf[:, t * TILE:(t + 1) * TILE]
                    if evac == "sigmoid":
                        C.act(dap, C.bank(bts[t]), AF.Sigmoid, r=[f"PB{bts[t]}"], w=[ka])
                    else:
                        C.act(dap, C.bank(bts[t]), AF.Copy, r=[f"PB{bts[t]}"], w=[dkey])
            C.scan(yb[si][:], ab_[si][:], kvb[si][:], C.c_zero, ALU.mult, ALU.add,
                   r=[ka, kkv, "consts"], w=[ky])
            C.scan(pb[si][:], ab_[si][:], ab_[si][:], C.c_one, ALU.mult, ALU.min,
                   r=[ka, "consts"], w=[kp_])
            C.tt(s1[c][:], qb[si][:], yb[si][:], ALU.mult, r=[kq, ky], w=[f"l0a_s1{c}"])
            if half == 1:
                fixup_half(0, ci)
            C.tt(s2[ci][:], qb[si][:], pb[si][:], ALU.mult, r=[kq, kp_], w=[f"l0a_s2h{half}{ci}"])
            C.copy(exch[half][:, ci:ci + 1], pb[si][:, T - 1:T], r=[kp_], w=[f"l0a_exch{half}"])
            C.copy(exch[half][:, 4 + ci:5 + ci], yb[si][:, T - 1:T], r=[ky], w=[f"l0a_exch{half}"])
        C.dma("P", cc_in[half][:, :], exch[half][:], r=[f"l0a_exch{half}"], w=[f"l0a_ccin{half}"])
        C.cc("AllGather", [cc_in[half][:, :]], [cc_out[half][:, :]],
             r=[f"l0a_ccin{half}"], w=[f"l0a_ccout{half}"])
        C.dma("Y", gsb[half][:].rearrange("p (r f) -> p r f", r=8),
              cc_out[half][:, :].rearrange("(r p) f -> p r f", p=128),
              r=[f"l0a_ccout{half}"], w=[f"l0a_gsb{half}"])
    for ci in range(4):
        fixup_half(1, ci)
    for t in range(NT):
        ln_full(C, s1, "l0a_s1", t * TILE, TILE, "resid")


def emit_l0_ffn(C, nc):
    U1 = nc.dram_tensor("u1", [D, 256], F32, kind="ExternalInput").ap()
    V1 = nc.dram_tensor("v1", [256, 4096], F32, kind="ExternalInput").ap()
    U2 = nc.dram_tensor("u2", [4096, 256], F32, kind="ExternalInput").ap()
    V2 = nc.dram_tensor("v2", [256, D], F32, kind="ExternalInput").ap()
    B1 = nc.dram_tensor("b1f", [128, 32], F32, kind="ExternalInput").ap()
    B2 = nc.dram_tensor("b2f", [128, 8], F32, kind="ExternalInput").ap()
    u1 = C.ab("u1sb", [128, 8 * 256], BF16)
    v1 = C.ab("v1sb", [128, 2 * 4096], BF16)
    u2 = C.ab("u2sb", [128, 32 * 256], BF16)
    v2 = C.ab("v2sb", [128, 2 * D], BF16)
    b1 = C.ab("b1sb", [128, 32], F32)
    b2 = C.ab("b2sb", [128, 8], F32)
    for k in range(8):
        C.dma("P", u1[:, k * 256:(k + 1) * 256], U1[k * 128:(k + 1) * 128, :], r=[], w=["u1"])
    for k in range(2):
        C.dma("P", v1[:, k * 4096:(k + 1) * 4096], V1[k * 128:(k + 1) * 128, :], r=[], w=["v1"])
    for k in range(32):
        C.dma("P", u2[:, k * 256:(k + 1) * 256], U2[k * 128:(k + 1) * 128, :], r=[], w=["u2"])
    for k in range(2):
        C.dma("P", v2[:, k * D:(k + 1) * D], V2[k * 128:(k + 1) * 128, :], r=[], w=["v2"])
    C.dma("Y", b1[:], B1, r=[], w=["b1"])
    C.dma("Y", b2[:], B2, r=[], w=["b2"])

    xn = [C.ab(f"ffn_xn{c}", [128, TILE], BF16) for c in range(KC)]
    p_sb = C.ab("ffn_p", [128, 2 * TILE], BF16)
    h_sb = C.ab("ffn_h", [128, 32 * TILE], BF16)
    q_sb = C.ab("ffn_q", [128, 2 * TILE], BF16)
    o_sb = [C.ab(f"ffn_o{c}", [128, TILE], BF16) for c in range(KC)]
    osq = [C.ab(f"ffn_osq{c}", [128, TILE], BF16) for c in range(KC)]

    phases = globals().get("_L0F_PHASES", 6)
    ntt = globals().get("_L0F_NT", NT)
    for t in range(ntt):
        t0 = t * TILE
        dsts = [(xn[c][:], f"ffn_xn{c}") for c in range(KC)]
        ln_full(C, C.xres, "xres", t0, TILE, "dst", dsts=dsts)
        if phases < 2:
            continue
        for rc in range(2):
            bi = C.rb()
            for kc in range(KC):
                C.mm(bi, u1[:, kc * 256 + rc * 128: kc * 256 + (rc + 1) * 128], xn[kc][:],
                     start=(kc == 0), stop=(kc == KC - 1), r=["u1", f"ffn_xn{kc}"])
            C.copy(p_sb[:, rc * TILE:(rc + 1) * TILE], C.bank(bi), r=[f"PB{bi}"], w=[f"ffn_p{rc}"], q="S")
        if phases < 3:
            continue
        for fc in range(32):
            bi = C.rb()
            for rc in range(2):
                C.mm(bi, v1[:, rc * 4096 + fc * 128: rc * 4096 + (fc + 1) * 128],
                     p_sb[:, rc * TILE:(rc + 1) * TILE],
                     start=(rc == 0), stop=(rc == 1), r=["v1", f"ffn_p{rc}"])
            C.act(h_sb[:, fc * TILE:(fc + 1) * TILE], C.bank(bi), AF.Gelu_apprx_tanh,
                  bias=b1[:, fc:fc + 1], r=[f"PB{bi}", "b1"], w=[f"ffn_h{fc}"])
        if phases < 4:
            continue
        for rc in range(2):
            bi = C.rb()
            for fc in range(32):
                C.mm(bi, u2[:, fc * 256 + rc * 128: fc * 256 + (rc + 1) * 128],
                     h_sb[:, fc * TILE:(fc + 1) * TILE],
                     start=(fc == 0), stop=(fc == 31), r=["u2", f"ffn_h{fc}"])
            C.copy(q_sb[:, rc * TILE:(rc + 1) * TILE], C.bank(bi), r=[f"PB{bi}"], w=[f"ffn_q{rc}"], q="S")
        if phases < 5:
            continue
        for dc in range(KC):
            bi = C.rb()
            for rc in range(2):
                C.mm(bi, v2[:, rc * D + dc * 128: rc * D + (dc + 1) * 128],
                     q_sb[:, rc * TILE:(rc + 1) * TILE],
                     start=(rc == 0), stop=(rc == 1), r=["v2", f"ffn_q{rc}"])
            C.ts(o_sb[dc][:], C.bank(bi), b2[:, dc:dc + 1], None, ALU.add,
                 r=[f"PB{bi}", "b2"], w=[f"ffn_o{dc}"])
            C.act(osq[dc][:], C.bank(bi), AF.Square, bias=b2[:, dc:dc + 1],
                  r=[f"PB{bi}", "b2"], w=[f"ffn_osq{dc}"])
        if phases < 6:
            continue
        srcs = [(o_sb[c][:], f"ffn_o{c}") for c in range(KC)]
        sqs = [(osq[c][:], f"ffn_osq{c}") for c in range(KC)]
        keys = ln_stats(C, srcs, sqs)
        ln_apply_resid(C, srcs, keys, t0, TILE)



def build_program(blocks, first, last):
    nc = bass.Bass("TRN2", target_bir_lowering=False, debug=False)
    C = Ctx(nc)
    make_fixtures(C, nc)
    C.arena_init()
    if first:
        emit_load_x_first(C, nc)
    else:
        emit_load_x_fm(C, nc)
    for b in blocks:
        C.arena_reset()
        if b == "l0f":
            emit_l0_ffn(C, nc)
        elif b == "l0a":
            emit_l0_attn(C, nc)
        elif b == "l1a":
            emit_l1_attn(C, nc)
        elif b == "l1f":
            emit_l1_ffn(C, nc)
    if last:
        emit_store_x_last(C, nc)
    else:
        emit_store_x_fm(C, nc)
    for name in globals().get("_TAPS", ()):
        h = C.reg[name]
        tp = nc.dram_tensor(f"tap_{name}", list(h[:].shape), h[:].dtype,
                            kind="ExternalOutput").ap()
        C.dma("Y", tp, h[:], r=[name], w=[], out=True)
    C.P.finalize_and_emit()
    return nc



def _fix_array():
    fix = np.zeros((128, 8 + 128 + 128), np.float32)
    fix[:, 0] = EPS_LN
    fix[:, 1] = EPS_ATTN
    fix[:, 2] = 0.0
    fix[:, 3] = 1.0
    fix[:, 4] = float(np.log(np.sqrt(1024.0)))
    fix[:, 8:136] = 1.0
    fix[:, 136:264] = np.eye(128, dtype=np.float32)
    return fix


def _common_weight_arrays(inp):
    a = {}
    a["u1"] = np.ascontiguousarray(inp["L0_ffn_U1"])
    a["v1"] = np.ascontiguousarray(inp["L0_ffn_V1"])
    a["u2"] = np.ascontiguousarray(inp["L0_ffn_U2"])
    a["v2"] = np.ascontiguousarray(inp["L0_ffn_V2"])
    a["b1f"] = np.ascontiguousarray(inp["L0_ffn_b1"].reshape(32, 128).T)
    a["b2f"] = np.ascontiguousarray(inp["L0_ffn_b2"].reshape(8, 128).T)
    a["wqa"] = np.ascontiguousarray(inp["L0_attn_w_qkva"].T)
    a["wq1"] = np.ascontiguousarray(np.asarray(inp["L1_attn_w_qkv"]).T)
    a["bq1"] = np.ascontiguousarray(np.asarray(inp["L1_attn_b_qkv"]).reshape(24, 128).T)
    a["wo1"] = np.ascontiguousarray(np.asarray(inp["L1_attn_w_out"]).T)
    a["bo1"] = np.ascontiguousarray(np.asarray(inp["L1_attn_b_out"]).reshape(8, 128).T)
    ii = np.arange(128)
    a["mut"] = (ii[:, None] <= ii[None, :]).astype(np.float32)
    a["hones"] = ((ii[:, None] < 64) == (ii[None, :] < 64)).astype(np.float32)
    a["cent"] = np.ascontiguousarray(np.asarray(inp["L1_ffn_centroids"]).T)
    a["lng"] = np.ascontiguousarray(inp["L1_ffn_ln_g"])
    a["lnb"] = np.ascontiguousarray(inp["L1_ffn_ln_b"])
    a["w1"] = np.ascontiguousarray(inp["L1_ffn_w1"])
    a["w2"] = np.ascontiguousarray(inp["L1_ffn_w2"])
    a["b1e"] = np.ascontiguousarray(
        np.asarray(inp["L1_ffn_b1"]).reshape(8, 32, 128).transpose(2, 0, 1).reshape(128, 256))
    a["b2e"] = np.ascontiguousarray(inp["L1_ffn_b2"])
    sel = np.zeros((8, 8 * 128), np.float32)
    for e in range(8):
        sel[e, e * 128:(e + 1) * 128] = 1.0
    a["sel"] = sel
    a["gam"] = np.ascontiguousarray(np.asarray(inp["L0_attn_gamma"]).reshape(8, 128).T)
    a["fix"] = _fix_array()
    return a


def _launch(blocks, first, last, per_core_maps):
    nc = build_program(blocks, first, last)
    res = run_bass_kernel_spmd(nc, per_core_maps, core_ids=list(range(NCORE)))
    return res.results


def kernel(**inputs):
    x = np.asarray(inputs["L0_attn_x"]).reshape(16384, D).astype(np.float32)
    shards = [np.ascontiguousarray(x[i * T:(i + 1) * T]) for i in range(NCORE)]
    wa = _common_weight_arrays(inputs)

    launches = globals().get("_TEST_LAUNCHES") or [(ALL_BLOCKS, True, True)]
    rmasks = []
    for r in range(NCORE):
        m = np.zeros((128, NCORE), np.float32)
        m[:, :r] = 1.0
        rmasks.append(m)
    cur = [{"xin": shards[i], "rmask": rmasks[i], "rmaskc": 1.0 - rmasks[i], **wa}
           for i in range(NCORE)]
    outs = None
    for blocks, first, last in launches:
        results = _launch(blocks, first, last, cur)
        if last:
            outs = [r["out"] for r in results]
        else:
            cur = [{**cur[i], "xfm": results[i]["xfo"]} for i in range(NCORE)]
    y = np.concatenate(outs, axis=0).reshape(1, 16384, D)
    return y.astype(np.float32)


def emit_l0_attn(C, nc):
    WQA = nc.dram_tensor("wqa", [D, 3 * D], F32, kind="ExternalInput").ap()
    GAM = nc.dram_tensor("gam", [128, 8], F32, kind="ExternalInput").ap()
    RM = C.dram_in("rmask", [128, 8])
    RMC = C.dram_in("rmaskc", [128, 8])
    gam = C.ab("gam_sb", [128, 8], F32)
    n2t = C.ab("lnst_n2", [128, TILE], F32)
    msct = C.ab("lnst_msc", [128, TILE], F32)
    C.lnst[0]["n2"] = C.lnst[1]["n2"] = n2t
    C.lnst[0]["msc"] = C.lnst[1]["msc"] = msct
    rm = C.ab("rm_sb", [128, 8], F32)
    rmc = C.ab("rmc_sb", [128, 8], F32)
    C.dma("Y", gam[:], GAM, r=[], w=["gamma"])
    C.dma("Y", rm[:], RM, r=[], w=["rmask"])
    C.dma("Y", rmc[:], RMC, r=[], w=["rmaskc"])

    h = [C.ab(f"l0a_h{c}", [128, T], BF16) for c in range(KC)]
    s1 = [C.ab(f"l0a_s1{c}", [128, T], BF16) for c in range(KC)]
    s2 = [C.ab(f"l0a_s2{i}", [128, T], BF16) for i in range(4)]
    qb = C.ab("l0a_q", [128, T], F32)
    ab = C.ab("l0a_a", [128, T], F32)
    kvb = C.ab("l0a_kv", [128, T], F32)
    yb = C.ab("l0a_y", [128, T], F32)
    pb = C.ab("l0a_p", [128, T], F32)
    wq = [C.ab(f"l0a_w{i}", [128, 8 * 128], BF16) for i in range(4)]
    exch = C.ab("l0a_exch", [128, 8], F32)
    gsb = C.ab("l0a_gsb", [128, 64], F32)
    carry = C.ab("l0a_carry", [128, 4], F32)
    aeff = C.ab("l0a_aeff", [128, 4], F32)
    yeff = C.ab("l0a_yeff", [128, 4], F32)

    cc_in = [nc.dram_tensor(f"l0a_ccin{hh}", [128, 8], F32) for hh in range(2)]
    cc_out = [nc.dram_tensor(f"l0a_ccout{hh}", [1024, 8], F32, addr_space="Shared")
              for hh in range(2)]

    for t in range(NT):
        dsts = [(h[c][:, t * TILE:(t + 1) * TILE], f"l0a_h{c}t{t}") for c in range(KC)]
        ln_full(C, C.xres, "xres", t * TILE, TILE, "dst", dsts=dsts, gamma=gam, rms=True)

    combined = [False, False]

    def fixup_half(hh, ci):
        cs = slice(hh * 4, hh * 4 + 4)
        if not combined[hh]:
            combined[hh] = True
            C.memset(carry[:, cs], 0.0, w=["l0a_carry"])
            for j in range(NCORE):
                C.ts(aeff[:], gsb[hh][:, j * 8:j * 8 + 4], rm[:, j:j + 1], rmc[:, j:j + 1],
                     ALU.mult, r=[f"l0a_gsb{hh}", "rmask", "rmaskc"], w=["l0a_aeff"], op1=ALU.add)
                C.ts(yeff[:], gsb[hh][:, j * 8 + 4:j * 8 + 8], rm[:, j:j + 1], None,
                     ALU.mult, r=[f"l0a_gsb{hh}", "rmask"], w=["l0a_yeff"])
                C.tt(carry[:, cs], carry[:, cs], aeff[:], ALU.mult, r=["l0a_carry", "l0a_aeff"], w=["l0a_carry"])
                C.tt(carry[:, cs], carry[:, cs], yeff[:], ALU.add, r=["l0a_carry", "l0a_yeff"], w=["l0a_carry"])
        c = hh * 4 + ci
        C.stt(s1[c][:], s2[ci][:], carry[:, c:c + 1], s1[c][:], ALU.mult, ALU.add,
              r=[f"l0a_s2h{hh}{ci}", "l0a_carry", f"l0a_s1{c}"], w=[f"l0a_s1{c}"])

    for half in range(2):
        for ci in range(4):
            c = half * 4 + ci
            for gi, (eg, dstbuf, dkey, evac) in enumerate([
                    (c, qb, "l0a_qb", "copy"),
                    (16 + c, ab, "l0a_ab", "sigmoid"),
                    (8 + c, kvb, "l0a_kvb", "copy")]):
                wbuf = wq[(3 * c + gi) % 4]
                wkey = f"l0a_w{(3 * c + gi) % 4}"
                C.dma("P", wbuf[:].rearrange("p (k e) -> p k e", k=KC),
                      WQA[:, eg * 128:(eg + 1) * 128].rearrange("(k p) e -> p k e", p=128),
                      r=[], w=[wkey])
                for t in range(NT):
                    bi = C.rb()
                    for k in range(KC):
                        C.mm(bi, wbuf[:, k * 128:(k + 1) * 128],
                             h[k][:, t * TILE:(t + 1) * TILE],
                             start=(k == 0), stop=(k == KC - 1), r=[wkey, f"l0a_h{k}t{t}"])
                    dap = dstbuf[:, t * TILE:(t + 1) * TILE]
                    if evac == "sigmoid":
                        C.act(dap, C.bank(bi), AF.Sigmoid, r=[f"PB{bi}"], w=[dkey])
                    else:
                        C.act(dap, C.bank(bi), AF.Copy, r=[f"PB{bi}"], w=[dkey])
            C.scan(yb[:], ab[:], kvb[:], C.c_zero, ALU.mult, ALU.add,
                   r=["l0a_ab", "l0a_kvb", "consts"], w=["l0a_yb"])
            C.scan(pb[:], ab[:], ab[:], C.c_one, ALU.mult, ALU.min,
                   r=["l0a_ab", "consts"], w=["l0a_pb"])
            C.tt(s1[c][:], qb[:], yb[:], ALU.mult, r=["l0a_qb", "l0a_yb"], w=[f"l0a_s1{c}"])
            C.tt(s2[ci][:], qb[:], pb[:], ALU.mult, r=["l0a_qb", "l0a_pb"], w=[f"l0a_s2{ci}"])
            C.copy(exch[:, ci:ci + 1], pb[:, T - 1:T], r=["l0a_pb"], w=["l0a_exch"])
            C.copy(exch[:, 4 + ci:5 + ci], yb[:, T - 1:T], r=["l0a_yb"], w=["l0a_exch"])
        C.dma("P", cc_in[half][:, :], exch[:], r=["l0a_exch"], w=[f"l0a_ccin{half}"])
        C.cc("AllGather", [cc_in[half][:, :]], [cc_out[half][:, :]],
             r=[f"l0a_ccin{half}"], w=[f"l0a_ccout{half}"])
        C.dma("P", gsb[:].rearrange("p (r f) -> p r f", r=8),
              cc_out[half][:, :].rearrange("(r p) f -> p r f", p=128),
              r=[f"l0a_ccout{half}"], w=["l0a_gsb"])
        C.memset(carry[:], 0.0, w=["l0a_carry"])
        for j in range(NCORE):
            C.ts(aeff[:], gsb[:, j * 8:j * 8 + 4], rm[:, j:j + 1], rmc[:, j:j + 1],
                 ALU.mult, r=["l0a_gsb", "rmask", "rmaskc"], w=["l0a_aeff"], op1=ALU.add)
            C.ts(yeff[:], gsb[:, j * 8 + 4:j * 8 + 8], rm[:, j:j + 1], None,
                 ALU.mult, r=["l0a_gsb", "rmask"], w=["l0a_yeff"])
            C.tt(carry[:], carry[:], aeff[:], ALU.mult, r=["l0a_carry", "l0a_aeff"], w=["l0a_carry"])
            C.tt(carry[:], carry[:], yeff[:], ALU.add, r=["l0a_carry", "l0a_yeff"], w=["l0a_carry"])
        for ci in range(4):
            c = half * 4 + ci
            C.stt(s1[c][:], s2[ci][:], carry[:, ci:ci + 1], s1[c][:], ALU.mult, ALU.add,
                  r=[f"l0a_s2{ci}", "l0a_carry", f"l0a_s1{c}"], w=[f"l0a_s1{c}"])
    for t in range(NT):
        ln_full(C, s1, "l0a_s1", t * TILE, TILE, "resid")


def emit_l1_attn(C, nc):
    WQ = nc.dram_tensor("wq1", [D, 3 * D], F32, kind="ExternalInput").ap()
    BQ = nc.dram_tensor("bq1", [128, 24], F32, kind="ExternalInput").ap()
    WO = nc.dram_tensor("wo1", [D, D], F32, kind="ExternalInput").ap()
    BO = nc.dram_tensor("bo1", [128, 8], F32, kind="ExternalInput").ap()
    MUT = nc.dram_tensor("mut", [128, 128], F32, kind="ExternalInput").ap()
    HON = nc.dram_tensor("hones", [128, 128], F32, kind="ExternalInput").ap()
    RM = C.dram_in("rmask", [128, 8])
    bq = C.ab("l1a_bq", [128, 24], F32)
    bo = C.ab("l1a_bo", [128, 8], F32)
    mut = C.ab("l1a_mut", [128, 128], BF16)
    hon = C.ab("l1a_hon", [128, 128], BF16)
    rm = C.ab("l1a_rm", [128, 8], F32)
    C.dma("Y", bq[:], BQ, r=[], w=["l1a_bq"])
    C.dma("Y", bo[:], BO, r=[], w=["l1a_bo"])
    C.dma("P", mut[:], MUT, r=[], w=["l1a_mut"])
    C.dma("P", hon[:], HON, r=[], w=["l1a_hon"])
    C.dma("Y", rm[:], RM, r=[], w=["l1a_rm"])

    xn = [C.ab(f"l1a_xn{c}", [128, 2 * TILE], BF16) for c in range(KC)]
    qp = [C.ab(f"l1a_qp{c}", [128, T], BF16) for c in range(KC)]
    kp = [C.ab(f"l1a_kp{c}", [128, T], BF16) for c in range(KC)]
    vt = [C.ab(f"l1a_vt{tc}", [128, D], BF16) for tc in range(16)]
    ktst = C.ab("l1a_ktst", [128, D], BF16)[:]
    wq = [C.ab(f"l1a_w{i}", [128, 8 * 128], BF16) for i in range(2)]
    st1 = C.lnst[0]["d1"]
    st2 = C.lnst[0]["d2"]
    vtmp = C.ab("l1a_vtmp", [128, TILE], BF16)
    zg = C.ab("l1a_zg", [128, 8], F32)
    zgg = C.ab("l1a_zgg", [128, 8], F32)
    gz = C.ab("l1a_gz", [128, 64], F32)
    sg = C.ab("l1a_sg", [128, D // 2], F32)
    srun = C.ab("l1a_srun", [128, D // 2], F32)
    srb = C.ab("l1a_srb", [128, D // 2], BF16)
    kcum = [C.ab(f"l1a_kcum{i}", [128, T], BF16) for i in range(2)]
    sstage_ap = kcum[0][:].bitcast(F32)[0:128, 0:D // 2]
    sm = C.ab("l1a_sm", [128, 256], BF16)
    nst0 = C.lnst[0]["d1"][0]
    nst = [nst0, nst0]
    o_sb = xn
    aeff = C.ab("l1a_ae", [128, 8], F32)

    ccz_in = nc.dram_tensor("l1a_ccz_in", [128, 8], F32)
    ccz_out = nc.dram_tensor("l1a_ccz_out", [1024, 8], F32, addr_space="Shared")
    ccs_in = nc.dram_tensor("l1a_ccs_in", [128, D // 2], F32)
    ccs_out = nc.dram_tensor("l1a_ccs_out", [1024, D // 2], F32, addr_space="Shared")

    for tg in range(2):
      for tl in range(2):
        t = 2 * tg + tl
        dsts = [(xn[c][:, tl * TILE:(tl + 1) * TILE], f"l1a_xn{c}t{t}") for c in range(KC)]
        ln_full(C, C.xres, "xres", t * TILE, TILE, "dst", dsts=dsts)
      for eg in range(24):
        wbuf = wq[eg % 2]
        wkey = f"l1a_w{eg % 2}"
        C.dma("P", wbuf[:].rearrange("p (k e) -> p k e", k=KC),
              WQ[:, eg * 128:(eg + 1) * 128].rearrange("(k p) e -> p k e", p=128),
              r=[], w=[wkey])
        bts = [C.rb(), C.rb()]
        for k in range(KC):
            lhs = wbuf[:, k * 128:(k + 1) * 128]
            for tl in range(2):
                t = 2 * tg + tl
                C.mm(bts[tl], lhs, xn[k][:, tl * TILE:(tl + 1) * TILE],
                     start=(k == 0), stop=(k == KC - 1), r=[wkey, f"l1a_xn{k}t{t}"])
        bcol = bq[:, eg:eg + 1]
        for tl in range(2):
            t = 2 * tg + tl
            bi = bts[tl]
            if eg < 16:
                c = eg % 8
                dst = (qp if eg < 8 else kp)[c]
                dkey = f"l1a_{'qp' if eg < 8 else 'kp'}{c}"
                s1, s2 = C.lnd2f[t % 2], st2[t % 2]
                k1, k2 = f"ln.d2f{t % 2}", f"ln.d2{t % 2}"
                C.ts(s1[:], C.bank(bi), bcol, 0.0, ALU.add, op1=ALU.min,
                     r=[f"PB{bi}", "l1a_bq"], w=[k1])
                C.act(s1[:], s1[:], AF.Exp, r=[k1], w=[k1])
                C.ts(s2[:], C.bank(bi), bcol, 0.0, ALU.add, op1=ALU.max,
                     r=[f"PB{bi}", "l1a_bq"], w=[k2])
                C.tt(dst[:, t * TILE:(t + 1) * TILE], s1[:], s2[:], ALU.add,
                     r=[k1, k2], w=[dkey])
            else:
                ec = eg - 16
                C.ts(vtmp[:], C.bank(bi), bcol, None, ALU.add,
                     r=[f"PB{bi}", "l1a_bq"], w=["l1a_vtmp"])
                bj = C.rb()
                for stl in range(4):
                    C.transpose(C.bank_bf(bj, 0, 128, stl * 128, (stl + 1) * 128),
                                vtmp[:, stl * 128:(stl + 1) * 128], C.identb[:],
                                r=["l1a_vtmp", "identb"], w=[f"PB{bj}"])
                for stl in range(4):
                    tc = t * 4 + stl
                    C.copy(vt[tc][:, ec * 128:(ec + 1) * 128],
                           C.bank_bf(bj, 0, 128, stl * 128, (stl + 1) * 128),
                           r=[f"PB{bj}"], w=[f"l1a_vt{tc}"])

    for c in range(KC):
        C.red(zg[:, c:c + 1], kp[c][:], ALU.add, r=[f"l1a_kp{c}"], w=["l1a_zg"])
    C.dma("P", ccz_in[:, :], zg[:], r=["l1a_zg"], w=["ccz_in"])
    C.cc("AllGather", [ccz_in[:, :]], [ccz_out[:, :]], r=["ccz_in"], w=["ccz_out"])
    C.dma("Y", gz[:].rearrange("p (r f) -> p r f", r=8),
          ccz_out[:, :].rearrange("(r p) f -> p r f", p=128), r=["ccz_out"], w=["l1a_gz"])
    C.memset(zgg[:], 0.0, w=["l1a_zgg"])
    for j in range(NCORE):
        C.ts(aeff[:], gz[:, j * 8:(j + 1) * 8], rm[:, j:j + 1], None, ALU.mult,
             r=["l1a_gz", "l1a_rm"], w=["l1a_ae"])
        C.tt(zgg[:], zgg[:], aeff[:], ALU.add, r=["l1a_zgg", "l1a_ae"], w=["l1a_zgg"])
    for tc in range(16):
        for ec in range(KC):
            bj = C.rb(0, 4)
            bap = C.bank_bf(bj, 0, 128, 0, 128)
            C.transpose(bap, kp[ec][:, tc * 128:(tc + 1) * 128], C.identb[:],
                        r=[f"l1a_kp{ec}", "identb"], w=[f"PB{bj}"])
            C.copy(ktst[0:128, ec * 128:(ec + 1) * 128], bap, r=[f"PB{bj}"], w=["l1a_ktst"])
        for h in range(16):
            ch, hp = h // 2, (h % 2) * 64
            bi = 4 + (h % 2)
            C.mm(bi, ktst[0:128, h * 64:(h + 1) * 64], vt[tc][:, h * 64:(h + 1) * 64],
                 start=True, stop=True, r=["l1a_ktst", f"l1a_vt{tc}"],
                 bank_ap=C.bank(bi, hp, hp + 64, 0, 64))
            if tc == 0:
                C.copy(sg[hp:hp + 64, ch * 64:(ch + 1) * 64], C.bank(bi, hp, hp + 64, 0, 64),
                       r=[f"PB{bi}"], w=["l1a_sg"], q="S")
            else:
                C.tt(sg[hp:hp + 64, ch * 64:(ch + 1) * 64], sg[hp:hp + 64, ch * 64:(ch + 1) * 64],
                     C.bank(bi, hp, hp + 64, 0, 64), ALU.add, r=["l1a_sg", f"PB{bi}"],
                     w=["l1a_sg"])

    C.dma("P", ccs_in[:, :], sg[:], r=["l1a_sg"], w=["ccs_in"])
    C.cc("AllGather", [ccs_in[:, :]], [ccs_out[:, :]], r=["ccs_in"], w=["ccs_out"])
    C.memset(srun[:], 0.0, w=["l1a_srun"] + [f"l1a_srun{h}" for h in range(16)])
    for c in range(KC):
        kc_b = kcum[c % 2]
        kck = f"l1a_kcum{c % 2}"
        C.scan(kc_b[:], kp[c][:], kp[c][:], zgg[:, c:c + 1], ALU.add, ALU.max,
               r=[f"l1a_kp{c}", "l1a_zgg"], w=[kck])
        C.tt(kc_b[:], qp[c][:], kc_b[:], ALU.mult, r=[f"l1a_qp{c}", kck],
             w=[kck])
        for t in range(NT):
            bi = C.rb(0, 4)
            C.mm(bi, hon[:], kc_b[:, t * TILE:(t + 1) * TILE],
                 start=True, stop=True, r=[kck, "l1a_hon"])
            C.act(nst0[:], C.bank(bi), AF.Ln, bias=C.c_eps_attn,
                  r=[f"PB{bi}", "consts"], w=["ln.d10"])
            C.act(nst0[:], nst0[:], AF.Exp, scale=-1.0,
                  r=["ln.d10"], w=["ln.d10"])
            C.tt(qp[c][:, t * TILE:(t + 1) * TILE], qp[c][:, t * TILE:(t + 1) * TILE],
                 nst0[:], ALU.mult, r=[f"l1a_qp{c}", "ln.d10"], w=[f"l1a_qp{c}"])
    for j in range(NCORE):
        C.dma("Y", sstage_ap, ccs_out[j * 128:(j + 1) * 128, :],
              r=["ccs_out"], w=["l1a_sstage", "l1a_kcum0"])
        C.stt(srun[:], sstage_ap, rm[:, j:j + 1], srun[:],
              ALU.mult, ALU.add, r=["l1a_sstage", "l1a_rm", "l1a_srun"], w=["l1a_srun"] + [f"l1a_srun{h}" for h in range(16)])
    C.copy(srb[:], srun[:], r=["l1a_srun"], w=["l1a_srb"] + [f"l1a_srb{h}" for h in range(16)])

    for tc in range(16):
        for ec in range(KC):
            bj = C.rb(0, 4)
            bap = C.bank_bf(bj, 0, 128, 0, 128)
            C.transpose(bap, kp[ec][:, tc * 128:(tc + 1) * 128], C.identb[:],
                        r=[f"l1a_kp{ec}", "identb"], w=[f"PB{bj}"])
            C.copy(ktst[0:128, ec * 128:(ec + 1) * 128], bap, r=[f"PB{bj}"], w=["l1a_ktst"])
        for ch in range(8):
            bc = 4 + (ch % 2)
            bu = C.rb(0, 4)
            for par in range(2):
                h = 2 * ch + par
                hp = par * 64
                q_ap = qp[ch][hp:hp + 64, tc * 128:(tc + 1) * 128]
                k_ap = kp[ch][hp:hp + 64, tc * 128:(tc + 1) * 128]
                smv = sm[:, par * 128:(par + 1) * 128]
                smk = f"l1a_sm{par}"
                sreg = (slice(hp, hp + 64), slice(ch * 64, (ch + 1) * 64))
                bs = C.rb(0, 4)
                C.mm(bs, k_ap, q_ap, start=True, stop=True,
                     r=[f"l1a_kp{ch}", f"l1a_qp{ch}"],
                     bank_ap=C.bank(bs, 0, 128, 0, 128))
                C.tt(smv, C.bank(bs, 0, 128, 0, 128), mut[:], ALU.mult,
                     r=[f"PB{bs}", "l1a_mut"], w=[smk])
                C.mm(bc, vt[tc][:, h * 64:(h + 1) * 64], smv, start=True, stop=False,
                     r=[f"l1a_vt{tc}", smk], bank_ap=C.bank(bc, hp, hp + 64, 0, 128))
                C.mm(bc, srb[sreg], q_ap, start=False, stop=True,
                     r=[f"l1a_srb{h}", f"l1a_qp{ch}"], bank_ap=C.bank(bc, hp, hp + 64, 0, 128))
                C.mm(bu, ktst[0:128, h * 64:(h + 1) * 64], vt[tc][:, h * 64:(h + 1) * 64],
                     start=True, stop=True, r=["l1a_ktst", f"l1a_vt{tc}"],
                     bank_ap=C.bank(bu, hp, hp + 64, 0, 64))
            sboth = (slice(0, 128), slice(ch * 64, (ch + 1) * 64))
            hk = [f"l1a_srun{2 * ch}", f"l1a_srun{2 * ch + 1}"]
            bk = [f"l1a_srb{2 * ch}", f"l1a_srb{2 * ch + 1}"]
            C.tt(srun[sboth], srun[sboth], C.bank(bu, 0, 128, 0, 64), ALU.add,
                 r=hk + [f"PB{bu}"], w=hk)
            C.copy(srb[sboth], srun[sboth], r=hk, w=bk, q="S")
            C.copy(qp[ch][:, tc * 128:(tc + 1) * 128], C.bank(bc, 0, 128, 0, 128),
                   r=[f"PB{bc}"], w=[f"l1a_qp{ch}"])

    for tg in range(2):
        for oc in range(KC):
            wbuf = wq[oc % 2]
            wkey = f"l1a_w{oc % 2}"
            C.dma("P", wbuf[:].rearrange("p (k e) -> p k e", k=KC),
                  WO[:, oc * 128:(oc + 1) * 128].rearrange("(k p) e -> p k e", p=128),
                  r=[], w=[wkey])
            bts = [C.rb(0, 4), C.rb(0, 4)]
            for k in range(KC):
                lhs = wbuf[:, k * 128:(k + 1) * 128]
                for tl in range(2):
                    t = 2 * tg + tl
                    C.mm(bts[tl], lhs, qp[k][:, t * TILE:(t + 1) * TILE],
                         start=(k == 0), stop=(k == KC - 1), r=[wkey, f"l1a_qp{k}"])
            for tl in range(2):
                t = 2 * tg + tl
                C.ts(o_sb[oc][:, tl * TILE:(tl + 1) * TILE], C.bank(bts[tl]), bo[:, oc:oc + 1],
                     None, ALU.add, r=[f"PB{bts[tl]}", "l1a_bo"], w=[f"l1a_xn{oc}o{t}"])
        for tl in range(2):
            t = 2 * tg + tl
            srcs = [(o_sb[c][:, tl * TILE:(tl + 1) * TILE], f"l1a_xn{c}o{t}") for c in range(KC)]
            pi = t % 2
            bS, bQ = C.rb(0, 4), C.rb(0, 4)
            for c in range(KC):
                sap, skey = srcs[c]
                sq = C.lnsq[pi * 2 + c % 2]
                sqk = f"ln{pi}.sq{c % 2}"
                C.act(sq[:], sap, AF.Square, r=[skey], w=[sqk])
                C.mm(bS, C.ones_bf[:], sap, start=(c == 0), stop=(c == KC - 1), r=[skey, "onesb"])
                C.mm(bQ, C.ones_bf[:], sq[:], start=(c == 0), stop=(c == KC - 1), r=[sqk, "onesb"])
            keys = ln_tilemath(C, bS, bQ, pi=pi)
            ln_apply_resid(C, srcs, keys, t * TILE, TILE)


def emit_l1_ffn(C, nc):
    CEN = nc.dram_tensor("cent", [D, 8], F32, kind="ExternalInput").ap()
    LNG = nc.dram_tensor("lng", [8, D], F32, kind="ExternalInput").ap()
    LNB = nc.dram_tensor("lnb", [8, D], F32, kind="ExternalInput").ap()
    W1 = nc.dram_tensor("w1", [8, D, 4096], F32, kind="ExternalInput").ap()
    W2 = nc.dram_tensor("w2", [8, 4096, D], F32, kind="ExternalInput").ap()
    B1E = nc.dram_tensor("b1e", [128, 256], F32, kind="ExternalInput").ap()
    B2E = nc.dram_tensor("b2e", [8, D], F32, kind="ExternalInput").ap()
    SEL = nc.dram_tensor("sel", [8, 8 * 128], F32, kind="ExternalInput").ap()
    RMIT = 1024

    centb = C.ab("l1f_cent", [128, 64], BF16)
    lngb = C.ab("l1f_lng", [8, D], BF16)
    lnbb = C.ab("l1f_lnb", [8, D], BF16)
    selb = C.ab("l1f_sel", [8, 8 * 128], BF16)
    b2eb = C.ab("l1f_b2e", [8, D], BF16)
    b1e = C.ab("l1f_b1e", [128, 256], F32)
    xi = [C.ab(f"l1f_xi{c}", [128, RMIT], BF16) for c in range(KC)]
    xln = [C.ab(f"l1f_xln{c}", [128, RMIT], BF16) for c in range(KC)]
    Hm = [C.ab(f"l1f_h{f}", [128, RMIT], BF16) for f in range(8)]
    ffn = [C.ab(f"l1f_ffn{c}", [128, RMIT], BF16) for c in range(KC)]
    w1b = [C.ab(f"l1f_w1{i}", [128, 8 * 1024], BF16) for i in range(2)]
    w2b = C.ab("l1f_w2", [128, 8 * 1024], BF16)
    aff = C.ab("l1f_aff", [8, RMIT], F32)
    onehot = C.ab("l1f_oh", [8, RMIT], BF16)
    tmul = C.ab("l1f_tm", [8, RMIT], BF16)
    alphab = C.ab("l1f_al", [128, RMIT], BF16)
    maskb = C.ab("l1f_mb", [128, RMIT], BF16)
    afft = C.ab("l1f_afft", [128, 8], F32)
    mcol = C.ab("l1f_mcol", [128, 1], F32)
    oht = C.ab("l1f_oht", [128, 8], BF16)

    for k in range(KC):
        C.dma("P", centb[:, k * 8:(k + 1) * 8], CEN[k * 128:(k + 1) * 128, :], r=[], w=["l1f_cent"])
    C.dma("P", lngb[:], LNG, r=[], w=["l1f_lng"])
    C.dma("P", lnbb[:], LNB, r=[], w=["l1f_lnb"])
    C.dma("P", selb[:], SEL, r=[], w=["l1f_sel"])
    C.dma("P", b2eb[:], B2E, r=[], w=["l1f_b2e"])
    C.dma("Y", b1e[:], B1E, r=[], w=["l1f_b1e"])

    st_d = C.lnst[0]["d1"] + C.lnst[0]["d2"]
    kd = ["ln.d10", "ln.d11", "ln.d20", "ln.d21"]

    for p in range(2):
        P0 = p * RMIT
        for st in range(2):
            dsts = [(xi[c][:, st * TILE:(st + 1) * TILE], f"l1f_xi{c}s{st}") for c in range(KC)]
            ln_full(C, C.xres, "xres", P0 + st * TILE, TILE, "dst", dsts=dsts)
        for st in range(2):
            bi = C.rb(0, 6)
            bap = C.bank(bi, 0, 8, 0, TILE)
            for k in range(KC):
                C.mm(bi, centb[:, k * 8:(k + 1) * 8], xi[k][:, st * TILE:(st + 1) * TILE],
                     start=(k == 0), stop=(k == KC - 1), r=["l1f_cent", f"l1f_xi{k}s{st}"],
                     bank_ap=bap)
            C.copy(aff[0:8, st * TILE:(st + 1) * TILE], bap, r=[f"PB{bi}"], w=["l1f_aff"])
        for sl in range(8):
            bj = C.rb(0, 6)
            C.transpose(C.bank(bj, 0, 128, 0, 8), aff[0:8, sl * 128:(sl + 1) * 128],
                        C.ident[0:8, 0:8], r=["l1f_aff"], w=[f"PB{bj}"])
            C.copy(afft[:], C.bank(bj, 0, 128, 0, 8), r=[f"PB{bj}"], w=["l1f_afft"])
            C.red(mcol[:], afft[:], ALU.max, r=["l1f_afft"], w=["l1f_mcol"])
            C.ts(oht[:], afft[:], mcol[:], None, ALU.is_ge, r=["l1f_afft", "l1f_mcol"],
                 w=["l1f_oht"])
            bk = C.rb(0, 6)
            C.transpose(C.bank_bf(bk, 0, 8, 0, 128), oht[:], C.identb[:],
                        r=["l1f_oht", "identb"], w=[f"PB{bk}"])
            C.copy(onehot[0:8, sl * 128:(sl + 1) * 128], C.bank_bf(bk, 0, 8, 0, 128),
                   r=[f"PB{bk}"], w=["l1f_oh"])
        C.tt(tmul[0:8, :], aff[0:8, :], onehot[0:8, :], ALU.mult,
             r=["l1f_aff", "l1f_oh"], w=["l1f_tm"])
        for st in range(2):
            bi = C.rb(0, 6)
            C.mm(bi, C.ones_bf[0:8, :], tmul[0:8, st * TILE:(st + 1) * TILE],
                 start=True, stop=True, r=["l1f_tm", "onesb"])
            C.act(alphab[:, st * TILE:(st + 1) * TILE], C.bank(bi), AF.Sigmoid,
                  r=[f"PB{bi}"], w=["l1f_al"])
        for dc in range(KC):
            for st in range(2):
                bi = C.rb(0, 6)
                C.mm(bi, b2eb[0:8, dc * 128:(dc + 1) * 128],
                     onehot[0:8, st * TILE:(st + 1) * TILE], start=True, stop=True,
                     r=["l1f_b2e", "l1f_oh"])
                C.copy(ffn[dc][:, st * TILE:(st + 1) * TILE], C.bank(bi),
                       r=[f"PB{bi}"], w=[f"l1f_ffn{dc}"])
        for st in range(2):
            srcs = []
            bS, bQ = C.rb(0, 6), C.rb(0, 6)
            for c in range(KC):
                sap = xi[c][:, st * TILE:(st + 1) * TILE]
                sq = C.lnsq[st * 2 + c % 2]
                C.act(sq[:], sap, AF.Square, r=[f"l1f_xi{c}s{st}"], w=[f"ln{st}.sq{c % 2}"])
                C.mm(bS, C.ones_bf[:], sap, start=(c == 0), stop=(c == KC - 1),
                     r=[f"l1f_xi{c}s{st}", "onesb"])
                C.mm(bQ, C.ones_bf[:], sq[:], start=(c == 0), stop=(c == KC - 1),
                     r=[f"ln{st}.sq{c % 2}", "onesb"])
                srcs.append((sap, f"l1f_xi{c}s{st}"))
            keys = ln_tilemath(C, bS, bQ, pi=st)
            mu_t, rstd_t = C.lnst[st]["mub"], C.lnst[st]["scb"]
            for dc in range(KC):
                sap = srcs[dc][0]
                d1 = st_d[dc % 2]
                C.tt(d1[:], sap, mu_t[:], ALU.subtract, r=[f"l1f_xi{dc}s{st}", keys["mu"]],
                     w=[kd[dc % 2]])
                C.tt(d1[:], d1[:], rstd_t[:], ALU.mult, r=[kd[dc % 2], keys["sc"]],
                     w=[kd[dc % 2]])
                bg = C.rb(0, 6)
                C.mm(bg, lngb[0:8, dc * 128:(dc + 1) * 128],
                     onehot[0:8, st * TILE:(st + 1) * TILE], start=True, stop=True,
                     r=["l1f_lng", "l1f_oh"])
                d2 = st_d[2 + dc % 2]
                C.tt(d2[:], d1[:], C.bank(bg), ALU.mult, r=[kd[dc % 2], f"PB{bg}"],
                     w=[kd[2 + dc % 2]])
                bb = C.rb(0, 6)
                C.mm(bb, lnbb[0:8, dc * 128:(dc + 1) * 128],
                     onehot[0:8, st * TILE:(st + 1) * TILE], start=True, stop=True,
                     r=["l1f_lnb", "l1f_oh"])
                C.tt(xln[dc][:, st * TILE:(st + 1) * TILE], d2[:], C.bank(bb), ALU.add,
                     r=[kd[2 + dc % 2], f"PB{bb}"], w=[f"l1f_xln{dc}s{st}"])
        for e in range(8):
            for g in range(4):
                wb = w1b[g % 2]
                wk = f"l1f_w1{g % 2}"
                C.dma("P", wb[:].rearrange("p (k f) -> p k f", k=KC),
                      W1[e][:, g * 1024:(g + 1) * 1024].rearrange("(k p) f -> p k f", p=128),
                      r=[], w=[wk])
                C.dma("P", w2b[:].rearrange("p (k f) -> p k f", k=KC),
                      W2[e][g * 1024:(g + 1) * 1024, :].rearrange("(k p) f -> p k f", p=128),
                      r=[], w=["l1f_w2k"])
                if g == 0:
                    for st in range(2):
                        bm = C.rb(0, 6)
                        C.mm(bm, selb[0:8, e * 128:(e + 1) * 128],
                             onehot[0:8, st * TILE:(st + 1) * TILE], start=True, stop=True,
                             r=["l1f_sel", "l1f_oh"])
                        C.copy(maskb[:, st * TILE:(st + 1) * TILE], C.bank(bm),
                               r=[f"PB{bm}"], w=["l1f_mb"], q="S")
                for fcl in range(8):
                    b0, b1_ = C.rb(0, 6), C.rb(0, 6)
                    for k in range(KC):
                        lhs = wb[:, k * 1024 + fcl * 128: k * 1024 + (fcl + 1) * 128]
                        C.mm(b0, lhs, xln[k][:, 0:TILE],
                             start=(k == 0), stop=(k == KC - 1), r=[wk, f"l1f_xln{k}s0"])
                        C.mm(b1_, lhs, xln[k][:, TILE:2 * TILE],
                             start=(k == 0), stop=(k == KC - 1), r=[wk, f"l1f_xln{k}s1"])
                    for st, bi in ((0, b0), (1, b1_)):
                        d1 = st_d[(2 * fcl + st) % 4]
                        kdx = kd[(2 * fcl + st) % 4]
                        C.act(d1[:], C.bank(bi), AF.Relu,
                              bias=b1e[:, e * 32 + g * 8 + fcl: e * 32 + g * 8 + fcl + 1],
                              r=[f"PB{bi}", "l1f_b1e"], w=[kdx])
                        C.tt(Hm[fcl][:, st * TILE:(st + 1) * TILE], d1[:],
                             maskb[:, st * TILE:(st + 1) * TILE],
                             ALU.mult, r=[kdx, "l1f_mb"], w=[f"l1f_h{fcl}"])
                for dc in range(KC):
                    b0, b1_ = C.rb(0, 6), C.rb(0, 6)
                    for fcl in range(8):
                        lhs = w2b[:, fcl * 1024 + dc * 128: fcl * 1024 + (dc + 1) * 128]
                        C.mm(b0, lhs, Hm[fcl][:, 0:TILE],
                             start=(fcl == 0), stop=(fcl == 7), r=["l1f_w2k", f"l1f_h{fcl}"])
                        C.mm(b1_, lhs, Hm[fcl][:, TILE:2 * TILE],
                             start=(fcl == 0), stop=(fcl == 7), r=["l1f_w2k", f"l1f_h{fcl}"])
                    for st, bo in ((0, b0), (1, b1_)):
                        fap = ffn[dc][:, st * TILE:(st + 1) * TILE]
                        C.tt(fap, fap, C.bank(bo), ALU.add,
                             r=[f"l1f_ffn{dc}", f"PB{bo}"], w=[f"l1f_ffn{dc}"])
        for st in range(2):
            for dc in range(KC):
                d1 = st_d[dc % 2]
                C.tt(d1[:], ffn[dc][:, st * TILE:(st + 1) * TILE],
                     alphab[:, st * TILE:(st + 1) * TILE], ALU.mult,
                     r=[f"l1f_ffn{dc}", "l1f_al"], w=[kd[dc % 2]])
                C.tt(xln[dc][:, st * TILE:(st + 1) * TILE], d1[:],
                     xi[dc][:, st * TILE:(st + 1) * TILE], ALU.add,
                     r=[kd[dc % 2], f"l1f_xi{dc}s{st}"], w=[f"l1f_xln{dc}c{st}"])
            srcs2 = [(xln[c][:, st * TILE:(st + 1) * TILE], f"l1f_xln{c}c{st}") for c in range(KC)]
            pi2 = st % 2
            bS2, bQ2 = C.rb(0, 6), C.rb(0, 6)
            for c in range(KC):
                sap2, skey2 = srcs2[c]
                sq2 = C.lnsq[pi2 * 2 + c % 2]
                sqk2 = f"ln{pi2}.sq{c % 2}"
                C.act(sq2[:], sap2, AF.Square, r=[skey2], w=[sqk2])
                C.mm(bS2, C.ones_bf[:], sap2, start=(c == 0), stop=(c == KC - 1), r=[skey2, "onesb"])
                C.mm(bQ2, C.ones_bf[:], sq2[:], start=(c == 0), stop=(c == KC - 1), r=[sqk2, "onesb"])
            keys2 = ln_tilemath(C, bS2, bQ2, pi=pi2)
            ln_apply_resid(C, srcs2, keys2, P0 + st * TILE, TILE)
